# version 07
# baseline (speedup 1.0000x reference)
import dataclasses
import numpy as np
import ml_dtypes
import concourse.bass as bass
import concourse.mybir as mybir
from concourse.bass_core_rust import IndirectOffsetOnAxis
from concourse.bass_utils import run_bass_kernel_spmd
from contextlib import ExitStack

F32 = mybir.dt.float32
BF16 = mybir.dt.bfloat16
U32 = mybir.dt.uint32
ALU = mybir.AluOpType
AF = mybir.ActivationFunctionType
AX = mybir.AxisListType

NTOK = 4096
D = 1024
NT = NTOK // 128
NDC = D // 128
LSTRIDE = 4352
EPS = 1e-5
NTOKG = 32768
I = 4096
CAP = 4096
NCC = CAP // 512
GPAD = 8448


def bcast_free(ap, n):
    return dataclasses.replace(ap, ap=type(ap.ap)([list(ap.ap[0]), [0, n]]))


class Plan:
    def __init__(self):
        self.items = []
        self.ctr = {}

    def on(self, eng, fn):
        self.items.append((eng, fn))

    def inc(self, name, k):
        self.ctr[name] = self.ctr.get(name, 0) + k
        return self.ctr[name]

    def val(self, name):
        return self.ctr.get(name, 0)

    def run(self, eng_name, eng):
        for e, fn in self.items:
            if e == eng_name:
                fn(eng)


def host_consts_l1(core_id):
    import ml_dtypes
    c = {}
    c["ident_f32"] = np.eye(128, dtype=np.float32)
    c["ident_bf16"] = np.eye(128, dtype=np.float32).astype(ml_dtypes.bfloat16)
    c["triu_s"] = np.triu(np.ones((128, 128), np.float32), k=1)
    c["iota_row"] = np.tile(np.arange(128, dtype=np.float32)[None, :], (128, 1))
    c["iotaP"] = np.arange(128, dtype=np.float32).reshape(128, 1)
    c["gidbase"] = np.full((128, 1), core_id * NTOK, np.float32)
    c["e4352"] = (np.arange(7, dtype=np.float32) * LSTRIDE).reshape(7, 1)
    c["onesrow"] = np.ones((1, 128), np.float32)
    c["onescol"] = np.ones((128, 1), np.float32)
    c["selb"] = np.eye(8, dtype=np.float32)[:, core_id:core_id + 1].copy()
    return c


def build_l1():
    nc = bass.Bass("TRN2", target_bir_lowering=False, debug=False)
    x_in = nc.dram_tensor("x_in", [NTOK, D], F32, kind="ExternalInput").ap()
    W_in = nc.dram_tensor("W_in", [D, D], F32, kind="ExternalInput").ap()
    battn = nc.dram_tensor("battn", [1, D], F32, kind="ExternalInput").ap()
    gamma = nc.dram_tensor("gamma", [1, D], F32, kind="ExternalInput").ap()
    beta = nc.dram_tensor("beta", [1, D], F32, kind="ExternalInput").ap()
    decay = nc.dram_tensor("decay", [1, 1], F32, kind="ExternalInput").ap()
    rw = nc.dram_tensor("rw", [8, D], F32, kind="ExternalInput").ap()
    rb = nc.dram_tensor("rb", [1, 8], F32, kind="ExternalInput").ap()
    cfull = nc.dram_tensor("cfull", [8, D], F32, kind="ExternalInput").ap()
    wmod = nc.dram_tensor("wmod", [D, 6 * D], F32, kind="ExternalInput").ap()
    ident_f32 = nc.dram_tensor("ident_f32", [128, 128], F32, kind="ExternalInput").ap()
    ident_bf16 = nc.dram_tensor("ident_bf16", [128, 128], BF16, kind="ExternalInput").ap()
    triu_s = nc.dram_tensor("triu_s", [128, 128], F32, kind="ExternalInput").ap()
    iota_row = nc.dram_tensor("iota_row", [128, 128], F32, kind="ExternalInput").ap()
    iotaP = nc.dram_tensor("iotaP", [128, 1], F32, kind="ExternalInput").ap()
    gidbase = nc.dram_tensor("gidbase", [128, 1], F32, kind="ExternalInput").ap()
    e4352 = nc.dram_tensor("e4352", [7, 1], F32, kind="ExternalInput").ap()
    onesrow = nc.dram_tensor("onesrow", [1, 128], F32, kind="ExternalInput").ap()
    onescol = nc.dram_tensor("onescol", [128, 1], F32, kind="ExternalInput").ap()
    selb = nc.dram_tensor("selb", [8, 1], F32, kind="ExternalInput").ap()
    xpp_o = nc.dram_tensor("xpp", [NTOK, D], F32, kind="ExternalOutput").ap()
    h2b_o = nc.dram_tensor("h2b", [NTOK, D], F32, kind="ExternalOutput").ap()
    w_o = nc.dram_tensor("wout", [NTOK, 8], F32, kind="ExternalOutput").ap()
    lists_o = nc.dram_tensor("lists", [7 * LSTRIDE, 1], F32, kind="ExternalOutput").ap()
    counts_o = nc.dram_tensor("counts", [7, 1], F32, kind="ExternalOutput").ap()
    gbrow_o = nc.dram_tensor("gbrow", [1, D], F32, kind="ExternalOutput").ap()
    ycs_o = nc.dram_tensor("ycs", [NTOK, D], BF16, kind="ExternalOutput").ap()
    gbh2_o = nc.dram_tensor("gbh2", [NTOK, D], F32, kind="ExternalOutput").ap()

    P = Plan()
    stack = ExitStack()
    with stack:
        idf = stack.enter_context(nc.sbuf_tensor("cid_f", [128, 128], F32))
        idb = stack.enter_context(nc.sbuf_tensor("cid_b", [128, 128], BF16))
        ctriu = stack.enter_context(nc.sbuf_tensor("ctriu", [128, 128], F32))
        ciota = stack.enter_context(nc.sbuf_tensor("ciota", [128, 128], F32))
        ciop = stack.enter_context(nc.sbuf_tensor("ciop", [128, 1], F32))
        cgidb = stack.enter_context(nc.sbuf_tensor("cgidb", [128, 1], F32))
        ce4352 = stack.enter_context(nc.sbuf_tensor("ce4352", [7, 1], F32))
        conesr = stack.enter_context(nc.sbuf_tensor("conesr", [1, 128], F32))
        conesc = stack.enter_context(nc.sbuf_tensor("conesc", [128, 1], F32))
        cselb = stack.enter_context(nc.sbuf_tensor("cselb", [8, 1], F32))
        xnT = stack.enter_context(nc.sbuf_tensor("xnT", [128, NDC, NTOK], BF16))
        WTb = stack.enter_context(nc.sbuf_tensor("WTb", [128, NDC, D], BF16))
        Gt = stack.enter_context(nc.sbuf_tensor("Gt", [128, D], F32))
        Bt = stack.enter_context(nc.sbuf_tensor("Bt", [128, D], F32))
        cb1t = stack.enter_context(nc.sbuf_tensor("cb1t", [128, D], F32))
        sbt = stack.enter_context(nc.sbuf_tensor("sbt", [128, D], F32))
        gbt = stack.enter_context(nc.sbuf_tensor("gbt", [128, D], F32))
        rbt = stack.enter_context(nc.sbuf_tensor("rbt", [128, 8], F32))
        rwTb = stack.enter_context(nc.sbuf_tensor("rwTb", [128, NDC, 8], BF16))
        g2 = stack.enter_context(nc.sbuf_tensor("g2", [2, NTOK], BF16))
        vb2 = stack.enter_context(nc.sbuf_tensor("vb2", [2, D], BF16))
        stat_mu = stack.enter_context(nc.sbuf_tensor("stat_mu", [128, NT], F32))
        stat_rs = stack.enter_context(nc.sbuf_tensor("stat_rs", [128, NT], F32))
        stat_s1 = stack.enter_context(nc.sbuf_tensor("stat_s1", [128, NT], F32))
        stat_sq = stack.enter_context(nc.sbuf_tensor("stat_sq", [128, NT], F32))
        gid0 = stack.enter_context(nc.sbuf_tensor("gid0", [128, 1], F32))
        acol = stack.enter_context(nc.sbuf_tensor("acol", [128, 1], F32))
        asc = stack.enter_context(nc.sbuf_tensor("asc", [1, 4], F32))
        psY0 = stack.enter_context(nc.psum_tensor("psY0", [128, 512], F32))
        psY1 = stack.enter_context(nc.psum_tensor("psY1", [128, 512], F32))
        psTB0 = stack.enter_context(nc.psum_tensor("psTB0", [128, 128], BF16))
        psTB1 = stack.enter_context(nc.psum_tensor("psTB1", [128, 128], BF16))
        psOf = stack.enter_context(nc.psum_tensor("psOf", [7, 1], F32))
        psRT = stack.enter_context(nc.psum_tensor("psRT", [128, 512], F32))
        SQ = stack.enter_context(nc.semaphore("sq"))
        GQ = stack.enter_context(nc.semaphore("gq"))
        SC = stack.enter_context(nc.semaphore("sc"))
        TM = stack.enter_context(nc.semaphore("tm"))
        VM = stack.enter_context(nc.semaphore("vm"))
        SM = stack.enter_context(nc.semaphore("sm"))
        XQ0 = stack.enter_context(nc.semaphore("xq0"))
        XQ1 = stack.enter_context(nc.semaphore("xq1"))
        WQ0 = stack.enter_context(nc.semaphore("wq0"))
        WQ1 = stack.enter_context(nc.semaphore("wq1"))
        SOX0 = stack.enter_context(nc.semaphore("sox0"))
        SOX1 = stack.enter_context(nc.semaphore("sox1"))
        SOW0 = stack.enter_context(nc.semaphore("sow0"))
        SOW1 = stack.enter_context(nc.semaphore("sow1"))
        SOH0 = stack.enter_context(nc.semaphore("soh0"))
        SOH1 = stack.enter_context(nc.semaphore("soh1"))
        LZ = stack.enter_context(nc.semaphore("lz"))
        block = stack.enter_context(nc.Block())

        SEMS = {"sq": SQ, "gq": GQ, "sc": SC, "tm": TM, "vm": VM, "sm": SM,
                "xq0": XQ0, "xq1": XQ1, "wq0": WQ0, "wq1": WQ1,
                "sox0": SOX0, "sox1": SOX1, "sow0": SOW0, "sow1": SOW1,
                "soh0": SOH0, "soh1": SOH1, "lz": LZ}

        def s_dma(out, in_, sem="sq"):
            S = SEMS[sem]
            def f(s, out=out, in_=in_, S=S):
                s.dma_start(out=out, in_=in_).then_inc(S, 16)
            P.on("sync", f)
            return P.inc(sem, 16)

        def s_barrier(sem="sq"):
            wait("sync", sem, P.val(sem))
            return P.val(sem)

        def g_dma(out, in_):
            def f(g, out=out, in_=in_):
                g.dma_start(out=out, in_=in_).then_inc(GQ, 16)
            P.on("gpsimd", f)
            return P.inc("gq", 16)

        def _eng_op(eng, sem_name, fn, tick):
            prev = P.val(sem_name)
            S = SEMS[sem_name]
            if prev > 0:
                def w(e, S=S, prev=prev):
                    e.wait_ge(S, prev)
                P.on(eng, w)
            if tick:
                n = P.inc(sem_name, 1)
                def f(e, fn=fn, S=S):
                    fn(e).then_inc(S, 1)
            else:
                n = prev
                def f(e, fn=fn):
                    fn(e)
            P.on(eng, f)
            return n

        def t_op(fn, tick=True):
            return _eng_op("tensor", "tm", fn, tick)

        def v_op(fn, tick=True):
            return _eng_op("vector", "vm", fn, tick)

        def sc_op(fn, tick=True):
            return _eng_op("scalar", "sm", fn, tick)

        def wait(eng_name, sem_name, val):
            if val <= 0:
                return
            sem = SEMS[sem_name]
            def f(e, sem=sem, val=val):
                e.wait_ge(sem, val)
            P.on(eng_name, f)

        def waits(eng_name, **kw):
            for k, v in kw.items():
                wait(eng_name, k, v)

        prep = ExitStack()
        msa = prep.enter_context(nc.sbuf_tensor("msa", [1, D], F32))
        mca = prep.enter_context(nc.sbuf_tensor("mca", [1, D], F32))
        mga = prep.enter_context(nc.sbuf_tensor("mga", [1, D], F32))
        msb = prep.enter_context(nc.sbuf_tensor("msb", [1, D], F32))
        mcb = prep.enter_context(nc.sbuf_tensor("mcb", [1, D], F32))
        mgb = prep.enter_context(nc.sbuf_tensor("mgb", [1, D], F32))
        r0 = prep.enter_context(nc.sbuf_tensor("r0", [1, D], F32))
        r1 = prep.enter_context(nc.sbuf_tensor("r1", [1, D], F32))
        modst = prep.enter_context(nc.sbuf_tensor("modst", [8, 512], F32))
        cstage = prep.enter_context(nc.sbuf_tensor("cstage", [8, D], F32))
        scT = prep.enter_context(nc.sbuf_tensor("scT", [128, 8, 8], F32))
        wmt = prep.enter_context(nc.sbuf_tensor("wmt", [128, 2, 512], F32))
        wst = prep.enter_context(nc.sbuf_tensor("wst", [128, 2, 128], F32))
        wtf = prep.enter_context(nc.sbuf_tensor("wtf", [128, D], F32))
        safm = prep.enter_context(nc.sbuf_tensor("safm", [128, 8], F32))
        ca1fm = prep.enter_context(nc.sbuf_tensor("ca1fm", [128, 8], F32))
        wbar = prep.enter_context(nc.sbuf_tensor("wbar", [128, 8], F32))
        vrow = prep.enter_context(nc.sbuf_tensor("vrow", [1, D], F32))
        bcrow = prep.enter_context(nc.sbuf_tensor("bcrow", [1, D], F32))
        smallsc = prep.enter_context(nc.sbuf_tensor("smallsc", [8, 8], F32))

        for dst, src in [(idf, ident_f32), (idb, ident_bf16), (ctriu, triu_s),
                         (ciota, iota_row), (ciop, iotaP), (cgidb, gidbase),
                         (ce4352, e4352), (conesr, onesrow), (conesc, onescol),
                         (cselb, selb)]:
            sl = tuple(slice(None) for _ in range(len(dst.shape)))
            s_dma(dst[sl], src[sl])
        s_dma(cstage[0:8, :], cfull[:, :])
        s_dma(r0[:, :], gamma[:, :])
        s_dma(r1[:, :], beta[:, :])
        s_dma(asc[0:1, 0:1], decay[:, :])
        s_dma(smallsc[0:1, 0:8], rb[:, :])
        c0 = s_barrier()

        waits("scalar", sq=c0)
        sc_op(lambda s: s.activation(out=cstage[0:8, :], in_=cstage[0:8, :], func=AF.Silu))
        tA = P.val("sm")
        waits("tensor", sm=tA)
        for dc in range(NDC):
            waits("tensor", vm=P.val("vm"))
            t_op(lambda t, dc=dc: t.transpose(psRT[0:128, 0:8], cstage[0:8, dc * 128:(dc + 1) * 128], idf[0:8, 0:8]))
            waits("vector", tm=P.val("tm"))
            v_op(lambda v, dc=dc: v.tensor_copy(out=scT[:, dc, :], in_=psRT[0:128, 0:8]))
        vSct = P.val("vm")
        waits("tensor", vm=vSct)
        rows6 = [msa, mca, mga, msb, mcb, mgb]
        mmticks = []
        for nch in range(12):
            waits("tensor", vm=P.val("vm"))
            for dc in range(NDC):
                it = nch * NDC + dc
                buf = it % 2
                if it >= 2:
                    wait("sync", "tm", mmticks[it - 2])
                ld = s_dma(wmt[:, buf, :], wmod[dc * 128:(dc + 1) * 128, nch * 512:(nch + 1) * 512], sem="wq%d" % buf)
                waits("tensor", **{("wq%d" % buf): ld})
                mmticks.append(t_op(lambda t, dc=dc, buf=buf: t.matmul(
                    psY0[0:8, :], scT[:, dc, :], wmt[:, buf, :],
                    start=(dc == 0), stop=(dc == NDC - 1))))
            waits("vector", tm=P.val("tm"))
            v_op(lambda v: v.tensor_copy(out=modst[0:8, :], in_=psY0[0:8, :]))
            waits("tensor", vm=P.val("vm"))
            t_op(lambda t: t.matmul(psRT[0:1, 0:512], cselb[:, :], modst[0:8, :], start=True, stop=True))
            waits("vector", tm=P.val("tm"))
            i, half = nch // 2, nch % 2
            v_op(lambda v, i=i, half=half: v.tensor_copy(
                out=rows6[i][0:1, half * 512:half * 512 + 512], in_=psRT[0:1, 0:512]))
        for gv in (mga, mgb):
            v_op(lambda v, gv=gv: v.tensor_scalar(out=bcrow[:, :], in0=gv[0:1, :], scalar1=0.0, scalar2=None, op0=ALU.is_ge))
            v_op(lambda v: v.tensor_scalar(out=bcrow[:, :], in0=bcrow[:, :], scalar1=2.0, scalar2=-1.0, op0=ALU.mult, op1=ALU.add))
            waits("scalar", vm=P.val("vm"))
            sc_op(lambda s, gv=gv: s.activation(out=vrow[:, :], in_=gv[0:1, :], func=AF.Abs))
            waits("vector", sm=P.val("sm"))
            v_op(lambda v: v.tensor_scalar_max(out=vrow[:, :], in0=vrow[:, :], scalar1=0.25))
            v_op(lambda v, gv=gv: v.tensor_mul(out=gv[0:1, :], in0=bcrow[:, :], in1=vrow[:, :]))
        v_op(lambda v: v.tensor_mul(out=r0[:, :], in0=r0[:, :], in1=mga[0:1, :]))
        v_op(lambda v: v.tensor_mul(out=r1[:, :], in0=r1[:, :], in1=mga[0:1, :]))
        v_op(lambda v: v.tensor_scalar_add(out=mcb[0:1, :], in0=mcb[0:1, :], scalar1=1.0))
        vClamp = P.val("vm")
        wait("sync", "vm", vClamp)
        s_dma(gbrow_o[:, :], mgb[0:1, :])
        waits("tensor", vm=vClamp)
        for row, tile in [(r0, Gt), (r1, Bt), (mcb, cb1t), (msb, sbt), (mgb, gbt)]:
            for half in range(2):
                waits("tensor", vm=P.val("vm"))
                t_op(lambda t, row=row, half=half: t.matmul(
                    psY1[:, :], conesr[:, :], row[0:1, half * 512:half * 512 + 512],
                    start=True, stop=True))
                waits("vector", tm=P.val("tm"))
                v_op(lambda v, tile=tile, half=half: v.tensor_copy(
                    out=tile[:, half * 512:half * 512 + 512], in_=psY1[:, :]))
        waits("tensor", vm=P.val("vm"))
        t_op(lambda t: t.matmul(psRT[:, 0:8], conesr[:, :], smallsc[0:1, 0:8], start=True, stop=True))
        waits("vector", tm=P.val("tm"))
        v_op(lambda v: v.tensor_copy(out=rbt[:, :], in_=psRT[:, 0:8]))

        waits("scalar", sq=c0)
        sc_op(lambda s: s.activation(out=asc[0:1, 1:2], in_=asc[0:1, 0:1], func=AF.Sigmoid))
        sA = P.val("sm")
        waits("vector", sm=sA)
        v_op(lambda v: v.memset(asc[0:1, 2:3], 1.0))
        v_op(lambda v: v.tensor_tensor_scan(
            g2[0:1, :], bcast_free(asc[0:1, 1:2], NTOK), bcast_free(asc[0:1, 2:3], NTOK),
            0.0, ALU.mult, ALU.add))
        vG = P.val("vm")
        wait("gpsimd", "vm", vG)
        for seg in range(NTOK // 128):
            g_dma(g2[1:2, seg * 128:(seg + 1) * 128], conesr[0:1, :])
        waits("tensor", vm=vG)
        waits("tensor", vm=P.val("vm"))
        t_op(lambda t: t.matmul(psRT[:, 8:9], conesr[:, :], asc[0:1, 1:2], start=True, stop=True))
        waits("vector", tm=P.val("tm"))
        v_op(lambda v: v.tensor_copy(out=acol[:, :], in_=psRT[:, 8:9]))
        v_op(lambda v: v.tensor_add(out=gid0[:, :], in0=ciop[:, :], in1=cgidb[:, :]))

        wait("sync", "vm", vSct)
        ldrw = s_dma(cstage[0:8, :], rw[:, :])
        s_barrier()
        waits("tensor", sq=ldrw, vm=vClamp)
        for dc in range(NDC):
            waits("tensor", vm=P.val("vm"))
            t_op(lambda t, dc=dc: t.transpose(psRT[0:128, 24:32], cstage[0:8, dc * 128:(dc + 1) * 128], idf[0:8, 0:8]))
            waits("vector", tm=P.val("tm"))
            v_op(lambda v, dc=dc: v.tensor_copy(out=rwTb[:, dc, :], in_=psRT[0:128, 24:32]))
            waits("tensor", vm=P.val("vm"))
            t_op(lambda t, dc=dc: t.transpose(psRT[0:128, 16:17], msa[0:1, dc * 128:(dc + 1) * 128], idf[0:1, 0:1]))
            waits("vector", tm=P.val("tm"))
            v_op(lambda v, dc=dc: v.tensor_copy(out=safm[:, dc:dc + 1], in_=psRT[0:128, 16:17]))
            waits("tensor", vm=P.val("vm"))
            t_op(lambda t, dc=dc: t.transpose(psRT[0:128, 17:18], mca[0:1, dc * 128:(dc + 1) * 128], idf[0:1, 0:1]))
            waits("vector", tm=P.val("tm"))
            v_op(lambda v, dc=dc: v.tensor_copy(out=ca1fm[:, dc:dc + 1], in_=psRT[0:128, 17:18]))
        v_op(lambda v: v.tensor_scalar_add(out=ca1fm[:, :], in0=ca1fm[:, :], scalar1=1.0))
        vFM = P.val("vm")
        waits("tensor", vm=vFM)
        for dc in range(NDC):
            for pc in range(NDC):
                buf = pc % 2
                if pc >= 2 or dc > 0:
                    wait("sync", "tm", P.val("tm"))
                ld = s_dma(wst[:, buf, :], W_in[pc * 128:(pc + 1) * 128, dc * 128:(dc + 1) * 128], sem="wq%d" % buf)
                waits("tensor", **{("wq%d" % buf): ld}, vm=P.val("vm"))
                t_op(lambda t, buf=buf: t.transpose(psY0[:, 0:128], wst[:, buf, :], idf[:, :]))
                waits("vector", tm=P.val("tm"))
                v_op(lambda v, pc=pc: v.tensor_copy(out=wtf[:, pc * 128:(pc + 1) * 128], in_=psY0[:, 0:128]))
            vW = P.val("vm")
            waits("tensor", vm=vW)
            t_op(lambda t, dc=dc: t.matmul(
                psRT[0:1, 0:512], safm[:, dc:dc + 1], wtf[:, 0:512],
                start=(dc == 0), stop=(dc == NDC - 1), skip_group_check=True))
            t_op(lambda t, dc=dc: t.matmul(
                psY1[0:1, 0:512], safm[:, dc:dc + 1], wtf[:, 512:1024],
                start=(dc == 0), stop=(dc == NDC - 1), skip_group_check=True))
            waits("vector", tm=P.val("tm"))
            v_op(lambda v, dc=dc: v.tensor_scalar_mul(out=wtf[:, :], in0=wtf[:, :], scalar1=ca1fm[:, dc:dc + 1]))
            v_op(lambda v, dc=dc: v.reduce_sum(wbar[:, dc:dc + 1], wtf[:, :], axis=AX.X))
            v_op(lambda v, dc=dc: v.tensor_scalar_mul(out=wbar[:, dc:dc + 1], in0=wbar[:, dc:dc + 1], scalar1=1.0 / D))
            v_op(lambda v, dc=dc: v.tensor_scalar(out=wtf[:, :], in0=wtf[:, :], scalar1=wbar[:, dc:dc + 1], scalar2=None, op0=ALU.subtract))
            v_op(lambda v, dc=dc: v.tensor_copy(out=WTb[:, dc, :], in_=wtf[:, :]))
        waits("vector", tm=P.val("tm"))
        v_op(lambda v: v.tensor_copy(out=vrow[0:1, 0:512], in_=psRT[0:1, 0:512]))
        v_op(lambda v: v.tensor_copy(out=vrow[0:1, 512:1024], in_=psY1[0:1, 0:512]))
        ldb2 = s_dma(bcrow[:, :], battn[:, :])
        s_barrier()
        v_op(lambda v: v.reduce_sum(smallsc[0:1, 1:2], vrow[:, :], axis=AX.X))
        v_op(lambda v: v.tensor_scalar_mul(out=smallsc[0:1, 1:2], in0=smallsc[0:1, 1:2], scalar1=1.0 / D))
        v_op(lambda v: v.tensor_scalar(out=vrow[:, :], in0=vrow[:, :], scalar1=smallsc[0:1, 1:2], scalar2=None, op0=ALU.subtract))
        waits("vector", sq=ldb2)
        v_op(lambda v: v.reduce_sum(smallsc[0:1, 2:3], bcrow[:, :], axis=AX.X))
        v_op(lambda v: v.tensor_scalar_mul(out=smallsc[0:1, 2:3], in0=smallsc[0:1, 2:3], scalar1=1.0 / D))
        v_op(lambda v: v.tensor_scalar(out=bcrow[:, :], in0=bcrow[:, :], scalar1=smallsc[0:1, 2:3], scalar2=None, op0=ALU.subtract))
        vVB = P.val("vm")
        wait("gpsimd", "vm", vVB)
        g_dma(vb2[0:1, :], vrow[0:1, :])
        g_dma(vb2[1:2, :], bcrow[0:1, :])

        bar = dict(sq=P.val("sq"), gq=P.val("gq"), tm=P.val("tm"),
                   vm=P.val("vm"), sm=P.val("sm"))
        for eng in ("sync", "gpsimd", "tensor", "vector", "scalar"):
            waits(eng, **bar)
        prep.close()

        loop = ExitStack()
        xt = loop.enter_context(nc.sbuf_tensor("xt", [128, 2, D], F32))
        yct = loop.enter_context(nc.sbuf_tensor("yct", [128, 2, D], BF16))
        xnbf = loop.enter_context(nc.sbuf_tensor("xnbf", [128, D], BF16))
        sqscr = loop.enter_context(nc.sbuf_tensor("sqscr", [128, D], F32))
        acscr = loop.enter_context(nc.sbuf_tensor("acscr", [128, D], F32))
        usb = loop.enter_context(nc.sbuf_tensor("usb", [128, D], F32))
        xpt = loop.enter_context(nc.sbuf_tensor("xpt", [128, 2, D], F32))
        h2t = loop.enter_context(nc.sbuf_tensor("h2t", [128, 2, D], F32))
        gbscr = loop.enter_context(nc.sbuf_tensor("gbscr", [128, 2, D], F32))
        h2bf = loop.enter_context(nc.sbuf_tensor("h2bf", [128, D], BF16))
        h2fm = loop.enter_context(nc.sbuf_tensor("h2fm", [128, 2, 128], BF16))
        lgfm = loop.enter_context(nc.sbuf_tensor("lgfm", [8, 128], F32))
        lgall = loop.enter_context(nc.sbuf_tensor("lgall", [128, NT, 8], F32))
        statYa = loop.enter_context(nc.sbuf_tensor("statYa", [128, NT], F32))
        statYb = loop.enter_context(nc.sbuf_tensor("statYb", [128, NT], F32))
        strsy = loop.enter_context(nc.sbuf_tensor("strsy", [128, NT], F32))
        st2s = loop.enter_context(nc.sbuf_tensor("st2s", [128, NT], F32))
        st2q = loop.enter_context(nc.sbuf_tensor("st2q", [128, NT], F32))
        stmu2 = loop.enter_context(nc.sbuf_tensor("stmu2", [128, NT], F32))
        strs2 = loop.enter_context(nc.sbuf_tensor("strs2", [128, NT], F32))
        tv1 = loop.enter_context(nc.sbuf_tensor("tv1", [128, 8], F32))
        tv2 = loop.enter_context(nc.sbuf_tensor("tv2", [128, 8], F32))
        m1c = loop.enter_context(nc.sbuf_tensor("m1c", [128, 1], F32))
        nm1c = loop.enter_context(nc.sbuf_tensor("nm1c", [128, 1], F32))
        m2c = loop.enter_context(nc.sbuf_tensor("m2c", [128, 1], F32))
        rcpc = loop.enter_context(nc.sbuf_tensor("rcpc", [128, 1], F32))
        wtile = loop.enter_context(nc.sbuf_tensor("wtile", [128, 2, 8], F32))
        masksb = loop.enter_context(nc.sbuf_tensor("masksb", [128, 7], F32))
        Psb = loop.enter_context(nc.sbuf_tensor("Psb", [128, 7], F32))
        cmpst = loop.enter_context(nc.sbuf_tensor("cmpst", [128, 7], F32))
        offs_sb = loop.enter_context(nc.sbuf_tensor("offs_sb", [7, 1], F32))
        idxf = loop.enter_context(nc.sbuf_tensor("idxf", [7, 1], F32))
        idxu = loop.enter_context(nc.sbuf_tensor("idxu", [7, 1], U32))
        gidj = loop.enter_context(nc.sbuf_tensor("gidj", [128, 1], F32))
        ohbuf = loop.enter_context(nc.sbuf_tensor("ohbuf", [128, 2, 128], F32))
        cmpT = loop.enter_context(nc.sbuf_tensor("cmpT", [7, 2, 128], F32))
        psY2 = loop.enter_context(nc.psum_tensor("psY2", [128, 512], F32))
        psY3 = loop.enter_context(nc.psum_tensor("psY3", [128, 512], F32))

        vZ = v_op(lambda v: v.memset(acscr[:, :], 0.0))
        wait("sync", "vm", vZ)
        def zl(s):
            s.dma_start(out=lists_o[:, :], in_=acscr[:, 0:238]).then_inc(LZ, 16)
        P.on("sync", zl)
        P.inc("lz", 16)
        wait("vector", "lz", 16)
        wait("gpsimd", "lz", 16)

        sticks, vticks = [], []
        for j in range(NT):
            buf = j % 2
            if j >= 2:
                wait("sync", "sm", sticks[j - 2])
                wait("sync", "vm", vticks[j - 2])
            ld = s_dma(xt[:, buf, :], x_in[j * 128:(j + 1) * 128, :], sem="xq%d" % buf)
            waits("scalar", **{("xq%d" % buf): ld})
            sticks.append(sc_op(lambda s, j=j, buf=buf: s.activation(
                out=sqscr[:, :], in_=xt[:, buf, :], func=AF.Square,
                accum_out=stat_sq[:, j:j + 1])))
            waits("vector", **{("xq%d" % buf): ld})
            vticks.append(v_op(lambda v, j=j, buf=buf: v.tensor_scalar(
                out=acscr[:, :], in0=xt[:, buf, :], scalar1=1.0, scalar2=None,
                op0=ALU.mult, op1=ALU.add, accum_out=stat_s1[:, j:j + 1])))
        waits("vector", sm=P.val("sm"))
        v_op(lambda v: v.tensor_scalar_mul(out=stat_mu[:, :], in0=stat_s1[:, :], scalar1=1.0 / D))
        v_op(lambda v: v.tensor_scalar_mul(out=stat_sq[:, :], in0=stat_sq[:, :], scalar1=1.0 / D))
        v_op(lambda v: v.tensor_mul(out=stat_s1[:, :], in0=stat_mu[:, :], in1=stat_mu[:, :]))
        v_op(lambda v: v.tensor_sub(out=stat_sq[:, :], in0=stat_sq[:, :], in1=stat_s1[:, :]))
        v_op(lambda v: v.tensor_scalar_add(out=stat_sq[:, :], in0=stat_sq[:, :], scalar1=EPS))
        v_op(lambda v: v.reciprocal(out=stat_rs[:, :], in_=stat_sq[:, :]))
        waits("scalar", vm=P.val("vm"))
        sc_op(lambda s: s.activation(out=stat_rs[:, :], in_=stat_rs[:, :], func=AF.Sqrt))
        sStats = P.val("sm")

        waits("vector", sm=sStats)
        wait("sync", "vm", P.val("vm"))
        wait("sync", "sm", P.val("sm"))
        xnticks = []
        trC = [0]
        pstTick = {0: 0, 1: 0}
        for j in range(NT):
            buf = j % 2
            if j >= 2:
                wait("sync", "vm", xnticks[j - 2][0])
                wait("sync", "sm", xnticks[j - 2][1])
            ld = s_dma(xt[:, buf, :], x_in[j * 128:(j + 1) * 128, :], sem="xq%d" % buf)
            waits("vector", **{("xq%d" % buf): ld})
            wait("vector", "tm", trC[0])
            v_op(lambda v, j=j, buf=buf: v.tensor_scalar(
                out=xnbf[:, :], in0=xt[:, buf, :], scalar1=stat_mu[:, j:j + 1],
                scalar2=stat_rs[:, j:j + 1], op0=ALU.subtract, op1=ALU.mult))
            waits("tensor", vm=P.val("vm"))
            for dc in range(NDC):
                pp = dc % 2
                pst = psTB0 if pp == 0 else psTB1
                waits("tensor", sm=pstTick[pp])
                t_op(lambda t, dc=dc, pst=pst: t.transpose(
                    pst[:, :], xnbf[:, dc * 128:(dc + 1) * 128], idb[:, :]))
                waits("scalar", tm=P.val("tm"))
                pstTick[pp] = sc_op(lambda s, j=j, dc=dc, pst=pst: s.copy(
                    out=xnT[:, dc, j * 128:(j + 1) * 128], in_=pst[:, :]))
            trC[0] = P.val("tm")
            xnticks.append((P.val("vm"), P.val("sm")))

        waits("vector", sm=P.val("sm"))
        for dc in range(NDC):
            v_op(lambda v, dc=dc: v.tensor_tensor_scan(
                xnT[:, dc, :], bcast_free(acol[:, 0:1], NTOK),
                xnT[:, dc, :], 0.0, ALU.mult, ALU.add))
        vScan = P.val("vm")

        wait("tensor", "vm", vScan)
        evT, scT4 = [], []
        soh_cnt = [0, 0]
        sow_cnt = [0, 0]
        for j in range(NT):
            pb = j % 2
            psA_ = psY0 if pb == 0 else psY2
            psB_ = psY1 if pb == 0 else psY3
            if j >= 2:
                wait("tensor", "vm", evT[j - 2])
                wait("tensor", "sm", scT4[j - 2])
            for ec in range(2):
                psY = psA_ if ec == 0 else psB_
                for dc in range(NDC):
                    t_op(lambda t, j=j, dc=dc, ec=ec, psY=psY: t.matmul(
                        psY[:, :], xnT[:, dc, j * 128:(j + 1) * 128],
                        WTb[:, dc, ec * 512:(ec + 1) * 512],
                        start=(dc == 0), stop=False), tick=False)
                t_op(lambda t, j=j, ec=ec, psY=psY: t.matmul(
                    psY[:, :], g2[0:2, j * 128:(j + 1) * 128],
                    vb2[0:2, ec * 512:(ec + 1) * 512], start=False, stop=True))
            tY = P.val("tm")
            waits("scalar", tm=tY)
            sc_op(lambda s, j=j, psA_=psA_: s.activation(out=sqscr[:, 0:512], in_=psA_[:, :], func=AF.Square, accum_out=statYa[:, j:j + 1]))
            sc_op(lambda s, j=j, psB_=psB_: s.activation(out=sqscr[:, 512:1024], in_=psB_[:, :], func=AF.Square, accum_out=statYb[:, j:j + 1]))
            scT4.append(P.val("sm"))
            waits("vector", tm=tY, sm=P.val("sm"))
            if j >= 2:
                wait("vector", "soh%d" % pb, 16 * soh_cnt[pb])
            v_op(lambda v, pb=pb, psA_=psA_: v.tensor_copy(out=yct[:, pb, 0:512], in_=psA_[:, :]))
            v_op(lambda v, pb=pb, psB_=psB_: v.tensor_copy(out=yct[:, pb, 512:1024], in_=psB_[:, :]))
            evT.append(P.val("vm"))
            wait("sync", "vm", evT[j])
            s_dma(ycs_o[j * 128:(j + 1) * 128, :], yct[:, pb, :], sem="soh%d" % pb)
            soh_cnt[pb] += 1
        waits("vector", sm=P.val("sm"))
        v_op(lambda v: v.tensor_add(out=statYa[:, :], in0=statYa[:, :], in1=statYb[:, :]))
        v_op(lambda v: v.tensor_scalar(out=statYa[:, :], in0=statYa[:, :], scalar1=1.0 / D, scalar2=EPS, op0=ALU.mult, op1=ALU.add))
        v_op(lambda v: v.reciprocal(out=strsy[:, :], in_=statYa[:, :]))
        waits("scalar", vm=P.val("vm"))
        sc_op(lambda s: s.activation(out=strsy[:, :], in_=strsy[:, :], func=AF.Sqrt))
        sRsy = P.val("sm")

        wait("sync", "vm", P.val("vm"))
        wait("sync", "sm", P.val("sm"))
        wait("sync", "soh0", P.val("soh0"))
        wait("sync", "soh1", P.val("soh1"))
        waits("vector", sm=sRsy)
        p5vT = []
        p5sT = []
        sox_cnt = [0, 0]
        wq_cnt = [0, 0]
        xq_cnt = [0, 0]
        for j in range(NT):
            buf = j % 2
            if j >= 2:
                wait("sync", "vm", p5vT[j - 2])
                wait("sync", "sm", p5sT[j - 2])
            ldx = s_dma(xt[:, buf, :], x_in[j * 128:(j + 1) * 128, :], sem="xq%d" % buf)
            xq_cnt[buf] = ldx
            ldy = s_dma(yct[:, buf, :], ycs_o[j * 128:(j + 1) * 128, :], sem="wq%d" % buf)
            wq_cnt[buf] = ldy
            waits("vector", **{("xq%d" % buf): ldx, ("wq%d" % buf): ldy})
            v_op(lambda v, j=j, buf=buf: v.scalar_tensor_tensor(
                out=usb[:, :], in0=yct[:, buf, :], scalar=strsy[:, j:j + 1],
                in1=Gt[:, :], op0=ALU.mult, op1=ALU.mult))
            if j >= 2:
                wait("vector", "sox%d" % buf, 16 * sox_cnt[buf])
            v_op(lambda v, buf=buf: v.tensor_add(out=xpt[:, buf, :], in0=usb[:, :], in1=xt[:, buf, :]))
            v_op(lambda v, j=j, buf=buf: v.scalar_tensor_tensor(
                out=xpt[:, buf, :], in0=xpt[:, buf, :], scalar=1.0, in1=Bt[:, :],
                op0=ALU.mult, op1=ALU.add, accum_out=st2s[:, j:j + 1]))
            p5vT.append(P.val("vm"))
            waits("scalar", vm=P.val("vm"))
            sc_op(lambda s, j=j, buf=buf: s.activation(
                out=sqscr[:, :], in_=xpt[:, buf, :], func=AF.Square,
                accum_out=st2q[:, j:j + 1]))
            p5sT.append(P.val("sm"))
            wait("sync", "vm", p5vT[j])
            s_dma(xpp_o[j * 128:(j + 1) * 128, :], xpt[:, buf, :], sem="sox%d" % buf)
            sox_cnt[buf] += 1
        waits("vector", sm=P.val("sm"))
        v_op(lambda v: v.tensor_scalar_mul(out=stmu2[:, :], in0=st2s[:, :], scalar1=1.0 / D))
        v_op(lambda v: v.tensor_scalar_mul(out=st2q[:, :], in0=st2q[:, :], scalar1=1.0 / D))
        v_op(lambda v: v.tensor_mul(out=st2s[:, :], in0=stmu2[:, :], in1=stmu2[:, :]))
        v_op(lambda v: v.tensor_sub(out=st2q[:, :], in0=st2q[:, :], in1=st2s[:, :]))
        v_op(lambda v: v.tensor_scalar_add(out=st2q[:, :], in0=st2q[:, :], scalar1=EPS))
        v_op(lambda v: v.reciprocal(out=strs2[:, :], in_=st2q[:, :]))
        waits("scalar", vm=P.val("vm"))
        sc_op(lambda s: s.activation(out=strs2[:, :], in_=strs2[:, :], func=AF.Sqrt))
        sRs2 = P.val("sm")

        wait("sync", "vm", P.val("vm"))
        wait("sync", "sm", P.val("sm"))
        wait("sync", "sox0", P.val("sox0"))
        wait("sync", "sox1", P.val("sox1"))
        waits("vector", sm=sRs2)
        p6T = []
        for j in range(NT):
            buf = j % 2
            if j >= 2:
                wait("sync", "vm", p6T[j - 2][0])
                wait("sync", "sm", p6T[j - 2][1])
                wait("sync", "tm", p6T[j - 2][2])
            ldp = s_dma(xt[:, buf, :], xpp_o[j * 128:(j + 1) * 128, :], sem="xq%d" % buf)
            waits("vector", **{("xq%d" % buf): ldp})
            v_op(lambda v, j=j, buf=buf: v.tensor_scalar(
                out=usb[:, :], in0=xt[:, buf, :], scalar1=stmu2[:, j:j + 1],
                scalar2=strs2[:, j:j + 1], op0=ALU.subtract, op1=ALU.mult))
            if j >= 2:
                wait("vector", "soh%d" % buf, 16 * soh_cnt[buf])
                wait("vector", "sow%d" % buf, 16 * sow_cnt[buf])
            v_op(lambda v, buf=buf: v.scalar_tensor_tensor(
                out=h2t[:, buf, :], in0=usb[:, :], scalar=1.0, in1=cb1t[:, :],
                op0=ALU.mult, op1=ALU.mult))
            v_op(lambda v, buf=buf: v.tensor_add(out=h2t[:, buf, :], in0=h2t[:, buf, :], in1=sbt[:, :]))
            vH2 = P.val("vm")
            wait("sync", "vm", vH2)
            s_dma(h2b_o[j * 128:(j + 1) * 128, :], h2t[:, buf, :], sem="soh%d" % buf)
            soh_cnt[buf] += 1
            v_op(lambda v, buf=buf: v.tensor_mul(out=gbscr[:, buf, :], in0=h2t[:, buf, :], in1=gbt[:, :]))
            vGB = P.val("vm")
            wait("sync", "vm", vGB)
            s_dma(gbh2_o[j * 128:(j + 1) * 128, :], gbscr[:, buf, :], sem="sow%d" % buf)
            sow_cnt[buf] += 1
            wait("vector", "tm", trC[0])
            v_op(lambda v, buf=buf: v.tensor_copy(out=h2bf[:, :], in_=h2t[:, buf, :]))
            vBF = P.val("vm")
            waits("tensor", vm=vBF)
            for dc in range(NDC):
                pp = dc % 2
                pst = psTB0 if pp == 0 else psTB1
                waits("tensor", sm=pstTick[pp])
                t_op(lambda t, dc=dc, pst=pst: t.transpose(
                    pst[:, :], h2bf[:, dc * 128:(dc + 1) * 128], idb[:, :]))
                waits("scalar", tm=P.val("tm"))
                pstTick[pp] = sc_op(lambda s, dc=dc, pst=pst: s.copy(out=h2fm[:, dc % 2, :], in_=pst[:, :]))
                waits("tensor", sm=P.val("sm"))
                t_op(lambda t, dc=dc: t.matmul(
                    psRT[0:8, 0:128], rwTb[:, dc, :], h2fm[:, dc % 2, :],
                    start=(dc == 0), stop=(dc == NDC - 1), skip_group_check=True),
                    tick=(dc == NDC - 1))
            trC[0] = P.val("tm")
            waits("vector", tm=P.val("tm"))
            v_op(lambda v: v.tensor_copy(out=lgfm[:, :], in_=psRT[0:8, 0:128]))
            waits("tensor", vm=P.val("vm"))
            t_op(lambda t: t.transpose(psRT[0:128, 128:136], lgfm[:, :], idf[0:8, 0:8]))
            waits("vector", tm=P.val("tm"))
            v_op(lambda v, j=j: v.tensor_add(out=lgall[:, j, :], in0=psRT[0:128, 128:136], in1=rbt[:, :]))
            p6T.append((P.val("vm"), P.val("sm"), P.val("tm")))

        wait("sync", "vm", P.val("vm"))
        wait("sync", "sm", P.val("sm"))
        wait("sync", "sow0", P.val("sow0"))
        wait("sync", "sow1", P.val("sow1"))
        scatticks = []
        ohticks = {0: 0, 1: 0}
        p7T = []
        for j in range(NT):
            buf = j % 2
            if j >= 2:
                wait("sync", "vm", p7T[j - 2])
            ldp = s_dma(xt[:, buf, :], xpp_o[j * 128:(j + 1) * 128, :], sem="xq%d" % buf)
            ldg = s_dma(gbscr[:, buf, :], gbh2_o[j * 128:(j + 1) * 128, :], sem="wq%d" % buf)
            v_op(lambda v, j=j: v.reduce_max(m1c[:, :], lgall[:, j, :], axis=AX.X))
            v_op(lambda v: v.tensor_scalar_mul(out=nm1c[:, :], in0=m1c[:, :], scalar1=-1.0))
            v_op(lambda v, j=j: v.tensor_scalar(out=tv1[:, :], in0=lgall[:, j, :], scalar1=m1c[:, 0:1], scalar2=None, op0=ALU.is_equal))
            v_op(lambda v, j=j: v.scalar_tensor_tensor(out=tv2[:, :], in0=tv1[:, :], scalar=-1e30, in1=lgall[:, j, :], op0=ALU.mult, op1=ALU.add))
            v_op(lambda v: v.reduce_max(m2c[:, :], tv2[:, :], axis=AX.X))
            v_op(lambda v, j=j: v.tensor_scalar(out=tv1[:, :], in0=lgall[:, j, :], scalar1=m2c[:, 0:1], scalar2=None, op0=ALU.is_ge))
            waits("scalar", vm=P.val("vm"))
            sc_op(lambda s, j=j: s.activation(out=tv2[:, :], in_=lgall[:, j, :], func=AF.Exp, bias=nm1c[:, 0:1], scale=1.0))
            waits("vector", sm=P.val("sm"))
            v_op(lambda v: v.reduce_sum(rcpc[:, :], tv2[:, :], axis=AX.X))
            v_op(lambda v: v.reciprocal(out=rcpc[:, :], in_=rcpc[:, :]))
            if j >= 2:
                wait("vector", "sow%d" % buf, 16 * sow_cnt[buf])
            v_op(lambda v, buf=buf: v.scalar_tensor_tensor(
                out=wtile[:, buf, :], in0=tv2[:, :], scalar=rcpc[:, 0:1], in1=tv1[:, :],
                op0=ALU.mult, op1=ALU.mult))
            vW = P.val("vm")
            wait("sync", "vm", vW)
            s_dma(w_o[j * 128:(j + 1) * 128, :], wtile[:, buf, :], sem="sow%d" % buf)
            sow_cnt[buf] += 1
            if j >= 2:
                wait("vector", "sox%d" % buf, 16 * sox_cnt[buf])
            waits("vector", **{("xq%d" % buf): ldp, ("wq%d" % buf): ldg})
            v_op(lambda v, buf=buf: v.scalar_tensor_tensor(
                out=xpt[:, buf, :], in0=gbscr[:, buf, :], scalar=wtile[:, buf, 7:8],
                in1=xt[:, buf, :], op0=ALU.mult, op1=ALU.add))
            vXf = P.val("vm")
            wait("sync", "vm", vXf)
            s_dma(xpp_o[j * 128:(j + 1) * 128, :], xpt[:, buf, :], sem="sox%d" % buf)
            sox_cnt[buf] += 1
            v_op(lambda v, buf=buf: v.tensor_scalar(out=masksb[:, :], in0=wtile[:, buf, 0:7], scalar1=0.0, scalar2=None, op0=ALU.is_gt))
            waits("tensor", vm=P.val("vm"))
            t_op(lambda t: t.matmul(psRT[:, 256:263], ctriu[:, :], masksb[:, :], start=True, stop=True, skip_group_check=True))
            waits("vector", tm=P.val("tm"))
            v_op(lambda v: v.tensor_copy(out=Psb[:, :], in_=psRT[:, 256:263]))
            if j == 0:
                v_op(lambda v: v.memset(offs_sb[:, :], 0.0))
            else:
                v_op(lambda v: v.tensor_copy(out=offs_sb[:, :], in_=psOf[:, :]))
            waits("tensor", vm=P.val("vm"))
            t_op(lambda t, j=j: t.matmul(psOf[:, :], masksb[:, :], conesc[:, :],
                                         start=(j == 0), stop=(j == NT - 1), skip_group_check=True))
            v_op(lambda v, j=j: v.tensor_scalar_add(out=gidj[:, :], in0=gid0[:, :], scalar1=float(j * 128)))
            wait("vector", "sc", P.val("sc"))
            v_op(lambda v: v.tensor_add(out=idxf[:, :], in0=offs_sb[:, :], in1=ce4352[:, :]))
            v_op(lambda v: v.tensor_copy(out=idxu[:, :], in_=idxf[:, :]))
            for e in range(7):
                ob = e % 2
                wait("vector", "tm", ohticks[ob])
                v_op(lambda v, e=e, ob=ob: v.tensor_scalar(
                    out=ohbuf[:, ob, :], in0=ciota[:, :], scalar1=Psb[:, e:e + 1],
                    scalar2=masksb[:, e:e + 1], op0=ALU.is_equal, op1=ALU.mult))
                waits("tensor", vm=P.val("vm"))
                ohticks[ob] = t_op(lambda t, e=e, ob=ob: t.matmul(
                    psRT[:, 288 + e:289 + e], ohbuf[:, ob, :], gidj[:, :],
                    start=True, stop=True, skip_group_check=True))
            waits("vector", tm=P.val("tm"))
            v_op(lambda v: v.tensor_copy(out=cmpst[:, :], in_=psRT[:, 288:295]))
            waits("tensor", vm=P.val("vm"))
            t_op(lambda t: t.transpose(psRT[0:7, 320:448], cmpst[:, :], idf[:, :]))
            waits("vector", tm=P.val("tm"))
            if j >= 2:
                wait("vector", "sc", scatticks[j - 2])
            cbuf = j % 2
            v_op(lambda v, cbuf=cbuf: v.tensor_copy(out=cmpT[0:7, cbuf, :], in_=psRT[0:7, 320:448]))
            vT2 = P.val("vm")
            wait("gpsimd", "vm", vT2)
            wait("gpsimd", "sc", P.val("sc"))
            def scat(g, cbuf=cbuf):
                g.indirect_dma_start(
                    out=lists_o[:, :], out_offset=IndirectOffsetOnAxis(idxu[:, :], 0),
                    in_=cmpT[0:7, cbuf, :], in_offset=None).then_inc(SC, 16)
            P.on("gpsimd", scat)
            scatticks.append(P.inc("sc", 16))
            p7T.append(P.val("vm"))

        waits("vector", tm=P.val("tm"))
        vCnt = v_op(lambda v: v.tensor_copy(out=offs_sb[:, :], in_=psOf[:, :]))
        wait("sync", "vm", vCnt)
        s_dma(counts_o[:, :], offs_sb[:, :])

        wait("sync", "sq", P.val("sq"))
        for snm in ("soh0", "soh1", "sow0", "sow1", "sox0", "sox1", "xq0", "xq1", "wq0", "wq1"):
            wait("sync", snm, P.val(snm))
        wait("gpsimd", "sc", P.val("sc"))
        wait("gpsimd", "gq", P.val("gq"))
        loop.close()

        @block.sync
        def _(s):
            P.run("sync", s)

        @block.tensor
        def _(t):
            P.run("tensor", t)

        @block.vector
        def _(v):
            P.run("vector", v)

        @block.scalar
        def _(s):
            P.run("scalar", s)

        @block.gpsimd
        def _(g):
            P.run("gpsimd", g)

    return nc

def host_consts_l2(core_id):
    import ml_dtypes
    c = {}
    c["ident_bf16"] = np.eye(128, dtype=np.float32).astype(ml_dtypes.bfloat16)
    c["ident_f32b"] = np.eye(128, dtype=np.float32)
    c["trashrow"] = np.full((1, 8), GPAD, np.uint32)
    c["ident8"] = np.eye(8, dtype=np.float32)
    c["triu8"] = np.triu(np.ones((8, 8), np.float32), k=1)
    return c


def build_l2():
    nc = bass.Bass("TRN2", target_bir_lowering=False, debug=False)
    h2fa = nc.dram_tensor("h2fa", [NTOKG, 512], F32, kind="ExternalInput").ap()
    h2fb = nc.dram_tensor("h2fb", [NTOKG, 512], F32, kind="ExternalInput").ap()
    mylists = nc.dram_tensor("mylists", [8, LSTRIDE], F32, kind="ExternalInput").ap()
    counts_e = nc.dram_tensor("counts_e", [8, 1], F32, kind="ExternalInput").ap()
    wi = nc.dram_tensor("wi", [D, I], F32, kind="ExternalInput").ap()
    wo = nc.dram_tensor("wo", [I, D], F32, kind="ExternalInput").ap()
    ident_bf16 = nc.dram_tensor("ident_bf16", [128, 128], BF16, kind="ExternalInput").ap()
    ident_f32b = nc.dram_tensor("ident_f32b", [128, 128], F32, kind="ExternalInput").ap()
    trashrow = nc.dram_tensor("trashrow", [1, 8], U32, kind="ExternalInput").ap()
    ident8 = nc.dram_tensor("ident8", [8, 8], F32, kind="ExternalInput").ap()
    triu8 = nc.dram_tensor("triu8", [8, 8], F32, kind="ExternalInput").ap()
    eout = nc.dram_tensor("eout", [CAP, D], F32, kind="ExternalOutput").ap()
    glist_t = nc.dram_tensor("glist", [GPAD + 4096, 1], F32, kind="ExternalOutput")
    glist = glist_t.ap()
    glist2 = glist_t.ap()

    P = Plan()
    stack = ExitStack()
    with stack:
        idb = stack.enter_context(nc.sbuf_tensor("idb", [128, 128], BF16))
        idf2 = stack.enter_context(nc.sbuf_tensor("idf2", [128, 128], F32))
        idx2 = stack.enter_context(nc.sbuf_tensor("idx2", [2, 8], U32))
        seg2 = stack.enter_context(nc.sbuf_tensor("seg2", [2, LSTRIDE], F32))
        id8 = stack.enter_context(nc.sbuf_tensor("id8", [8, 8], F32))
        tr8 = stack.enter_context(nc.sbuf_tensor("tr8", [8, 8], F32))
        cnts = stack.enter_context(nc.sbuf_tensor("cnts", [8, 1], F32))
        cumo = stack.enter_context(nc.sbuf_tensor("cumo", [8, 1], F32))
        cumr = stack.enter_context(nc.sbuf_tensor("cumr", [1, 8], F32))
        cumru = stack.enter_context(nc.sbuf_tensor("cumru", [1, 8], U32))
        idlist = stack.enter_context(nc.sbuf_tensor("idlist", [128, 32], F32))
        idu = stack.enter_context(nc.sbuf_tensor("idu", [128, 32], U32))
        wi_sb = stack.enter_context(nc.sbuf_tensor("wi_sb", [128, 8, I], BF16))
        wo_sb = stack.enter_context(nc.sbuf_tensor("wo_sb", [128, 32, D], BF16))
        gbuf = stack.enter_context(nc.sbuf_tensor("gbuf", [128, 2, D], F32))
        h2fm = stack.enter_context(nc.sbuf_tensor("h2fm", [128, 8, 512], BF16))
        hbf = stack.enter_context(nc.sbuf_tensor("hbf", [128, 32, 512], BF16))
        outsb = stack.enter_context(nc.sbuf_tensor("outsb", [128, 2, D], F32))
        ps1a = stack.enter_context(nc.psum_tensor("ps1a", [128, 512], F32))
        ps1b = stack.enter_context(nc.psum_tensor("ps1b", [128, 512], F32))
        ps2a = stack.enter_context(nc.psum_tensor("ps2a", [128, 512], F32))
        ps2b = stack.enter_context(nc.psum_tensor("ps2b", [128, 512], F32))
        psta = stack.enter_context(nc.psum_tensor("psta", [128, 128], F32))
        pstb = stack.enter_context(nc.psum_tensor("pstb", [128, 128], F32))
        psm = stack.enter_context(nc.psum_tensor("psm", [8, 8], F32))
        SQ = stack.enter_context(nc.semaphore("sq"))
        GL = stack.enter_context(nc.semaphore("gl"))
        GA0 = stack.enter_context(nc.semaphore("ga0"))
        GA1 = stack.enter_context(nc.semaphore("ga1"))
        WIQ = stack.enter_context(nc.semaphore("wiq"))
        WOQ = stack.enter_context(nc.semaphore("woq"))
        EO0 = stack.enter_context(nc.semaphore("eo0"))
        EO1 = stack.enter_context(nc.semaphore("eo1"))
        TM = stack.enter_context(nc.semaphore("tm"))
        VM = stack.enter_context(nc.semaphore("vm"))
        SM = stack.enter_context(nc.semaphore("sm"))
        block = stack.enter_context(nc.Block())

        SEMS = {"sq": SQ, "gl": GL, "ga0": GA0, "ga1": GA1, "wiq": WIQ, "woq": WOQ,
                "eo0": EO0, "eo1": EO1, "tm": TM, "vm": VM, "sm": SM}

        def s_dma(out, in_, sem="sq"):
            S = SEMS[sem]
            def f(s, out=out, in_=in_, S=S):
                s.dma_start(out=out, in_=in_).then_inc(S, 16)
            P.on("sync", f)
            return P.inc(sem, 16)

        def g_dma(out, in_, sem):
            S = SEMS[sem]
            def f(g, out=out, in_=in_, S=S):
                g.dma_start(out=out, in_=in_).then_inc(S, 16)
            P.on("gpsimd", f)
            return P.inc(sem, 16)

        def _eng_op(eng, sem_name, fn, tick):
            prev = P.val(sem_name)
            S = SEMS[sem_name]
            if prev > 0:
                def w(e, S=S, prev=prev):
                    e.wait_ge(S, prev)
                P.on(eng, w)
            if tick:
                n = P.inc(sem_name, 1)
                def f(e, fn=fn, S=S):
                    fn(e).then_inc(S, 1)
            else:
                n = prev
                def f(e, fn=fn):
                    fn(e)
            P.on(eng, f)
            return n

        def t_op(fn, tick=True):
            return _eng_op("tensor", "tm", fn, tick)

        def v_op(fn, tick=True):
            return _eng_op("vector", "vm", fn, tick)

        def wait(eng_name, sem_name, val):
            if val <= 0:
                return
            sem = SEMS[sem_name]
            def f(e, sem=sem, val=val):
                e.wait_ge(sem, val)
            P.on(eng_name, f)

        def waits(eng_name, **kw):
            for k, v in kw.items():
                wait(eng_name, k, v)

        s_dma(idb[:, :], ident_bf16[:, :])
        s_dma(idf2[:, :], ident_f32b[:, :])
        s_dma(id8[:, :], ident8[:, :])
        s_dma(tr8[:, :], triu8[:, :])
        s_dma(cnts[:, :], counts_e[:, :])
        s_dma(idx2[1:2, :], trashrow[:, :])
        c0 = P.val("sq")
        wait("sync", "sq", c0)
        for pc in range(8):
            g_dma(wi_sb[:, pc, :], wi[pc * 128:(pc + 1) * 128, :], "wiq")
        wiT = P.val("wiq")
        for pc in range(32):
            g_dma(wo_sb[:, pc, :], wo[pc * 128:(pc + 1) * 128, :], "woq")
        woT = P.val("woq")

        waits("tensor", sq=c0)
        t_op(lambda t: t.matmul(psm[:, 0:1], tr8[:, :], cnts[:, :], start=True, stop=True))
        waits("vector", tm=P.val("tm"))
        v_op(lambda v: v.tensor_copy(out=cumo[:, :], in_=psm[:, 0:1]))
        v_op(lambda v: v.tensor_scalar_min(out=cumo[:, :], in0=cumo[:, :], scalar1=4224.0))
        waits("tensor", vm=P.val("vm"))
        t_op(lambda t: t.transpose(psm[0:1, 0:8], cumo[:, :], id8[:, :]))
        waits("vector", tm=P.val("tm"))
        v_op(lambda v: v.tensor_copy(out=cumr[:, :], in_=psm[0:1, 0:8]))
        v_op(lambda v: v.tensor_copy(out=cumru[:, :], in_=cumr[:, :]))
        vM0 = v_op(lambda v: v.memset(seg2[:, :], 0.0), tick=True)
        vCum = P.val("vm")
        wait("gpsimd", "vm", vCum)
        def _mvidx(g):
            g.dma_start(out=idx2[0:1, :], in_=cumru[0:1, :]).then_inc(GL, 16)
        P.on("gpsimd", _mvidx)
        P.inc("gl", 16)
        wait("gpsimd", "gl", P.val("gl"))
        wait("gpsimd", "vm", P.val("vm"))
        for b in range(8):
            nrow = 2 if b < 7 else 1
            def mv2(g, b=b, nrow=nrow):
                g.dma_start(out=seg2[0:nrow, :], in_=mylists[b:b + nrow, :]).then_inc(GL, 16)
            P.on("gpsimd", mv2)
            P.inc("gl", 16)
            wait("gpsimd", "gl", P.val("gl"))
            def seg(g, b=b):
                g.indirect_dma_start(
                    out=glist2[:, :], out_offset=IndirectOffsetOnAxis(idx2[:, b:b + 1], 0),
                    in_=seg2[0:2, 0:4096], in_offset=None).then_inc(GL, 16)
            P.on("gpsimd", seg)
            P.inc("gl", 16)
            wait("gpsimd", "gl", P.val("gl"))
        def ldid(s):
            src = glist[0:4096, 0:1].rearrange("(c p) a -> (a p) c", p=128)
            s.dma_start(out=idlist[:, :], in_=src, allow_slow_non_contiguous=True).then_inc(SQ, 16)
        wait("sync", "gl", P.val("gl"))
        P.on("sync", ldid)
        ilT = P.inc("sq", 16)
        wait("sync", "sq", ilT)
        waits("vector", sq=ilT)
        v_op(lambda v: v.tensor_scalar(out=idlist[:, :], in0=idlist[:, :], scalar1=0.0, scalar2=32767.0, op0=ALU.max, op1=ALU.min))
        v_op(lambda v: v.tensor_copy(out=idu[:, :], in_=idlist[:, :]))
        vIdu = P.val("vm")

        wait("gpsimd", "vm", vIdu)
        wait("tensor", "wiq", wiT)
        relu_ticks = []
        trT = []
        gtrT = {}
        for cc in range(NCC):
            for g4 in range(4):
                gslot = cc * 4 + g4
                gb = gslot % 2
                if gslot >= 2:
                    wait("gpsimd", "tm", gtrT[gslot - 2])
                gsem = SEMS["ga%d" % gb]
                def gat(g, cc=cc, g4=g4, gb=gb, gsem=gsem):
                    g.indirect_dma_start(
                        out=gbuf[:, gb, 0:512], out_offset=None,
                        in_=h2fa[:, :],
                        in_offset=IndirectOffsetOnAxis(idu[:, cc * 4 + g4:cc * 4 + g4 + 1], 0),
                    ).then_inc(gsem, 16)
                    g.indirect_dma_start(
                        out=gbuf[:, gb, 512:1024], out_offset=None,
                        in_=h2fb[:, :],
                        in_offset=IndirectOffsetOnAxis(idu[:, cc * 4 + g4:cc * 4 + g4 + 1], 0),
                    ).then_inc(gsem, 16)
                P.on("gpsimd", gat)
                P.inc("ga%d" % gb, 16)
                gaT = P.inc("ga%d" % gb, 16)
                wait("tensor", "ga%d" % gb, gaT)
                for dc in range(8):
                    it = g4 * 8 + dc
                    pst = psta if it % 2 == 0 else pstb
                    if True:
                        wait("tensor", "vm", max(P.val("vm") - 1, 0))
                    t_op(lambda t, gb=gb, dc=dc, pst=pst: t.transpose(
                        pst[:, :], gbuf[:, gb, dc * 128:(dc + 1) * 128], idf2[:, :]))
                    waits("vector", tm=P.val("tm"))
                    v_op(lambda v, g4=g4, dc=dc, pst=pst: v.tensor_copy(
                        out=h2fm[:, dc, g4 * 128:(g4 + 1) * 128], in_=pst[:, :]))
                gtrT[gslot] = P.val("tm")
            trT.append(P.val("tm"))
            vFM = P.val("vm")
            waits("tensor", vm=vFM)
            for ich in range(32):
                psu = ps1a if ich % 2 == 0 else ps1b
                if ich >= 2:
                    wait("tensor", "vm", relu_ticks[-2])
                for dc in range(8):
                    t_op(lambda t, ich=ich, dc=dc, psu=psu: t.matmul(
                        psu[:, :], wi_sb[:, dc, ich * 128:(ich + 1) * 128],
                        h2fm[:, dc, :], start=(dc == 0), stop=(dc == 7)),
                        tick=(dc == 7))
                waits("vector", tm=P.val("tm"))
                relu_ticks.append(v_op(lambda v, cc=cc, ich=ich, psu=psu: v.tensor_scalar_max(
                    out=hbf[:, ich, :], in0=psu[:, :], scalar1=0.0)))
            vRelu = P.val("vm")
            if cc == 0:
                wait("tensor", "woq", woT)
            waits("tensor", vm=vRelu)
            for csl in range(4):
                ob = csl % 2
                for hch in range(2):
                    psu = ps2a if hch == 0 else ps2b
                    for ich in range(32):
                        t_op(lambda t, csl=csl, hch=hch, ich=ich, psu=psu: t.matmul(
                            psu[:, :], hbf[:, ich, csl * 128:(csl + 1) * 128],
                            wo_sb[:, ich, hch * 512:(hch + 1) * 512],
                            start=(ich == 0), stop=(ich == 31)),
                            tick=(ich == 31))
                tmm2 = P.val("tm")
                eos = "eo%d" % ob
                wait("vector", eos, P.val(eos))
                waits("vector", tm=tmm2)
                v_op(lambda v, ob=ob: v.tensor_copy(out=outsb[:, ob, 0:512], in_=ps2a[:, :]))
                v_op(lambda v, ob=ob: v.tensor_copy(out=outsb[:, ob, 512:1024], in_=ps2b[:, :]))
                vEv = P.val("vm")
                wait("sync", "vm", vEv)
                row0 = cc * 512 + csl * 128
                s_dma(eout[row0:row0 + 128, :], outsb[:, ob, :], sem=eos)
                waits("tensor", vm=vEv)

        wait("sync", "sq", P.val("sq"))
        for snm in ("eo0", "eo1"):
            wait("sync", snm, P.val(snm))
        wait("gpsimd", "ga0", P.val("ga0"))
        wait("gpsimd", "ga1", P.val("ga1"))
        wait("gpsimd", "gl", P.val("gl"))
        wait("gpsimd", "wiq", P.val("wiq"))
        wait("gpsimd", "woq", P.val("woq"))

        @block.sync
        def _(s):
            P.run("sync", s)

        @block.tensor
        def _(t):
            P.run("tensor", t)

        @block.vector
        def _(v):
            P.run("vector", v)

        @block.scalar
        def _(s):
            P.run("scalar", s)

        @block.gpsimd
        def _(g):
            P.run("gpsimd", g)

    return nc

def host_consts_l3(core_id):
    c = {}
    c["triu_s"] = np.triu(np.ones((128, 128), np.float32), k=1)
    c["triu8"] = np.triu(np.ones((8, 8), np.float32), k=1)
    c["ones128"] = np.ones((128, 128), np.float32)
    c["onesrow"] = np.ones((1, 128), np.float32)
    c["selb"] = np.eye(8, dtype=np.float32)[:, core_id:core_id + 1].copy()
    c["e4096p1"] = np.tile((np.arange(7, dtype=np.float32) * CAP + 1.0)[None, :], (128, 1))
    return c



def build_l3():
    nc = bass.Bass("TRN2", target_bir_lowering=False, debug=False)
    xpp = nc.dram_tensor("xpp", [NTOK, D], F32, kind="ExternalInput").ap()
    wb = nc.dram_tensor("wb", [NTOK, 8], F32, kind="ExternalInput").ap()
    eoutf = nc.dram_tensor("eoutf", [7 * CAP, D], F32, kind="ExternalInput").ap()
    countsf = nc.dram_tensor("countsf", [8, 7], F32, kind="ExternalInput").ap()
    gbrow = nc.dram_tensor("gbrow", [1, D], F32, kind="ExternalInput").ap()
    triu_s = nc.dram_tensor("triu_s", [128, 128], F32, kind="ExternalInput").ap()
    triu8 = nc.dram_tensor("triu8", [8, 8], F32, kind="ExternalInput").ap()
    ones128 = nc.dram_tensor("ones128", [128, 128], F32, kind="ExternalInput").ap()
    onesrow = nc.dram_tensor("onesrow", [1, 128], F32, kind="ExternalInput").ap()
    selb = nc.dram_tensor("selb", [8, 1], F32, kind="ExternalInput").ap()
    e4096p1 = nc.dram_tensor("e4096p1", [128, 7], F32, kind="ExternalInput").ap()
    out_o = nc.dram_tensor("out", [NTOK, D], F32, kind="ExternalOutput").ap()

    P = Plan()
    stack = ExitStack()
    with stack:
        ctriu = stack.enter_context(nc.sbuf_tensor("ctriu", [128, 128], F32))
        ctr8 = stack.enter_context(nc.sbuf_tensor("ctr8", [8, 8], F32))
        cones = stack.enter_context(nc.sbuf_tensor("cones", [128, 128], F32))
        conesr = stack.enter_context(nc.sbuf_tensor("conesr", [1, 128], F32))
        cselb = stack.enter_context(nc.sbuf_tensor("cselb", [8, 1], F32))
        ce4096 = stack.enter_context(nc.sbuf_tensor("ce4096", [128, 7], F32))
        cntf = stack.enter_context(nc.sbuf_tensor("cntf", [8, 7], F32))
        cumall = stack.enter_context(nc.sbuf_tensor("cumall", [8, 7], F32))
        baser = stack.enter_context(nc.sbuf_tensor("baser", [1, 7], F32))
        baset = stack.enter_context(nc.sbuf_tensor("baset", [128, 7], F32))
        gbr = stack.enter_context(nc.sbuf_tensor("gbr", [1, D], F32))
        gbt = stack.enter_context(nc.sbuf_tensor("gbt", [128, D], F32))
        wt = stack.enter_context(nc.sbuf_tensor("wt", [128, 2, 8], F32))
        xt = stack.enter_context(nc.sbuf_tensor("xt", [128, 2, D], F32))
        g1 = stack.enter_context(nc.sbuf_tensor("g1", [128, 2, D], F32))
        g2 = stack.enter_context(nc.sbuf_tensor("g2", [128, 2, D], F32))
        outt = stack.enter_context(nc.sbuf_tensor("outt", [128, 2, D], F32))
        masks = stack.enter_context(nc.sbuf_tensor("masks", [128, 7], F32))
        Psb = stack.enter_context(nc.sbuf_tensor("Psb", [128, 7], F32))
        offs = stack.enter_context(nc.sbuf_tensor("offs", [128, 7], F32))
        slotg = stack.enter_context(nc.sbuf_tensor("slotg", [128, 7], F32))
        kept = stack.enter_context(nc.sbuf_tensor("kept", [128, 7], F32))
        Am = stack.enter_context(nc.sbuf_tensor("Am", [128, 7], F32))
        sel = stack.enter_context(nc.sbuf_tensor("sel", [128, 7], F32))
        tmp7 = stack.enter_context(nc.sbuf_tensor("tmp7", [128, 7], F32))
        m1 = stack.enter_context(nc.sbuf_tensor("m1", [128, 1], F32))
        w1 = stack.enter_context(nc.sbuf_tensor("w1", [128, 1], F32))
        w2 = stack.enter_context(nc.sbuf_tensor("w2", [128, 1], F32))
        r1u = stack.enter_context(nc.sbuf_tensor("r1u", [128, 2, 1], U32))
        r2u = stack.enter_context(nc.sbuf_tensor("r2u", [128, 2, 1], U32))
        rf = stack.enter_context(nc.sbuf_tensor("rf", [128, 1], F32))
        bcd = stack.enter_context(nc.sbuf_tensor("bcd", [1, 256], BF16))
        psA = stack.enter_context(nc.psum_tensor("psA", [128, 512], F32))
        psOf = stack.enter_context(nc.psum_tensor("psOf", [128, 7], F32))
        SQ = stack.enter_context(nc.semaphore("sq"))
        WQ0 = stack.enter_context(nc.semaphore("wq0"))
        WQ1 = stack.enter_context(nc.semaphore("wq1"))
        XQ0 = stack.enter_context(nc.semaphore("xq0"))
        XQ1 = stack.enter_context(nc.semaphore("xq1"))
        GA0 = stack.enter_context(nc.semaphore("ga0"))
        GA1 = stack.enter_context(nc.semaphore("ga1"))
        OQ0 = stack.enter_context(nc.semaphore("oq0"))
        OQ1 = stack.enter_context(nc.semaphore("oq1"))
        TM = stack.enter_context(nc.semaphore("tm"))
        VM = stack.enter_context(nc.semaphore("vm"))
        block = stack.enter_context(nc.Block())

        SEMS = {"sq": SQ, "wq0": WQ0, "wq1": WQ1, "xq0": XQ0, "xq1": XQ1,
                "ga0": GA0, "ga1": GA1, "oq0": OQ0, "oq1": OQ1, "tm": TM, "vm": VM}

        def s_dma(out, in_, sem="sq"):
            S = SEMS[sem]
            def f(s, out=out, in_=in_, S=S):
                s.dma_start(out=out, in_=in_).then_inc(S, 16)
            P.on("sync", f)
            return P.inc(sem, 16)

        def _eng_op(eng, sem_name, fn, tick):
            prev = P.val(sem_name)
            S = SEMS[sem_name]
            if prev > 0:
                def w(e, S=S, prev=prev):
                    e.wait_ge(S, prev)
                P.on(eng, w)
            if tick:
                n = P.inc(sem_name, 1)
                def f(e, fn=fn, S=S):
                    fn(e).then_inc(S, 1)
            else:
                n = prev
                def f(e, fn=fn):
                    fn(e)
            P.on(eng, f)
            return n

        def t_op(fn, tick=True):
            return _eng_op("tensor", "tm", fn, tick)

        def v_op(fn, tick=True):
            return _eng_op("vector", "vm", fn, tick)

        def wait(eng_name, sem_name, val):
            if val <= 0:
                return
            sem = SEMS[sem_name]
            def f(e, sem=sem, val=val):
                e.wait_ge(sem, val)
            P.on(eng_name, f)

        def waits(eng_name, **kw):
            for k, v in kw.items():
                wait(eng_name, k, v)

        for dst, src in [(ctriu, triu_s), (ctr8, triu8), (cones, ones128),
                         (conesr, onesrow), (cselb, selb), (ce4096, e4096p1),
                         (cntf, countsf), (gbr, gbrow)]:
            sl = tuple(slice(None) for _ in range(len(dst.shape)))
            s_dma(dst[sl], src[sl])
        c0 = P.val("sq")
        wait("sync", "sq", c0)
        waits("vector", sq=c0)
        v_op(lambda v: v.tensor_copy(out=bcd[0:1, :], in_=conesr[0:1, :].bitcast(BF16)))
        waits("tensor", sq=c0)
        t_op(lambda t: t.matmul(psA[0:8, 0:7], ctr8[:, :], cntf[:, :], start=True, stop=True))
        waits("vector", tm=P.val("tm"))
        v_op(lambda v: v.tensor_copy(out=cumall[:, :], in_=psA[0:8, 0:7]))
        waits("tensor", vm=P.val("vm"))
        t_op(lambda t: t.matmul(psA[0:1, 0:7], cselb[:, :], cumall[:, :], start=True, stop=True))
        waits("vector", tm=P.val("tm"))
        v_op(lambda v: v.tensor_copy(out=baser[:, :], in_=psA[0:1, 0:7]))
        waits("tensor", vm=P.val("vm"))
        t_op(lambda t: t.matmul(psA[:, 0:7], conesr[:, :], baser[:, :], start=True, stop=True))
        waits("vector", tm=P.val("tm"))
        v_op(lambda v: v.tensor_copy(out=baset[:, :], in_=psA[:, 0:7]))
        for half in range(2):
            waits("tensor", vm=P.val("vm"))
            t_op(lambda t, half=half: t.matmul(psA[:, 0:512], conesr[:, :],
                                               gbr[0:1, half * 512:half * 512 + 512],
                                               start=True, stop=True))
            waits("vector", tm=P.val("tm"))
            v_op(lambda v, half=half: v.tensor_copy(out=gbt[:, half * 512:half * 512 + 512], in_=psA[:, 0:512]))
        vPrep = P.val("vm")

        wqc = [0, 0]
        xqc = [0, 0]
        gac = [0, 0]
        oqc = [0, 0]
        vOutTicks = []
        psOfTick = [0]
        for j in range(NT):
            buf = j % 2
            if j >= 2:
                wait("sync", "vm", vOutTicks[j - 2])
            ldw = s_dma(wt[:, buf, :], wb[j * 128:(j + 1) * 128, :], sem="wq%d" % buf)
            wqc[buf] += 1
            ldx = s_dma(xt[:, buf, :], xpp[j * 128:(j + 1) * 128, :], sem="xq%d" % buf)
            xqc[buf] += 1
            waits("vector", **{("wq%d" % buf): 16 * wqc[buf]})
            wait("vector", "tm", psOfTick[0])
            v_op(lambda v, buf=buf: v.tensor_scalar(out=masks[:, :], in0=wt[:, buf, 0:7], scalar1=0.0, scalar2=None, op0=ALU.is_gt))
            waits("tensor", vm=P.val("vm"))
            t_op(lambda t: t.matmul(psA[:, 32:39], ctriu[:, :], masks[:, :], start=True, stop=True, skip_group_check=True))
            waits("vector", tm=P.val("tm"))
            v_op(lambda v: v.tensor_copy(out=Psb[:, :], in_=psA[:, 32:39]))
            if j == 0:
                v_op(lambda v: v.memset(offs[:, :], 0.0))
            else:
                v_op(lambda v: v.tensor_copy(out=offs[:, :], in_=psOf[:, :]))
            waits("tensor", vm=P.val("vm"))
            psOfTick[0] = t_op(lambda t, j=j: t.matmul(psOf[:, :], cones[:, :], masks[:, :],
                                         start=(j == 0), stop=(j == NT - 1), skip_group_check=True))
            v_op(lambda v: v.tensor_add(out=slotg[:, :], in0=baset[:, :], in1=offs[:, :]))
            v_op(lambda v: v.tensor_add(out=slotg[:, :], in0=slotg[:, :], in1=Psb[:, :]))
            v_op(lambda v: v.tensor_scalar(out=kept[:, :], in0=slotg[:, :], scalar1=float(CAP), scalar2=None, op0=ALU.is_lt))
            v_op(lambda v: v.tensor_mul(out=kept[:, :], in0=kept[:, :], in1=masks[:, :]))
            v_op(lambda v: v.tensor_add(out=Am[:, :], in0=slotg[:, :], in1=ce4096[:, :]))
            v_op(lambda v: v.tensor_mul(out=Am[:, :], in0=Am[:, :], in1=kept[:, :]))
            v_op(lambda v: v.reduce_max(m1[:, :], Am[:, :], axis=AX.X))
            v_op(lambda v: v.tensor_scalar(out=sel[:, :], in0=Am[:, :], scalar1=m1[:, 0:1], scalar2=None, op0=ALU.is_equal))
            v_op(lambda v: v.tensor_mul(out=sel[:, :], in0=sel[:, :], in1=kept[:, :]))
            v_op(lambda v, buf=buf: v.tensor_mul(out=tmp7[:, :], in0=sel[:, :], in1=wt[:, buf, 0:7]))
            v_op(lambda v: v.reduce_sum(w1[:, :], tmp7[:, :], axis=AX.X))
            v_op(lambda v: v.tensor_scalar(out=rf[:, :], in0=m1[:, :], scalar1=-1.0, scalar2=0.0, op0=ALU.add, op1=ALU.max))
            if j >= 2:
                wait("vector", "ga%d" % buf, 32 * gac[buf])
            v_op(lambda v, buf=buf: v.tensor_copy(out=r1u[:, buf, :], in_=rf[:, :]))
            v_op(lambda v: v.tensor_scalar(out=tmp7[:, :], in0=sel[:, :], scalar1=-1.0, scalar2=1.0, op0=ALU.mult, op1=ALU.add))
            v_op(lambda v: v.tensor_mul(out=kept[:, :], in0=kept[:, :], in1=tmp7[:, :]))
            v_op(lambda v: v.tensor_mul(out=Am[:, :], in0=Am[:, :], in1=tmp7[:, :]))
            v_op(lambda v: v.reduce_max(m1[:, :], Am[:, :], axis=AX.X))
            v_op(lambda v: v.tensor_scalar(out=sel[:, :], in0=Am[:, :], scalar1=m1[:, 0:1], scalar2=None, op0=ALU.is_equal))
            v_op(lambda v: v.tensor_mul(out=sel[:, :], in0=sel[:, :], in1=kept[:, :]))
            v_op(lambda v, buf=buf: v.tensor_mul(out=tmp7[:, :], in0=sel[:, :], in1=wt[:, buf, 0:7]))
            v_op(lambda v: v.reduce_sum(w2[:, :], tmp7[:, :], axis=AX.X))
            v_op(lambda v: v.tensor_scalar(out=rf[:, :], in0=m1[:, :], scalar1=-1.0, scalar2=0.0, op0=ALU.add, op1=ALU.max))
            v_op(lambda v, buf=buf: v.tensor_copy(out=r2u[:, buf, :], in_=rf[:, :]))
            vIdx = P.val("vm")
            wait("gpsimd", "vm", vIdx)
            ges = "ga%d" % buf
            def gat1(g, buf=buf):
                g.indirect_dma_start(out=g1[:, buf, :], out_offset=None, in_=eoutf[:, :],
                                     in_offset=IndirectOffsetOnAxis(r1u[:, buf, :], 0)).then_inc(SEMS["ga%d" % buf], 16)
            P.on("gpsimd", gat1)
            P.inc(ges, 16)
            def gat2(g, buf=buf):
                g.indirect_dma_start(out=g2[:, buf, :], out_offset=None, in_=eoutf[:, :],
                                     in_offset=IndirectOffsetOnAxis(r2u[:, buf, :], 0)).then_inc(SEMS["ga%d" % buf], 16)
            P.on("gpsimd", gat2)
            P.inc(ges, 16)
            gac[buf] += 1
            waits("vector", **{ges: P.val(ges), ("xq%d" % buf): 16 * xqc[buf]})
            if j >= 2:
                wait("vector", "oq%d" % buf, 16 * oqc[buf])
            v_op(lambda v, buf=buf: v.tensor_scalar_mul(out=outt[:, buf, :], in0=g1[:, buf, :], scalar1=w1[:, 0:1]))
            v_op(lambda v, buf=buf: v.scalar_tensor_tensor(out=outt[:, buf, :], in0=g2[:, buf, :], scalar=w2[:, 0:1], in1=outt[:, buf, :], op0=ALU.mult, op1=ALU.add))
            v_op(lambda v, buf=buf: v.tensor_mul(out=outt[:, buf, :], in0=outt[:, buf, :], in1=gbt[:, :]))
            v_op(lambda v, buf=buf: v.tensor_add(out=outt[:, buf, :], in0=outt[:, buf, :], in1=xt[:, buf, :]))
            vOut = P.val("vm")
            vOutTicks.append(vOut)
            wait("sync", "vm", vOut)
            s_dma(out_o[j * 128:(j + 1) * 128, :], outt[:, buf, :], sem="oq%d" % buf)
            oqc[buf] += 1

        wait("sync", "oq0", P.val("oq0"))
        wait("sync", "oq1", P.val("oq1"))
        wait("sync", "sq", P.val("sq"))
        wait("gpsimd", "ga0", P.val("ga0"))
        wait("gpsimd", "ga1", P.val("ga1"))

        @block.sync
        def _(s):
            P.run("sync", s)

        @block.tensor
        def _(t):
            P.run("tensor", t)

        @block.vector
        def _(v):
            P.run("vector", v)

        @block.scalar
        def _(s):
            P.run("scalar", s)

        @block.gpsimd
        def _(g):
            P.run("gpsimd", g)

    return nc



_CACHE = {}


def _get_programs():
    if "nc1" not in _CACHE:
        _CACHE["nc1"] = build_l1()
        _CACHE["nc2"] = build_l2()
        _CACHE["nc3"] = build_l3()
        _CACHE["c1"] = [host_consts_l1(b) for b in range(8)]
        _CACHE["c2"] = [host_consts_l2(e) for e in range(8)]
        _CACHE["c3"] = [host_consts_l3(b) for b in range(8)]
    return _CACHE


def kernel(L0_attn_Z, L0_attn_decay, L0_attn_W, L0_attn_b, L0_attn_gamma,
           L0_attn_beta, L0_ffn_router_w, L0_ffn_router_b, L0_ffn_experts_inter,
           L0_ffn_experts_out, L0_c, L0_wmod):
    C = _get_programs()
    x = np.ascontiguousarray(np.asarray(L0_attn_Z, dtype=np.float32))
    W = np.ascontiguousarray(np.asarray(L0_attn_W, dtype=np.float32))
    battn = np.ascontiguousarray(np.asarray(L0_attn_b, dtype=np.float32)).reshape(1, D)
    gamma = np.ascontiguousarray(np.asarray(L0_attn_gamma, dtype=np.float32)).reshape(1, D)
    beta = np.ascontiguousarray(np.asarray(L0_attn_beta, dtype=np.float32)).reshape(1, D)
    decay = np.ascontiguousarray(np.asarray(L0_attn_decay, dtype=np.float32)).reshape(1, 1)
    rw = np.ascontiguousarray(np.asarray(L0_ffn_router_w, dtype=np.float32))
    rb = np.ascontiguousarray(np.asarray(L0_ffn_router_b, dtype=np.float32)).reshape(1, 8)
    wi_all = np.asarray(L0_ffn_experts_inter, dtype=np.float32)
    wo_all = np.asarray(L0_ffn_experts_out, dtype=np.float32)
    cfull = np.ascontiguousarray(np.asarray(L0_c, dtype=np.float32))
    wmod = np.ascontiguousarray(np.asarray(L0_wmod, dtype=np.float32))

    in1 = []
    for b in range(8):
        m = dict(C["c1"][b])
        m["x_in"] = np.ascontiguousarray(x[b])
        m["W_in"] = W
        m["battn"] = battn
        m["gamma"] = gamma
        m["beta"] = beta
        m["decay"] = decay
        m["rw"] = rw
        m["rb"] = rb
        m["cfull"] = cfull
        m["wmod"] = wmod
        in1.append(m)
    r1 = run_bass_kernel_spmd(C["nc1"], in1, core_ids=list(range(8))).results

    h2_full = np.concatenate([np.asarray(r1[b]["h2b"]) for b in range(8)], axis=0)
    h2a = np.ascontiguousarray(h2_full[:, 0:512])
    h2b_ = np.ascontiguousarray(h2_full[:, 512:1024])
    lists_all = np.stack([np.asarray(r1[b]["lists"]).reshape(7, LSTRIDE) for b in range(8)])
    counts_all = np.stack([np.asarray(r1[b]["counts"]).reshape(7) for b in range(8)])

    in2 = []
    for k in range(8):
        e = k if k < 7 else 0
        m = dict(C["c2"][k])
        m["h2fa"] = h2a
        m["h2fb"] = h2b_
        m["mylists"] = np.ascontiguousarray(lists_all[:, e, :])
        m["counts_e"] = np.ascontiguousarray(counts_all[:, e:e + 1])
        m["wi"] = np.ascontiguousarray(wi_all[e])
        m["wo"] = np.ascontiguousarray(wo_all[e])
        in2.append(m)
    r2 = run_bass_kernel_spmd(C["nc2"], in2, core_ids=list(range(8))).results
    eoutf = np.concatenate([np.asarray(r2[k]["eout"]) for k in range(7)], axis=0)

    in3 = []
    for b in range(8):
        m = dict(C["c3"][b])
        m["xpp"] = np.asarray(r1[b]["xpp"])
        m["wb"] = np.asarray(r1[b]["wout"])
        m["eoutf"] = eoutf
        m["countsf"] = np.ascontiguousarray(counts_all)
        m["gbrow"] = np.asarray(r1[b]["gbrow"])
        in3.append(m)
    r3 = run_bass_kernel_spmd(C["nc3"], in3, core_ids=list(range(8))).results
    out = np.stack([np.asarray(r3[b]["out"]) for b in range(8)], axis=0)
    return out.astype(np.float32, copy=False)


# version 08
# speedup vs baseline: 1.2325x; 1.2325x over previous
import dataclasses
import numpy as np
import ml_dtypes
import concourse.bass as bass
import concourse.mybir as mybir
from concourse.bass_core_rust import IndirectOffsetOnAxis
from concourse.bass_utils import run_bass_kernel_spmd
from contextlib import ExitStack

F32 = mybir.dt.float32
BF16 = mybir.dt.bfloat16
U32 = mybir.dt.uint32
ALU = mybir.AluOpType
AF = mybir.ActivationFunctionType
AX = mybir.AxisListType

NTOK = 4096
D = 1024
NT = NTOK // 128
NDC = D // 128
LSTRIDE = 4352
EPS = 1e-5
NTOKG = 32768
I = 4096
CAP = 4096
NCC = CAP // 512
GPAD = 8448


def bcast_free(ap, n):
    return dataclasses.replace(ap, ap=type(ap.ap)([list(ap.ap[0]), [0, n]]))


class Plan:
    def __init__(self):
        self.items = []
        self.ctr = {}

    def on(self, eng, fn):
        self.items.append((eng, fn))

    def inc(self, name, k):
        self.ctr[name] = self.ctr.get(name, 0) + k
        return self.ctr[name]

    def val(self, name):
        return self.ctr.get(name, 0)

    def run(self, eng_name, eng):
        for e, fn in self.items:
            if e == eng_name:
                fn(eng)


def host_consts_l1(core_id):
    import ml_dtypes
    c = {}
    c["ident_f32"] = np.eye(128, dtype=np.float32)
    c["ident_bf16"] = np.eye(128, dtype=np.float32).astype(ml_dtypes.bfloat16)
    c["triu_s"] = np.triu(np.ones((128, 128), np.float32), k=1)
    c["iota_row"] = np.tile(np.arange(128, dtype=np.float32)[None, :], (128, 1))
    c["iotaP"] = np.arange(128, dtype=np.float32).reshape(128, 1)
    c["gidbase"] = np.full((128, 1), core_id * NTOK, np.float32)
    c["e4352"] = (np.arange(7, dtype=np.float32) * LSTRIDE).reshape(7, 1)
    c["onesrow"] = np.ones((1, 128), np.float32)
    c["onescol"] = np.ones((128, 1), np.float32)
    c["selb"] = np.eye(8, dtype=np.float32)[:, core_id:core_id + 1].copy()
    return c


def build_l1():
    nc = bass.Bass("TRN2", target_bir_lowering=False, debug=False)
    x_in = nc.dram_tensor("x_in", [NTOK, D], F32, kind="ExternalInput").ap()
    W_in = nc.dram_tensor("W_in", [D, D], F32, kind="ExternalInput").ap()
    battn = nc.dram_tensor("battn", [1, D], F32, kind="ExternalInput").ap()
    gamma = nc.dram_tensor("gamma", [1, D], F32, kind="ExternalInput").ap()
    beta = nc.dram_tensor("beta", [1, D], F32, kind="ExternalInput").ap()
    decay = nc.dram_tensor("decay", [1, 1], F32, kind="ExternalInput").ap()
    rw = nc.dram_tensor("rw", [8, D], F32, kind="ExternalInput").ap()
    rb = nc.dram_tensor("rb", [1, 8], F32, kind="ExternalInput").ap()
    cfull = nc.dram_tensor("cfull", [8, D], F32, kind="ExternalInput").ap()
    wmod = nc.dram_tensor("wmod", [D, 6 * D], F32, kind="ExternalInput").ap()
    ident_f32 = nc.dram_tensor("ident_f32", [128, 128], F32, kind="ExternalInput").ap()
    ident_bf16 = nc.dram_tensor("ident_bf16", [128, 128], BF16, kind="ExternalInput").ap()
    triu_s = nc.dram_tensor("triu_s", [128, 128], F32, kind="ExternalInput").ap()
    iota_row = nc.dram_tensor("iota_row", [128, 128], F32, kind="ExternalInput").ap()
    iotaP = nc.dram_tensor("iotaP", [128, 1], F32, kind="ExternalInput").ap()
    gidbase = nc.dram_tensor("gidbase", [128, 1], F32, kind="ExternalInput").ap()
    e4352 = nc.dram_tensor("e4352", [7, 1], F32, kind="ExternalInput").ap()
    onesrow = nc.dram_tensor("onesrow", [1, 128], F32, kind="ExternalInput").ap()
    onescol = nc.dram_tensor("onescol", [128, 1], F32, kind="ExternalInput").ap()
    selb = nc.dram_tensor("selb", [8, 1], F32, kind="ExternalInput").ap()
    xpp_o = nc.dram_tensor("xpp", [NTOK, D], F32, kind="ExternalOutput").ap()
    h2b_o = nc.dram_tensor("h2b", [NTOK, D], F32, kind="ExternalOutput").ap()
    w_o = nc.dram_tensor("wout", [NTOK, 8], F32, kind="ExternalOutput").ap()
    lists_o = nc.dram_tensor("lists", [7 * LSTRIDE, 1], F32, kind="ExternalOutput").ap()
    counts_o = nc.dram_tensor("counts", [7, 1], F32, kind="ExternalOutput").ap()
    gbrow_o = nc.dram_tensor("gbrow", [1, D], F32, kind="ExternalOutput").ap()
    ycs_o = nc.dram_tensor("ycs", [NTOK, D], BF16, kind="ExternalOutput").ap()
    gbh2_o = nc.dram_tensor("gbh2", [NTOK, D], F32, kind="ExternalOutput").ap()

    P = Plan()
    stack = ExitStack()
    with stack:
        idf = stack.enter_context(nc.sbuf_tensor("cid_f", [128, 128], F32))
        idb = stack.enter_context(nc.sbuf_tensor("cid_b", [128, 128], BF16))
        ctriu = stack.enter_context(nc.sbuf_tensor("ctriu", [128, 128], F32))
        ciota = stack.enter_context(nc.sbuf_tensor("ciota", [128, 128], F32))
        ciop = stack.enter_context(nc.sbuf_tensor("ciop", [128, 1], F32))
        cgidb = stack.enter_context(nc.sbuf_tensor("cgidb", [128, 1], F32))
        ce4352 = stack.enter_context(nc.sbuf_tensor("ce4352", [7, 1], F32))
        conesr = stack.enter_context(nc.sbuf_tensor("conesr", [1, 128], F32))
        conesc = stack.enter_context(nc.sbuf_tensor("conesc", [128, 1], F32))
        cselb = stack.enter_context(nc.sbuf_tensor("cselb", [8, 1], F32))
        xnT = stack.enter_context(nc.sbuf_tensor("xnT", [128, NDC, NTOK], BF16))
        WTb = stack.enter_context(nc.sbuf_tensor("WTb", [128, NDC, D], BF16))
        Gt = stack.enter_context(nc.sbuf_tensor("Gt", [128, D], F32))
        Bt = stack.enter_context(nc.sbuf_tensor("Bt", [128, D], F32))
        cb1t = stack.enter_context(nc.sbuf_tensor("cb1t", [128, D], F32))
        sbt = stack.enter_context(nc.sbuf_tensor("sbt", [128, D], F32))
        gbt = stack.enter_context(nc.sbuf_tensor("gbt", [128, D], F32))
        rbt = stack.enter_context(nc.sbuf_tensor("rbt", [128, 8], F32))
        rwTb = stack.enter_context(nc.sbuf_tensor("rwTb", [128, NDC, 8], BF16))
        g2 = stack.enter_context(nc.sbuf_tensor("g2", [2, NTOK], BF16))
        vb2 = stack.enter_context(nc.sbuf_tensor("vb2", [2, D], BF16))
        stat_mu = stack.enter_context(nc.sbuf_tensor("stat_mu", [128, NT], F32))
        stat_rs = stack.enter_context(nc.sbuf_tensor("stat_rs", [128, NT], F32))
        stat_s1 = stack.enter_context(nc.sbuf_tensor("stat_s1", [128, NT], F32))
        stat_sq = stack.enter_context(nc.sbuf_tensor("stat_sq", [128, NT], F32))
        gid0 = stack.enter_context(nc.sbuf_tensor("gid0", [128, 1], F32))
        acol = stack.enter_context(nc.sbuf_tensor("acol", [128, 1], F32))
        asc = stack.enter_context(nc.sbuf_tensor("asc", [1, 4], F32))
        psY0 = stack.enter_context(nc.psum_tensor("psY0", [128, 512], F32))
        psY1 = stack.enter_context(nc.psum_tensor("psY1", [128, 512], F32))
        psTB0 = stack.enter_context(nc.psum_tensor("psTB0", [128, 128], BF16))
        psTB1 = stack.enter_context(nc.psum_tensor("psTB1", [128, 128], BF16))
        psOf = stack.enter_context(nc.psum_tensor("psOf", [7, 1], F32))
        psRT = stack.enter_context(nc.psum_tensor("psRT", [128, 512], F32))
        SQ = stack.enter_context(nc.semaphore("sq"))
        GQ = stack.enter_context(nc.semaphore("gq"))
        SC = stack.enter_context(nc.semaphore("sc"))
        TM = stack.enter_context(nc.semaphore("tm"))
        VM = stack.enter_context(nc.semaphore("vm"))
        SM = stack.enter_context(nc.semaphore("sm"))
        XQ0 = stack.enter_context(nc.semaphore("xq0"))
        XQ1 = stack.enter_context(nc.semaphore("xq1"))
        WQ0 = stack.enter_context(nc.semaphore("wq0"))
        WQ1 = stack.enter_context(nc.semaphore("wq1"))
        SOX0 = stack.enter_context(nc.semaphore("sox0"))
        SOX1 = stack.enter_context(nc.semaphore("sox1"))
        SOW0 = stack.enter_context(nc.semaphore("sow0"))
        SOW1 = stack.enter_context(nc.semaphore("sow1"))
        SOH0 = stack.enter_context(nc.semaphore("soh0"))
        SOH1 = stack.enter_context(nc.semaphore("soh1"))
        LZ = stack.enter_context(nc.semaphore("lz"))
        block = stack.enter_context(nc.Block())

        SEMS = {"sq": SQ, "gq": GQ, "sc": SC, "tm": TM, "vm": VM, "sm": SM,
                "xq0": XQ0, "xq1": XQ1, "wq0": WQ0, "wq1": WQ1,
                "sox0": SOX0, "sox1": SOX1, "sow0": SOW0, "sow1": SOW1,
                "soh0": SOH0, "soh1": SOH1, "lz": LZ}

        def s_dma(out, in_, sem="sq"):
            S = SEMS[sem]
            def f(s, out=out, in_=in_, S=S):
                s.dma_start(out=out, in_=in_).then_inc(S, 16)
            P.on("sync", f)
            return P.inc(sem, 16)

        def s_barrier(sem="sq"):
            wait("sync", sem, P.val(sem))
            return P.val(sem)

        def g_dma(out, in_):
            def f(g, out=out, in_=in_):
                g.dma_start(out=out, in_=in_).then_inc(GQ, 16)
            P.on("gpsimd", f)
            return P.inc("gq", 16)

        def _eng_op(eng, sem_name, fn, tick):
            prev = P.val(sem_name)
            S = SEMS[sem_name]
            if prev > 0:
                def w(e, S=S, prev=prev):
                    e.wait_ge(S, prev)
                P.on(eng, w)
            if tick:
                n = P.inc(sem_name, 1)
                def f(e, fn=fn, S=S):
                    fn(e).then_inc(S, 1)
            else:
                n = prev
                def f(e, fn=fn):
                    fn(e)
            P.on(eng, f)
            return n

        def t_op(fn, tick=True):
            return _eng_op("tensor", "tm", fn, tick)

        def v_op(fn, tick=True):
            return _eng_op("vector", "vm", fn, tick)

        def sc_op(fn, tick=True):
            return _eng_op("scalar", "sm", fn, tick)

        def wait(eng_name, sem_name, val):
            if val <= 0:
                return
            sem = SEMS[sem_name]
            def f(e, sem=sem, val=val):
                e.wait_ge(sem, val)
            P.on(eng_name, f)

        def waits(eng_name, **kw):
            for k, v in kw.items():
                wait(eng_name, k, v)

        prep = ExitStack()
        msa = prep.enter_context(nc.sbuf_tensor("msa", [1, D], F32))
        mca = prep.enter_context(nc.sbuf_tensor("mca", [1, D], F32))
        mga = prep.enter_context(nc.sbuf_tensor("mga", [1, D], F32))
        msb = prep.enter_context(nc.sbuf_tensor("msb", [1, D], F32))
        mcb = prep.enter_context(nc.sbuf_tensor("mcb", [1, D], F32))
        mgb = prep.enter_context(nc.sbuf_tensor("mgb", [1, D], F32))
        r0 = prep.enter_context(nc.sbuf_tensor("r0", [1, D], F32))
        r1 = prep.enter_context(nc.sbuf_tensor("r1", [1, D], F32))
        modst = prep.enter_context(nc.sbuf_tensor("modst", [8, 512], F32))
        cstage = prep.enter_context(nc.sbuf_tensor("cstage", [8, D], F32))
        scT = prep.enter_context(nc.sbuf_tensor("scT", [128, 8, 8], F32))
        wmt = prep.enter_context(nc.sbuf_tensor("wmt", [128, 2, 512], F32))
        wst = prep.enter_context(nc.sbuf_tensor("wst", [128, 2, 128], F32))
        wtf = prep.enter_context(nc.sbuf_tensor("wtf", [128, D], F32))
        safm = prep.enter_context(nc.sbuf_tensor("safm", [128, 8], F32))
        ca1fm = prep.enter_context(nc.sbuf_tensor("ca1fm", [128, 8], F32))
        wbar = prep.enter_context(nc.sbuf_tensor("wbar", [128, 8], F32))
        vrow = prep.enter_context(nc.sbuf_tensor("vrow", [1, D], F32))
        bcrow = prep.enter_context(nc.sbuf_tensor("bcrow", [1, D], F32))
        smallsc = prep.enter_context(nc.sbuf_tensor("smallsc", [8, 8], F32))

        for dst, src in [(idf, ident_f32), (idb, ident_bf16), (ctriu, triu_s),
                         (ciota, iota_row), (ciop, iotaP), (cgidb, gidbase),
                         (ce4352, e4352), (conesr, onesrow), (conesc, onescol),
                         (cselb, selb)]:
            sl = tuple(slice(None) for _ in range(len(dst.shape)))
            s_dma(dst[sl], src[sl])
        s_dma(cstage[0:8, :], cfull[:, :])
        s_dma(r0[:, :], gamma[:, :])
        s_dma(r1[:, :], beta[:, :])
        s_dma(asc[0:1, 0:1], decay[:, :])
        s_dma(smallsc[0:1, 0:8], rb[:, :])
        c0 = s_barrier()

        waits("scalar", sq=c0)
        sc_op(lambda s: s.activation(out=cstage[0:8, :], in_=cstage[0:8, :], func=AF.Silu))
        tA = P.val("sm")
        waits("tensor", sm=tA)
        for dc in range(NDC):
            waits("tensor", vm=P.val("vm"))
            t_op(lambda t, dc=dc: t.transpose(psRT[0:128, 0:8], cstage[0:8, dc * 128:(dc + 1) * 128], idf[0:8, 0:8]))
            waits("vector", tm=P.val("tm"))
            v_op(lambda v, dc=dc: v.tensor_copy(out=scT[:, dc, :], in_=psRT[0:128, 0:8]))
        vSct = P.val("vm")
        waits("tensor", vm=vSct)
        rows6 = [msa, mca, mga, msb, mcb, mgb]
        mmticks = []
        for nch in range(12):
            waits("tensor", vm=P.val("vm"))
            for dc in range(NDC):
                it = nch * NDC + dc
                buf = it % 2
                if it >= 2:
                    wait("sync", "tm", mmticks[it - 2])
                ld = s_dma(wmt[:, buf, :], wmod[dc * 128:(dc + 1) * 128, nch * 512:(nch + 1) * 512], sem="wq%d" % buf)
                waits("tensor", **{("wq%d" % buf): ld})
                mmticks.append(t_op(lambda t, dc=dc, buf=buf: t.matmul(
                    psY0[0:8, :], scT[:, dc, :], wmt[:, buf, :],
                    start=(dc == 0), stop=(dc == NDC - 1))))
            waits("vector", tm=P.val("tm"))
            v_op(lambda v: v.tensor_copy(out=modst[0:8, :], in_=psY0[0:8, :]))
            waits("tensor", vm=P.val("vm"))
            t_op(lambda t: t.matmul(psRT[0:1, 0:512], cselb[:, :], modst[0:8, :], start=True, stop=True))
            waits("vector", tm=P.val("tm"))
            i, half = nch // 2, nch % 2
            v_op(lambda v, i=i, half=half: v.tensor_copy(
                out=rows6[i][0:1, half * 512:half * 512 + 512], in_=psRT[0:1, 0:512]))
        for gv in (mga, mgb):
            v_op(lambda v, gv=gv: v.tensor_scalar(out=bcrow[:, :], in0=gv[0:1, :], scalar1=0.0, scalar2=None, op0=ALU.is_ge))
            v_op(lambda v: v.tensor_scalar(out=bcrow[:, :], in0=bcrow[:, :], scalar1=2.0, scalar2=-1.0, op0=ALU.mult, op1=ALU.add))
            waits("scalar", vm=P.val("vm"))
            sc_op(lambda s, gv=gv: s.activation(out=vrow[:, :], in_=gv[0:1, :], func=AF.Abs))
            waits("vector", sm=P.val("sm"))
            v_op(lambda v: v.tensor_scalar_max(out=vrow[:, :], in0=vrow[:, :], scalar1=0.25))
            v_op(lambda v, gv=gv: v.tensor_mul(out=gv[0:1, :], in0=bcrow[:, :], in1=vrow[:, :]))
        v_op(lambda v: v.tensor_mul(out=r0[:, :], in0=r0[:, :], in1=mga[0:1, :]))
        v_op(lambda v: v.tensor_mul(out=r1[:, :], in0=r1[:, :], in1=mga[0:1, :]))
        v_op(lambda v: v.tensor_scalar_add(out=mcb[0:1, :], in0=mcb[0:1, :], scalar1=1.0))
        vClamp = P.val("vm")
        wait("sync", "vm", vClamp)
        s_dma(gbrow_o[:, :], mgb[0:1, :])
        waits("tensor", vm=vClamp)
        for row, tile in [(r0, Gt), (r1, Bt), (mcb, cb1t), (msb, sbt), (mgb, gbt)]:
            for half in range(2):
                waits("tensor", vm=P.val("vm"))
                t_op(lambda t, row=row, half=half: t.matmul(
                    psY1[:, :], conesr[:, :], row[0:1, half * 512:half * 512 + 512],
                    start=True, stop=True))
                waits("vector", tm=P.val("tm"))
                v_op(lambda v, tile=tile, half=half: v.tensor_copy(
                    out=tile[:, half * 512:half * 512 + 512], in_=psY1[:, :]))
        waits("tensor", vm=P.val("vm"))
        t_op(lambda t: t.matmul(psRT[:, 0:8], conesr[:, :], smallsc[0:1, 0:8], start=True, stop=True))
        waits("vector", tm=P.val("tm"))
        v_op(lambda v: v.tensor_copy(out=rbt[:, :], in_=psRT[:, 0:8]))

        waits("scalar", sq=c0)
        sc_op(lambda s: s.activation(out=asc[0:1, 1:2], in_=asc[0:1, 0:1], func=AF.Sigmoid))
        sA = P.val("sm")
        waits("vector", sm=sA)
        v_op(lambda v: v.memset(asc[0:1, 2:3], 1.0))
        v_op(lambda v: v.tensor_tensor_scan(
            g2[0:1, :], bcast_free(asc[0:1, 1:2], NTOK), bcast_free(asc[0:1, 2:3], NTOK),
            0.0, ALU.mult, ALU.add))
        vG = P.val("vm")
        wait("gpsimd", "vm", vG)
        for seg in range(NTOK // 128):
            g_dma(g2[1:2, seg * 128:(seg + 1) * 128], conesr[0:1, :])
        waits("tensor", vm=vG)
        waits("tensor", vm=P.val("vm"))
        t_op(lambda t: t.matmul(psRT[:, 8:9], conesr[:, :], asc[0:1, 1:2], start=True, stop=True))
        waits("vector", tm=P.val("tm"))
        v_op(lambda v: v.tensor_copy(out=acol[:, :], in_=psRT[:, 8:9]))
        v_op(lambda v: v.tensor_add(out=gid0[:, :], in0=ciop[:, :], in1=cgidb[:, :]))

        wait("sync", "vm", vSct)
        ldrw = s_dma(cstage[0:8, :], rw[:, :])
        s_barrier()
        waits("tensor", sq=ldrw, vm=vClamp)
        for dc in range(NDC):
            waits("tensor", vm=P.val("vm"))
            t_op(lambda t, dc=dc: t.transpose(psRT[0:128, 24:32], cstage[0:8, dc * 128:(dc + 1) * 128], idf[0:8, 0:8]))
            waits("vector", tm=P.val("tm"))
            v_op(lambda v, dc=dc: v.tensor_copy(out=rwTb[:, dc, :], in_=psRT[0:128, 24:32]))
            waits("tensor", vm=P.val("vm"))
            t_op(lambda t, dc=dc: t.transpose(psRT[0:128, 16:17], msa[0:1, dc * 128:(dc + 1) * 128], idf[0:1, 0:1]))
            waits("vector", tm=P.val("tm"))
            v_op(lambda v, dc=dc: v.tensor_copy(out=safm[:, dc:dc + 1], in_=psRT[0:128, 16:17]))
            waits("tensor", vm=P.val("vm"))
            t_op(lambda t, dc=dc: t.transpose(psRT[0:128, 17:18], mca[0:1, dc * 128:(dc + 1) * 128], idf[0:1, 0:1]))
            waits("vector", tm=P.val("tm"))
            v_op(lambda v, dc=dc: v.tensor_copy(out=ca1fm[:, dc:dc + 1], in_=psRT[0:128, 17:18]))
        v_op(lambda v: v.tensor_scalar_add(out=ca1fm[:, :], in0=ca1fm[:, :], scalar1=1.0))
        vFM = P.val("vm")
        waits("tensor", vm=vFM)
        for dc in range(NDC):
            for pc in range(NDC):
                buf = pc % 2
                if pc >= 2 or dc > 0:
                    wait("sync", "tm", P.val("tm"))
                ld = s_dma(wst[:, buf, :], W_in[pc * 128:(pc + 1) * 128, dc * 128:(dc + 1) * 128], sem="wq%d" % buf)
                waits("tensor", **{("wq%d" % buf): ld}, vm=P.val("vm"))
                t_op(lambda t, buf=buf: t.transpose(psY0[:, 0:128], wst[:, buf, :], idf[:, :]))
                waits("vector", tm=P.val("tm"))
                v_op(lambda v, pc=pc: v.tensor_copy(out=wtf[:, pc * 128:(pc + 1) * 128], in_=psY0[:, 0:128]))
            vW = P.val("vm")
            waits("tensor", vm=vW)
            t_op(lambda t, dc=dc: t.matmul(
                psRT[0:1, 0:512], safm[:, dc:dc + 1], wtf[:, 0:512],
                start=(dc == 0), stop=(dc == NDC - 1), skip_group_check=True))
            t_op(lambda t, dc=dc: t.matmul(
                psY1[0:1, 0:512], safm[:, dc:dc + 1], wtf[:, 512:1024],
                start=(dc == 0), stop=(dc == NDC - 1), skip_group_check=True))
            waits("vector", tm=P.val("tm"))
            v_op(lambda v, dc=dc: v.tensor_scalar_mul(out=wtf[:, :], in0=wtf[:, :], scalar1=ca1fm[:, dc:dc + 1]))
            v_op(lambda v, dc=dc: v.reduce_sum(wbar[:, dc:dc + 1], wtf[:, :], axis=AX.X))
            v_op(lambda v, dc=dc: v.tensor_scalar_mul(out=wbar[:, dc:dc + 1], in0=wbar[:, dc:dc + 1], scalar1=1.0 / D))
            v_op(lambda v, dc=dc: v.tensor_scalar(out=wtf[:, :], in0=wtf[:, :], scalar1=wbar[:, dc:dc + 1], scalar2=None, op0=ALU.subtract))
            v_op(lambda v, dc=dc: v.tensor_copy(out=WTb[:, dc, :], in_=wtf[:, :]))
        waits("vector", tm=P.val("tm"))
        v_op(lambda v: v.tensor_copy(out=vrow[0:1, 0:512], in_=psRT[0:1, 0:512]))
        v_op(lambda v: v.tensor_copy(out=vrow[0:1, 512:1024], in_=psY1[0:1, 0:512]))
        ldb2 = s_dma(bcrow[:, :], battn[:, :])
        s_barrier()
        v_op(lambda v: v.reduce_sum(smallsc[0:1, 1:2], vrow[:, :], axis=AX.X))
        v_op(lambda v: v.tensor_scalar_mul(out=smallsc[0:1, 1:2], in0=smallsc[0:1, 1:2], scalar1=1.0 / D))
        v_op(lambda v: v.tensor_scalar(out=vrow[:, :], in0=vrow[:, :], scalar1=smallsc[0:1, 1:2], scalar2=None, op0=ALU.subtract))
        waits("vector", sq=ldb2)
        v_op(lambda v: v.reduce_sum(smallsc[0:1, 2:3], bcrow[:, :], axis=AX.X))
        v_op(lambda v: v.tensor_scalar_mul(out=smallsc[0:1, 2:3], in0=smallsc[0:1, 2:3], scalar1=1.0 / D))
        v_op(lambda v: v.tensor_scalar(out=bcrow[:, :], in0=bcrow[:, :], scalar1=smallsc[0:1, 2:3], scalar2=None, op0=ALU.subtract))
        vVB = P.val("vm")
        wait("gpsimd", "vm", vVB)
        g_dma(vb2[0:1, :], vrow[0:1, :])
        g_dma(vb2[1:2, :], bcrow[0:1, :])

        bar = dict(sq=P.val("sq"), gq=P.val("gq"), tm=P.val("tm"),
                   vm=P.val("vm"), sm=P.val("sm"))
        for eng in ("sync", "gpsimd", "tensor", "vector", "scalar"):
            waits(eng, **bar)
        prep.close()

        loop = ExitStack()
        xt = loop.enter_context(nc.sbuf_tensor("xt", [128, 2, D], F32))
        yct = loop.enter_context(nc.sbuf_tensor("yct", [128, 2, D], BF16))
        xnbf = loop.enter_context(nc.sbuf_tensor("xnbf", [128, D], BF16))
        sqscr = loop.enter_context(nc.sbuf_tensor("sqscr", [128, D], F32))
        acscr = loop.enter_context(nc.sbuf_tensor("acscr", [128, D], F32))
        usb = loop.enter_context(nc.sbuf_tensor("usb", [128, D], F32))
        xpt = loop.enter_context(nc.sbuf_tensor("xpt", [128, 2, D], F32))
        h2t = loop.enter_context(nc.sbuf_tensor("h2t", [128, 2, D], F32))
        gbscr = loop.enter_context(nc.sbuf_tensor("gbscr", [128, 2, D], F32))
        h2bf = loop.enter_context(nc.sbuf_tensor("h2bf", [128, D], BF16))
        h2fm = loop.enter_context(nc.sbuf_tensor("h2fm", [128, 2, 128], BF16))
        lgfm = loop.enter_context(nc.sbuf_tensor("lgfm", [8, 128], F32))
        lgall = loop.enter_context(nc.sbuf_tensor("lgall", [128, NT, 8], F32))
        statYa = loop.enter_context(nc.sbuf_tensor("statYa", [128, NT], F32))
        statYb = loop.enter_context(nc.sbuf_tensor("statYb", [128, NT], F32))
        strsy = loop.enter_context(nc.sbuf_tensor("strsy", [128, NT], F32))
        st2s = loop.enter_context(nc.sbuf_tensor("st2s", [128, NT], F32))
        st2q = loop.enter_context(nc.sbuf_tensor("st2q", [128, NT], F32))
        stmu2 = loop.enter_context(nc.sbuf_tensor("stmu2", [128, NT], F32))
        strs2 = loop.enter_context(nc.sbuf_tensor("strs2", [128, NT], F32))
        tv1 = loop.enter_context(nc.sbuf_tensor("tv1", [128, 8], F32))
        tv2 = loop.enter_context(nc.sbuf_tensor("tv2", [128, 8], F32))
        m1c = loop.enter_context(nc.sbuf_tensor("m1c", [128, 1], F32))
        nm1c = loop.enter_context(nc.sbuf_tensor("nm1c", [128, 1], F32))
        m2c = loop.enter_context(nc.sbuf_tensor("m2c", [128, 1], F32))
        rcpc = loop.enter_context(nc.sbuf_tensor("rcpc", [128, 1], F32))
        wtile = loop.enter_context(nc.sbuf_tensor("wtile", [128, 2, 8], F32))
        masksb = loop.enter_context(nc.sbuf_tensor("masksb", [128, 7], F32))
        Psb = loop.enter_context(nc.sbuf_tensor("Psb", [128, 7], F32))
        cmpst = loop.enter_context(nc.sbuf_tensor("cmpst", [128, 7], F32))
        offs_sb = loop.enter_context(nc.sbuf_tensor("offs_sb", [7, 1], F32))
        idxf = loop.enter_context(nc.sbuf_tensor("idxf", [7, 1], F32))
        idxu = loop.enter_context(nc.sbuf_tensor("idxu", [7, 1], U32))
        gidj = loop.enter_context(nc.sbuf_tensor("gidj", [128, 1], F32))
        ohbuf = loop.enter_context(nc.sbuf_tensor("ohbuf", [128, 2, 128], F32))
        cmpT = loop.enter_context(nc.sbuf_tensor("cmpT", [7, 2, 128], F32))
        psY2 = loop.enter_context(nc.psum_tensor("psY2", [128, 512], F32))
        psY3 = loop.enter_context(nc.psum_tensor("psY3", [128, 512], F32))

        vZ = v_op(lambda v: v.memset(acscr[:, :], 0.0))
        wait("sync", "vm", vZ)
        def zl(s):
            s.dma_start(out=lists_o[:, :], in_=acscr[:, 0:238]).then_inc(LZ, 16)
        P.on("sync", zl)
        P.inc("lz", 16)
        wait("vector", "lz", 16)
        wait("gpsimd", "lz", 16)

        sticks, vticks = [], []
        for j in range(NT):
            buf = j % 2
            if j >= 2:
                wait("sync", "sm", sticks[j - 2])
                wait("sync", "vm", vticks[j - 2])
            ld = s_dma(xt[:, buf, :], x_in[j * 128:(j + 1) * 128, :], sem="xq%d" % buf)
            waits("scalar", **{("xq%d" % buf): ld})
            sticks.append(sc_op(lambda s, j=j, buf=buf: s.activation(
                out=sqscr[:, :], in_=xt[:, buf, :], func=AF.Square,
                accum_out=stat_sq[:, j:j + 1])))
            waits("vector", **{("xq%d" % buf): ld})
            vticks.append(v_op(lambda v, j=j, buf=buf: v.tensor_scalar(
                out=acscr[:, :], in0=xt[:, buf, :], scalar1=1.0, scalar2=None,
                op0=ALU.mult, op1=ALU.add, accum_out=stat_s1[:, j:j + 1])))
        waits("vector", sm=P.val("sm"))
        v_op(lambda v: v.tensor_scalar_mul(out=stat_mu[:, :], in0=stat_s1[:, :], scalar1=1.0 / D))
        v_op(lambda v: v.tensor_scalar_mul(out=stat_sq[:, :], in0=stat_sq[:, :], scalar1=1.0 / D))
        v_op(lambda v: v.tensor_mul(out=stat_s1[:, :], in0=stat_mu[:, :], in1=stat_mu[:, :]))
        v_op(lambda v: v.tensor_sub(out=stat_sq[:, :], in0=stat_sq[:, :], in1=stat_s1[:, :]))
        v_op(lambda v: v.tensor_scalar_add(out=stat_sq[:, :], in0=stat_sq[:, :], scalar1=EPS))
        v_op(lambda v: v.reciprocal(out=stat_rs[:, :], in_=stat_sq[:, :]))
        waits("scalar", vm=P.val("vm"))
        sc_op(lambda s: s.activation(out=stat_rs[:, :], in_=stat_rs[:, :], func=AF.Sqrt))
        sStats = P.val("sm")

        waits("vector", sm=sStats)
        wait("sync", "vm", P.val("vm"))
        wait("sync", "sm", P.val("sm"))
        xnticks = []
        trC = [0]
        pstTick = {0: 0, 1: 0}
        for j in range(NT):
            buf = j % 2
            if j >= 2:
                wait("sync", "vm", xnticks[j - 2][0])
                wait("sync", "sm", xnticks[j - 2][1])
            ld = s_dma(xt[:, buf, :], x_in[j * 128:(j + 1) * 128, :], sem="xq%d" % buf)
            waits("vector", **{("xq%d" % buf): ld})
            wait("vector", "tm", trC[0])
            v_op(lambda v, j=j, buf=buf: v.tensor_scalar(
                out=xnbf[:, :], in0=xt[:, buf, :], scalar1=stat_mu[:, j:j + 1],
                scalar2=stat_rs[:, j:j + 1], op0=ALU.subtract, op1=ALU.mult))
            waits("tensor", vm=P.val("vm"))
            for dc in range(NDC):
                pp = dc % 2
                pst = psTB0 if pp == 0 else psTB1
                waits("tensor", sm=pstTick[pp])
                t_op(lambda t, dc=dc, pst=pst: t.transpose(
                    pst[:, :], xnbf[:, dc * 128:(dc + 1) * 128], idb[:, :]))
                waits("scalar", tm=P.val("tm"))
                pstTick[pp] = sc_op(lambda s, j=j, dc=dc, pst=pst: s.copy(
                    out=xnT[:, dc, j * 128:(j + 1) * 128], in_=pst[:, :]))
            trC[0] = P.val("tm")
            xnticks.append((P.val("vm"), P.val("sm")))

        waits("vector", sm=P.val("sm"))
        for dc in range(NDC):
            v_op(lambda v, dc=dc: v.tensor_tensor_scan(
                xnT[:, dc, :], bcast_free(acol[:, 0:1], NTOK),
                xnT[:, dc, :], 0.0, ALU.mult, ALU.add))
        vScan = P.val("vm")

        wait("tensor", "vm", vScan)
        evT, scT4 = [], []
        soh_cnt = [0, 0]
        sow_cnt = [0, 0]
        for j in range(NT):
            pb = j % 2
            psA_ = psY0 if pb == 0 else psY2
            psB_ = psY1 if pb == 0 else psY3
            if j >= 2:
                wait("tensor", "vm", evT[j - 2])
                wait("tensor", "sm", scT4[j - 2])
            for ec in range(2):
                psY = psA_ if ec == 0 else psB_
                for dc in range(NDC):
                    t_op(lambda t, j=j, dc=dc, ec=ec, psY=psY: t.matmul(
                        psY[:, :], xnT[:, dc, j * 128:(j + 1) * 128],
                        WTb[:, dc, ec * 512:(ec + 1) * 512],
                        start=(dc == 0), stop=False), tick=False)
                t_op(lambda t, j=j, ec=ec, psY=psY: t.matmul(
                    psY[:, :], g2[0:2, j * 128:(j + 1) * 128],
                    vb2[0:2, ec * 512:(ec + 1) * 512], start=False, stop=True))
            tY = P.val("tm")
            waits("scalar", tm=tY)
            sc_op(lambda s, j=j, psA_=psA_: s.activation(out=sqscr[:, 0:512], in_=psA_[:, :], func=AF.Square, accum_out=statYa[:, j:j + 1]))
            sc_op(lambda s, j=j, psB_=psB_: s.activation(out=sqscr[:, 512:1024], in_=psB_[:, :], func=AF.Square, accum_out=statYb[:, j:j + 1]))
            scT4.append(P.val("sm"))
            waits("vector", tm=tY, sm=P.val("sm"))
            if j >= 2:
                wait("vector", "soh%d" % pb, 16 * soh_cnt[pb])
            v_op(lambda v, pb=pb, psA_=psA_: v.tensor_copy(out=yct[:, pb, 0:512], in_=psA_[:, :]))
            v_op(lambda v, pb=pb, psB_=psB_: v.tensor_copy(out=yct[:, pb, 512:1024], in_=psB_[:, :]))
            evT.append(P.val("vm"))
            wait("sync", "vm", evT[j])
            s_dma(ycs_o[j * 128:(j + 1) * 128, :], yct[:, pb, :], sem="soh%d" % pb)
            soh_cnt[pb] += 1
        waits("vector", sm=P.val("sm"))
        v_op(lambda v: v.tensor_add(out=statYa[:, :], in0=statYa[:, :], in1=statYb[:, :]))
        v_op(lambda v: v.tensor_scalar(out=statYa[:, :], in0=statYa[:, :], scalar1=1.0 / D, scalar2=EPS, op0=ALU.mult, op1=ALU.add))
        v_op(lambda v: v.reciprocal(out=strsy[:, :], in_=statYa[:, :]))
        waits("scalar", vm=P.val("vm"))
        sc_op(lambda s: s.activation(out=strsy[:, :], in_=strsy[:, :], func=AF.Sqrt))
        sRsy = P.val("sm")

        wait("sync", "vm", P.val("vm"))
        wait("sync", "sm", P.val("sm"))
        wait("sync", "soh0", P.val("soh0"))
        wait("sync", "soh1", P.val("soh1"))
        waits("vector", sm=sRsy)
        p5vT = []
        p5sT = []
        sox_cnt = [0, 0]
        ldx_t = {}
        ldy_t = {}
        def p5_load(j):
            buf = j % 2
            ldx_t[j] = s_dma(xt[:, buf, :], x_in[j * 128:(j + 1) * 128, :], sem="xq%d" % buf)
            ldy_t[j] = s_dma(yct[:, buf, :], ycs_o[j * 128:(j + 1) * 128, :], sem="wq%d" % buf)
        p5_load(0)
        p5_load(1)
        for j in range(NT):
            buf = j % 2
            waits("vector", **{("xq%d" % buf): ldx_t[j], ("wq%d" % buf): ldy_t[j]})
            v_op(lambda v, j=j, buf=buf: v.scalar_tensor_tensor(
                out=usb[:, :], in0=yct[:, buf, :], scalar=strsy[:, j:j + 1],
                in1=Gt[:, :], op0=ALU.mult, op1=ALU.mult))
            if j >= 2:
                wait("vector", "sox%d" % buf, 16 * sox_cnt[buf])
            v_op(lambda v, buf=buf: v.tensor_add(out=xpt[:, buf, :], in0=usb[:, :], in1=xt[:, buf, :]))
            v_op(lambda v, j=j, buf=buf: v.scalar_tensor_tensor(
                out=xpt[:, buf, :], in0=xpt[:, buf, :], scalar=1.0, in1=Bt[:, :],
                op0=ALU.mult, op1=ALU.add, accum_out=st2s[:, j:j + 1]))
            p5vT.append(P.val("vm"))
            waits("scalar", vm=P.val("vm"))
            sc_op(lambda s, j=j, buf=buf: s.activation(
                out=sqscr[:, :], in_=xpt[:, buf, :], func=AF.Square,
                accum_out=st2q[:, j:j + 1]))
            p5sT.append(P.val("sm"))
            wait("sync", "vm", p5vT[j])
            s_dma(xpp_o[j * 128:(j + 1) * 128, :], xpt[:, buf, :], sem="sox%d" % buf)
            sox_cnt[buf] += 1
            if j + 2 < NT:
                wait("sync", "sm", p5sT[j])
                p5_load(j + 2)
        waits("vector", sm=P.val("sm"))
        v_op(lambda v: v.tensor_scalar_mul(out=stmu2[:, :], in0=st2s[:, :], scalar1=1.0 / D))
        v_op(lambda v: v.tensor_scalar_mul(out=st2q[:, :], in0=st2q[:, :], scalar1=1.0 / D))
        v_op(lambda v: v.tensor_mul(out=st2s[:, :], in0=stmu2[:, :], in1=stmu2[:, :]))
        v_op(lambda v: v.tensor_sub(out=st2q[:, :], in0=st2q[:, :], in1=st2s[:, :]))
        v_op(lambda v: v.tensor_scalar_add(out=st2q[:, :], in0=st2q[:, :], scalar1=EPS))
        v_op(lambda v: v.reciprocal(out=strs2[:, :], in_=st2q[:, :]))
        waits("scalar", vm=P.val("vm"))
        sc_op(lambda s: s.activation(out=strs2[:, :], in_=strs2[:, :], func=AF.Sqrt))
        sRs2 = P.val("sm")

        wait("sync", "vm", P.val("vm"))
        wait("sync", "sm", P.val("sm"))
        wait("sync", "sox0", P.val("sox0"))
        wait("sync", "sox1", P.val("sox1"))
        waits("vector", sm=sRs2)
        p6T = []
        ldp_t = {}
        def p6_load(j):
            buf = j % 2
            ldp_t[j] = s_dma(xt[:, buf, :], xpp_o[j * 128:(j + 1) * 128, :], sem="xq%d" % buf)
        p6_load(0)
        p6_load(1)
        def h2slice(j):
            return xnT[:, j // 4, (j % 4) * 1024:(j % 4 + 1) * 1024]
        for j in range(NT):
            buf = j % 2
            waits("vector", **{("xq%d" % buf): ldp_t[j]})
            v_op(lambda v, j=j, buf=buf: v.tensor_scalar(
                out=usb[:, :], in0=xt[:, buf, :], scalar1=stmu2[:, j:j + 1],
                scalar2=strs2[:, j:j + 1], op0=ALU.subtract, op1=ALU.mult))
            if j >= 2:
                wait("vector", "soh%d" % buf, 16 * soh_cnt[buf])
                wait("vector", "sow%d" % buf, 16 * sow_cnt[buf])
            v_op(lambda v, buf=buf: v.scalar_tensor_tensor(
                out=h2t[:, buf, :], in0=usb[:, :], scalar=1.0, in1=cb1t[:, :],
                op0=ALU.mult, op1=ALU.mult))
            v_op(lambda v, buf=buf: v.tensor_add(out=h2t[:, buf, :], in0=h2t[:, buf, :], in1=sbt[:, :]))
            vH2 = P.val("vm")
            wait("sync", "vm", vH2)
            s_dma(h2b_o[j * 128:(j + 1) * 128, :], h2t[:, buf, :], sem="soh%d" % buf)
            soh_cnt[buf] += 1
            v_op(lambda v, buf=buf: v.tensor_mul(out=gbscr[:, buf, :], in0=h2t[:, buf, :], in1=gbt[:, :]))
            vGB = P.val("vm")
            wait("sync", "vm", vGB)
            s_dma(gbh2_o[j * 128:(j + 1) * 128, :], gbscr[:, buf, :], sem="sow%d" % buf)
            sow_cnt[buf] += 1
            v_op(lambda v, j=j, buf=buf: v.tensor_copy(out=h2slice(j), in_=h2t[:, buf, :]))
            p6T.append(P.val("vm"))
            if j + 2 < NT:
                wait("sync", "vm", p6T[j])
                p6_load(j + 2)

        waits("tensor", vm=P.val("vm"))
        for j in range(NT):
            cpT = {}
            for dc in range(NDC + 2):
                if dc < NDC:
                    pp = dc % 2
                    pst = psTB0 if pp == 0 else psTB1
                    waits("tensor", sm=pstTick[pp])
                    t_op(lambda t, j=j, dc=dc, pst=pst: t.transpose(
                        pst[:, :], h2slice(j)[:, dc * 128:(dc + 1) * 128], idb[:, :]))
                if dc >= 2:
                    e = dc - 2
                    waits("tensor", sm=cpT[e])
                    t_op(lambda t, e=e: t.matmul(
                        psRT[0:8, 0:128], rwTb[:, e, :], h2fm[:, e % 2, :],
                        start=(e == 0), stop=(e == NDC - 1), skip_group_check=True))
                if dc < NDC:
                    pp = dc % 2
                    pst = psTB0 if pp == 0 else psTB1
                    waits("scalar", tm=P.val("tm"))
                    pstTick[pp] = sc_op(lambda s, dc=dc, pst=pst: s.copy(
                        out=h2fm[:, dc % 2, :], in_=pst[:, :]))
                    cpT[dc] = pstTick[pp]
            waits("scalar", tm=P.val("tm"), vm=P.val("vm"))
            sc_op(lambda s: s.copy(out=lgfm[:, :], in_=psRT[0:8, 0:128]))
            waits("tensor", sm=P.val("sm"))
            t_op(lambda t: t.transpose(psRT[0:128, 128:136], lgfm[:, :], idf[0:8, 0:8]))
            waits("vector", tm=P.val("tm"))
            v_op(lambda v, j=j: v.tensor_add(out=lgall[:, j, :], in0=psRT[0:128, 128:136], in1=rbt[:, :]))
        wait("sync", "vm", P.val("vm"))
        wait("sync", "sm", P.val("sm"))
        wait("sync", "sow0", P.val("sow0"))
        wait("sync", "sow1", P.val("sow1"))
        scatticks = []
        ohticks = {0: 0, 1: 0}
        p7T = []
        l7x = {}
        l7g = {}
        def p7_load(j):
            buf = j % 2
            l7x[j] = s_dma(xt[:, buf, :], xpp_o[j * 128:(j + 1) * 128, :], sem="xq%d" % buf)
            l7g[j] = s_dma(gbscr[:, buf, :], gbh2_o[j * 128:(j + 1) * 128, :], sem="wq%d" % buf)
        p7_load(0)
        p7_load(1)
        for j in range(NT):
            buf = j % 2
            ldp = l7x[j]
            ldg = l7g[j]
            v_op(lambda v, j=j: v.reduce_max(m1c[:, :], lgall[:, j, :], axis=AX.X))
            v_op(lambda v: v.tensor_scalar_mul(out=nm1c[:, :], in0=m1c[:, :], scalar1=-1.0))
            v_op(lambda v, j=j: v.tensor_scalar(out=tv1[:, :], in0=lgall[:, j, :], scalar1=m1c[:, 0:1], scalar2=None, op0=ALU.is_equal))
            v_op(lambda v, j=j: v.scalar_tensor_tensor(out=tv2[:, :], in0=tv1[:, :], scalar=-1e30, in1=lgall[:, j, :], op0=ALU.mult, op1=ALU.add))
            v_op(lambda v: v.reduce_max(m2c[:, :], tv2[:, :], axis=AX.X))
            v_op(lambda v, j=j: v.tensor_scalar(out=tv1[:, :], in0=lgall[:, j, :], scalar1=m2c[:, 0:1], scalar2=None, op0=ALU.is_ge))
            waits("scalar", vm=P.val("vm"))
            sc_op(lambda s, j=j: s.activation(out=tv2[:, :], in_=lgall[:, j, :], func=AF.Exp, bias=nm1c[:, 0:1], scale=1.0))
            waits("vector", sm=P.val("sm"))
            v_op(lambda v: v.reduce_sum(rcpc[:, :], tv2[:, :], axis=AX.X))
            v_op(lambda v: v.reciprocal(out=rcpc[:, :], in_=rcpc[:, :]))
            if j >= 2:
                wait("vector", "sow%d" % buf, 16 * sow_cnt[buf])
            v_op(lambda v, buf=buf: v.scalar_tensor_tensor(
                out=wtile[:, buf, :], in0=tv2[:, :], scalar=rcpc[:, 0:1], in1=tv1[:, :],
                op0=ALU.mult, op1=ALU.mult))
            vW = P.val("vm")
            wait("sync", "vm", vW)
            s_dma(w_o[j * 128:(j + 1) * 128, :], wtile[:, buf, :], sem="sow%d" % buf)
            sow_cnt[buf] += 1
            if j >= 2:
                wait("vector", "sox%d" % buf, 16 * sox_cnt[buf])
            waits("vector", **{("xq%d" % buf): ldp, ("wq%d" % buf): ldg})
            v_op(lambda v, buf=buf: v.scalar_tensor_tensor(
                out=xpt[:, buf, :], in0=gbscr[:, buf, :], scalar=wtile[:, buf, 7:8],
                in1=xt[:, buf, :], op0=ALU.mult, op1=ALU.add))
            vXf = P.val("vm")
            wait("sync", "vm", vXf)
            s_dma(xpp_o[j * 128:(j + 1) * 128, :], xpt[:, buf, :], sem="sox%d" % buf)
            sox_cnt[buf] += 1
            v_op(lambda v, buf=buf: v.tensor_scalar(out=masksb[:, :], in0=wtile[:, buf, 0:7], scalar1=0.0, scalar2=None, op0=ALU.is_gt))
            waits("tensor", vm=P.val("vm"))
            t_op(lambda t: t.matmul(psRT[:, 256:263], ctriu[:, :], masksb[:, :], start=True, stop=True, skip_group_check=True))
            waits("vector", tm=P.val("tm"))
            v_op(lambda v: v.tensor_copy(out=Psb[:, :], in_=psRT[:, 256:263]))
            if j == 0:
                v_op(lambda v: v.memset(offs_sb[:, :], 0.0))
            else:
                v_op(lambda v: v.tensor_copy(out=offs_sb[:, :], in_=psOf[:, :]))
            waits("tensor", vm=P.val("vm"))
            t_op(lambda t, j=j: t.matmul(psOf[:, :], masksb[:, :], conesc[:, :],
                                         start=(j == 0), stop=(j == NT - 1), skip_group_check=True))
            v_op(lambda v, j=j: v.tensor_scalar_add(out=gidj[:, :], in0=gid0[:, :], scalar1=float(j * 128)))
            wait("vector", "sc", P.val("sc"))
            v_op(lambda v: v.tensor_add(out=idxf[:, :], in0=offs_sb[:, :], in1=ce4352[:, :]))
            v_op(lambda v: v.tensor_copy(out=idxu[:, :], in_=idxf[:, :]))
            for e in range(7):
                ob = e % 2
                wait("vector", "tm", ohticks[ob])
                v_op(lambda v, e=e, ob=ob: v.tensor_scalar(
                    out=ohbuf[:, ob, :], in0=ciota[:, :], scalar1=Psb[:, e:e + 1],
                    scalar2=masksb[:, e:e + 1], op0=ALU.is_equal, op1=ALU.mult))
                waits("tensor", vm=P.val("vm"))
                ohticks[ob] = t_op(lambda t, e=e, ob=ob: t.matmul(
                    psRT[:, 288 + e:289 + e], ohbuf[:, ob, :], gidj[:, :],
                    start=True, stop=True, skip_group_check=True))
            waits("vector", tm=P.val("tm"))
            v_op(lambda v: v.tensor_copy(out=cmpst[:, :], in_=psRT[:, 288:295]))
            waits("tensor", vm=P.val("vm"))
            t_op(lambda t: t.transpose(psRT[0:7, 320:448], cmpst[:, :], idf[:, :]))
            waits("vector", tm=P.val("tm"))
            if j >= 2:
                wait("vector", "sc", scatticks[j - 2])
            cbuf = j % 2
            v_op(lambda v, cbuf=cbuf: v.tensor_copy(out=cmpT[0:7, cbuf, :], in_=psRT[0:7, 320:448]))
            vT2 = P.val("vm")
            wait("gpsimd", "vm", vT2)
            wait("gpsimd", "sc", P.val("sc"))
            def scat(g, cbuf=cbuf):
                g.indirect_dma_start(
                    out=lists_o[:, :], out_offset=IndirectOffsetOnAxis(idxu[:, :], 0),
                    in_=cmpT[0:7, cbuf, :], in_offset=None).then_inc(SC, 16)
            P.on("gpsimd", scat)
            scatticks.append(P.inc("sc", 16))
            p7T.append(P.val("vm"))
            if j + 2 < NT:
                wait("sync", "vm", p7T[j])
                p7_load(j + 2)

        waits("vector", tm=P.val("tm"))
        vCnt = v_op(lambda v: v.tensor_copy(out=offs_sb[:, :], in_=psOf[:, :]))
        wait("sync", "vm", vCnt)
        s_dma(counts_o[:, :], offs_sb[:, :])

        wait("sync", "sq", P.val("sq"))
        for snm in ("soh0", "soh1", "sow0", "sow1", "sox0", "sox1", "xq0", "xq1", "wq0", "wq1"):
            wait("sync", snm, P.val(snm))
        wait("gpsimd", "sc", P.val("sc"))
        wait("gpsimd", "gq", P.val("gq"))
        loop.close()

        @block.sync
        def _(s):
            P.run("sync", s)

        @block.tensor
        def _(t):
            P.run("tensor", t)

        @block.vector
        def _(v):
            P.run("vector", v)

        @block.scalar
        def _(s):
            P.run("scalar", s)

        @block.gpsimd
        def _(g):
            P.run("gpsimd", g)

    return nc

def host_consts_l2(core_id):
    import ml_dtypes
    c = {}
    c["ident_bf16"] = np.eye(128, dtype=np.float32).astype(ml_dtypes.bfloat16)
    c["ident_f32b"] = np.eye(128, dtype=np.float32)
    c["trashrow"] = np.full((1, 8), GPAD, np.uint32)
    c["ident8"] = np.eye(8, dtype=np.float32)
    c["triu8"] = np.triu(np.ones((8, 8), np.float32), k=1)
    return c


def build_l2():
    nc = bass.Bass("TRN2", target_bir_lowering=False, debug=False)
    h2fa = nc.dram_tensor("h2fa", [NTOKG, 512], F32, kind="ExternalInput").ap()
    h2fb = nc.dram_tensor("h2fb", [NTOKG, 512], F32, kind="ExternalInput").ap()
    mylists = nc.dram_tensor("mylists", [8, LSTRIDE], F32, kind="ExternalInput").ap()
    counts_e = nc.dram_tensor("counts_e", [8, 1], F32, kind="ExternalInput").ap()
    wi = nc.dram_tensor("wi", [D, I], F32, kind="ExternalInput").ap()
    wo = nc.dram_tensor("wo", [I, D], F32, kind="ExternalInput").ap()
    ident_bf16 = nc.dram_tensor("ident_bf16", [128, 128], BF16, kind="ExternalInput").ap()
    ident_f32b = nc.dram_tensor("ident_f32b", [128, 128], F32, kind="ExternalInput").ap()
    trashrow = nc.dram_tensor("trashrow", [1, 8], U32, kind="ExternalInput").ap()
    ident8 = nc.dram_tensor("ident8", [8, 8], F32, kind="ExternalInput").ap()
    triu8 = nc.dram_tensor("triu8", [8, 8], F32, kind="ExternalInput").ap()
    eout = nc.dram_tensor("eout", [CAP, D], F32, kind="ExternalOutput").ap()
    glist_t = nc.dram_tensor("glist", [GPAD + 4096, 1], F32, kind="ExternalOutput")
    glist = glist_t.ap()
    glist2 = glist_t.ap()

    P = Plan()
    stack = ExitStack()
    with stack:
        idb = stack.enter_context(nc.sbuf_tensor("idb", [128, 128], BF16))
        idf2 = stack.enter_context(nc.sbuf_tensor("idf2", [128, 128], F32))
        idx2 = stack.enter_context(nc.sbuf_tensor("idx2", [2, 8], U32))
        seg2 = stack.enter_context(nc.sbuf_tensor("seg2", [2, LSTRIDE], F32))
        id8 = stack.enter_context(nc.sbuf_tensor("id8", [8, 8], F32))
        tr8 = stack.enter_context(nc.sbuf_tensor("tr8", [8, 8], F32))
        cnts = stack.enter_context(nc.sbuf_tensor("cnts", [8, 1], F32))
        cumo = stack.enter_context(nc.sbuf_tensor("cumo", [8, 1], F32))
        cumr = stack.enter_context(nc.sbuf_tensor("cumr", [1, 8], F32))
        cumru = stack.enter_context(nc.sbuf_tensor("cumru", [1, 8], U32))
        idlist = stack.enter_context(nc.sbuf_tensor("idlist", [128, 32], F32))
        idu = stack.enter_context(nc.sbuf_tensor("idu", [128, 32], U32))
        wi_sb = stack.enter_context(nc.sbuf_tensor("wi_sb", [128, 8, I], BF16))
        wo_sb = stack.enter_context(nc.sbuf_tensor("wo_sb", [128, 32, D], BF16))
        gbuf = stack.enter_context(nc.sbuf_tensor("gbuf", [128, 2, D], F32))
        h2fm = stack.enter_context(nc.sbuf_tensor("h2fm", [128, 8, 512], BF16))
        hbf = stack.enter_context(nc.sbuf_tensor("hbf", [128, 32, 512], BF16))
        outsb = stack.enter_context(nc.sbuf_tensor("outsb", [128, 2, D], F32))
        ps1a = stack.enter_context(nc.psum_tensor("ps1a", [128, 512], F32))
        ps1b = stack.enter_context(nc.psum_tensor("ps1b", [128, 512], F32))
        ps2a = stack.enter_context(nc.psum_tensor("ps2a", [128, 512], F32))
        ps2b = stack.enter_context(nc.psum_tensor("ps2b", [128, 512], F32))
        psta = stack.enter_context(nc.psum_tensor("psta", [128, 128], F32))
        pstb = stack.enter_context(nc.psum_tensor("pstb", [128, 128], F32))
        psm = stack.enter_context(nc.psum_tensor("psm", [8, 8], F32))
        SQ = stack.enter_context(nc.semaphore("sq"))
        GL = stack.enter_context(nc.semaphore("gl"))
        GA0 = stack.enter_context(nc.semaphore("ga0"))
        GA1 = stack.enter_context(nc.semaphore("ga1"))
        WIQ = stack.enter_context(nc.semaphore("wiq"))
        WOQ = stack.enter_context(nc.semaphore("woq"))
        EO0 = stack.enter_context(nc.semaphore("eo0"))
        EO1 = stack.enter_context(nc.semaphore("eo1"))
        TM = stack.enter_context(nc.semaphore("tm"))
        VM = stack.enter_context(nc.semaphore("vm"))
        SM = stack.enter_context(nc.semaphore("sm"))
        block = stack.enter_context(nc.Block())

        SEMS = {"sq": SQ, "gl": GL, "ga0": GA0, "ga1": GA1, "wiq": WIQ, "woq": WOQ,
                "eo0": EO0, "eo1": EO1, "tm": TM, "vm": VM, "sm": SM}

        def s_dma(out, in_, sem="sq"):
            S = SEMS[sem]
            def f(s, out=out, in_=in_, S=S):
                s.dma_start(out=out, in_=in_).then_inc(S, 16)
            P.on("sync", f)
            return P.inc(sem, 16)

        def g_dma(out, in_, sem):
            S = SEMS[sem]
            def f(g, out=out, in_=in_, S=S):
                g.dma_start(out=out, in_=in_).then_inc(S, 16)
            P.on("gpsimd", f)
            return P.inc(sem, 16)

        def _eng_op(eng, sem_name, fn, tick):
            prev = P.val(sem_name)
            S = SEMS[sem_name]
            if prev > 0:
                def w(e, S=S, prev=prev):
                    e.wait_ge(S, prev)
                P.on(eng, w)
            if tick:
                n = P.inc(sem_name, 1)
                def f(e, fn=fn, S=S):
                    fn(e).then_inc(S, 1)
            else:
                n = prev
                def f(e, fn=fn):
                    fn(e)
            P.on(eng, f)
            return n

        def t_op(fn, tick=True):
            return _eng_op("tensor", "tm", fn, tick)

        def v_op(fn, tick=True):
            return _eng_op("vector", "vm", fn, tick)

        def wait(eng_name, sem_name, val):
            if val <= 0:
                return
            sem = SEMS[sem_name]
            def f(e, sem=sem, val=val):
                e.wait_ge(sem, val)
            P.on(eng_name, f)

        def waits(eng_name, **kw):
            for k, v in kw.items():
                wait(eng_name, k, v)

        s_dma(idb[:, :], ident_bf16[:, :])
        s_dma(idf2[:, :], ident_f32b[:, :])
        s_dma(id8[:, :], ident8[:, :])
        s_dma(tr8[:, :], triu8[:, :])
        s_dma(cnts[:, :], counts_e[:, :])
        s_dma(idx2[1:2, :], trashrow[:, :])
        c0 = P.val("sq")
        wait("sync", "sq", c0)
        for pc in range(8):
            g_dma(wi_sb[:, pc, :], wi[pc * 128:(pc + 1) * 128, :], "wiq")
        wiT = P.val("wiq")
        for pc in range(32):
            g_dma(wo_sb[:, pc, :], wo[pc * 128:(pc + 1) * 128, :], "woq")
        woT = P.val("woq")

        waits("tensor", sq=c0)
        t_op(lambda t: t.matmul(psm[:, 0:1], tr8[:, :], cnts[:, :], start=True, stop=True))
        waits("vector", tm=P.val("tm"))
        v_op(lambda v: v.tensor_copy(out=cumo[:, :], in_=psm[:, 0:1]))
        v_op(lambda v: v.tensor_scalar_min(out=cumo[:, :], in0=cumo[:, :], scalar1=4224.0))
        waits("tensor", vm=P.val("vm"))
        t_op(lambda t: t.transpose(psm[0:1, 0:8], cumo[:, :], id8[:, :]))
        waits("vector", tm=P.val("tm"))
        v_op(lambda v: v.tensor_copy(out=cumr[:, :], in_=psm[0:1, 0:8]))
        v_op(lambda v: v.tensor_copy(out=cumru[:, :], in_=cumr[:, :]))
        vM0 = v_op(lambda v: v.memset(seg2[:, :], 0.0), tick=True)
        vCum = P.val("vm")
        wait("gpsimd", "vm", vCum)
        def _mvidx(g):
            g.dma_start(out=idx2[0:1, :], in_=cumru[0:1, :]).then_inc(GL, 16)
        P.on("gpsimd", _mvidx)
        P.inc("gl", 16)
        wait("gpsimd", "gl", P.val("gl"))
        wait("gpsimd", "vm", P.val("vm"))
        for b in range(8):
            nrow = 2 if b < 7 else 1
            def mv2(g, b=b, nrow=nrow):
                g.dma_start(out=seg2[0:nrow, :], in_=mylists[b:b + nrow, :]).then_inc(GL, 16)
            P.on("gpsimd", mv2)
            P.inc("gl", 16)
            wait("gpsimd", "gl", P.val("gl"))
            def seg(g, b=b):
                g.indirect_dma_start(
                    out=glist2[:, :], out_offset=IndirectOffsetOnAxis(idx2[:, b:b + 1], 0),
                    in_=seg2[0:2, 0:4096], in_offset=None).then_inc(GL, 16)
            P.on("gpsimd", seg)
            P.inc("gl", 16)
            wait("gpsimd", "gl", P.val("gl"))
        def ldid(s):
            src = glist[0:4096, 0:1].rearrange("(c p) a -> (a p) c", p=128)
            s.dma_start(out=idlist[:, :], in_=src, allow_slow_non_contiguous=True).then_inc(SQ, 16)
        wait("sync", "gl", P.val("gl"))
        P.on("sync", ldid)
        ilT = P.inc("sq", 16)
        wait("sync", "sq", ilT)
        waits("vector", sq=ilT)
        v_op(lambda v: v.tensor_scalar(out=idlist[:, :], in0=idlist[:, :], scalar1=0.0, scalar2=32767.0, op0=ALU.max, op1=ALU.min))
        v_op(lambda v: v.tensor_copy(out=idu[:, :], in_=idlist[:, :]))
        vIdu = P.val("vm")

        wait("gpsimd", "vm", vIdu)
        wait("tensor", "wiq", wiT)
        relu_ticks = []
        trT = []
        gtrT = {}
        for cc in range(NCC):
            for g4 in range(4):
                gslot = cc * 4 + g4
                gb = gslot % 2
                if gslot >= 2:
                    wait("gpsimd", "tm", gtrT[gslot - 2])
                gsem = SEMS["ga%d" % gb]
                def gat(g, cc=cc, g4=g4, gb=gb, gsem=gsem):
                    g.indirect_dma_start(
                        out=gbuf[:, gb, 0:512], out_offset=None,
                        in_=h2fa[:, :],
                        in_offset=IndirectOffsetOnAxis(idu[:, cc * 4 + g4:cc * 4 + g4 + 1], 0),
                    ).then_inc(gsem, 16)
                    g.indirect_dma_start(
                        out=gbuf[:, gb, 512:1024], out_offset=None,
                        in_=h2fb[:, :],
                        in_offset=IndirectOffsetOnAxis(idu[:, cc * 4 + g4:cc * 4 + g4 + 1], 0),
                    ).then_inc(gsem, 16)
                P.on("gpsimd", gat)
                P.inc("ga%d" % gb, 16)
                gaT = P.inc("ga%d" % gb, 16)
                wait("tensor", "ga%d" % gb, gaT)
                for dc in range(8):
                    it = g4 * 8 + dc
                    pst = psta if it % 2 == 0 else pstb
                    if True:
                        wait("tensor", "vm", max(P.val("vm") - 1, 0))
                    t_op(lambda t, gb=gb, dc=dc, pst=pst: t.transpose(
                        pst[:, :], gbuf[:, gb, dc * 128:(dc + 1) * 128], idf2[:, :]))
                    waits("vector", tm=P.val("tm"))
                    v_op(lambda v, g4=g4, dc=dc, pst=pst: v.tensor_copy(
                        out=h2fm[:, dc, g4 * 128:(g4 + 1) * 128], in_=pst[:, :]))
                gtrT[gslot] = P.val("tm")
            trT.append(P.val("tm"))
            vFM = P.val("vm")
            waits("tensor", vm=vFM)
            for ich in range(32):
                psu = ps1a if ich % 2 == 0 else ps1b
                if ich >= 2:
                    wait("tensor", "vm", relu_ticks[-2])
                for dc in range(8):
                    t_op(lambda t, ich=ich, dc=dc, psu=psu: t.matmul(
                        psu[:, :], wi_sb[:, dc, ich * 128:(ich + 1) * 128],
                        h2fm[:, dc, :], start=(dc == 0), stop=(dc == 7)),
                        tick=(dc == 7))
                waits("vector", tm=P.val("tm"))
                relu_ticks.append(v_op(lambda v, cc=cc, ich=ich, psu=psu: v.tensor_scalar_max(
                    out=hbf[:, ich, :], in0=psu[:, :], scalar1=0.0)))
            vRelu = P.val("vm")
            if cc == 0:
                wait("tensor", "woq", woT)
            waits("tensor", vm=vRelu)
            for csl in range(4):
                ob = csl % 2
                for hch in range(2):
                    psu = ps2a if hch == 0 else ps2b
                    for ich in range(32):
                        t_op(lambda t, csl=csl, hch=hch, ich=ich, psu=psu: t.matmul(
                            psu[:, :], hbf[:, ich, csl * 128:(csl + 1) * 128],
                            wo_sb[:, ich, hch * 512:(hch + 1) * 512],
                            start=(ich == 0), stop=(ich == 31)),
                            tick=(ich == 31))
                tmm2 = P.val("tm")
                eos = "eo%d" % ob
                wait("vector", eos, P.val(eos))
                waits("vector", tm=tmm2)
                v_op(lambda v, ob=ob: v.tensor_copy(out=outsb[:, ob, 0:512], in_=ps2a[:, :]))
                v_op(lambda v, ob=ob: v.tensor_copy(out=outsb[:, ob, 512:1024], in_=ps2b[:, :]))
                vEv = P.val("vm")
                wait("sync", "vm", vEv)
                row0 = cc * 512 + csl * 128
                s_dma(eout[row0:row0 + 128, :], outsb[:, ob, :], sem=eos)
                waits("tensor", vm=vEv)

        wait("sync", "sq", P.val("sq"))
        for snm in ("eo0", "eo1"):
            wait("sync", snm, P.val(snm))
        wait("gpsimd", "ga0", P.val("ga0"))
        wait("gpsimd", "ga1", P.val("ga1"))
        wait("gpsimd", "gl", P.val("gl"))
        wait("gpsimd", "wiq", P.val("wiq"))
        wait("gpsimd", "woq", P.val("woq"))

        @block.sync
        def _(s):
            P.run("sync", s)

        @block.tensor
        def _(t):
            P.run("tensor", t)

        @block.vector
        def _(v):
            P.run("vector", v)

        @block.scalar
        def _(s):
            P.run("scalar", s)

        @block.gpsimd
        def _(g):
            P.run("gpsimd", g)

    return nc

def host_consts_l3(core_id):
    c = {}
    c["triu_s"] = np.triu(np.ones((128, 128), np.float32), k=1)
    c["triu8"] = np.triu(np.ones((8, 8), np.float32), k=1)
    c["ones128"] = np.ones((128, 128), np.float32)
    c["onesrow"] = np.ones((1, 128), np.float32)
    c["selb"] = np.eye(8, dtype=np.float32)[:, core_id:core_id + 1].copy()
    c["e4096p1"] = np.tile((np.arange(7, dtype=np.float32) * CAP + 1.0)[None, :], (128, 1))
    return c



def build_l3():
    nc = bass.Bass("TRN2", target_bir_lowering=False, debug=False)
    xpp = nc.dram_tensor("xpp", [NTOK, D], F32, kind="ExternalInput").ap()
    wb = nc.dram_tensor("wb", [NTOK, 8], F32, kind="ExternalInput").ap()
    eoutf = nc.dram_tensor("eoutf", [7 * CAP, D], F32, kind="ExternalInput").ap()
    countsf = nc.dram_tensor("countsf", [8, 7], F32, kind="ExternalInput").ap()
    gbrow = nc.dram_tensor("gbrow", [1, D], F32, kind="ExternalInput").ap()
    triu_s = nc.dram_tensor("triu_s", [128, 128], F32, kind="ExternalInput").ap()
    triu8 = nc.dram_tensor("triu8", [8, 8], F32, kind="ExternalInput").ap()
    ones128 = nc.dram_tensor("ones128", [128, 128], F32, kind="ExternalInput").ap()
    onesrow = nc.dram_tensor("onesrow", [1, 128], F32, kind="ExternalInput").ap()
    selb = nc.dram_tensor("selb", [8, 1], F32, kind="ExternalInput").ap()
    e4096p1 = nc.dram_tensor("e4096p1", [128, 7], F32, kind="ExternalInput").ap()
    out_o = nc.dram_tensor("out", [NTOK, D], F32, kind="ExternalOutput").ap()

    P = Plan()
    stack = ExitStack()
    with stack:
        ctriu = stack.enter_context(nc.sbuf_tensor("ctriu", [128, 128], F32))
        ctr8 = stack.enter_context(nc.sbuf_tensor("ctr8", [8, 8], F32))
        cones = stack.enter_context(nc.sbuf_tensor("cones", [128, 128], F32))
        conesr = stack.enter_context(nc.sbuf_tensor("conesr", [1, 128], F32))
        cselb = stack.enter_context(nc.sbuf_tensor("cselb", [8, 1], F32))
        ce4096 = stack.enter_context(nc.sbuf_tensor("ce4096", [128, 7], F32))
        cntf = stack.enter_context(nc.sbuf_tensor("cntf", [8, 7], F32))
        cumall = stack.enter_context(nc.sbuf_tensor("cumall", [8, 7], F32))
        baser = stack.enter_context(nc.sbuf_tensor("baser", [1, 7], F32))
        baset = stack.enter_context(nc.sbuf_tensor("baset", [128, 7], F32))
        gbr = stack.enter_context(nc.sbuf_tensor("gbr", [1, D], F32))
        gbt = stack.enter_context(nc.sbuf_tensor("gbt", [128, D], F32))
        wt = stack.enter_context(nc.sbuf_tensor("wt", [128, 2, 8], F32))
        xt = stack.enter_context(nc.sbuf_tensor("xt", [128, 2, D], F32))
        g1 = stack.enter_context(nc.sbuf_tensor("g1", [128, 2, D], F32))
        g2 = stack.enter_context(nc.sbuf_tensor("g2", [128, 2, D], F32))
        outt = stack.enter_context(nc.sbuf_tensor("outt", [128, 2, D], F32))
        masks = stack.enter_context(nc.sbuf_tensor("masks", [128, 7], F32))
        Psb = stack.enter_context(nc.sbuf_tensor("Psb", [128, 7], F32))
        offs = stack.enter_context(nc.sbuf_tensor("offs", [128, 7], F32))
        slotg = stack.enter_context(nc.sbuf_tensor("slotg", [128, 7], F32))
        kept = stack.enter_context(nc.sbuf_tensor("kept", [128, 7], F32))
        Am = stack.enter_context(nc.sbuf_tensor("Am", [128, 7], F32))
        sel = stack.enter_context(nc.sbuf_tensor("sel", [128, 7], F32))
        tmp7 = stack.enter_context(nc.sbuf_tensor("tmp7", [128, 7], F32))
        m1 = stack.enter_context(nc.sbuf_tensor("m1", [128, 1], F32))
        w1 = stack.enter_context(nc.sbuf_tensor("w1", [128, 1], F32))
        w2 = stack.enter_context(nc.sbuf_tensor("w2", [128, 1], F32))
        r1u = stack.enter_context(nc.sbuf_tensor("r1u", [128, 2, 1], U32))
        r2u = stack.enter_context(nc.sbuf_tensor("r2u", [128, 2, 1], U32))
        rf = stack.enter_context(nc.sbuf_tensor("rf", [128, 1], F32))
        bcd = stack.enter_context(nc.sbuf_tensor("bcd", [1, 256], BF16))
        psA = stack.enter_context(nc.psum_tensor("psA", [128, 512], F32))
        psOf = stack.enter_context(nc.psum_tensor("psOf", [128, 7], F32))
        SQ = stack.enter_context(nc.semaphore("sq"))
        WQ0 = stack.enter_context(nc.semaphore("wq0"))
        WQ1 = stack.enter_context(nc.semaphore("wq1"))
        XQ0 = stack.enter_context(nc.semaphore("xq0"))
        XQ1 = stack.enter_context(nc.semaphore("xq1"))
        GA0 = stack.enter_context(nc.semaphore("ga0"))
        GA1 = stack.enter_context(nc.semaphore("ga1"))
        OQ0 = stack.enter_context(nc.semaphore("oq0"))
        OQ1 = stack.enter_context(nc.semaphore("oq1"))
        TM = stack.enter_context(nc.semaphore("tm"))
        VM = stack.enter_context(nc.semaphore("vm"))
        block = stack.enter_context(nc.Block())

        SEMS = {"sq": SQ, "wq0": WQ0, "wq1": WQ1, "xq0": XQ0, "xq1": XQ1,
                "ga0": GA0, "ga1": GA1, "oq0": OQ0, "oq1": OQ1, "tm": TM, "vm": VM}

        def s_dma(out, in_, sem="sq"):
            S = SEMS[sem]
            def f(s, out=out, in_=in_, S=S):
                s.dma_start(out=out, in_=in_).then_inc(S, 16)
            P.on("sync", f)
            return P.inc(sem, 16)

        def _eng_op(eng, sem_name, fn, tick):
            prev = P.val(sem_name)
            S = SEMS[sem_name]
            if prev > 0:
                def w(e, S=S, prev=prev):
                    e.wait_ge(S, prev)
                P.on(eng, w)
            if tick:
                n = P.inc(sem_name, 1)
                def f(e, fn=fn, S=S):
                    fn(e).then_inc(S, 1)
            else:
                n = prev
                def f(e, fn=fn):
                    fn(e)
            P.on(eng, f)
            return n

        def t_op(fn, tick=True):
            return _eng_op("tensor", "tm", fn, tick)

        def v_op(fn, tick=True):
            return _eng_op("vector", "vm", fn, tick)

        def wait(eng_name, sem_name, val):
            if val <= 0:
                return
            sem = SEMS[sem_name]
            def f(e, sem=sem, val=val):
                e.wait_ge(sem, val)
            P.on(eng_name, f)

        def waits(eng_name, **kw):
            for k, v in kw.items():
                wait(eng_name, k, v)

        for dst, src in [(ctriu, triu_s), (ctr8, triu8), (cones, ones128),
                         (conesr, onesrow), (cselb, selb), (ce4096, e4096p1),
                         (cntf, countsf), (gbr, gbrow)]:
            sl = tuple(slice(None) for _ in range(len(dst.shape)))
            s_dma(dst[sl], src[sl])
        c0 = P.val("sq")
        wait("sync", "sq", c0)
        waits("vector", sq=c0)
        v_op(lambda v: v.tensor_copy(out=bcd[0:1, :], in_=conesr[0:1, :].bitcast(BF16)))
        waits("tensor", sq=c0)
        t_op(lambda t: t.matmul(psA[0:8, 0:7], ctr8[:, :], cntf[:, :], start=True, stop=True))
        waits("vector", tm=P.val("tm"))
        v_op(lambda v: v.tensor_copy(out=cumall[:, :], in_=psA[0:8, 0:7]))
        waits("tensor", vm=P.val("vm"))
        t_op(lambda t: t.matmul(psA[0:1, 0:7], cselb[:, :], cumall[:, :], start=True, stop=True))
        waits("vector", tm=P.val("tm"))
        v_op(lambda v: v.tensor_copy(out=baser[:, :], in_=psA[0:1, 0:7]))
        waits("tensor", vm=P.val("vm"))
        t_op(lambda t: t.matmul(psA[:, 0:7], conesr[:, :], baser[:, :], start=True, stop=True))
        waits("vector", tm=P.val("tm"))
        v_op(lambda v: v.tensor_copy(out=baset[:, :], in_=psA[:, 0:7]))
        for half in range(2):
            waits("tensor", vm=P.val("vm"))
            t_op(lambda t, half=half: t.matmul(psA[:, 0:512], conesr[:, :],
                                               gbr[0:1, half * 512:half * 512 + 512],
                                               start=True, stop=True))
            waits("vector", tm=P.val("tm"))
            v_op(lambda v, half=half: v.tensor_copy(out=gbt[:, half * 512:half * 512 + 512], in_=psA[:, 0:512]))
        vPrep = P.val("vm")

        wqc = [0, 0]
        xqc = [0, 0]
        gac = [0, 0]
        oqc = [0, 0]
        vOutTicks = []
        psOfTick = [0]
        for j in range(NT):
            buf = j % 2
            if j >= 2:
                wait("sync", "vm", vOutTicks[j - 2])
            ldw = s_dma(wt[:, buf, :], wb[j * 128:(j + 1) * 128, :], sem="wq%d" % buf)
            wqc[buf] += 1
            ldx = s_dma(xt[:, buf, :], xpp[j * 128:(j + 1) * 128, :], sem="xq%d" % buf)
            xqc[buf] += 1
            waits("vector", **{("wq%d" % buf): 16 * wqc[buf]})
            wait("vector", "tm", psOfTick[0])
            v_op(lambda v, buf=buf: v.tensor_scalar(out=masks[:, :], in0=wt[:, buf, 0:7], scalar1=0.0, scalar2=None, op0=ALU.is_gt))
            waits("tensor", vm=P.val("vm"))
            t_op(lambda t: t.matmul(psA[:, 32:39], ctriu[:, :], masks[:, :], start=True, stop=True, skip_group_check=True))
            waits("vector", tm=P.val("tm"))
            v_op(lambda v: v.tensor_copy(out=Psb[:, :], in_=psA[:, 32:39]))
            if j == 0:
                v_op(lambda v: v.memset(offs[:, :], 0.0))
            else:
                v_op(lambda v: v.tensor_copy(out=offs[:, :], in_=psOf[:, :]))
            waits("tensor", vm=P.val("vm"))
            psOfTick[0] = t_op(lambda t, j=j: t.matmul(psOf[:, :], cones[:, :], masks[:, :],
                                         start=(j == 0), stop=(j == NT - 1), skip_group_check=True))
            v_op(lambda v: v.tensor_add(out=slotg[:, :], in0=baset[:, :], in1=offs[:, :]))
            v_op(lambda v: v.tensor_add(out=slotg[:, :], in0=slotg[:, :], in1=Psb[:, :]))
            v_op(lambda v: v.tensor_scalar(out=kept[:, :], in0=slotg[:, :], scalar1=float(CAP), scalar2=None, op0=ALU.is_lt))
            v_op(lambda v: v.tensor_mul(out=kept[:, :], in0=kept[:, :], in1=masks[:, :]))
            v_op(lambda v: v.tensor_add(out=Am[:, :], in0=slotg[:, :], in1=ce4096[:, :]))
            v_op(lambda v: v.tensor_mul(out=Am[:, :], in0=Am[:, :], in1=kept[:, :]))
            v_op(lambda v: v.reduce_max(m1[:, :], Am[:, :], axis=AX.X))
            v_op(lambda v: v.tensor_scalar(out=sel[:, :], in0=Am[:, :], scalar1=m1[:, 0:1], scalar2=None, op0=ALU.is_equal))
            v_op(lambda v: v.tensor_mul(out=sel[:, :], in0=sel[:, :], in1=kept[:, :]))
            v_op(lambda v, buf=buf: v.tensor_mul(out=tmp7[:, :], in0=sel[:, :], in1=wt[:, buf, 0:7]))
            v_op(lambda v: v.reduce_sum(w1[:, :], tmp7[:, :], axis=AX.X))
            v_op(lambda v: v.tensor_scalar(out=rf[:, :], in0=m1[:, :], scalar1=-1.0, scalar2=0.0, op0=ALU.add, op1=ALU.max))
            if j >= 2:
                wait("vector", "ga%d" % buf, 32 * gac[buf])
            v_op(lambda v, buf=buf: v.tensor_copy(out=r1u[:, buf, :], in_=rf[:, :]))
            v_op(lambda v: v.tensor_scalar(out=tmp7[:, :], in0=sel[:, :], scalar1=-1.0, scalar2=1.0, op0=ALU.mult, op1=ALU.add))
            v_op(lambda v: v.tensor_mul(out=kept[:, :], in0=kept[:, :], in1=tmp7[:, :]))
            v_op(lambda v: v.tensor_mul(out=Am[:, :], in0=Am[:, :], in1=tmp7[:, :]))
            v_op(lambda v: v.reduce_max(m1[:, :], Am[:, :], axis=AX.X))
            v_op(lambda v: v.tensor_scalar(out=sel[:, :], in0=Am[:, :], scalar1=m1[:, 0:1], scalar2=None, op0=ALU.is_equal))
            v_op(lambda v: v.tensor_mul(out=sel[:, :], in0=sel[:, :], in1=kept[:, :]))
            v_op(lambda v, buf=buf: v.tensor_mul(out=tmp7[:, :], in0=sel[:, :], in1=wt[:, buf, 0:7]))
            v_op(lambda v: v.reduce_sum(w2[:, :], tmp7[:, :], axis=AX.X))
            v_op(lambda v: v.tensor_scalar(out=rf[:, :], in0=m1[:, :], scalar1=-1.0, scalar2=0.0, op0=ALU.add, op1=ALU.max))
            v_op(lambda v, buf=buf: v.tensor_copy(out=r2u[:, buf, :], in_=rf[:, :]))
            vIdx = P.val("vm")
            wait("gpsimd", "vm", vIdx)
            ges = "ga%d" % buf
            def gat1(g, buf=buf):
                g.indirect_dma_start(out=g1[:, buf, :], out_offset=None, in_=eoutf[:, :],
                                     in_offset=IndirectOffsetOnAxis(r1u[:, buf, :], 0)).then_inc(SEMS["ga%d" % buf], 16)
            P.on("gpsimd", gat1)
            P.inc(ges, 16)
            def gat2(g, buf=buf):
                g.indirect_dma_start(out=g2[:, buf, :], out_offset=None, in_=eoutf[:, :],
                                     in_offset=IndirectOffsetOnAxis(r2u[:, buf, :], 0)).then_inc(SEMS["ga%d" % buf], 16)
            P.on("gpsimd", gat2)
            P.inc(ges, 16)
            gac[buf] += 1
            waits("vector", **{ges: P.val(ges), ("xq%d" % buf): 16 * xqc[buf]})
            if j >= 2:
                wait("vector", "oq%d" % buf, 16 * oqc[buf])
            v_op(lambda v, buf=buf: v.tensor_scalar_mul(out=outt[:, buf, :], in0=g1[:, buf, :], scalar1=w1[:, 0:1]))
            v_op(lambda v, buf=buf: v.scalar_tensor_tensor(out=outt[:, buf, :], in0=g2[:, buf, :], scalar=w2[:, 0:1], in1=outt[:, buf, :], op0=ALU.mult, op1=ALU.add))
            v_op(lambda v, buf=buf: v.tensor_mul(out=outt[:, buf, :], in0=outt[:, buf, :], in1=gbt[:, :]))
            v_op(lambda v, buf=buf: v.tensor_add(out=outt[:, buf, :], in0=outt[:, buf, :], in1=xt[:, buf, :]))
            vOut = P.val("vm")
            vOutTicks.append(vOut)
            wait("sync", "vm", vOut)
            s_dma(out_o[j * 128:(j + 1) * 128, :], outt[:, buf, :], sem="oq%d" % buf)
            oqc[buf] += 1

        wait("sync", "oq0", P.val("oq0"))
        wait("sync", "oq1", P.val("oq1"))
        wait("sync", "sq", P.val("sq"))
        wait("gpsimd", "ga0", P.val("ga0"))
        wait("gpsimd", "ga1", P.val("ga1"))

        @block.sync
        def _(s):
            P.run("sync", s)

        @block.tensor
        def _(t):
            P.run("tensor", t)

        @block.vector
        def _(v):
            P.run("vector", v)

        @block.scalar
        def _(s):
            P.run("scalar", s)

        @block.gpsimd
        def _(g):
            P.run("gpsimd", g)

    return nc



_CACHE = {}


def _get_programs():
    if "nc1" not in _CACHE:
        _CACHE["nc1"] = build_l1()
        _CACHE["nc2"] = build_l2()
        _CACHE["nc3"] = build_l3()
        _CACHE["c1"] = [host_consts_l1(b) for b in range(8)]
        _CACHE["c2"] = [host_consts_l2(e) for e in range(8)]
        _CACHE["c3"] = [host_consts_l3(b) for b in range(8)]
    return _CACHE


def kernel(L0_attn_Z, L0_attn_decay, L0_attn_W, L0_attn_b, L0_attn_gamma,
           L0_attn_beta, L0_ffn_router_w, L0_ffn_router_b, L0_ffn_experts_inter,
           L0_ffn_experts_out, L0_c, L0_wmod):
    C = _get_programs()
    x = np.ascontiguousarray(np.asarray(L0_attn_Z, dtype=np.float32))
    W = np.ascontiguousarray(np.asarray(L0_attn_W, dtype=np.float32))
    battn = np.ascontiguousarray(np.asarray(L0_attn_b, dtype=np.float32)).reshape(1, D)
    gamma = np.ascontiguousarray(np.asarray(L0_attn_gamma, dtype=np.float32)).reshape(1, D)
    beta = np.ascontiguousarray(np.asarray(L0_attn_beta, dtype=np.float32)).reshape(1, D)
    decay = np.ascontiguousarray(np.asarray(L0_attn_decay, dtype=np.float32)).reshape(1, 1)
    rw = np.ascontiguousarray(np.asarray(L0_ffn_router_w, dtype=np.float32))
    rb = np.ascontiguousarray(np.asarray(L0_ffn_router_b, dtype=np.float32)).reshape(1, 8)
    wi_all = np.asarray(L0_ffn_experts_inter, dtype=np.float32)
    wo_all = np.asarray(L0_ffn_experts_out, dtype=np.float32)
    cfull = np.ascontiguousarray(np.asarray(L0_c, dtype=np.float32))
    wmod = np.ascontiguousarray(np.asarray(L0_wmod, dtype=np.float32))

    in1 = []
    for b in range(8):
        m = dict(C["c1"][b])
        m["x_in"] = np.ascontiguousarray(x[b])
        m["W_in"] = W
        m["battn"] = battn
        m["gamma"] = gamma
        m["beta"] = beta
        m["decay"] = decay
        m["rw"] = rw
        m["rb"] = rb
        m["cfull"] = cfull
        m["wmod"] = wmod
        in1.append(m)
    r1 = run_bass_kernel_spmd(C["nc1"], in1, core_ids=list(range(8))).results

    h2_full = np.concatenate([np.asarray(r1[b]["h2b"]) for b in range(8)], axis=0)
    h2a = np.ascontiguousarray(h2_full[:, 0:512])
    h2b_ = np.ascontiguousarray(h2_full[:, 512:1024])
    lists_all = np.stack([np.asarray(r1[b]["lists"]).reshape(7, LSTRIDE) for b in range(8)])
    counts_all = np.stack([np.asarray(r1[b]["counts"]).reshape(7) for b in range(8)])

    in2 = []
    for k in range(8):
        e = k if k < 7 else 0
        m = dict(C["c2"][k])
        m["h2fa"] = h2a
        m["h2fb"] = h2b_
        m["mylists"] = np.ascontiguousarray(lists_all[:, e, :])
        m["counts_e"] = np.ascontiguousarray(counts_all[:, e:e + 1])
        m["wi"] = np.ascontiguousarray(wi_all[e])
        m["wo"] = np.ascontiguousarray(wo_all[e])
        in2.append(m)
    r2 = run_bass_kernel_spmd(C["nc2"], in2, core_ids=list(range(8))).results
    eoutf = np.concatenate([np.asarray(r2[k]["eout"]) for k in range(7)], axis=0)

    in3 = []
    for b in range(8):
        m = dict(C["c3"][b])
        m["xpp"] = np.asarray(r1[b]["xpp"])
        m["wb"] = np.asarray(r1[b]["wout"])
        m["eoutf"] = eoutf
        m["countsf"] = np.ascontiguousarray(counts_all)
        m["gbrow"] = np.asarray(r1[b]["gbrow"])
        in3.append(m)
    r3 = run_bass_kernel_spmd(C["nc3"], in3, core_ids=list(range(8))).results
    out = np.stack([np.asarray(r3[b]["out"]) for b in range(8)], axis=0)
    return out.astype(np.float32, copy=False)


# version 09
# speedup vs baseline: 1.3213x; 1.0721x over previous
import dataclasses
import numpy as np
import ml_dtypes
import concourse.bass as bass
import concourse.mybir as mybir
from concourse.bass_core_rust import IndirectOffsetOnAxis
from concourse.bass_utils import run_bass_kernel_spmd
from contextlib import ExitStack

F32 = mybir.dt.float32
BF16 = mybir.dt.bfloat16
U32 = mybir.dt.uint32
ALU = mybir.AluOpType
AF = mybir.ActivationFunctionType
AX = mybir.AxisListType

NTOK = 4096
D = 1024
NT = NTOK // 128
NDC = D // 128
LSTRIDE = 4352
EPS = 1e-5
NTOKG = 32768
I = 4096
CAP = 4096
NCC = CAP // 512
GPAD = 8448


def bcast_free(ap, n):
    return dataclasses.replace(ap, ap=type(ap.ap)([list(ap.ap[0]), [0, n]]))


class Plan:
    def __init__(self):
        self.items = []
        self.ctr = {}

    def on(self, eng, fn):
        self.items.append((eng, fn))

    def inc(self, name, k):
        self.ctr[name] = self.ctr.get(name, 0) + k
        return self.ctr[name]

    def val(self, name):
        return self.ctr.get(name, 0)

    def run(self, eng_name, eng):
        for e, fn in self.items:
            if e == eng_name:
                fn(eng)


def host_consts_l1(core_id):
    import ml_dtypes
    c = {}
    c["ident_f32"] = np.eye(128, dtype=np.float32)
    c["ident_bf16"] = np.eye(128, dtype=np.float32).astype(ml_dtypes.bfloat16)
    c["triu_s"] = np.triu(np.ones((128, 128), np.float32), k=1)
    c["iota_row"] = np.tile(np.arange(128, dtype=np.float32)[None, :], (128, 1))
    c["iotaP"] = np.arange(128, dtype=np.float32).reshape(128, 1)
    c["gidbase"] = np.full((128, 1), core_id * NTOK, np.float32)
    c["e4352"] = (np.arange(7, dtype=np.float32) * LSTRIDE).reshape(7, 1)
    c["onesrow"] = np.ones((1, 128), np.float32)
    c["onescol"] = np.ones((128, 1), np.float32)
    c["selb"] = np.eye(8, dtype=np.float32)[:, core_id:core_id + 1].copy()
    return c


def build_l1():
    nc = bass.Bass("TRN2", target_bir_lowering=False, debug=False)
    x_in = nc.dram_tensor("x_in", [NTOK, D], F32, kind="ExternalInput").ap()
    W_in = nc.dram_tensor("W_in", [D, D], F32, kind="ExternalInput").ap()
    battn = nc.dram_tensor("battn", [1, D], F32, kind="ExternalInput").ap()
    gamma = nc.dram_tensor("gamma", [1, D], F32, kind="ExternalInput").ap()
    beta = nc.dram_tensor("beta", [1, D], F32, kind="ExternalInput").ap()
    decay = nc.dram_tensor("decay", [1, 1], F32, kind="ExternalInput").ap()
    rw = nc.dram_tensor("rw", [8, D], F32, kind="ExternalInput").ap()
    rb = nc.dram_tensor("rb", [1, 8], F32, kind="ExternalInput").ap()
    cfull = nc.dram_tensor("cfull", [8, D], F32, kind="ExternalInput").ap()
    wmod = nc.dram_tensor("wmod", [D, 6 * D], F32, kind="ExternalInput").ap()
    ident_f32 = nc.dram_tensor("ident_f32", [128, 128], F32, kind="ExternalInput").ap()
    ident_bf16 = nc.dram_tensor("ident_bf16", [128, 128], BF16, kind="ExternalInput").ap()
    triu_s = nc.dram_tensor("triu_s", [128, 128], F32, kind="ExternalInput").ap()
    iota_row = nc.dram_tensor("iota_row", [128, 128], F32, kind="ExternalInput").ap()
    iotaP = nc.dram_tensor("iotaP", [128, 1], F32, kind="ExternalInput").ap()
    gidbase = nc.dram_tensor("gidbase", [128, 1], F32, kind="ExternalInput").ap()
    e4352 = nc.dram_tensor("e4352", [7, 1], F32, kind="ExternalInput").ap()
    onesrow = nc.dram_tensor("onesrow", [1, 128], F32, kind="ExternalInput").ap()
    onescol = nc.dram_tensor("onescol", [128, 1], F32, kind="ExternalInput").ap()
    selb = nc.dram_tensor("selb", [8, 1], F32, kind="ExternalInput").ap()
    xpp_o = nc.dram_tensor("xpp", [NTOK, D], F32, kind="ExternalOutput").ap()
    h2b_o = nc.dram_tensor("h2b", [NTOK, D], F32, kind="ExternalOutput").ap()
    w_o = nc.dram_tensor("wout", [NTOK, 8], F32, kind="ExternalOutput").ap()
    lists_o = nc.dram_tensor("lists", [7 * LSTRIDE, 1], F32, kind="ExternalOutput").ap()
    counts_o = nc.dram_tensor("counts", [7, 1], F32, kind="ExternalOutput").ap()
    gbrow_o = nc.dram_tensor("gbrow", [1, D], F32, kind="ExternalOutput").ap()
    ycs_o = nc.dram_tensor("ycs", [NTOK, D], BF16, kind="ExternalOutput").ap()
    gbh2_o = nc.dram_tensor("gbh2", [NTOK, D], F32, kind="ExternalOutput").ap()

    P = Plan()
    stack = ExitStack()
    with stack:
        idf = stack.enter_context(nc.sbuf_tensor("cid_f", [128, 128], F32))
        idb = stack.enter_context(nc.sbuf_tensor("cid_b", [128, 128], BF16))
        ctriu = stack.enter_context(nc.sbuf_tensor("ctriu", [128, 128], F32))
        ciota = stack.enter_context(nc.sbuf_tensor("ciota", [128, 128], F32))
        ciop = stack.enter_context(nc.sbuf_tensor("ciop", [128, 1], F32))
        cgidb = stack.enter_context(nc.sbuf_tensor("cgidb", [128, 1], F32))
        ce4352 = stack.enter_context(nc.sbuf_tensor("ce4352", [7, 1], F32))
        conesr = stack.enter_context(nc.sbuf_tensor("conesr", [1, 128], F32))
        conesc = stack.enter_context(nc.sbuf_tensor("conesc", [128, 1], F32))
        cselb = stack.enter_context(nc.sbuf_tensor("cselb", [8, 1], F32))
        xnT = stack.enter_context(nc.sbuf_tensor("xnT", [128, NDC, NTOK], BF16))
        WTb = stack.enter_context(nc.sbuf_tensor("WTb", [128, NDC, D], BF16))
        Gt = stack.enter_context(nc.sbuf_tensor("Gt", [128, D], F32))
        Bt = stack.enter_context(nc.sbuf_tensor("Bt", [128, D], F32))
        cb1t = stack.enter_context(nc.sbuf_tensor("cb1t", [128, D], F32))
        sbt = stack.enter_context(nc.sbuf_tensor("sbt", [128, D], F32))
        gbt = stack.enter_context(nc.sbuf_tensor("gbt", [128, D], F32))
        rbt = stack.enter_context(nc.sbuf_tensor("rbt", [128, 8], F32))
        rwTb = stack.enter_context(nc.sbuf_tensor("rwTb", [128, NDC, 8], BF16))
        g2 = stack.enter_context(nc.sbuf_tensor("g2", [2, NTOK], BF16))
        vb2 = stack.enter_context(nc.sbuf_tensor("vb2", [2, D], BF16))
        stat_mu = stack.enter_context(nc.sbuf_tensor("stat_mu", [128, NT], F32))
        stat_rs = stack.enter_context(nc.sbuf_tensor("stat_rs", [128, NT], F32))
        stat_s1 = stack.enter_context(nc.sbuf_tensor("stat_s1", [128, NT], F32))
        stat_sq = stack.enter_context(nc.sbuf_tensor("stat_sq", [128, NT], F32))
        gid0 = stack.enter_context(nc.sbuf_tensor("gid0", [128, 1], F32))
        acol = stack.enter_context(nc.sbuf_tensor("acol", [128, 1], F32))
        asc = stack.enter_context(nc.sbuf_tensor("asc", [1, 4], F32))
        psY0 = stack.enter_context(nc.psum_tensor("psY0", [128, 512], F32))
        psY1 = stack.enter_context(nc.psum_tensor("psY1", [128, 512], F32))
        psY2 = stack.enter_context(nc.psum_tensor("psY2", [128, 512], F32))
        psY3 = stack.enter_context(nc.psum_tensor("psY3", [128, 512], F32))
        psTB0 = stack.enter_context(nc.psum_tensor("psTB0", [128, 128], BF16))
        psTB1 = stack.enter_context(nc.psum_tensor("psTB1", [128, 128], BF16))
        psOf = stack.enter_context(nc.psum_tensor("psOf", [7, 1], F32))
        psRT = stack.enter_context(nc.psum_tensor("psRT", [128, 512], F32))
        SQ = stack.enter_context(nc.semaphore("sq"))
        GQ = stack.enter_context(nc.semaphore("gq"))
        SC = stack.enter_context(nc.semaphore("sc"))
        TM = stack.enter_context(nc.semaphore("tm"))
        VM = stack.enter_context(nc.semaphore("vm"))
        SM = stack.enter_context(nc.semaphore("sm"))
        XQ0 = stack.enter_context(nc.semaphore("xq0"))
        XQ1 = stack.enter_context(nc.semaphore("xq1"))
        WQ0 = stack.enter_context(nc.semaphore("wq0"))
        WQ1 = stack.enter_context(nc.semaphore("wq1"))
        SOX0 = stack.enter_context(nc.semaphore("sox0"))
        SOX1 = stack.enter_context(nc.semaphore("sox1"))
        SOW0 = stack.enter_context(nc.semaphore("sow0"))
        SOW1 = stack.enter_context(nc.semaphore("sow1"))
        SOH0 = stack.enter_context(nc.semaphore("soh0"))
        SOH1 = stack.enter_context(nc.semaphore("soh1"))
        LZ = stack.enter_context(nc.semaphore("lz"))
        WSB = stack.enter_context(nc.semaphore("wsb"))
        WM0 = stack.enter_context(nc.semaphore("wm0"))
        WM1 = stack.enter_context(nc.semaphore("wm1"))
        WM2 = stack.enter_context(nc.semaphore("wm2"))
        WM3 = stack.enter_context(nc.semaphore("wm3"))
        block = stack.enter_context(nc.Block())

        SEMS = {"sq": SQ, "gq": GQ, "sc": SC, "tm": TM, "vm": VM, "sm": SM,
                "xq0": XQ0, "xq1": XQ1, "wq0": WQ0, "wq1": WQ1,
                "sox0": SOX0, "sox1": SOX1, "sow0": SOW0, "sow1": SOW1,
                "soh0": SOH0, "soh1": SOH1, "lz": LZ, "wsb": WSB,
                "wm0": WM0, "wm1": WM1, "wm2": WM2, "wm3": WM3}

        def s_dma(out, in_, sem="sq"):
            S = SEMS[sem]
            def f(s, out=out, in_=in_, S=S):
                s.dma_start(out=out, in_=in_).then_inc(S, 16)
            P.on("sync", f)
            return P.inc(sem, 16)

        def s_barrier(sem="sq"):
            wait("sync", sem, P.val(sem))
            return P.val(sem)

        def g_dma(out, in_, sem=None):
            S = GQ if sem is None else SEMS[sem]
            nm = "gq" if sem is None else sem
            def f(g, out=out, in_=in_, S=S):
                g.dma_start(out=out, in_=in_).then_inc(S, 16)
            P.on("gpsimd", f)
            return P.inc(nm, 16)

        def _eng_op(eng, sem_name, fn, tick):
            prev = P.val(sem_name)
            S = SEMS[sem_name]
            if prev > 0:
                def w(e, S=S, prev=prev):
                    e.wait_ge(S, prev)
                P.on(eng, w)
            if tick:
                n = P.inc(sem_name, 1)
                def f(e, fn=fn, S=S):
                    fn(e).then_inc(S, 1)
            else:
                n = prev
                def f(e, fn=fn):
                    fn(e)
            P.on(eng, f)
            return n

        def t_op(fn, tick=True):
            return _eng_op("tensor", "tm", fn, tick)

        def v_op(fn, tick=True):
            return _eng_op("vector", "vm", fn, tick)

        def sc_op(fn, tick=True):
            return _eng_op("scalar", "sm", fn, tick)

        def wait(eng_name, sem_name, val):
            if val <= 0:
                return
            sem = SEMS[sem_name]
            def f(e, sem=sem, val=val):
                e.wait_ge(sem, val)
            P.on(eng_name, f)

        def waits(eng_name, **kw):
            for k, v in kw.items():
                wait(eng_name, k, v)

        prep = ExitStack()
        msa = prep.enter_context(nc.sbuf_tensor("msa", [1, D], F32))
        mca = prep.enter_context(nc.sbuf_tensor("mca", [1, D], F32))
        mga = prep.enter_context(nc.sbuf_tensor("mga", [1, D], F32))
        msb = prep.enter_context(nc.sbuf_tensor("msb", [1, D], F32))
        mcb = prep.enter_context(nc.sbuf_tensor("mcb", [1, D], F32))
        mgb = prep.enter_context(nc.sbuf_tensor("mgb", [1, D], F32))
        r0 = prep.enter_context(nc.sbuf_tensor("r0", [1, D], F32))
        r1 = prep.enter_context(nc.sbuf_tensor("r1", [1, D], F32))
        modst = prep.enter_context(nc.sbuf_tensor("modst", [8, 512], F32))
        cstage = prep.enter_context(nc.sbuf_tensor("cstage", [8, D], F32))
        scT = prep.enter_context(nc.sbuf_tensor("scT", [128, 8, 8], BF16))
        cstb = prep.enter_context(nc.sbuf_tensor("cstb", [8, D], BF16))
        wmt = prep.enter_context(nc.sbuf_tensor("wmt", [128, 4, 512], BF16))
        Wsb = prep.enter_context(nc.sbuf_tensor("Wsb", [128, NDC, D], F32))
        wtf2 = prep.enter_context(nc.sbuf_tensor("wtf2", [128, 2, D], F32))
        safm = prep.enter_context(nc.sbuf_tensor("safm", [128, 8], F32))
        ca1fm = prep.enter_context(nc.sbuf_tensor("ca1fm", [128, 8], F32))
        wbar = prep.enter_context(nc.sbuf_tensor("wbar", [128, 8], F32))
        vrow = prep.enter_context(nc.sbuf_tensor("vrow", [1, D], F32))
        bcrow = prep.enter_context(nc.sbuf_tensor("bcrow", [1, D], F32))
        smallsc = prep.enter_context(nc.sbuf_tensor("smallsc", [8, 8], F32))

        for dst, src in [(idf, ident_f32), (idb, ident_bf16), (ctriu, triu_s),
                         (ciota, iota_row), (ciop, iotaP), (cgidb, gidbase),
                         (ce4352, e4352), (conesr, onesrow), (conesc, onescol),
                         (cselb, selb)]:
            sl = tuple(slice(None) for _ in range(len(dst.shape)))
            s_dma(dst[sl], src[sl])
        s_dma(cstage[0:8, :], cfull[:, :])
        s_dma(r0[:, :], gamma[:, :])
        s_dma(r1[:, :], beta[:, :])
        s_dma(asc[0:1, 0:1], decay[:, :])
        s_dma(smallsc[0:1, 0:8], rb[:, :])
        c0 = s_barrier()
        for pc in range(NDC):
            ldW = s_dma(Wsb[:, pc, :], W_in[pc * 128:(pc + 1) * 128, :], sem="wsb")

        waits("scalar", sq=c0)
        sc_op(lambda s: s.activation(out=cstage[0:8, :], in_=cstage[0:8, :], func=AF.Silu))
        tA = P.val("sm")
        waits("vector", sm=tA)
        v_op(lambda v: v.tensor_copy(out=cstb[0:8, :], in_=cstage[0:8, :]))
        waits("tensor", vm=P.val("vm"))
        scTev = {}
        for dc in range(NDC):
            pst = psTB0 if dc % 2 == 0 else psTB1
            if dc >= 2:
                wait("tensor", "vm", scTev[dc - 2])
            t_op(lambda t, dc=dc, pst=pst: t.transpose(pst[0:128, 0:8], cstb[0:8, dc * 128:(dc + 1) * 128], idb[0:8, 0:8]))
            waits("vector", tm=P.val("tm"))
            scTev[dc] = v_op(lambda v, dc=dc, pst=pst: v.tensor_copy(out=scT[:, dc, :], in_=pst[0:128, 0:8]))
        vSct = P.val("vm")
        waits("tensor", vm=vSct)
        rows6 = [msa, mca, mga, msb, mcb, mgb]
        mmticks = []
        wmsems = ["wm0", "wm1", "wm2", "wm3"]
        for nch in range(12):
            waits("tensor", vm=P.val("vm"))
            for dc in range(NDC):
                it = nch * NDC + dc
                buf = it % 4
                if it >= 4:
                    wait("gpsimd", "tm", mmticks[it - 4])
                ldw = g_dma(wmt[:, buf, :], wmod[dc * 128:(dc + 1) * 128, nch * 512:(nch + 1) * 512], sem=wmsems[buf])
                waits("tensor", **{wmsems[buf]: ldw})
                mmticks.append(t_op(lambda t, dc=dc, buf=buf: t.matmul(
                    psY0[0:8, :], scT[:, dc, :], wmt[:, buf, :],
                    start=(dc == 0), stop=(dc == NDC - 1))))
            waits("vector", tm=P.val("tm"))
            v_op(lambda v: v.tensor_copy(out=modst[0:8, :], in_=psY0[0:8, :]))
            waits("tensor", vm=P.val("vm"))
            t_op(lambda t: t.matmul(psRT[0:1, 0:512], cselb[:, :], modst[0:8, :], start=True, stop=True))
            waits("vector", tm=P.val("tm"))
            i, half = nch // 2, nch % 2
            v_op(lambda v, i=i, half=half: v.tensor_copy(
                out=rows6[i][0:1, half * 512:half * 512 + 512], in_=psRT[0:1, 0:512]))
        for gv in (mga, mgb):
            v_op(lambda v, gv=gv: v.tensor_scalar(out=bcrow[:, :], in0=gv[0:1, :], scalar1=0.0, scalar2=None, op0=ALU.is_ge))
            v_op(lambda v: v.tensor_scalar(out=bcrow[:, :], in0=bcrow[:, :], scalar1=2.0, scalar2=-1.0, op0=ALU.mult, op1=ALU.add))
            waits("scalar", vm=P.val("vm"))
            sc_op(lambda s, gv=gv: s.activation(out=vrow[:, :], in_=gv[0:1, :], func=AF.Abs))
            waits("vector", sm=P.val("sm"))
            v_op(lambda v: v.tensor_scalar_max(out=vrow[:, :], in0=vrow[:, :], scalar1=0.25))
            v_op(lambda v, gv=gv: v.tensor_mul(out=gv[0:1, :], in0=bcrow[:, :], in1=vrow[:, :]))
        v_op(lambda v: v.tensor_mul(out=r0[:, :], in0=r0[:, :], in1=mga[0:1, :]))
        v_op(lambda v: v.tensor_mul(out=r1[:, :], in0=r1[:, :], in1=mga[0:1, :]))
        v_op(lambda v: v.tensor_scalar_add(out=mcb[0:1, :], in0=mcb[0:1, :], scalar1=1.0))
        vClamp = P.val("vm")
        wait("sync", "vm", vClamp)
        s_dma(gbrow_o[:, :], mgb[0:1, :])
        waits("tensor", vm=vClamp)
        for row, tile in [(r0, Gt), (r1, Bt), (mcb, cb1t), (msb, sbt), (mgb, gbt)]:
            for half in range(2):
                waits("tensor", vm=P.val("vm"))
                t_op(lambda t, row=row, half=half: t.matmul(
                    psY1[:, :], conesr[:, :], row[0:1, half * 512:half * 512 + 512],
                    start=True, stop=True))
                waits("vector", tm=P.val("tm"))
                v_op(lambda v, tile=tile, half=half: v.tensor_copy(
                    out=tile[:, half * 512:half * 512 + 512], in_=psY1[:, :]))
        waits("tensor", vm=P.val("vm"))
        t_op(lambda t: t.matmul(psRT[:, 0:8], conesr[:, :], smallsc[0:1, 0:8], start=True, stop=True))
        waits("vector", tm=P.val("tm"))
        v_op(lambda v: v.tensor_copy(out=rbt[:, :], in_=psRT[:, 0:8]))

        waits("scalar", sq=c0)
        sc_op(lambda s: s.activation(out=asc[0:1, 1:2], in_=asc[0:1, 0:1], func=AF.Sigmoid))
        sA = P.val("sm")
        waits("vector", sm=sA)
        v_op(lambda v: v.memset(asc[0:1, 2:3], 1.0))
        v_op(lambda v: v.tensor_tensor_scan(
            g2[0:1, :], bcast_free(asc[0:1, 1:2], NTOK), bcast_free(asc[0:1, 2:3], NTOK),
            0.0, ALU.mult, ALU.add))
        vG = P.val("vm")
        wait("gpsimd", "vm", vG)
        for seg in range(NTOK // 128):
            g_dma(g2[1:2, seg * 128:(seg + 1) * 128], conesr[0:1, :])
        waits("tensor", vm=vG)
        waits("tensor", vm=P.val("vm"))
        t_op(lambda t: t.matmul(psRT[:, 8:9], conesr[:, :], asc[0:1, 1:2], start=True, stop=True))
        waits("vector", tm=P.val("tm"))
        v_op(lambda v: v.tensor_copy(out=acol[:, :], in_=psRT[:, 8:9]))
        v_op(lambda v: v.tensor_add(out=gid0[:, :], in0=ciop[:, :], in1=cgidb[:, :]))

        wait("sync", "vm", vSct)
        ldrw = s_dma(cstage[0:8, :], rw[:, :])
        s_barrier()
        waits("tensor", sq=ldrw, vm=vClamp)
        for dc in range(NDC):
            waits("tensor", vm=P.val("vm"))
            t_op(lambda t, dc=dc: t.transpose(psRT[0:128, 24:32], cstage[0:8, dc * 128:(dc + 1) * 128], idf[0:8, 0:8]))
            waits("vector", tm=P.val("tm"))
            v_op(lambda v, dc=dc: v.tensor_copy(out=rwTb[:, dc, :], in_=psRT[0:128, 24:32]))
            waits("tensor", vm=P.val("vm"))
            t_op(lambda t, dc=dc: t.transpose(psRT[0:128, 16:17], msa[0:1, dc * 128:(dc + 1) * 128], idf[0:1, 0:1]))
            waits("vector", tm=P.val("tm"))
            v_op(lambda v, dc=dc: v.tensor_copy(out=safm[:, dc:dc + 1], in_=psRT[0:128, 16:17]))
            waits("tensor", vm=P.val("vm"))
            t_op(lambda t, dc=dc: t.transpose(psRT[0:128, 17:18], mca[0:1, dc * 128:(dc + 1) * 128], idf[0:1, 0:1]))
            waits("vector", tm=P.val("tm"))
            v_op(lambda v, dc=dc: v.tensor_copy(out=ca1fm[:, dc:dc + 1], in_=psRT[0:128, 17:18]))
        v_op(lambda v: v.tensor_scalar_add(out=ca1fm[:, :], in0=ca1fm[:, :], scalar1=1.0))
        vFM = P.val("vm")
        waits("tensor", vm=vFM)
        evacS = {}
        vchain = {}
        for dcx in range(NDC + 1):
            if dcx < NDC:
                dc = dcx
                for pc in range(NDC):
                    ti = dc * 8 + pc
                    bank = psY2 if ti % 2 == 0 else psY3
                    if ti == 0:
                        waits("tensor", wsb=ldW)
                    if ti >= 2:
                        wait("tensor", "sm", evacS[ti - 2])
                    tt = t_op(lambda t, dc=dc, pc=pc, bank=bank: t.transpose(
                        bank[:, 0:128], Wsb[:, pc, dc * 128:(dc + 1) * 128], idf[:, :]))
                    if pc == 0 and dc >= 2:
                        wait("scalar", "vm", vchain[dc - 2])
                    wait("scalar", "tm", tt)
                    evacS[ti] = sc_op(lambda s, dc=dc, pc=pc, bank=bank: s.activation(
                        out=wtf2[:, dc % 2, pc * 128:(pc + 1) * 128], in_=bank[:, 0:128],
                        func=AF.Copy))
            if dcx >= 1:
                dcp = dcx - 1
                wtfv = wtf2[:, dcp % 2, :]
                wait("tensor", "sm", evacS[dcp * 8 + 7])
                t_op(lambda t, dcp=dcp, wtfv=wtfv: t.matmul(
                    psRT[0:1, 0:512], safm[:, dcp:dcp + 1], wtfv[:, 0:512],
                    start=(dcp == 0), stop=(dcp == NDC - 1), skip_group_check=True))
                t_op(lambda t, dcp=dcp, wtfv=wtfv: t.matmul(
                    psY1[0:1, 0:512], safm[:, dcp:dcp + 1], wtfv[:, 512:1024],
                    start=(dcp == 0), stop=(dcp == NDC - 1), skip_group_check=True))
                waits("vector", tm=P.val("tm"))
                v_op(lambda v, dcp=dcp, wtfv=wtfv: v.tensor_scalar_mul(
                    out=wtfv[:, :], in0=wtfv[:, :], scalar1=ca1fm[:, dcp:dcp + 1]))
                v_op(lambda v, dcp=dcp, wtfv=wtfv: v.reduce_sum(wbar[:, dcp:dcp + 1], wtfv[:, :], axis=AX.X))
                v_op(lambda v, dcp=dcp: v.tensor_scalar_mul(
                    out=wbar[:, dcp:dcp + 1], in0=wbar[:, dcp:dcp + 1], scalar1=1.0 / D))
                vchain[dcp] = v_op(lambda v, dcp=dcp, wtfv=wtfv: v.tensor_scalar(
                    out=WTb[:, dcp, :], in0=wtfv[:, :], scalar1=wbar[:, dcp:dcp + 1],
                    scalar2=None, op0=ALU.subtract))
        waits("vector", tm=P.val("tm"))
        v_op(lambda v: v.tensor_copy(out=vrow[0:1, 0:512], in_=psRT[0:1, 0:512]))
        v_op(lambda v: v.tensor_copy(out=vrow[0:1, 512:1024], in_=psY1[0:1, 0:512]))
        ldb2 = s_dma(bcrow[:, :], battn[:, :])
        s_barrier()
        v_op(lambda v: v.reduce_sum(smallsc[0:1, 1:2], vrow[:, :], axis=AX.X))
        v_op(lambda v: v.tensor_scalar_mul(out=smallsc[0:1, 1:2], in0=smallsc[0:1, 1:2], scalar1=1.0 / D))
        v_op(lambda v: v.tensor_scalar(out=vrow[:, :], in0=vrow[:, :], scalar1=smallsc[0:1, 1:2], scalar2=None, op0=ALU.subtract))
        waits("vector", sq=ldb2)
        v_op(lambda v: v.reduce_sum(smallsc[0:1, 2:3], bcrow[:, :], axis=AX.X))
        v_op(lambda v: v.tensor_scalar_mul(out=smallsc[0:1, 2:3], in0=smallsc[0:1, 2:3], scalar1=1.0 / D))
        v_op(lambda v: v.tensor_scalar(out=bcrow[:, :], in0=bcrow[:, :], scalar1=smallsc[0:1, 2:3], scalar2=None, op0=ALU.subtract))
        vVB = P.val("vm")
        wait("gpsimd", "vm", vVB)
        g_dma(vb2[0:1, :], vrow[0:1, :])
        g_dma(vb2[1:2, :], bcrow[0:1, :])

        bar = dict(sq=P.val("sq"), gq=P.val("gq"), tm=P.val("tm"),
                   vm=P.val("vm"), sm=P.val("sm"))
        for eng in ("sync", "gpsimd", "tensor", "vector", "scalar"):
            waits(eng, **bar)
        prep.close()

        loop = ExitStack()
        xt = loop.enter_context(nc.sbuf_tensor("xt", [128, 2, D], F32))
        yct = loop.enter_context(nc.sbuf_tensor("yct", [128, 2, D], BF16))
        xnbf = loop.enter_context(nc.sbuf_tensor("xnbf", [128, D], BF16))
        sqscr = loop.enter_context(nc.sbuf_tensor("sqscr", [128, D], F32))
        acscr = loop.enter_context(nc.sbuf_tensor("acscr", [128, D], F32))
        usb = loop.enter_context(nc.sbuf_tensor("usb", [128, D], F32))
        xpt = loop.enter_context(nc.sbuf_tensor("xpt", [128, 2, D], F32))
        h2t = loop.enter_context(nc.sbuf_tensor("h2t", [128, 2, D], F32))
        gbscr = loop.enter_context(nc.sbuf_tensor("gbscr", [128, 2, D], F32))
        h2bf = loop.enter_context(nc.sbuf_tensor("h2bf", [128, D], BF16))
        h2fm = loop.enter_context(nc.sbuf_tensor("h2fm", [128, 2, 128], BF16))
        lgfm = loop.enter_context(nc.sbuf_tensor("lgfm", [8, 128], F32))
        lgall = loop.enter_context(nc.sbuf_tensor("lgall", [128, NT, 8], F32))
        statYa = loop.enter_context(nc.sbuf_tensor("statYa", [128, NT], F32))
        statYb = loop.enter_context(nc.sbuf_tensor("statYb", [128, NT], F32))
        strsy = loop.enter_context(nc.sbuf_tensor("strsy", [128, NT], F32))
        st2s = loop.enter_context(nc.sbuf_tensor("st2s", [128, NT], F32))
        st2q = loop.enter_context(nc.sbuf_tensor("st2q", [128, NT], F32))
        stmu2 = loop.enter_context(nc.sbuf_tensor("stmu2", [128, NT], F32))
        strs2 = loop.enter_context(nc.sbuf_tensor("strs2", [128, NT], F32))
        tv1 = loop.enter_context(nc.sbuf_tensor("tv1", [128, 8], F32))
        tv2 = loop.enter_context(nc.sbuf_tensor("tv2", [128, 8], F32))
        m1c = loop.enter_context(nc.sbuf_tensor("m1c", [128, 1], F32))
        nm1c = loop.enter_context(nc.sbuf_tensor("nm1c", [128, 1], F32))
        m2c = loop.enter_context(nc.sbuf_tensor("m2c", [128, 1], F32))
        rcpc = loop.enter_context(nc.sbuf_tensor("rcpc", [128, 1], F32))
        wtile = loop.enter_context(nc.sbuf_tensor("wtile", [128, 2, 8], F32))
        masksb = loop.enter_context(nc.sbuf_tensor("masksb", [128, 7], F32))
        Psb = loop.enter_context(nc.sbuf_tensor("Psb", [128, 7], F32))
        cmpst = loop.enter_context(nc.sbuf_tensor("cmpst", [128, 7], F32))
        offs_sb = loop.enter_context(nc.sbuf_tensor("offs_sb", [7, 1], F32))
        idxf = loop.enter_context(nc.sbuf_tensor("idxf", [7, 1], F32))
        idxu = loop.enter_context(nc.sbuf_tensor("idxu", [7, 1], U32))
        gidj = loop.enter_context(nc.sbuf_tensor("gidj", [128, 1], F32))
        ohbuf = loop.enter_context(nc.sbuf_tensor("ohbuf", [128, 2, 128], F32))
        cmpT = loop.enter_context(nc.sbuf_tensor("cmpT", [7, 2, 128], F32))

        vZ = v_op(lambda v: v.memset(acscr[:, :], 0.0))
        wait("sync", "vm", vZ)
        def zl(s):
            s.dma_start(out=lists_o[:, :], in_=acscr[:, 0:238]).then_inc(LZ, 16)
        P.on("sync", zl)
        P.inc("lz", 16)
        wait("vector", "lz", 16)
        wait("gpsimd", "lz", 16)

        sticks, vticks = [], []
        for j in range(NT):
            buf = j % 2
            if j >= 2:
                wait("sync", "sm", sticks[j - 2])
                wait("sync", "vm", vticks[j - 2])
            ld = s_dma(xt[:, buf, :], x_in[j * 128:(j + 1) * 128, :], sem="xq%d" % buf)
            waits("scalar", **{("xq%d" % buf): ld})
            sticks.append(sc_op(lambda s, j=j, buf=buf: s.activation(
                out=sqscr[:, :], in_=xt[:, buf, :], func=AF.Square,
                accum_out=stat_sq[:, j:j + 1])))
            waits("vector", **{("xq%d" % buf): ld})
            vticks.append(v_op(lambda v, j=j, buf=buf: v.tensor_scalar(
                out=acscr[:, :], in0=xt[:, buf, :], scalar1=1.0, scalar2=None,
                op0=ALU.mult, op1=ALU.add, accum_out=stat_s1[:, j:j + 1])))
        waits("vector", sm=P.val("sm"))
        v_op(lambda v: v.tensor_scalar_mul(out=stat_mu[:, :], in0=stat_s1[:, :], scalar1=1.0 / D))
        v_op(lambda v: v.tensor_scalar_mul(out=stat_sq[:, :], in0=stat_sq[:, :], scalar1=1.0 / D))
        v_op(lambda v: v.tensor_mul(out=stat_s1[:, :], in0=stat_mu[:, :], in1=stat_mu[:, :]))
        v_op(lambda v: v.tensor_sub(out=stat_sq[:, :], in0=stat_sq[:, :], in1=stat_s1[:, :]))
        v_op(lambda v: v.tensor_scalar_add(out=stat_sq[:, :], in0=stat_sq[:, :], scalar1=EPS))
        v_op(lambda v: v.reciprocal(out=stat_rs[:, :], in_=stat_sq[:, :]))
        waits("scalar", vm=P.val("vm"))
        sc_op(lambda s: s.activation(out=stat_rs[:, :], in_=stat_rs[:, :], func=AF.Sqrt))
        sStats = P.val("sm")

        waits("vector", sm=sStats)
        wait("sync", "vm", P.val("vm"))
        wait("sync", "sm", P.val("sm"))
        xnticks = []
        trC = [0]
        pstTick = {0: 0, 1: 0}
        for j in range(NT):
            buf = j % 2
            if j >= 2:
                wait("sync", "vm", xnticks[j - 2][0])
                wait("sync", "sm", xnticks[j - 2][1])
            ld = s_dma(xt[:, buf, :], x_in[j * 128:(j + 1) * 128, :], sem="xq%d" % buf)
            waits("vector", **{("xq%d" % buf): ld})
            wait("vector", "tm", trC[0])
            v_op(lambda v, j=j, buf=buf: v.tensor_scalar(
                out=xnbf[:, :], in0=xt[:, buf, :], scalar1=stat_mu[:, j:j + 1],
                scalar2=stat_rs[:, j:j + 1], op0=ALU.subtract, op1=ALU.mult))
            waits("tensor", vm=P.val("vm"))
            for dc in range(NDC):
                pp = dc % 2
                pst = psTB0 if pp == 0 else psTB1
                waits("tensor", sm=pstTick[pp])
                t_op(lambda t, dc=dc, pst=pst: t.transpose(
                    pst[:, :], xnbf[:, dc * 128:(dc + 1) * 128], idb[:, :]))
                waits("scalar", tm=P.val("tm"))
                pstTick[pp] = sc_op(lambda s, j=j, dc=dc, pst=pst: s.copy(
                    out=xnT[:, dc, j * 128:(j + 1) * 128], in_=pst[:, :]))
            trC[0] = P.val("tm")
            xnticks.append((P.val("vm"), P.val("sm")))

        waits("vector", sm=P.val("sm"))
        for dc in range(NDC):
            v_op(lambda v, dc=dc: v.tensor_tensor_scan(
                xnT[:, dc, :], bcast_free(acol[:, 0:1], NTOK),
                xnT[:, dc, :], 0.0, ALU.mult, ALU.add))
        vScan = P.val("vm")

        wait("tensor", "vm", vScan)
        evT, scT4 = [], []
        soh_cnt = [0, 0]
        sow_cnt = [0, 0]
        for j in range(NT):
            pb = j % 2
            psA_ = psY0 if pb == 0 else psY2
            psB_ = psY1 if pb == 0 else psY3
            if j >= 2:
                wait("tensor", "vm", evT[j - 2])
                wait("tensor", "sm", scT4[j - 2])
            for ec in range(2):
                psY = psA_ if ec == 0 else psB_
                for dc in range(NDC):
                    t_op(lambda t, j=j, dc=dc, ec=ec, psY=psY: t.matmul(
                        psY[:, :], xnT[:, dc, j * 128:(j + 1) * 128],
                        WTb[:, dc, ec * 512:(ec + 1) * 512],
                        start=(dc == 0), stop=False), tick=False)
                t_op(lambda t, j=j, ec=ec, psY=psY: t.matmul(
                    psY[:, :], g2[0:2, j * 128:(j + 1) * 128],
                    vb2[0:2, ec * 512:(ec + 1) * 512], start=False, stop=True))
            tY = P.val("tm")
            waits("scalar", tm=tY)
            sc_op(lambda s, j=j, psA_=psA_: s.activation(out=sqscr[:, 0:512], in_=psA_[:, :], func=AF.Square, accum_out=statYa[:, j:j + 1]))
            sc_op(lambda s, j=j, psB_=psB_: s.activation(out=sqscr[:, 512:1024], in_=psB_[:, :], func=AF.Square, accum_out=statYb[:, j:j + 1]))
            scT4.append(P.val("sm"))
            waits("vector", tm=tY, sm=P.val("sm"))
            if j >= 2:
                wait("vector", "soh%d" % pb, 16 * soh_cnt[pb])
            v_op(lambda v, pb=pb, psA_=psA_: v.tensor_copy(out=yct[:, pb, 0:512], in_=psA_[:, :]))
            v_op(lambda v, pb=pb, psB_=psB_: v.tensor_copy(out=yct[:, pb, 512:1024], in_=psB_[:, :]))
            evT.append(P.val("vm"))
            wait("sync", "vm", evT[j])
            s_dma(ycs_o[j * 128:(j + 1) * 128, :], yct[:, pb, :], sem="soh%d" % pb)
            soh_cnt[pb] += 1
        waits("vector", sm=P.val("sm"))
        v_op(lambda v: v.tensor_add(out=statYa[:, :], in0=statYa[:, :], in1=statYb[:, :]))
        v_op(lambda v: v.tensor_scalar(out=statYa[:, :], in0=statYa[:, :], scalar1=1.0 / D, scalar2=EPS, op0=ALU.mult, op1=ALU.add))
        v_op(lambda v: v.reciprocal(out=strsy[:, :], in_=statYa[:, :]))
        waits("scalar", vm=P.val("vm"))
        sc_op(lambda s: s.activation(out=strsy[:, :], in_=strsy[:, :], func=AF.Sqrt))
        sRsy = P.val("sm")

        wait("sync", "vm", P.val("vm"))
        wait("sync", "sm", P.val("sm"))
        wait("sync", "soh0", P.val("soh0"))
        wait("sync", "soh1", P.val("soh1"))
        waits("vector", sm=sRsy)
        p5vT = []
        p5sT = []
        sox_cnt = [0, 0]
        ldx_t = {}
        ldy_t = {}
        def p5_load(j):
            buf = j % 2
            ldx_t[j] = s_dma(xt[:, buf, :], x_in[j * 128:(j + 1) * 128, :], sem="xq%d" % buf)
            ldy_t[j] = s_dma(yct[:, buf, :], ycs_o[j * 128:(j + 1) * 128, :], sem="wq%d" % buf)
        p5_load(0)
        p5_load(1)
        for j in range(NT):
            buf = j % 2
            waits("vector", **{("xq%d" % buf): ldx_t[j], ("wq%d" % buf): ldy_t[j]})
            v_op(lambda v, j=j, buf=buf: v.scalar_tensor_tensor(
                out=usb[:, :], in0=yct[:, buf, :], scalar=strsy[:, j:j + 1],
                in1=Gt[:, :], op0=ALU.mult, op1=ALU.mult))
            if j >= 2:
                wait("vector", "sox%d" % buf, 16 * sox_cnt[buf])
            v_op(lambda v, buf=buf: v.tensor_add(out=xpt[:, buf, :], in0=usb[:, :], in1=xt[:, buf, :]))
            v_op(lambda v, j=j, buf=buf: v.scalar_tensor_tensor(
                out=xpt[:, buf, :], in0=xpt[:, buf, :], scalar=1.0, in1=Bt[:, :],
                op0=ALU.mult, op1=ALU.add, accum_out=st2s[:, j:j + 1]))
            p5vT.append(P.val("vm"))
            waits("scalar", vm=P.val("vm"))
            sc_op(lambda s, j=j, buf=buf: s.activation(
                out=sqscr[:, :], in_=xpt[:, buf, :], func=AF.Square,
                accum_out=st2q[:, j:j + 1]))
            p5sT.append(P.val("sm"))
            wait("sync", "vm", p5vT[j])
            s_dma(xpp_o[j * 128:(j + 1) * 128, :], xpt[:, buf, :], sem="sox%d" % buf)
            sox_cnt[buf] += 1
            if j + 2 < NT:
                wait("sync", "sm", p5sT[j])
                p5_load(j + 2)
        waits("vector", sm=P.val("sm"))
        v_op(lambda v: v.tensor_scalar_mul(out=stmu2[:, :], in0=st2s[:, :], scalar1=1.0 / D))
        v_op(lambda v: v.tensor_scalar_mul(out=st2q[:, :], in0=st2q[:, :], scalar1=1.0 / D))
        v_op(lambda v: v.tensor_mul(out=st2s[:, :], in0=stmu2[:, :], in1=stmu2[:, :]))
        v_op(lambda v: v.tensor_sub(out=st2q[:, :], in0=st2q[:, :], in1=st2s[:, :]))
        v_op(lambda v: v.tensor_scalar_add(out=st2q[:, :], in0=st2q[:, :], scalar1=EPS))
        v_op(lambda v: v.reciprocal(out=strs2[:, :], in_=st2q[:, :]))
        waits("scalar", vm=P.val("vm"))
        sc_op(lambda s: s.activation(out=strs2[:, :], in_=strs2[:, :], func=AF.Sqrt))
        sRs2 = P.val("sm")

        wait("sync", "vm", P.val("vm"))
        wait("sync", "sm", P.val("sm"))
        wait("sync", "sox0", P.val("sox0"))
        wait("sync", "sox1", P.val("sox1"))
        waits("vector", sm=sRs2)
        p6T = []
        ldp_t = {}
        def p6_load(j):
            buf = j % 2
            ldp_t[j] = s_dma(xt[:, buf, :], xpp_o[j * 128:(j + 1) * 128, :], sem="xq%d" % buf)
        p6_load(0)
        p6_load(1)
        def h2slice(j):
            return xnT[:, j // 4, (j % 4) * 1024:(j % 4 + 1) * 1024]
        for j in range(NT):
            buf = j % 2
            waits("vector", **{("xq%d" % buf): ldp_t[j]})
            v_op(lambda v, j=j, buf=buf: v.tensor_scalar(
                out=usb[:, :], in0=xt[:, buf, :], scalar1=stmu2[:, j:j + 1],
                scalar2=strs2[:, j:j + 1], op0=ALU.subtract, op1=ALU.mult))
            if j >= 2:
                wait("vector", "soh%d" % buf, 16 * soh_cnt[buf])
                wait("vector", "sow%d" % buf, 16 * sow_cnt[buf])
            v_op(lambda v, buf=buf: v.scalar_tensor_tensor(
                out=h2t[:, buf, :], in0=usb[:, :], scalar=1.0, in1=cb1t[:, :],
                op0=ALU.mult, op1=ALU.mult))
            v_op(lambda v, buf=buf: v.tensor_add(out=h2t[:, buf, :], in0=h2t[:, buf, :], in1=sbt[:, :]))
            vH2 = P.val("vm")
            wait("sync", "vm", vH2)
            s_dma(h2b_o[j * 128:(j + 1) * 128, :], h2t[:, buf, :], sem="soh%d" % buf)
            soh_cnt[buf] += 1
            v_op(lambda v, buf=buf: v.tensor_mul(out=gbscr[:, buf, :], in0=h2t[:, buf, :], in1=gbt[:, :]))
            vGB = P.val("vm")
            wait("sync", "vm", vGB)
            s_dma(gbh2_o[j * 128:(j + 1) * 128, :], gbscr[:, buf, :], sem="sow%d" % buf)
            sow_cnt[buf] += 1
            v_op(lambda v, j=j, buf=buf: v.tensor_copy(out=h2slice(j), in_=h2t[:, buf, :]))
            p6T.append(P.val("vm"))
            if j + 2 < NT:
                wait("sync", "vm", p6T[j])
                p6_load(j + 2)

        waits("tensor", vm=P.val("vm"))
        for j in range(NT):
            cpT = {}
            for dc in range(NDC + 2):
                if dc < NDC:
                    pp = dc % 2
                    pst = psTB0 if pp == 0 else psTB1
                    waits("tensor", sm=pstTick[pp])
                    t_op(lambda t, j=j, dc=dc, pst=pst: t.transpose(
                        pst[:, :], h2slice(j)[:, dc * 128:(dc + 1) * 128], idb[:, :]))
                if dc >= 2:
                    e = dc - 2
                    waits("tensor", sm=cpT[e])
                    t_op(lambda t, e=e: t.matmul(
                        psRT[0:8, 0:128], rwTb[:, e, :], h2fm[:, e % 2, :],
                        start=(e == 0), stop=(e == NDC - 1), skip_group_check=True))
                if dc < NDC:
                    pp = dc % 2
                    pst = psTB0 if pp == 0 else psTB1
                    waits("scalar", tm=P.val("tm"))
                    pstTick[pp] = sc_op(lambda s, dc=dc, pst=pst: s.copy(
                        out=h2fm[:, dc % 2, :], in_=pst[:, :]))
                    cpT[dc] = pstTick[pp]
            waits("scalar", tm=P.val("tm"), vm=P.val("vm"))
            sc_op(lambda s: s.copy(out=lgfm[:, :], in_=psRT[0:8, 0:128]))
            waits("tensor", sm=P.val("sm"))
            t_op(lambda t: t.transpose(psRT[0:128, 128:136], lgfm[:, :], idf[0:8, 0:8]))
            waits("vector", tm=P.val("tm"))
            v_op(lambda v, j=j: v.tensor_add(out=lgall[:, j, :], in0=psRT[0:128, 128:136], in1=rbt[:, :]))
        wait("sync", "vm", P.val("vm"))
        wait("sync", "sm", P.val("sm"))
        wait("sync", "sow0", P.val("sow0"))
        wait("sync", "sow1", P.val("sow1"))
        scatticks = []
        ohticks = {0: 0, 1: 0}
        p7T = []
        l7x = {}
        l7g = {}
        def p7_load(j):
            buf = j % 2
            l7x[j] = s_dma(xt[:, buf, :], xpp_o[j * 128:(j + 1) * 128, :], sem="xq%d" % buf)
            l7g[j] = s_dma(gbscr[:, buf, :], gbh2_o[j * 128:(j + 1) * 128, :], sem="wq%d" % buf)
        p7_load(0)
        p7_load(1)
        for j in range(NT):
            buf = j % 2
            ldp = l7x[j]
            ldg = l7g[j]
            v_op(lambda v, j=j: v.reduce_max(m1c[:, :], lgall[:, j, :], axis=AX.X))
            v_op(lambda v: v.tensor_scalar_mul(out=nm1c[:, :], in0=m1c[:, :], scalar1=-1.0))
            v_op(lambda v, j=j: v.tensor_scalar(out=tv1[:, :], in0=lgall[:, j, :], scalar1=m1c[:, 0:1], scalar2=None, op0=ALU.is_equal))
            v_op(lambda v, j=j: v.scalar_tensor_tensor(out=tv2[:, :], in0=tv1[:, :], scalar=-1e30, in1=lgall[:, j, :], op0=ALU.mult, op1=ALU.add))
            v_op(lambda v: v.reduce_max(m2c[:, :], tv2[:, :], axis=AX.X))
            v_op(lambda v, j=j: v.tensor_scalar(out=tv1[:, :], in0=lgall[:, j, :], scalar1=m2c[:, 0:1], scalar2=None, op0=ALU.is_ge))
            waits("scalar", vm=P.val("vm"))
            sc_op(lambda s, j=j: s.activation(out=tv2[:, :], in_=lgall[:, j, :], func=AF.Exp, bias=nm1c[:, 0:1], scale=1.0))
            waits("vector", sm=P.val("sm"))
            v_op(lambda v: v.reduce_sum(rcpc[:, :], tv2[:, :], axis=AX.X))
            v_op(lambda v: v.reciprocal(out=rcpc[:, :], in_=rcpc[:, :]))
            if j >= 2:
                wait("vector", "sow%d" % buf, 16 * sow_cnt[buf])
            v_op(lambda v, buf=buf: v.scalar_tensor_tensor(
                out=wtile[:, buf, :], in0=tv2[:, :], scalar=rcpc[:, 0:1], in1=tv1[:, :],
                op0=ALU.mult, op1=ALU.mult))
            vW = P.val("vm")
            wait("sync", "vm", vW)
            s_dma(w_o[j * 128:(j + 1) * 128, :], wtile[:, buf, :], sem="sow%d" % buf)
            sow_cnt[buf] += 1
            if j >= 2:
                wait("vector", "sox%d" % buf, 16 * sox_cnt[buf])
            waits("vector", **{("xq%d" % buf): ldp, ("wq%d" % buf): ldg})
            v_op(lambda v, buf=buf: v.scalar_tensor_tensor(
                out=xpt[:, buf, :], in0=gbscr[:, buf, :], scalar=wtile[:, buf, 7:8],
                in1=xt[:, buf, :], op0=ALU.mult, op1=ALU.add))
            vXf = P.val("vm")
            wait("sync", "vm", vXf)
            s_dma(xpp_o[j * 128:(j + 1) * 128, :], xpt[:, buf, :], sem="sox%d" % buf)
            sox_cnt[buf] += 1
            v_op(lambda v, buf=buf: v.tensor_scalar(out=masksb[:, :], in0=wtile[:, buf, 0:7], scalar1=0.0, scalar2=None, op0=ALU.is_gt))
            waits("tensor", vm=P.val("vm"))
            t_op(lambda t: t.matmul(psRT[:, 256:263], ctriu[:, :], masksb[:, :], start=True, stop=True, skip_group_check=True))
            waits("vector", tm=P.val("tm"))
            v_op(lambda v: v.tensor_copy(out=Psb[:, :], in_=psRT[:, 256:263]))
            if j == 0:
                v_op(lambda v: v.memset(offs_sb[:, :], 0.0))
            else:
                v_op(lambda v: v.tensor_copy(out=offs_sb[:, :], in_=psOf[:, :]))
            waits("tensor", vm=P.val("vm"))
            t_op(lambda t, j=j: t.matmul(psOf[:, :], masksb[:, :], conesc[:, :],
                                         start=(j == 0), stop=(j == NT - 1), skip_group_check=True))
            v_op(lambda v, j=j: v.tensor_scalar_add(out=gidj[:, :], in0=gid0[:, :], scalar1=float(j * 128)))
            wait("vector", "sc", P.val("sc"))
            v_op(lambda v: v.tensor_add(out=idxf[:, :], in0=offs_sb[:, :], in1=ce4352[:, :]))
            v_op(lambda v: v.tensor_copy(out=idxu[:, :], in_=idxf[:, :]))
            for e in range(7):
                ob = e % 2
                wait("vector", "tm", ohticks[ob])
                v_op(lambda v, e=e, ob=ob: v.tensor_scalar(
                    out=ohbuf[:, ob, :], in0=ciota[:, :], scalar1=Psb[:, e:e + 1],
                    scalar2=masksb[:, e:e + 1], op0=ALU.is_equal, op1=ALU.mult))
                waits("tensor", vm=P.val("vm"))
                ohticks[ob] = t_op(lambda t, e=e, ob=ob: t.matmul(
                    psRT[:, 288 + e:289 + e], ohbuf[:, ob, :], gidj[:, :],
                    start=True, stop=True, skip_group_check=True))
            waits("vector", tm=P.val("tm"))
            v_op(lambda v: v.tensor_copy(out=cmpst[:, :], in_=psRT[:, 288:295]))
            waits("tensor", vm=P.val("vm"))
            t_op(lambda t: t.transpose(psRT[0:7, 320:448], cmpst[:, :], idf[:, :]))
            waits("vector", tm=P.val("tm"))
            if j >= 2:
                wait("vector", "sc", scatticks[j - 2])
            cbuf = j % 2
            v_op(lambda v, cbuf=cbuf: v.tensor_copy(out=cmpT[0:7, cbuf, :], in_=psRT[0:7, 320:448]))
            vT2 = P.val("vm")
            wait("gpsimd", "vm", vT2)
            wait("gpsimd", "sc", P.val("sc"))
            def scat(g, cbuf=cbuf):
                g.indirect_dma_start(
                    out=lists_o[:, :], out_offset=IndirectOffsetOnAxis(idxu[:, :], 0),
                    in_=cmpT[0:7, cbuf, :], in_offset=None).then_inc(SC, 16)
            P.on("gpsimd", scat)
            scatticks.append(P.inc("sc", 16))
            p7T.append(P.val("vm"))
            if j + 2 < NT:
                wait("sync", "vm", p7T[j])
                p7_load(j + 2)

        waits("vector", tm=P.val("tm"))
        vCnt = v_op(lambda v: v.tensor_copy(out=offs_sb[:, :], in_=psOf[:, :]))
        wait("sync", "vm", vCnt)
        s_dma(counts_o[:, :], offs_sb[:, :])

        wait("sync", "sq", P.val("sq"))
        for snm in ("soh0", "soh1", "sow0", "sow1", "sox0", "sox1", "xq0", "xq1", "wq0", "wq1"):
            wait("sync", snm, P.val(snm))
        wait("gpsimd", "sc", P.val("sc"))
        wait("gpsimd", "gq", P.val("gq"))
        loop.close()

        @block.sync
        def _(s):
            P.run("sync", s)

        @block.tensor
        def _(t):
            P.run("tensor", t)

        @block.vector
        def _(v):
            P.run("vector", v)

        @block.scalar
        def _(s):
            P.run("scalar", s)

        @block.gpsimd
        def _(g):
            P.run("gpsimd", g)

    return nc

def host_consts_l2(core_id):
    import ml_dtypes
    c = {}
    c["ident_bf16"] = np.eye(128, dtype=np.float32).astype(ml_dtypes.bfloat16)
    c["ident_f32b"] = np.eye(128, dtype=np.float32)
    c["trashrow"] = np.full((1, 8), GPAD, np.uint32)
    c["ident8"] = np.eye(8, dtype=np.float32)
    c["triu8"] = np.triu(np.ones((8, 8), np.float32), k=1)
    return c


def build_l2():
    nc = bass.Bass("TRN2", target_bir_lowering=False, debug=False)
    h2fa = nc.dram_tensor("h2fa", [NTOKG, 512], F32, kind="ExternalInput").ap()
    h2fb = nc.dram_tensor("h2fb", [NTOKG, 512], F32, kind="ExternalInput").ap()
    mylists = nc.dram_tensor("mylists", [8, LSTRIDE], F32, kind="ExternalInput").ap()
    counts_e = nc.dram_tensor("counts_e", [8, 1], F32, kind="ExternalInput").ap()
    wi = nc.dram_tensor("wi", [D, I], F32, kind="ExternalInput").ap()
    wo = nc.dram_tensor("wo", [I, D], F32, kind="ExternalInput").ap()
    ident_bf16 = nc.dram_tensor("ident_bf16", [128, 128], BF16, kind="ExternalInput").ap()
    ident_f32b = nc.dram_tensor("ident_f32b", [128, 128], F32, kind="ExternalInput").ap()
    trashrow = nc.dram_tensor("trashrow", [1, 8], U32, kind="ExternalInput").ap()
    ident8 = nc.dram_tensor("ident8", [8, 8], F32, kind="ExternalInput").ap()
    triu8 = nc.dram_tensor("triu8", [8, 8], F32, kind="ExternalInput").ap()
    eout = nc.dram_tensor("eout", [CAP, D], F32, kind="ExternalOutput").ap()
    glist_t = nc.dram_tensor("glist", [GPAD + 4096, 1], F32, kind="ExternalOutput")
    glist = glist_t.ap()
    glist2 = glist_t.ap()

    P = Plan()
    stack = ExitStack()
    with stack:
        idb = stack.enter_context(nc.sbuf_tensor("idb", [128, 128], BF16))
        idf2 = stack.enter_context(nc.sbuf_tensor("idf2", [128, 128], F32))
        idx2 = stack.enter_context(nc.sbuf_tensor("idx2", [2, 8], U32))
        seg2 = stack.enter_context(nc.sbuf_tensor("seg2", [2, LSTRIDE], F32))
        id8 = stack.enter_context(nc.sbuf_tensor("id8", [8, 8], F32))
        tr8 = stack.enter_context(nc.sbuf_tensor("tr8", [8, 8], F32))
        cnts = stack.enter_context(nc.sbuf_tensor("cnts", [8, 1], F32))
        cumo = stack.enter_context(nc.sbuf_tensor("cumo", [8, 1], F32))
        cumr = stack.enter_context(nc.sbuf_tensor("cumr", [1, 8], F32))
        cumru = stack.enter_context(nc.sbuf_tensor("cumru", [1, 8], U32))
        idlist = stack.enter_context(nc.sbuf_tensor("idlist", [128, 32], F32))
        idu = stack.enter_context(nc.sbuf_tensor("idu", [128, 32], U32))
        wi_sb = stack.enter_context(nc.sbuf_tensor("wi_sb", [128, 8, I], BF16))
        wo_sb = stack.enter_context(nc.sbuf_tensor("wo_sb", [128, 32, D], BF16))
        gbuf = stack.enter_context(nc.sbuf_tensor("gbuf", [128, 2, D], F32))
        h2fm = stack.enter_context(nc.sbuf_tensor("h2fm", [128, 8, 512], BF16))
        hbf = stack.enter_context(nc.sbuf_tensor("hbf", [128, 32, 512], BF16))
        outsb = stack.enter_context(nc.sbuf_tensor("outsb", [128, 2, D], F32))
        ps1a = stack.enter_context(nc.psum_tensor("ps1a", [128, 512], F32))
        ps1b = stack.enter_context(nc.psum_tensor("ps1b", [128, 512], F32))
        ps2a = stack.enter_context(nc.psum_tensor("ps2a", [128, 512], F32))
        ps2b = stack.enter_context(nc.psum_tensor("ps2b", [128, 512], F32))
        psta = stack.enter_context(nc.psum_tensor("psta", [128, 128], F32))
        pstb = stack.enter_context(nc.psum_tensor("pstb", [128, 128], F32))
        psm = stack.enter_context(nc.psum_tensor("psm", [8, 8], F32))
        SQ = stack.enter_context(nc.semaphore("sq"))
        GL = stack.enter_context(nc.semaphore("gl"))
        GA0 = stack.enter_context(nc.semaphore("ga0"))
        GA1 = stack.enter_context(nc.semaphore("ga1"))
        WIQ = stack.enter_context(nc.semaphore("wiq"))
        WOQ = stack.enter_context(nc.semaphore("woq"))
        EO0 = stack.enter_context(nc.semaphore("eo0"))
        EO1 = stack.enter_context(nc.semaphore("eo1"))
        TM = stack.enter_context(nc.semaphore("tm"))
        VM = stack.enter_context(nc.semaphore("vm"))
        SM = stack.enter_context(nc.semaphore("sm"))
        block = stack.enter_context(nc.Block())

        SEMS = {"sq": SQ, "gl": GL, "ga0": GA0, "ga1": GA1, "wiq": WIQ, "woq": WOQ,
                "eo0": EO0, "eo1": EO1, "tm": TM, "vm": VM, "sm": SM}

        def s_dma(out, in_, sem="sq"):
            S = SEMS[sem]
            def f(s, out=out, in_=in_, S=S):
                s.dma_start(out=out, in_=in_).then_inc(S, 16)
            P.on("sync", f)
            return P.inc(sem, 16)

        def g_dma(out, in_, sem):
            S = SEMS[sem]
            def f(g, out=out, in_=in_, S=S):
                g.dma_start(out=out, in_=in_).then_inc(S, 16)
            P.on("gpsimd", f)
            return P.inc(sem, 16)

        def _eng_op(eng, sem_name, fn, tick):
            prev = P.val(sem_name)
            S = SEMS[sem_name]
            if prev > 0:
                def w(e, S=S, prev=prev):
                    e.wait_ge(S, prev)
                P.on(eng, w)
            if tick:
                n = P.inc(sem_name, 1)
                def f(e, fn=fn, S=S):
                    fn(e).then_inc(S, 1)
            else:
                n = prev
                def f(e, fn=fn):
                    fn(e)
            P.on(eng, f)
            return n

        def t_op(fn, tick=True):
            return _eng_op("tensor", "tm", fn, tick)

        def v_op(fn, tick=True):
            return _eng_op("vector", "vm", fn, tick)

        def wait(eng_name, sem_name, val):
            if val <= 0:
                return
            sem = SEMS[sem_name]
            def f(e, sem=sem, val=val):
                e.wait_ge(sem, val)
            P.on(eng_name, f)

        def waits(eng_name, **kw):
            for k, v in kw.items():
                wait(eng_name, k, v)

        s_dma(idb[:, :], ident_bf16[:, :])
        s_dma(idf2[:, :], ident_f32b[:, :])
        s_dma(id8[:, :], ident8[:, :])
        s_dma(tr8[:, :], triu8[:, :])
        s_dma(cnts[:, :], counts_e[:, :])
        s_dma(idx2[1:2, :], trashrow[:, :])
        c0 = P.val("sq")
        wait("sync", "sq", c0)
        for pc in range(8):
            g_dma(wi_sb[:, pc, :], wi[pc * 128:(pc + 1) * 128, :], "wiq")
        wiT = P.val("wiq")
        for pc in range(32):
            g_dma(wo_sb[:, pc, :], wo[pc * 128:(pc + 1) * 128, :], "woq")
        woT = P.val("woq")

        waits("tensor", sq=c0)
        t_op(lambda t: t.matmul(psm[:, 0:1], tr8[:, :], cnts[:, :], start=True, stop=True))
        waits("vector", tm=P.val("tm"))
        v_op(lambda v: v.tensor_copy(out=cumo[:, :], in_=psm[:, 0:1]))
        v_op(lambda v: v.tensor_scalar_min(out=cumo[:, :], in0=cumo[:, :], scalar1=4224.0))
        waits("tensor", vm=P.val("vm"))
        t_op(lambda t: t.transpose(psm[0:1, 0:8], cumo[:, :], id8[:, :]))
        waits("vector", tm=P.val("tm"))
        v_op(lambda v: v.tensor_copy(out=cumr[:, :], in_=psm[0:1, 0:8]))
        v_op(lambda v: v.tensor_copy(out=cumru[:, :], in_=cumr[:, :]))
        vM0 = v_op(lambda v: v.memset(seg2[:, :], 0.0), tick=True)
        vCum = P.val("vm")
        wait("gpsimd", "vm", vCum)
        def _mvidx(g):
            g.dma_start(out=idx2[0:1, :], in_=cumru[0:1, :]).then_inc(GL, 16)
        P.on("gpsimd", _mvidx)
        P.inc("gl", 16)
        wait("gpsimd", "gl", P.val("gl"))
        wait("gpsimd", "vm", P.val("vm"))
        for b in range(8):
            nrow = 2 if b < 7 else 1
            def mv2(g, b=b, nrow=nrow):
                g.dma_start(out=seg2[0:nrow, :], in_=mylists[b:b + nrow, :]).then_inc(GL, 16)
            P.on("gpsimd", mv2)
            P.inc("gl", 16)
            wait("gpsimd", "gl", P.val("gl"))
            def seg(g, b=b):
                g.indirect_dma_start(
                    out=glist2[:, :], out_offset=IndirectOffsetOnAxis(idx2[:, b:b + 1], 0),
                    in_=seg2[0:2, 0:4096], in_offset=None).then_inc(GL, 16)
            P.on("gpsimd", seg)
            P.inc("gl", 16)
            wait("gpsimd", "gl", P.val("gl"))
        def ldid(s):
            src = glist[0:4096, 0:1].rearrange("(c p) a -> (a p) c", p=128)
            s.dma_start(out=idlist[:, :], in_=src, allow_slow_non_contiguous=True).then_inc(SQ, 16)
        wait("sync", "gl", P.val("gl"))
        P.on("sync", ldid)
        ilT = P.inc("sq", 16)
        wait("sync", "sq", ilT)
        waits("vector", sq=ilT)
        v_op(lambda v: v.tensor_scalar(out=idlist[:, :], in0=idlist[:, :], scalar1=0.0, scalar2=32767.0, op0=ALU.max, op1=ALU.min))
        v_op(lambda v: v.tensor_copy(out=idu[:, :], in_=idlist[:, :]))
        vIdu = P.val("vm")

        wait("gpsimd", "vm", vIdu)
        wait("tensor", "wiq", wiT)
        relu_ticks = []
        trT = []
        gtrT = {}
        for cc in range(NCC):
            for g4 in range(4):
                gslot = cc * 4 + g4
                gb = gslot % 2
                if gslot >= 2:
                    wait("gpsimd", "tm", gtrT[gslot - 2])
                gsem = SEMS["ga%d" % gb]
                def gat(g, cc=cc, g4=g4, gb=gb, gsem=gsem):
                    g.indirect_dma_start(
                        out=gbuf[:, gb, 0:512], out_offset=None,
                        in_=h2fa[:, :],
                        in_offset=IndirectOffsetOnAxis(idu[:, cc * 4 + g4:cc * 4 + g4 + 1], 0),
                    ).then_inc(gsem, 16)
                    g.indirect_dma_start(
                        out=gbuf[:, gb, 512:1024], out_offset=None,
                        in_=h2fb[:, :],
                        in_offset=IndirectOffsetOnAxis(idu[:, cc * 4 + g4:cc * 4 + g4 + 1], 0),
                    ).then_inc(gsem, 16)
                P.on("gpsimd", gat)
                P.inc("ga%d" % gb, 16)
                gaT = P.inc("ga%d" % gb, 16)
                wait("tensor", "ga%d" % gb, gaT)
                for dc in range(8):
                    it = g4 * 8 + dc
                    pst = psta if it % 2 == 0 else pstb
                    if True:
                        wait("tensor", "vm", max(P.val("vm") - 1, 0))
                    t_op(lambda t, gb=gb, dc=dc, pst=pst: t.transpose(
                        pst[:, :], gbuf[:, gb, dc * 128:(dc + 1) * 128], idf2[:, :]))
                    waits("vector", tm=P.val("tm"))
                    v_op(lambda v, g4=g4, dc=dc, pst=pst: v.tensor_copy(
                        out=h2fm[:, dc, g4 * 128:(g4 + 1) * 128], in_=pst[:, :]))
                gtrT[gslot] = P.val("tm")
            trT.append(P.val("tm"))
            vFM = P.val("vm")
            waits("tensor", vm=vFM)
            for ich in range(32):
                psu = ps1a if ich % 2 == 0 else ps1b
                if ich >= 2:
                    wait("tensor", "vm", relu_ticks[-2])
                for dc in range(8):
                    t_op(lambda t, ich=ich, dc=dc, psu=psu: t.matmul(
                        psu[:, :], wi_sb[:, dc, ich * 128:(ich + 1) * 128],
                        h2fm[:, dc, :], start=(dc == 0), stop=(dc == 7)),
                        tick=(dc == 7))
                waits("vector", tm=P.val("tm"))
                relu_ticks.append(v_op(lambda v, cc=cc, ich=ich, psu=psu: v.tensor_scalar_max(
                    out=hbf[:, ich, :], in0=psu[:, :], scalar1=0.0)))
            vRelu = P.val("vm")
            if cc == 0:
                wait("tensor", "woq", woT)
            waits("tensor", vm=vRelu)
            for csl in range(4):
                ob = csl % 2
                for hch in range(2):
                    psu = ps2a if hch == 0 else ps2b
                    for ich in range(32):
                        t_op(lambda t, csl=csl, hch=hch, ich=ich, psu=psu: t.matmul(
                            psu[:, :], hbf[:, ich, csl * 128:(csl + 1) * 128],
                            wo_sb[:, ich, hch * 512:(hch + 1) * 512],
                            start=(ich == 0), stop=(ich == 31)),
                            tick=(ich == 31))
                tmm2 = P.val("tm")
                eos = "eo%d" % ob
                wait("vector", eos, P.val(eos))
                waits("vector", tm=tmm2)
                v_op(lambda v, ob=ob: v.tensor_copy(out=outsb[:, ob, 0:512], in_=ps2a[:, :]))
                v_op(lambda v, ob=ob: v.tensor_copy(out=outsb[:, ob, 512:1024], in_=ps2b[:, :]))
                vEv = P.val("vm")
                wait("sync", "vm", vEv)
                row0 = cc * 512 + csl * 128
                s_dma(eout[row0:row0 + 128, :], outsb[:, ob, :], sem=eos)
                waits("tensor", vm=vEv)

        wait("sync", "sq", P.val("sq"))
        for snm in ("eo0", "eo1"):
            wait("sync", snm, P.val(snm))
        wait("gpsimd", "ga0", P.val("ga0"))
        wait("gpsimd", "ga1", P.val("ga1"))
        wait("gpsimd", "gl", P.val("gl"))
        wait("gpsimd", "wiq", P.val("wiq"))
        wait("gpsimd", "woq", P.val("woq"))

        @block.sync
        def _(s):
            P.run("sync", s)

        @block.tensor
        def _(t):
            P.run("tensor", t)

        @block.vector
        def _(v):
            P.run("vector", v)

        @block.scalar
        def _(s):
            P.run("scalar", s)

        @block.gpsimd
        def _(g):
            P.run("gpsimd", g)

    return nc

def host_consts_l3(core_id):
    c = {}
    c["triu_s"] = np.triu(np.ones((128, 128), np.float32), k=1)
    c["triu8"] = np.triu(np.ones((8, 8), np.float32), k=1)
    c["ones128"] = np.ones((128, 128), np.float32)
    c["onesrow"] = np.ones((1, 128), np.float32)
    c["selb"] = np.eye(8, dtype=np.float32)[:, core_id:core_id + 1].copy()
    c["e4096p1"] = np.tile((np.arange(7, dtype=np.float32) * CAP + 1.0)[None, :], (128, 1))
    return c



def build_l3():
    nc = bass.Bass("TRN2", target_bir_lowering=False, debug=False)
    xpp = nc.dram_tensor("xpp", [NTOK, D], F32, kind="ExternalInput").ap()
    wb = nc.dram_tensor("wb", [NTOK, 8], F32, kind="ExternalInput").ap()
    eoutf = nc.dram_tensor("eoutf", [7 * CAP, D], F32, kind="ExternalInput").ap()
    countsf = nc.dram_tensor("countsf", [8, 7], F32, kind="ExternalInput").ap()
    gbrow = nc.dram_tensor("gbrow", [1, D], F32, kind="ExternalInput").ap()
    triu_s = nc.dram_tensor("triu_s", [128, 128], F32, kind="ExternalInput").ap()
    triu8 = nc.dram_tensor("triu8", [8, 8], F32, kind="ExternalInput").ap()
    ones128 = nc.dram_tensor("ones128", [128, 128], F32, kind="ExternalInput").ap()
    onesrow = nc.dram_tensor("onesrow", [1, 128], F32, kind="ExternalInput").ap()
    selb = nc.dram_tensor("selb", [8, 1], F32, kind="ExternalInput").ap()
    e4096p1 = nc.dram_tensor("e4096p1", [128, 7], F32, kind="ExternalInput").ap()
    out_o = nc.dram_tensor("out", [NTOK, D], F32, kind="ExternalOutput").ap()

    P = Plan()
    stack = ExitStack()
    with stack:
        ctriu = stack.enter_context(nc.sbuf_tensor("ctriu", [128, 128], F32))
        ctr8 = stack.enter_context(nc.sbuf_tensor("ctr8", [8, 8], F32))
        cones = stack.enter_context(nc.sbuf_tensor("cones", [128, 128], F32))
        conesr = stack.enter_context(nc.sbuf_tensor("conesr", [1, 128], F32))
        cselb = stack.enter_context(nc.sbuf_tensor("cselb", [8, 1], F32))
        ce4096 = stack.enter_context(nc.sbuf_tensor("ce4096", [128, 7], F32))
        cntf = stack.enter_context(nc.sbuf_tensor("cntf", [8, 7], F32))
        cumall = stack.enter_context(nc.sbuf_tensor("cumall", [8, 7], F32))
        baser = stack.enter_context(nc.sbuf_tensor("baser", [1, 7], F32))
        baset = stack.enter_context(nc.sbuf_tensor("baset", [128, 7], F32))
        gbr = stack.enter_context(nc.sbuf_tensor("gbr", [1, D], F32))
        gbt = stack.enter_context(nc.sbuf_tensor("gbt", [128, D], F32))
        wt = stack.enter_context(nc.sbuf_tensor("wt", [128, 2, 8], F32))
        xt = stack.enter_context(nc.sbuf_tensor("xt", [128, 2, D], F32))
        g1 = stack.enter_context(nc.sbuf_tensor("g1", [128, 2, D], F32))
        g2 = stack.enter_context(nc.sbuf_tensor("g2", [128, 2, D], F32))
        outt = stack.enter_context(nc.sbuf_tensor("outt", [128, 2, D], F32))
        masks = stack.enter_context(nc.sbuf_tensor("masks", [128, 7], F32))
        Psb = stack.enter_context(nc.sbuf_tensor("Psb", [128, 7], F32))
        offs = stack.enter_context(nc.sbuf_tensor("offs", [128, 7], F32))
        slotg = stack.enter_context(nc.sbuf_tensor("slotg", [128, 7], F32))
        kept = stack.enter_context(nc.sbuf_tensor("kept", [128, 7], F32))
        Am = stack.enter_context(nc.sbuf_tensor("Am", [128, 7], F32))
        sel = stack.enter_context(nc.sbuf_tensor("sel", [128, 7], F32))
        tmp7 = stack.enter_context(nc.sbuf_tensor("tmp7", [128, 7], F32))
        m1 = stack.enter_context(nc.sbuf_tensor("m1", [128, 1], F32))
        w1 = stack.enter_context(nc.sbuf_tensor("w1", [128, 1], F32))
        w2 = stack.enter_context(nc.sbuf_tensor("w2", [128, 1], F32))
        r1u = stack.enter_context(nc.sbuf_tensor("r1u", [128, 2, 1], U32))
        r2u = stack.enter_context(nc.sbuf_tensor("r2u", [128, 2, 1], U32))
        rf = stack.enter_context(nc.sbuf_tensor("rf", [128, 1], F32))
        bcd = stack.enter_context(nc.sbuf_tensor("bcd", [1, 256], BF16))
        psA = stack.enter_context(nc.psum_tensor("psA", [128, 512], F32))
        psOf = stack.enter_context(nc.psum_tensor("psOf", [128, 7], F32))
        SQ = stack.enter_context(nc.semaphore("sq"))
        WQ0 = stack.enter_context(nc.semaphore("wq0"))
        WQ1 = stack.enter_context(nc.semaphore("wq1"))
        XQ0 = stack.enter_context(nc.semaphore("xq0"))
        XQ1 = stack.enter_context(nc.semaphore("xq1"))
        GA0 = stack.enter_context(nc.semaphore("ga0"))
        GA1 = stack.enter_context(nc.semaphore("ga1"))
        OQ0 = stack.enter_context(nc.semaphore("oq0"))
        OQ1 = stack.enter_context(nc.semaphore("oq1"))
        TM = stack.enter_context(nc.semaphore("tm"))
        VM = stack.enter_context(nc.semaphore("vm"))
        block = stack.enter_context(nc.Block())

        SEMS = {"sq": SQ, "wq0": WQ0, "wq1": WQ1, "xq0": XQ0, "xq1": XQ1,
                "ga0": GA0, "ga1": GA1, "oq0": OQ0, "oq1": OQ1, "tm": TM, "vm": VM}

        def s_dma(out, in_, sem="sq"):
            S = SEMS[sem]
            def f(s, out=out, in_=in_, S=S):
                s.dma_start(out=out, in_=in_).then_inc(S, 16)
            P.on("sync", f)
            return P.inc(sem, 16)

        def _eng_op(eng, sem_name, fn, tick):
            prev = P.val(sem_name)
            S = SEMS[sem_name]
            if prev > 0:
                def w(e, S=S, prev=prev):
                    e.wait_ge(S, prev)
                P.on(eng, w)
            if tick:
                n = P.inc(sem_name, 1)
                def f(e, fn=fn, S=S):
                    fn(e).then_inc(S, 1)
            else:
                n = prev
                def f(e, fn=fn):
                    fn(e)
            P.on(eng, f)
            return n

        def t_op(fn, tick=True):
            return _eng_op("tensor", "tm", fn, tick)

        def v_op(fn, tick=True):
            return _eng_op("vector", "vm", fn, tick)

        def wait(eng_name, sem_name, val):
            if val <= 0:
                return
            sem = SEMS[sem_name]
            def f(e, sem=sem, val=val):
                e.wait_ge(sem, val)
            P.on(eng_name, f)

        def waits(eng_name, **kw):
            for k, v in kw.items():
                wait(eng_name, k, v)

        for dst, src in [(ctriu, triu_s), (ctr8, triu8), (cones, ones128),
                         (conesr, onesrow), (cselb, selb), (ce4096, e4096p1),
                         (cntf, countsf), (gbr, gbrow)]:
            sl = tuple(slice(None) for _ in range(len(dst.shape)))
            s_dma(dst[sl], src[sl])
        c0 = P.val("sq")
        wait("sync", "sq", c0)
        waits("vector", sq=c0)
        v_op(lambda v: v.tensor_copy(out=bcd[0:1, :], in_=conesr[0:1, :].bitcast(BF16)))
        waits("tensor", sq=c0)
        t_op(lambda t: t.matmul(psA[0:8, 0:7], ctr8[:, :], cntf[:, :], start=True, stop=True))
        waits("vector", tm=P.val("tm"))
        v_op(lambda v: v.tensor_copy(out=cumall[:, :], in_=psA[0:8, 0:7]))
        waits("tensor", vm=P.val("vm"))
        t_op(lambda t: t.matmul(psA[0:1, 0:7], cselb[:, :], cumall[:, :], start=True, stop=True))
        waits("vector", tm=P.val("tm"))
        v_op(lambda v: v.tensor_copy(out=baser[:, :], in_=psA[0:1, 0:7]))
        waits("tensor", vm=P.val("vm"))
        t_op(lambda t: t.matmul(psA[:, 0:7], conesr[:, :], baser[:, :], start=True, stop=True))
        waits("vector", tm=P.val("tm"))
        v_op(lambda v: v.tensor_copy(out=baset[:, :], in_=psA[:, 0:7]))
        for half in range(2):
            waits("tensor", vm=P.val("vm"))
            t_op(lambda t, half=half: t.matmul(psA[:, 0:512], conesr[:, :],
                                               gbr[0:1, half * 512:half * 512 + 512],
                                               start=True, stop=True))
            waits("vector", tm=P.val("tm"))
            v_op(lambda v, half=half: v.tensor_copy(out=gbt[:, half * 512:half * 512 + 512], in_=psA[:, 0:512]))
        vPrep = P.val("vm")

        wqc = [0, 0]
        xqc = [0, 0]
        gac = [0, 0]
        oqc = [0, 0]
        vOutTicks = []
        psOfTick = [0]
        for j in range(NT):
            buf = j % 2
            if j >= 2:
                wait("sync", "vm", vOutTicks[j - 2])
            ldw = s_dma(wt[:, buf, :], wb[j * 128:(j + 1) * 128, :], sem="wq%d" % buf)
            wqc[buf] += 1
            ldx = s_dma(xt[:, buf, :], xpp[j * 128:(j + 1) * 128, :], sem="xq%d" % buf)
            xqc[buf] += 1
            waits("vector", **{("wq%d" % buf): 16 * wqc[buf]})
            wait("vector", "tm", psOfTick[0])
            v_op(lambda v, buf=buf: v.tensor_scalar(out=masks[:, :], in0=wt[:, buf, 0:7], scalar1=0.0, scalar2=None, op0=ALU.is_gt))
            waits("tensor", vm=P.val("vm"))
            t_op(lambda t: t.matmul(psA[:, 32:39], ctriu[:, :], masks[:, :], start=True, stop=True, skip_group_check=True))
            waits("vector", tm=P.val("tm"))
            v_op(lambda v: v.tensor_copy(out=Psb[:, :], in_=psA[:, 32:39]))
            if j == 0:
                v_op(lambda v: v.memset(offs[:, :], 0.0))
            else:
                v_op(lambda v: v.tensor_copy(out=offs[:, :], in_=psOf[:, :]))
            waits("tensor", vm=P.val("vm"))
            psOfTick[0] = t_op(lambda t, j=j: t.matmul(psOf[:, :], cones[:, :], masks[:, :],
                                         start=(j == 0), stop=(j == NT - 1), skip_group_check=True))
            v_op(lambda v: v.tensor_add(out=slotg[:, :], in0=baset[:, :], in1=offs[:, :]))
            v_op(lambda v: v.tensor_add(out=slotg[:, :], in0=slotg[:, :], in1=Psb[:, :]))
            v_op(lambda v: v.tensor_scalar(out=kept[:, :], in0=slotg[:, :], scalar1=float(CAP), scalar2=None, op0=ALU.is_lt))
            v_op(lambda v: v.tensor_mul(out=kept[:, :], in0=kept[:, :], in1=masks[:, :]))
            v_op(lambda v: v.tensor_add(out=Am[:, :], in0=slotg[:, :], in1=ce4096[:, :]))
            v_op(lambda v: v.tensor_mul(out=Am[:, :], in0=Am[:, :], in1=kept[:, :]))
            v_op(lambda v: v.reduce_max(m1[:, :], Am[:, :], axis=AX.X))
            v_op(lambda v: v.tensor_scalar(out=sel[:, :], in0=Am[:, :], scalar1=m1[:, 0:1], scalar2=None, op0=ALU.is_equal))
            v_op(lambda v: v.tensor_mul(out=sel[:, :], in0=sel[:, :], in1=kept[:, :]))
            v_op(lambda v, buf=buf: v.tensor_mul(out=tmp7[:, :], in0=sel[:, :], in1=wt[:, buf, 0:7]))
            v_op(lambda v: v.reduce_sum(w1[:, :], tmp7[:, :], axis=AX.X))
            v_op(lambda v: v.tensor_scalar(out=rf[:, :], in0=m1[:, :], scalar1=-1.0, scalar2=0.0, op0=ALU.add, op1=ALU.max))
            if j >= 2:
                wait("vector", "ga%d" % buf, 32 * gac[buf])
            v_op(lambda v, buf=buf: v.tensor_copy(out=r1u[:, buf, :], in_=rf[:, :]))
            v_op(lambda v: v.tensor_scalar(out=tmp7[:, :], in0=sel[:, :], scalar1=-1.0, scalar2=1.0, op0=ALU.mult, op1=ALU.add))
            v_op(lambda v: v.tensor_mul(out=kept[:, :], in0=kept[:, :], in1=tmp7[:, :]))
            v_op(lambda v: v.tensor_mul(out=Am[:, :], in0=Am[:, :], in1=tmp7[:, :]))
            v_op(lambda v: v.reduce_max(m1[:, :], Am[:, :], axis=AX.X))
            v_op(lambda v: v.tensor_scalar(out=sel[:, :], in0=Am[:, :], scalar1=m1[:, 0:1], scalar2=None, op0=ALU.is_equal))
            v_op(lambda v: v.tensor_mul(out=sel[:, :], in0=sel[:, :], in1=kept[:, :]))
            v_op(lambda v, buf=buf: v.tensor_mul(out=tmp7[:, :], in0=sel[:, :], in1=wt[:, buf, 0:7]))
            v_op(lambda v: v.reduce_sum(w2[:, :], tmp7[:, :], axis=AX.X))
            v_op(lambda v: v.tensor_scalar(out=rf[:, :], in0=m1[:, :], scalar1=-1.0, scalar2=0.0, op0=ALU.add, op1=ALU.max))
            v_op(lambda v, buf=buf: v.tensor_copy(out=r2u[:, buf, :], in_=rf[:, :]))
            vIdx = P.val("vm")
            wait("gpsimd", "vm", vIdx)
            ges = "ga%d" % buf
            def gat1(g, buf=buf):
                g.indirect_dma_start(out=g1[:, buf, :], out_offset=None, in_=eoutf[:, :],
                                     in_offset=IndirectOffsetOnAxis(r1u[:, buf, :], 0)).then_inc(SEMS["ga%d" % buf], 16)
            P.on("gpsimd", gat1)
            P.inc(ges, 16)
            def gat2(g, buf=buf):
                g.indirect_dma_start(out=g2[:, buf, :], out_offset=None, in_=eoutf[:, :],
                                     in_offset=IndirectOffsetOnAxis(r2u[:, buf, :], 0)).then_inc(SEMS["ga%d" % buf], 16)
            P.on("gpsimd", gat2)
            P.inc(ges, 16)
            gac[buf] += 1
            waits("vector", **{ges: P.val(ges), ("xq%d" % buf): 16 * xqc[buf]})
            if j >= 2:
                wait("vector", "oq%d" % buf, 16 * oqc[buf])
            v_op(lambda v, buf=buf: v.tensor_scalar_mul(out=outt[:, buf, :], in0=g1[:, buf, :], scalar1=w1[:, 0:1]))
            v_op(lambda v, buf=buf: v.scalar_tensor_tensor(out=outt[:, buf, :], in0=g2[:, buf, :], scalar=w2[:, 0:1], in1=outt[:, buf, :], op0=ALU.mult, op1=ALU.add))
            v_op(lambda v, buf=buf: v.tensor_mul(out=outt[:, buf, :], in0=outt[:, buf, :], in1=gbt[:, :]))
            v_op(lambda v, buf=buf: v.tensor_add(out=outt[:, buf, :], in0=outt[:, buf, :], in1=xt[:, buf, :]))
            vOut = P.val("vm")
            vOutTicks.append(vOut)
            wait("sync", "vm", vOut)
            s_dma(out_o[j * 128:(j + 1) * 128, :], outt[:, buf, :], sem="oq%d" % buf)
            oqc[buf] += 1

        wait("sync", "oq0", P.val("oq0"))
        wait("sync", "oq1", P.val("oq1"))
        wait("sync", "sq", P.val("sq"))
        wait("gpsimd", "ga0", P.val("ga0"))
        wait("gpsimd", "ga1", P.val("ga1"))

        @block.sync
        def _(s):
            P.run("sync", s)

        @block.tensor
        def _(t):
            P.run("tensor", t)

        @block.vector
        def _(v):
            P.run("vector", v)

        @block.scalar
        def _(s):
            P.run("scalar", s)

        @block.gpsimd
        def _(g):
            P.run("gpsimd", g)

    return nc



_CACHE = {}


def _get_programs():
    if "nc1" not in _CACHE:
        _CACHE["nc1"] = build_l1()
        _CACHE["nc2"] = build_l2()
        _CACHE["nc3"] = build_l3()
        _CACHE["c1"] = [host_consts_l1(b) for b in range(8)]
        _CACHE["c2"] = [host_consts_l2(e) for e in range(8)]
        _CACHE["c3"] = [host_consts_l3(b) for b in range(8)]
    return _CACHE


def kernel(L0_attn_Z, L0_attn_decay, L0_attn_W, L0_attn_b, L0_attn_gamma,
           L0_attn_beta, L0_ffn_router_w, L0_ffn_router_b, L0_ffn_experts_inter,
           L0_ffn_experts_out, L0_c, L0_wmod):
    C = _get_programs()
    x = np.ascontiguousarray(np.asarray(L0_attn_Z, dtype=np.float32))
    W = np.ascontiguousarray(np.asarray(L0_attn_W, dtype=np.float32))
    battn = np.ascontiguousarray(np.asarray(L0_attn_b, dtype=np.float32)).reshape(1, D)
    gamma = np.ascontiguousarray(np.asarray(L0_attn_gamma, dtype=np.float32)).reshape(1, D)
    beta = np.ascontiguousarray(np.asarray(L0_attn_beta, dtype=np.float32)).reshape(1, D)
    decay = np.ascontiguousarray(np.asarray(L0_attn_decay, dtype=np.float32)).reshape(1, 1)
    rw = np.ascontiguousarray(np.asarray(L0_ffn_router_w, dtype=np.float32))
    rb = np.ascontiguousarray(np.asarray(L0_ffn_router_b, dtype=np.float32)).reshape(1, 8)
    wi_all = np.asarray(L0_ffn_experts_inter, dtype=np.float32)
    wo_all = np.asarray(L0_ffn_experts_out, dtype=np.float32)
    cfull = np.ascontiguousarray(np.asarray(L0_c, dtype=np.float32))
    wmod = np.ascontiguousarray(np.asarray(L0_wmod, dtype=np.float32))

    in1 = []
    for b in range(8):
        m = dict(C["c1"][b])
        m["x_in"] = np.ascontiguousarray(x[b])
        m["W_in"] = W
        m["battn"] = battn
        m["gamma"] = gamma
        m["beta"] = beta
        m["decay"] = decay
        m["rw"] = rw
        m["rb"] = rb
        m["cfull"] = cfull
        m["wmod"] = wmod
        in1.append(m)
    r1 = run_bass_kernel_spmd(C["nc1"], in1, core_ids=list(range(8))).results

    h2_full = np.concatenate([np.asarray(r1[b]["h2b"]) for b in range(8)], axis=0)
    h2a = np.ascontiguousarray(h2_full[:, 0:512])
    h2b_ = np.ascontiguousarray(h2_full[:, 512:1024])
    lists_all = np.stack([np.asarray(r1[b]["lists"]).reshape(7, LSTRIDE) for b in range(8)])
    counts_all = np.stack([np.asarray(r1[b]["counts"]).reshape(7) for b in range(8)])

    in2 = []
    for k in range(8):
        e = k if k < 7 else 0
        m = dict(C["c2"][k])
        m["h2fa"] = h2a
        m["h2fb"] = h2b_
        m["mylists"] = np.ascontiguousarray(lists_all[:, e, :])
        m["counts_e"] = np.ascontiguousarray(counts_all[:, e:e + 1])
        m["wi"] = np.ascontiguousarray(wi_all[e])
        m["wo"] = np.ascontiguousarray(wo_all[e])
        in2.append(m)
    r2 = run_bass_kernel_spmd(C["nc2"], in2, core_ids=list(range(8))).results
    eoutf = np.concatenate([np.asarray(r2[k]["eout"]) for k in range(7)], axis=0)

    in3 = []
    for b in range(8):
        m = dict(C["c3"][b])
        m["xpp"] = np.asarray(r1[b]["xpp"])
        m["wb"] = np.asarray(r1[b]["wout"])
        m["eoutf"] = eoutf
        m["countsf"] = np.ascontiguousarray(counts_all)
        m["gbrow"] = np.asarray(r1[b]["gbrow"])
        in3.append(m)
    r3 = run_bass_kernel_spmd(C["nc3"], in3, core_ids=list(range(8))).results
    out = np.stack([np.asarray(r3[b]["out"]) for b in range(8)], axis=0)
    return out.astype(np.float32, copy=False)


# version 10
# speedup vs baseline: 1.5118x; 1.1441x over previous
import dataclasses
import numpy as np
import ml_dtypes
import concourse.bass as bass
import concourse.mybir as mybir
from concourse.bass_core_rust import IndirectOffsetOnAxis
from concourse.bass_utils import run_bass_kernel_spmd
from contextlib import ExitStack

F32 = mybir.dt.float32
BF16 = mybir.dt.bfloat16
U32 = mybir.dt.uint32
ALU = mybir.AluOpType
AF = mybir.ActivationFunctionType
AX = mybir.AxisListType

NTOK = 4096
D = 1024
NT = NTOK // 128
NDC = D // 128
LSTRIDE = 4352
EPS = 1e-5
NTOKG = 32768
I = 4096
CAP = 4096
NCC = CAP // 512
GPAD = 8448


def bcast_free(ap, n):
    return dataclasses.replace(ap, ap=type(ap.ap)([list(ap.ap[0]), [0, n]]))


class Plan:
    def __init__(self):
        self.items = []
        self.ctr = {}

    def on(self, eng, fn):
        self.items.append((eng, fn))

    def inc(self, name, k):
        self.ctr[name] = self.ctr.get(name, 0) + k
        return self.ctr[name]

    def val(self, name):
        return self.ctr.get(name, 0)

    def run(self, eng_name, eng):
        for e, fn in self.items:
            if e == eng_name:
                fn(eng)


def host_consts_l1(core_id):
    import ml_dtypes
    c = {}
    c["ident_f32"] = np.eye(128, dtype=np.float32)
    c["ident_bf16"] = np.eye(128, dtype=np.float32).astype(ml_dtypes.bfloat16)
    c["triu_s"] = np.triu(np.ones((128, 128), np.float32), k=1)
    c["iota_row"] = np.tile(np.arange(128, dtype=np.float32)[None, :], (128, 1))
    c["iotaP"] = np.arange(128, dtype=np.float32).reshape(128, 1)
    c["gidbase"] = np.full((128, 1), core_id * NTOK, np.float32)
    c["e4352"] = (np.arange(7, dtype=np.float32) * LSTRIDE).reshape(7, 1)
    c["onesrow"] = np.ones((1, 128), np.float32)
    c["onescol"] = np.ones((128, 1), np.float32)
    c["selb"] = np.eye(8, dtype=np.float32)[:, core_id:core_id + 1].copy()
    return c


def build_l1():
    nc = bass.Bass("TRN2", target_bir_lowering=False, debug=False)
    x_in = nc.dram_tensor("x_in", [NTOK, D], F32, kind="ExternalInput").ap()
    W_in = nc.dram_tensor("W_in", [D, D], F32, kind="ExternalInput").ap()
    battn = nc.dram_tensor("battn", [1, D], F32, kind="ExternalInput").ap()
    gamma = nc.dram_tensor("gamma", [1, D], F32, kind="ExternalInput").ap()
    beta = nc.dram_tensor("beta", [1, D], F32, kind="ExternalInput").ap()
    decay = nc.dram_tensor("decay", [1, 1], F32, kind="ExternalInput").ap()
    rw = nc.dram_tensor("rw", [8, D], F32, kind="ExternalInput").ap()
    rb = nc.dram_tensor("rb", [1, 8], F32, kind="ExternalInput").ap()
    cfull = nc.dram_tensor("cfull", [8, D], F32, kind="ExternalInput").ap()
    wmod = nc.dram_tensor("wmod", [D, 6 * D], F32, kind="ExternalInput").ap()
    ident_f32 = nc.dram_tensor("ident_f32", [128, 128], F32, kind="ExternalInput").ap()
    ident_bf16 = nc.dram_tensor("ident_bf16", [128, 128], BF16, kind="ExternalInput").ap()
    triu_s = nc.dram_tensor("triu_s", [128, 128], F32, kind="ExternalInput").ap()
    iota_row = nc.dram_tensor("iota_row", [128, 128], F32, kind="ExternalInput").ap()
    iotaP = nc.dram_tensor("iotaP", [128, 1], F32, kind="ExternalInput").ap()
    gidbase = nc.dram_tensor("gidbase", [128, 1], F32, kind="ExternalInput").ap()
    e4352 = nc.dram_tensor("e4352", [7, 1], F32, kind="ExternalInput").ap()
    onesrow = nc.dram_tensor("onesrow", [1, 128], F32, kind="ExternalInput").ap()
    onescol = nc.dram_tensor("onescol", [128, 1], F32, kind="ExternalInput").ap()
    selb = nc.dram_tensor("selb", [8, 1], F32, kind="ExternalInput").ap()
    xpp_o = nc.dram_tensor("xpp", [NTOK, D], F32, kind="ExternalOutput").ap()
    h2b_o = nc.dram_tensor("h2b", [NTOK, D], F32, kind="ExternalOutput").ap()
    w_o = nc.dram_tensor("wout", [NTOK, 8], F32, kind="ExternalOutput").ap()
    lists_o = nc.dram_tensor("lists", [7 * LSTRIDE, 1], F32, kind="ExternalOutput").ap()
    counts_o = nc.dram_tensor("counts", [7, 1], F32, kind="ExternalOutput").ap()
    gbrow_o = nc.dram_tensor("gbrow", [1, D], F32, kind="ExternalOutput").ap()
    ycs_o = nc.dram_tensor("ycs", [NTOK, D], BF16, kind="ExternalOutput").ap()
    gbh2_o = nc.dram_tensor("gbh2", [NTOK, D], F32, kind="ExternalOutput").ap()

    P = Plan()
    stack = ExitStack()
    with stack:
        idf = stack.enter_context(nc.sbuf_tensor("cid_f", [128, 128], F32))
        idb = stack.enter_context(nc.sbuf_tensor("cid_b", [128, 128], BF16))
        ctriu = stack.enter_context(nc.sbuf_tensor("ctriu", [128, 128], F32))
        ciota = stack.enter_context(nc.sbuf_tensor("ciota", [128, 128], F32))
        ciop = stack.enter_context(nc.sbuf_tensor("ciop", [128, 1], F32))
        cgidb = stack.enter_context(nc.sbuf_tensor("cgidb", [128, 1], F32))
        ce4352 = stack.enter_context(nc.sbuf_tensor("ce4352", [7, 1], F32))
        conesr = stack.enter_context(nc.sbuf_tensor("conesr", [1, 128], F32))
        conesc = stack.enter_context(nc.sbuf_tensor("conesc", [128, 1], F32))
        cselb = stack.enter_context(nc.sbuf_tensor("cselb", [8, 1], F32))
        xnT = stack.enter_context(nc.sbuf_tensor("xnT", [128, NDC, NTOK], BF16))
        WTb = stack.enter_context(nc.sbuf_tensor("WTb", [128, NDC, D], BF16))
        Gt = stack.enter_context(nc.sbuf_tensor("Gt", [128, D], F32))
        Bt = stack.enter_context(nc.sbuf_tensor("Bt", [128, D], F32))
        cb1t = stack.enter_context(nc.sbuf_tensor("cb1t", [128, D], F32))
        sbt = stack.enter_context(nc.sbuf_tensor("sbt", [128, D], F32))
        gbt = stack.enter_context(nc.sbuf_tensor("gbt", [128, D], F32))
        rbt = stack.enter_context(nc.sbuf_tensor("rbt", [128, 8], F32))
        rwTb = stack.enter_context(nc.sbuf_tensor("rwTb", [128, NDC, 8], BF16))
        g2 = stack.enter_context(nc.sbuf_tensor("g2", [2, NTOK], BF16))
        vb2 = stack.enter_context(nc.sbuf_tensor("vb2", [2, D], BF16))
        stat_mu = stack.enter_context(nc.sbuf_tensor("stat_mu", [128, NT], F32))
        stat_rs = stack.enter_context(nc.sbuf_tensor("stat_rs", [128, NT], F32))
        stat_s1 = stack.enter_context(nc.sbuf_tensor("stat_s1", [128, NT], F32))
        stat_sq = stack.enter_context(nc.sbuf_tensor("stat_sq", [128, NT], F32))
        gid0 = stack.enter_context(nc.sbuf_tensor("gid0", [128, 1], F32))
        acol = stack.enter_context(nc.sbuf_tensor("acol", [128, 1], F32))
        asc = stack.enter_context(nc.sbuf_tensor("asc", [1, 4], F32))
        psY0 = stack.enter_context(nc.psum_tensor("psY0", [128, 512], F32))
        psY1 = stack.enter_context(nc.psum_tensor("psY1", [128, 512], F32))
        psY2 = stack.enter_context(nc.psum_tensor("psY2", [128, 512], F32))
        psY3 = stack.enter_context(nc.psum_tensor("psY3", [128, 512], F32))
        psTB0 = stack.enter_context(nc.psum_tensor("psTB0", [128, 128], BF16))
        psTB1 = stack.enter_context(nc.psum_tensor("psTB1", [128, 128], BF16))
        psOf = stack.enter_context(nc.psum_tensor("psOf", [7, 1], F32))
        psRT = stack.enter_context(nc.psum_tensor("psRT", [128, 512], F32))
        SQ = stack.enter_context(nc.semaphore("sq"))
        GQ = stack.enter_context(nc.semaphore("gq"))
        SC = stack.enter_context(nc.semaphore("sc"))
        TM = stack.enter_context(nc.semaphore("tm"))
        VM = stack.enter_context(nc.semaphore("vm"))
        SM = stack.enter_context(nc.semaphore("sm"))
        XQ0 = stack.enter_context(nc.semaphore("xq0"))
        XQ1 = stack.enter_context(nc.semaphore("xq1"))
        WQ0 = stack.enter_context(nc.semaphore("wq0"))
        WQ1 = stack.enter_context(nc.semaphore("wq1"))
        SOX0 = stack.enter_context(nc.semaphore("sox0"))
        SOX1 = stack.enter_context(nc.semaphore("sox1"))
        SOW0 = stack.enter_context(nc.semaphore("sow0"))
        SOW1 = stack.enter_context(nc.semaphore("sow1"))
        SOH0 = stack.enter_context(nc.semaphore("soh0"))
        SOH1 = stack.enter_context(nc.semaphore("soh1"))
        LZ = stack.enter_context(nc.semaphore("lz"))
        WSB = stack.enter_context(nc.semaphore("wsb"))
        WM0 = stack.enter_context(nc.semaphore("wm0"))
        WM1 = stack.enter_context(nc.semaphore("wm1"))
        WM2 = stack.enter_context(nc.semaphore("wm2"))
        WM3 = stack.enter_context(nc.semaphore("wm3"))
        block = stack.enter_context(nc.Block())

        SEMS = {"sq": SQ, "gq": GQ, "sc": SC, "tm": TM, "vm": VM, "sm": SM,
                "xq0": XQ0, "xq1": XQ1, "wq0": WQ0, "wq1": WQ1,
                "sox0": SOX0, "sox1": SOX1, "sow0": SOW0, "sow1": SOW1,
                "soh0": SOH0, "soh1": SOH1, "lz": LZ, "wsb": WSB,
                "wm0": WM0, "wm1": WM1, "wm2": WM2, "wm3": WM3}

        def s_dma(out, in_, sem="sq"):
            S = SEMS[sem]
            def f(s, out=out, in_=in_, S=S):
                s.dma_start(out=out, in_=in_).then_inc(S, 16)
            P.on("sync", f)
            return P.inc(sem, 16)

        def s_barrier(sem="sq"):
            wait("sync", sem, P.val(sem))
            return P.val(sem)

        def g_dma(out, in_, sem=None):
            S = GQ if sem is None else SEMS[sem]
            nm = "gq" if sem is None else sem
            def f(g, out=out, in_=in_, S=S):
                g.dma_start(out=out, in_=in_).then_inc(S, 16)
            P.on("gpsimd", f)
            return P.inc(nm, 16)

        def _eng_op(eng, sem_name, fn, tick):
            prev = P.val(sem_name)
            S = SEMS[sem_name]
            if prev > 0:
                def w(e, S=S, prev=prev):
                    e.wait_ge(S, prev)
                P.on(eng, w)
            if tick:
                n = P.inc(sem_name, 1)
                def f(e, fn=fn, S=S):
                    fn(e).then_inc(S, 1)
            else:
                n = prev
                def f(e, fn=fn):
                    fn(e)
            P.on(eng, f)
            return n

        def t_op(fn, tick=True):
            return _eng_op("tensor", "tm", fn, tick)

        def v_op(fn, tick=True):
            return _eng_op("vector", "vm", fn, tick)

        def sc_op(fn, tick=True):
            return _eng_op("scalar", "sm", fn, tick)

        def wait(eng_name, sem_name, val):
            if val <= 0:
                return
            sem = SEMS[sem_name]
            def f(e, sem=sem, val=val):
                e.wait_ge(sem, val)
            P.on(eng_name, f)

        def waits(eng_name, **kw):
            for k, v in kw.items():
                wait(eng_name, k, v)

        prep = ExitStack()
        msa = prep.enter_context(nc.sbuf_tensor("msa", [1, D], F32))
        mca = prep.enter_context(nc.sbuf_tensor("mca", [1, D], F32))
        mga = prep.enter_context(nc.sbuf_tensor("mga", [1, D], F32))
        msb = prep.enter_context(nc.sbuf_tensor("msb", [1, D], F32))
        mcb = prep.enter_context(nc.sbuf_tensor("mcb", [1, D], F32))
        mgb = prep.enter_context(nc.sbuf_tensor("mgb", [1, D], F32))
        r0 = prep.enter_context(nc.sbuf_tensor("r0", [1, D], F32))
        r1 = prep.enter_context(nc.sbuf_tensor("r1", [1, D], F32))
        modst = prep.enter_context(nc.sbuf_tensor("modst", [8, 512], F32))
        cstage = prep.enter_context(nc.sbuf_tensor("cstage", [8, D], F32))
        scT = prep.enter_context(nc.sbuf_tensor("scT", [128, 8, 8], BF16))
        cstb = prep.enter_context(nc.sbuf_tensor("cstb", [8, D], BF16))
        wmt = prep.enter_context(nc.sbuf_tensor("wmt", [128, 4, 512], BF16))
        Wsb = prep.enter_context(nc.sbuf_tensor("Wsb", [128, NDC, D], F32))
        wtf2 = prep.enter_context(nc.sbuf_tensor("wtf2", [128, 2, D], F32))
        safm = prep.enter_context(nc.sbuf_tensor("safm", [128, 8], F32))
        ca1fm = prep.enter_context(nc.sbuf_tensor("ca1fm", [128, 8], F32))
        wbar = prep.enter_context(nc.sbuf_tensor("wbar", [128, 8], F32))
        vrow = prep.enter_context(nc.sbuf_tensor("vrow", [1, D], F32))
        bcrow = prep.enter_context(nc.sbuf_tensor("bcrow", [1, D], F32))
        smallsc = prep.enter_context(nc.sbuf_tensor("smallsc", [8, 8], F32))

        for dst, src in [(idf, ident_f32), (idb, ident_bf16), (ctriu, triu_s),
                         (ciota, iota_row), (ciop, iotaP), (cgidb, gidbase),
                         (ce4352, e4352), (conesr, onesrow), (conesc, onescol),
                         (cselb, selb)]:
            sl = tuple(slice(None) for _ in range(len(dst.shape)))
            s_dma(dst[sl], src[sl])
        s_dma(cstage[0:8, :], cfull[:, :])
        s_dma(r0[:, :], gamma[:, :])
        s_dma(r1[:, :], beta[:, :])
        s_dma(asc[0:1, 0:1], decay[:, :])
        s_dma(smallsc[0:1, 0:8], rb[:, :])
        c0 = s_barrier()
        for pc in range(NDC):
            ldW = s_dma(Wsb[:, pc, :], W_in[pc * 128:(pc + 1) * 128, :], sem="wsb")

        waits("scalar", sq=c0)
        sc_op(lambda s: s.activation(out=cstage[0:8, :], in_=cstage[0:8, :], func=AF.Silu))
        tA = P.val("sm")
        waits("vector", sm=tA)
        v_op(lambda v: v.tensor_copy(out=cstb[0:8, :], in_=cstage[0:8, :]))
        waits("tensor", vm=P.val("vm"))
        scTev = {}
        for dc in range(NDC):
            pst = psTB0 if dc % 2 == 0 else psTB1
            if dc >= 2:
                wait("tensor", "vm", scTev[dc - 2])
            t_op(lambda t, dc=dc, pst=pst: t.transpose(pst[0:128, 0:8], cstb[0:8, dc * 128:(dc + 1) * 128], idb[0:8, 0:8]))
            waits("vector", tm=P.val("tm"))
            scTev[dc] = v_op(lambda v, dc=dc, pst=pst: v.tensor_copy(out=scT[:, dc, :], in_=pst[0:128, 0:8]))
        vSct = P.val("vm")
        waits("tensor", vm=vSct)
        rows6 = [msa, mca, mga, msb, mcb, mgb]
        mmticks = []
        wmsems = ["wm0", "wm1", "wm2", "wm3"]
        for nch in range(12):
            waits("tensor", vm=P.val("vm"))
            for dc in range(NDC):
                it = nch * NDC + dc
                buf = it % 4
                if it >= 4:
                    wait("gpsimd", "tm", mmticks[it - 4])
                ldw = g_dma(wmt[:, buf, :], wmod[dc * 128:(dc + 1) * 128, nch * 512:(nch + 1) * 512], sem=wmsems[buf])
                waits("tensor", **{wmsems[buf]: ldw})
                mmticks.append(t_op(lambda t, dc=dc, buf=buf: t.matmul(
                    psY0[0:8, :], scT[:, dc, :], wmt[:, buf, :],
                    start=(dc == 0), stop=(dc == NDC - 1))))
            waits("vector", tm=P.val("tm"))
            v_op(lambda v: v.tensor_copy(out=modst[0:8, :], in_=psY0[0:8, :]))
            waits("tensor", vm=P.val("vm"))
            t_op(lambda t: t.matmul(psRT[0:1, 0:512], cselb[:, :], modst[0:8, :], start=True, stop=True))
            waits("vector", tm=P.val("tm"))
            i, half = nch // 2, nch % 2
            v_op(lambda v, i=i, half=half: v.tensor_copy(
                out=rows6[i][0:1, half * 512:half * 512 + 512], in_=psRT[0:1, 0:512]))
        for gv in (mga, mgb):
            v_op(lambda v, gv=gv: v.tensor_scalar(out=bcrow[:, :], in0=gv[0:1, :], scalar1=0.0, scalar2=None, op0=ALU.is_ge))
            v_op(lambda v: v.tensor_scalar(out=bcrow[:, :], in0=bcrow[:, :], scalar1=2.0, scalar2=-1.0, op0=ALU.mult, op1=ALU.add))
            waits("scalar", vm=P.val("vm"))
            sc_op(lambda s, gv=gv: s.activation(out=vrow[:, :], in_=gv[0:1, :], func=AF.Abs))
            waits("vector", sm=P.val("sm"))
            v_op(lambda v: v.tensor_scalar_max(out=vrow[:, :], in0=vrow[:, :], scalar1=0.25))
            v_op(lambda v, gv=gv: v.tensor_mul(out=gv[0:1, :], in0=bcrow[:, :], in1=vrow[:, :]))
        v_op(lambda v: v.tensor_mul(out=r0[:, :], in0=r0[:, :], in1=mga[0:1, :]))
        v_op(lambda v: v.tensor_mul(out=r1[:, :], in0=r1[:, :], in1=mga[0:1, :]))
        v_op(lambda v: v.tensor_scalar_add(out=mcb[0:1, :], in0=mcb[0:1, :], scalar1=1.0))
        vClamp = P.val("vm")
        wait("sync", "vm", vClamp)
        s_dma(gbrow_o[:, :], mgb[0:1, :])
        waits("tensor", vm=vClamp)
        for row, tile in [(r0, Gt), (r1, Bt), (mcb, cb1t), (msb, sbt), (mgb, gbt)]:
            for half in range(2):
                waits("tensor", vm=P.val("vm"))
                t_op(lambda t, row=row, half=half: t.matmul(
                    psY1[:, :], conesr[:, :], row[0:1, half * 512:half * 512 + 512],
                    start=True, stop=True))
                waits("vector", tm=P.val("tm"))
                v_op(lambda v, tile=tile, half=half: v.tensor_copy(
                    out=tile[:, half * 512:half * 512 + 512], in_=psY1[:, :]))
        waits("tensor", vm=P.val("vm"))
        t_op(lambda t: t.matmul(psRT[:, 0:8], conesr[:, :], smallsc[0:1, 0:8], start=True, stop=True))
        waits("vector", tm=P.val("tm"))
        v_op(lambda v: v.tensor_copy(out=rbt[:, :], in_=psRT[:, 0:8]))

        waits("scalar", sq=c0)
        sc_op(lambda s: s.activation(out=asc[0:1, 1:2], in_=asc[0:1, 0:1], func=AF.Sigmoid))
        sA = P.val("sm")
        waits("vector", sm=sA)
        v_op(lambda v: v.memset(asc[0:1, 2:3], 1.0))
        v_op(lambda v: v.tensor_tensor_scan(
            g2[0:1, :], bcast_free(asc[0:1, 1:2], NTOK), bcast_free(asc[0:1, 2:3], NTOK),
            0.0, ALU.mult, ALU.add))
        vG = P.val("vm")
        wait("gpsimd", "vm", vG)
        for seg in range(NTOK // 128):
            g_dma(g2[1:2, seg * 128:(seg + 1) * 128], conesr[0:1, :])
        waits("tensor", vm=vG)
        waits("tensor", vm=P.val("vm"))
        t_op(lambda t: t.matmul(psRT[:, 8:9], conesr[:, :], asc[0:1, 1:2], start=True, stop=True))
        waits("vector", tm=P.val("tm"))
        v_op(lambda v: v.tensor_copy(out=acol[:, :], in_=psRT[:, 8:9]))
        v_op(lambda v: v.tensor_add(out=gid0[:, :], in0=ciop[:, :], in1=cgidb[:, :]))

        wait("sync", "vm", vSct)
        ldrw = s_dma(cstage[0:8, :], rw[:, :])
        s_barrier()
        waits("tensor", sq=ldrw, vm=vClamp)
        for dc in range(NDC):
            waits("tensor", vm=P.val("vm"))
            t_op(lambda t, dc=dc: t.transpose(psRT[0:128, 24:32], cstage[0:8, dc * 128:(dc + 1) * 128], idf[0:8, 0:8]))
            waits("vector", tm=P.val("tm"))
            v_op(lambda v, dc=dc: v.tensor_copy(out=rwTb[:, dc, :], in_=psRT[0:128, 24:32]))
            waits("tensor", vm=P.val("vm"))
            t_op(lambda t, dc=dc: t.transpose(psRT[0:128, 16:17], msa[0:1, dc * 128:(dc + 1) * 128], idf[0:1, 0:1]))
            waits("vector", tm=P.val("tm"))
            v_op(lambda v, dc=dc: v.tensor_copy(out=safm[:, dc:dc + 1], in_=psRT[0:128, 16:17]))
            waits("tensor", vm=P.val("vm"))
            t_op(lambda t, dc=dc: t.transpose(psRT[0:128, 17:18], mca[0:1, dc * 128:(dc + 1) * 128], idf[0:1, 0:1]))
            waits("vector", tm=P.val("tm"))
            v_op(lambda v, dc=dc: v.tensor_copy(out=ca1fm[:, dc:dc + 1], in_=psRT[0:128, 17:18]))
        v_op(lambda v: v.tensor_scalar_add(out=ca1fm[:, :], in0=ca1fm[:, :], scalar1=1.0))
        vFM = P.val("vm")
        waits("tensor", vm=vFM)
        evacS = {}
        vchain = {}
        for dcx in range(NDC + 1):
            if dcx < NDC:
                dc = dcx
                for pc in range(NDC):
                    ti = dc * 8 + pc
                    bank = psY2 if ti % 2 == 0 else psY3
                    if ti == 0:
                        waits("tensor", wsb=ldW)
                    if ti >= 2:
                        wait("tensor", "sm", evacS[ti - 2])
                    tt = t_op(lambda t, dc=dc, pc=pc, bank=bank: t.transpose(
                        bank[:, 0:128], Wsb[:, pc, dc * 128:(dc + 1) * 128], idf[:, :]))
                    if pc == 0 and dc >= 2:
                        wait("scalar", "vm", vchain[dc - 2])
                    wait("scalar", "tm", tt)
                    evacS[ti] = sc_op(lambda s, dc=dc, pc=pc, bank=bank: s.activation(
                        out=wtf2[:, dc % 2, pc * 128:(pc + 1) * 128], in_=bank[:, 0:128],
                        func=AF.Copy))
            if dcx >= 1:
                dcp = dcx - 1
                wtfv = wtf2[:, dcp % 2, :]
                wait("tensor", "sm", evacS[dcp * 8 + 7])
                t_op(lambda t, dcp=dcp, wtfv=wtfv: t.matmul(
                    psRT[0:1, 0:512], safm[:, dcp:dcp + 1], wtfv[:, 0:512],
                    start=(dcp == 0), stop=(dcp == NDC - 1), skip_group_check=True))
                t_op(lambda t, dcp=dcp, wtfv=wtfv: t.matmul(
                    psY1[0:1, 0:512], safm[:, dcp:dcp + 1], wtfv[:, 512:1024],
                    start=(dcp == 0), stop=(dcp == NDC - 1), skip_group_check=True))
                waits("vector", tm=P.val("tm"))
                v_op(lambda v, dcp=dcp, wtfv=wtfv: v.tensor_scalar_mul(
                    out=wtfv[:, :], in0=wtfv[:, :], scalar1=ca1fm[:, dcp:dcp + 1]))
                v_op(lambda v, dcp=dcp, wtfv=wtfv: v.reduce_sum(wbar[:, dcp:dcp + 1], wtfv[:, :], axis=AX.X))
                v_op(lambda v, dcp=dcp: v.tensor_scalar_mul(
                    out=wbar[:, dcp:dcp + 1], in0=wbar[:, dcp:dcp + 1], scalar1=1.0 / D))
                vchain[dcp] = v_op(lambda v, dcp=dcp, wtfv=wtfv: v.tensor_scalar(
                    out=WTb[:, dcp, :], in0=wtfv[:, :], scalar1=wbar[:, dcp:dcp + 1],
                    scalar2=None, op0=ALU.subtract))
        waits("vector", tm=P.val("tm"))
        v_op(lambda v: v.tensor_copy(out=vrow[0:1, 0:512], in_=psRT[0:1, 0:512]))
        v_op(lambda v: v.tensor_copy(out=vrow[0:1, 512:1024], in_=psY1[0:1, 0:512]))
        ldb2 = s_dma(bcrow[:, :], battn[:, :])
        s_barrier()
        v_op(lambda v: v.reduce_sum(smallsc[0:1, 1:2], vrow[:, :], axis=AX.X))
        v_op(lambda v: v.tensor_scalar_mul(out=smallsc[0:1, 1:2], in0=smallsc[0:1, 1:2], scalar1=1.0 / D))
        v_op(lambda v: v.tensor_scalar(out=vrow[:, :], in0=vrow[:, :], scalar1=smallsc[0:1, 1:2], scalar2=None, op0=ALU.subtract))
        waits("vector", sq=ldb2)
        v_op(lambda v: v.reduce_sum(smallsc[0:1, 2:3], bcrow[:, :], axis=AX.X))
        v_op(lambda v: v.tensor_scalar_mul(out=smallsc[0:1, 2:3], in0=smallsc[0:1, 2:3], scalar1=1.0 / D))
        v_op(lambda v: v.tensor_scalar(out=bcrow[:, :], in0=bcrow[:, :], scalar1=smallsc[0:1, 2:3], scalar2=None, op0=ALU.subtract))
        vVB = P.val("vm")
        wait("gpsimd", "vm", vVB)
        g_dma(vb2[0:1, :], vrow[0:1, :])
        g_dma(vb2[1:2, :], bcrow[0:1, :])

        bar = dict(sq=P.val("sq"), gq=P.val("gq"), tm=P.val("tm"),
                   vm=P.val("vm"), sm=P.val("sm"))
        for eng in ("sync", "gpsimd", "tensor", "vector", "scalar"):
            waits(eng, **bar)
        prep.close()

        loop = ExitStack()
        xt = loop.enter_context(nc.sbuf_tensor("xt", [128, 2, D], F32))
        yct = loop.enter_context(nc.sbuf_tensor("yct", [128, 2, D], BF16))
        xnbf = loop.enter_context(nc.sbuf_tensor("xnbf", [128, D], BF16))
        sqscr = loop.enter_context(nc.sbuf_tensor("sqscr", [128, D], F32))
        acscr = loop.enter_context(nc.sbuf_tensor("acscr", [128, D], F32))
        usb = loop.enter_context(nc.sbuf_tensor("usb", [128, D], F32))
        xpt = loop.enter_context(nc.sbuf_tensor("xpt", [128, 2, D], F32))
        h2t = loop.enter_context(nc.sbuf_tensor("h2t", [128, 2, D], F32))
        gbscr = loop.enter_context(nc.sbuf_tensor("gbscr", [128, 2, D], F32))
        h2bf = loop.enter_context(nc.sbuf_tensor("h2bf", [128, D], BF16))
        h2fm = loop.enter_context(nc.sbuf_tensor("h2fm", [128, 2, 128], BF16))
        lgfm = loop.enter_context(nc.sbuf_tensor("lgfm", [8, 128], F32))
        lgall = loop.enter_context(nc.sbuf_tensor("lgall", [128, NT, 8], F32))
        statYa = loop.enter_context(nc.sbuf_tensor("statYa", [128, NT], F32))
        statYb = loop.enter_context(nc.sbuf_tensor("statYb", [128, NT], F32))
        strsy = loop.enter_context(nc.sbuf_tensor("strsy", [128, NT], F32))
        st2s = loop.enter_context(nc.sbuf_tensor("st2s", [128, NT], F32))
        st2q = loop.enter_context(nc.sbuf_tensor("st2q", [128, NT], F32))
        stmu2 = loop.enter_context(nc.sbuf_tensor("stmu2", [128, NT], F32))
        strs2 = loop.enter_context(nc.sbuf_tensor("strs2", [128, NT], F32))
        tv1 = loop.enter_context(nc.sbuf_tensor("tv1", [128, 8], F32))
        tv2 = loop.enter_context(nc.sbuf_tensor("tv2", [128, 8], F32))
        m1c = loop.enter_context(nc.sbuf_tensor("m1c", [128, 1], F32))
        nm1c = loop.enter_context(nc.sbuf_tensor("nm1c", [128, 1], F32))
        m2c = loop.enter_context(nc.sbuf_tensor("m2c", [128, 1], F32))
        rcpc = loop.enter_context(nc.sbuf_tensor("rcpc", [128, 1], F32))
        wtile = loop.enter_context(nc.sbuf_tensor("wtile", [128, 2, 8], F32))
        masksb = loop.enter_context(nc.sbuf_tensor("masksb", [128, 7], F32))
        Psb = loop.enter_context(nc.sbuf_tensor("Psb", [128, 7], F32))
        cmpst = loop.enter_context(nc.sbuf_tensor("cmpst", [128, 7], F32))
        offs_sb = loop.enter_context(nc.sbuf_tensor("offs_sb", [7, 1], F32))
        idxf = loop.enter_context(nc.sbuf_tensor("idxf", [7, 1], F32))
        idxu = loop.enter_context(nc.sbuf_tensor("idxu", [7, 1], U32))
        gidj = loop.enter_context(nc.sbuf_tensor("gidj", [128, 1], F32))
        ohbuf = loop.enter_context(nc.sbuf_tensor("ohbuf", [128, 2, 128], F32))
        cmpT = loop.enter_context(nc.sbuf_tensor("cmpT", [7, 2, 128], F32))

        vZ = v_op(lambda v: v.memset(acscr[:, :], 0.0))
        wait("sync", "vm", vZ)
        def zl(s):
            s.dma_start(out=lists_o[:, :], in_=acscr[:, 0:238]).then_inc(LZ, 16)
        P.on("sync", zl)
        P.inc("lz", 16)
        wait("vector", "lz", 16)
        wait("gpsimd", "lz", 16)

        sticks, vticks = [], []
        for j in range(NT):
            buf = j % 2
            if j >= 2:
                wait("sync", "sm", sticks[j - 2])
                wait("sync", "vm", vticks[j - 2])
            ld = s_dma(xt[:, buf, :], x_in[j * 128:(j + 1) * 128, :], sem="xq%d" % buf)
            waits("scalar", **{("xq%d" % buf): ld})
            sticks.append(sc_op(lambda s, j=j, buf=buf: s.activation(
                out=sqscr[:, :], in_=xt[:, buf, :], func=AF.Square,
                accum_out=stat_sq[:, j:j + 1])))
            waits("vector", **{("xq%d" % buf): ld})
            vticks.append(v_op(lambda v, j=j, buf=buf: v.tensor_scalar(
                out=acscr[:, :], in0=xt[:, buf, :], scalar1=1.0, scalar2=None,
                op0=ALU.mult, op1=ALU.add, accum_out=stat_s1[:, j:j + 1])))
        waits("vector", sm=P.val("sm"))
        v_op(lambda v: v.tensor_scalar_mul(out=stat_mu[:, :], in0=stat_s1[:, :], scalar1=1.0 / D))
        v_op(lambda v: v.tensor_scalar_mul(out=stat_sq[:, :], in0=stat_sq[:, :], scalar1=1.0 / D))
        v_op(lambda v: v.tensor_mul(out=stat_s1[:, :], in0=stat_mu[:, :], in1=stat_mu[:, :]))
        v_op(lambda v: v.tensor_sub(out=stat_sq[:, :], in0=stat_sq[:, :], in1=stat_s1[:, :]))
        v_op(lambda v: v.tensor_scalar_add(out=stat_sq[:, :], in0=stat_sq[:, :], scalar1=EPS))
        v_op(lambda v: v.reciprocal(out=stat_rs[:, :], in_=stat_sq[:, :]))
        waits("scalar", vm=P.val("vm"))
        sc_op(lambda s: s.activation(out=stat_rs[:, :], in_=stat_rs[:, :], func=AF.Sqrt))
        sStats = P.val("sm")

        waits("vector", sm=sStats)
        wait("sync", "vm", P.val("vm"))
        wait("sync", "sm", P.val("sm"))
        xnticks = []
        trC = [0]
        pstTick = {0: 0, 1: 0}
        for j in range(NT):
            buf = j % 2
            if j >= 2:
                wait("sync", "vm", xnticks[j - 2][0])
                wait("sync", "sm", xnticks[j - 2][1])
            ld = s_dma(xt[:, buf, :], x_in[j * 128:(j + 1) * 128, :], sem="xq%d" % buf)
            waits("vector", **{("xq%d" % buf): ld})
            wait("vector", "tm", trC[0])
            v_op(lambda v, j=j, buf=buf: v.tensor_scalar(
                out=xnbf[:, :], in0=xt[:, buf, :], scalar1=stat_mu[:, j:j + 1],
                scalar2=stat_rs[:, j:j + 1], op0=ALU.subtract, op1=ALU.mult))
            waits("tensor", vm=P.val("vm"))
            for dc in range(NDC):
                pp = dc % 2
                pst = psTB0 if pp == 0 else psTB1
                waits("tensor", sm=pstTick[pp])
                t_op(lambda t, dc=dc, pst=pst: t.transpose(
                    pst[:, :], xnbf[:, dc * 128:(dc + 1) * 128], idb[:, :]))
                waits("scalar", tm=P.val("tm"))
                pstTick[pp] = sc_op(lambda s, j=j, dc=dc, pst=pst: s.copy(
                    out=xnT[:, dc, j * 128:(j + 1) * 128], in_=pst[:, :]))
            trC[0] = P.val("tm")
            xnticks.append((P.val("vm"), P.val("sm")))

        waits("vector", sm=P.val("sm"))
        for dc in range(NDC):
            v_op(lambda v, dc=dc: v.tensor_tensor_scan(
                xnT[:, dc, :], bcast_free(acol[:, 0:1], NTOK),
                xnT[:, dc, :], 0.0, ALU.mult, ALU.add))
        vScan = P.val("vm")

        wait("tensor", "vm", vScan)
        evT, scT4 = [], []
        soh_cnt = [0, 0]
        sow_cnt = [0, 0]
        for j in range(NT):
            pb = j % 2
            psA_ = psY0 if pb == 0 else psY2
            psB_ = psY1 if pb == 0 else psY3
            if j >= 2:
                wait("tensor", "vm", evT[j - 2])
                wait("tensor", "sm", scT4[j - 2])
            for ec in range(2):
                psY = psA_ if ec == 0 else psB_
                for dc in range(NDC):
                    t_op(lambda t, j=j, dc=dc, ec=ec, psY=psY: t.matmul(
                        psY[:, :], xnT[:, dc, j * 128:(j + 1) * 128],
                        WTb[:, dc, ec * 512:(ec + 1) * 512],
                        start=(dc == 0), stop=False), tick=False)
                t_op(lambda t, j=j, ec=ec, psY=psY: t.matmul(
                    psY[:, :], g2[0:2, j * 128:(j + 1) * 128],
                    vb2[0:2, ec * 512:(ec + 1) * 512], start=False, stop=True))
            tY = P.val("tm")
            waits("scalar", tm=tY)
            sc_op(lambda s, j=j, psA_=psA_: s.activation(out=sqscr[:, 0:512], in_=psA_[:, :], func=AF.Square, accum_out=statYa[:, j:j + 1]))
            sc_op(lambda s, j=j, psB_=psB_: s.activation(out=sqscr[:, 512:1024], in_=psB_[:, :], func=AF.Square, accum_out=statYb[:, j:j + 1]))
            scT4.append(P.val("sm"))
            waits("vector", tm=tY, sm=P.val("sm"))
            if j >= 2:
                wait("vector", "soh%d" % pb, 16 * soh_cnt[pb])
            v_op(lambda v, pb=pb, psA_=psA_: v.tensor_copy(out=yct[:, pb, 0:512], in_=psA_[:, :]))
            v_op(lambda v, pb=pb, psB_=psB_: v.tensor_copy(out=yct[:, pb, 512:1024], in_=psB_[:, :]))
            evT.append(P.val("vm"))
            wait("sync", "vm", evT[j])
            s_dma(ycs_o[j * 128:(j + 1) * 128, :], yct[:, pb, :], sem="soh%d" % pb)
            soh_cnt[pb] += 1
        waits("vector", sm=P.val("sm"))
        v_op(lambda v: v.tensor_add(out=statYa[:, :], in0=statYa[:, :], in1=statYb[:, :]))
        v_op(lambda v: v.tensor_scalar(out=statYa[:, :], in0=statYa[:, :], scalar1=1.0 / D, scalar2=EPS, op0=ALU.mult, op1=ALU.add))
        v_op(lambda v: v.reciprocal(out=strsy[:, :], in_=statYa[:, :]))
        waits("scalar", vm=P.val("vm"))
        sc_op(lambda s: s.activation(out=strsy[:, :], in_=strsy[:, :], func=AF.Sqrt))
        sRsy = P.val("sm")

        wait("sync", "vm", P.val("vm"))
        wait("sync", "sm", P.val("sm"))
        wait("sync", "soh0", P.val("soh0"))
        wait("sync", "soh1", P.val("soh1"))
        waits("vector", sm=sRsy)
        p5vT = []
        p5sT = []
        sox_cnt = [0, 0]
        ldx_t = {}
        ldy_t = {}
        def p5_load(j):
            buf = j % 2
            ldx_t[j] = s_dma(xt[:, buf, :], x_in[j * 128:(j + 1) * 128, :], sem="xq%d" % buf)
            ldy_t[j] = s_dma(yct[:, buf, :], ycs_o[j * 128:(j + 1) * 128, :], sem="wq%d" % buf)
        p5_load(0)
        p5_load(1)
        for j in range(NT):
            buf = j % 2
            waits("vector", **{("xq%d" % buf): ldx_t[j], ("wq%d" % buf): ldy_t[j]})
            v_op(lambda v, j=j, buf=buf: v.scalar_tensor_tensor(
                out=usb[:, :], in0=yct[:, buf, :], scalar=strsy[:, j:j + 1],
                in1=Gt[:, :], op0=ALU.mult, op1=ALU.mult))
            if j >= 2:
                wait("vector", "sox%d" % buf, 16 * sox_cnt[buf])
            v_op(lambda v, buf=buf: v.tensor_add(out=xpt[:, buf, :], in0=usb[:, :], in1=xt[:, buf, :]))
            v_op(lambda v, j=j, buf=buf: v.scalar_tensor_tensor(
                out=xpt[:, buf, :], in0=xpt[:, buf, :], scalar=1.0, in1=Bt[:, :],
                op0=ALU.mult, op1=ALU.add, accum_out=st2s[:, j:j + 1]))
            p5vT.append(P.val("vm"))
            waits("scalar", vm=P.val("vm"))
            sc_op(lambda s, j=j, buf=buf: s.activation(
                out=sqscr[:, :], in_=xpt[:, buf, :], func=AF.Square,
                accum_out=st2q[:, j:j + 1]))
            p5sT.append(P.val("sm"))
            wait("sync", "vm", p5vT[j])
            s_dma(xpp_o[j * 128:(j + 1) * 128, :], xpt[:, buf, :], sem="sox%d" % buf)
            sox_cnt[buf] += 1
            if j + 2 < NT:
                wait("sync", "sm", p5sT[j])
                p5_load(j + 2)
        waits("vector", sm=P.val("sm"))
        v_op(lambda v: v.tensor_scalar_mul(out=stmu2[:, :], in0=st2s[:, :], scalar1=1.0 / D))
        v_op(lambda v: v.tensor_scalar_mul(out=st2q[:, :], in0=st2q[:, :], scalar1=1.0 / D))
        v_op(lambda v: v.tensor_mul(out=st2s[:, :], in0=stmu2[:, :], in1=stmu2[:, :]))
        v_op(lambda v: v.tensor_sub(out=st2q[:, :], in0=st2q[:, :], in1=st2s[:, :]))
        v_op(lambda v: v.tensor_scalar_add(out=st2q[:, :], in0=st2q[:, :], scalar1=EPS))
        v_op(lambda v: v.reciprocal(out=strs2[:, :], in_=st2q[:, :]))
        waits("scalar", vm=P.val("vm"))
        sc_op(lambda s: s.activation(out=strs2[:, :], in_=strs2[:, :], func=AF.Sqrt))
        sRs2 = P.val("sm")

        wait("sync", "vm", P.val("vm"))
        wait("sync", "sm", P.val("sm"))
        wait("sync", "sox0", P.val("sox0"))
        wait("sync", "sox1", P.val("sox1"))
        waits("vector", sm=sRs2)
        p6T = []
        ldp_t = {}
        def p6_load(j):
            buf = j % 2
            ldp_t[j] = s_dma(xt[:, buf, :], xpp_o[j * 128:(j + 1) * 128, :], sem="xq%d" % buf)
        p6_load(0)
        p6_load(1)
        def h2slice(j):
            return xnT[:, j // 4, (j % 4) * 1024:(j % 4 + 1) * 1024]
        for j in range(NT):
            buf = j % 2
            waits("vector", **{("xq%d" % buf): ldp_t[j]})
            v_op(lambda v, j=j, buf=buf: v.tensor_scalar(
                out=usb[:, :], in0=xt[:, buf, :], scalar1=stmu2[:, j:j + 1],
                scalar2=strs2[:, j:j + 1], op0=ALU.subtract, op1=ALU.mult))
            if j >= 2:
                wait("vector", "soh%d" % buf, 16 * soh_cnt[buf])
                wait("vector", "sow%d" % buf, 16 * sow_cnt[buf])
            v_op(lambda v, buf=buf: v.scalar_tensor_tensor(
                out=h2t[:, buf, :], in0=usb[:, :], scalar=1.0, in1=cb1t[:, :],
                op0=ALU.mult, op1=ALU.mult))
            v_op(lambda v, buf=buf: v.tensor_add(out=h2t[:, buf, :], in0=h2t[:, buf, :], in1=sbt[:, :]))
            vH2 = P.val("vm")
            wait("sync", "vm", vH2)
            s_dma(h2b_o[j * 128:(j + 1) * 128, :], h2t[:, buf, :], sem="soh%d" % buf)
            soh_cnt[buf] += 1
            v_op(lambda v, buf=buf: v.tensor_mul(out=gbscr[:, buf, :], in0=h2t[:, buf, :], in1=gbt[:, :]))
            vGB = P.val("vm")
            wait("sync", "vm", vGB)
            s_dma(gbh2_o[j * 128:(j + 1) * 128, :], gbscr[:, buf, :], sem="sow%d" % buf)
            sow_cnt[buf] += 1
            v_op(lambda v, j=j, buf=buf: v.tensor_copy(out=h2slice(j), in_=h2t[:, buf, :]))
            p6T.append(P.val("vm"))
            if j + 2 < NT:
                wait("sync", "vm", p6T[j])
                p6_load(j + 2)

        waits("tensor", vm=P.val("vm"))
        for j in range(NT):
            cpT = {}
            for dc in range(NDC + 2):
                if dc < NDC:
                    pp = dc % 2
                    pst = psTB0 if pp == 0 else psTB1
                    waits("tensor", sm=pstTick[pp])
                    t_op(lambda t, j=j, dc=dc, pst=pst: t.transpose(
                        pst[:, :], h2slice(j)[:, dc * 128:(dc + 1) * 128], idb[:, :]))
                if dc >= 2:
                    e = dc - 2
                    waits("tensor", sm=cpT[e])
                    t_op(lambda t, e=e: t.matmul(
                        psRT[0:8, 0:128], rwTb[:, e, :], h2fm[:, e % 2, :],
                        start=(e == 0), stop=(e == NDC - 1), skip_group_check=True))
                if dc < NDC:
                    pp = dc % 2
                    pst = psTB0 if pp == 0 else psTB1
                    waits("scalar", tm=P.val("tm"))
                    pstTick[pp] = sc_op(lambda s, dc=dc, pst=pst: s.copy(
                        out=h2fm[:, dc % 2, :], in_=pst[:, :]))
                    cpT[dc] = pstTick[pp]
            waits("scalar", tm=P.val("tm"), vm=P.val("vm"))
            sc_op(lambda s: s.copy(out=lgfm[:, :], in_=psRT[0:8, 0:128]))
            waits("tensor", sm=P.val("sm"))
            t_op(lambda t: t.transpose(psRT[0:128, 128:136], lgfm[:, :], idf[0:8, 0:8]))
            waits("vector", tm=P.val("tm"))
            v_op(lambda v, j=j: v.tensor_add(out=lgall[:, j, :], in0=psRT[0:128, 128:136], in1=rbt[:, :]))
        wait("sync", "vm", P.val("vm"))
        wait("sync", "sm", P.val("sm"))
        wait("sync", "sow0", P.val("sow0"))
        wait("sync", "sow1", P.val("sow1"))
        scatticks = []
        ohticks = {0: 0, 1: 0}
        p7T = []
        l7x = {}
        l7g = {}
        def p7_load(j):
            buf = j % 2
            l7x[j] = s_dma(xt[:, buf, :], xpp_o[j * 128:(j + 1) * 128, :], sem="xq%d" % buf)
            l7g[j] = s_dma(gbscr[:, buf, :], gbh2_o[j * 128:(j + 1) * 128, :], sem="wq%d" % buf)
        p7_load(0)
        p7_load(1)
        for j in range(NT):
            buf = j % 2
            ldp = l7x[j]
            ldg = l7g[j]
            v_op(lambda v, j=j: v.reduce_max(m1c[:, :], lgall[:, j, :], axis=AX.X))
            v_op(lambda v: v.tensor_scalar_mul(out=nm1c[:, :], in0=m1c[:, :], scalar1=-1.0))
            v_op(lambda v, j=j: v.tensor_scalar(out=tv1[:, :], in0=lgall[:, j, :], scalar1=m1c[:, 0:1], scalar2=None, op0=ALU.is_equal))
            v_op(lambda v, j=j: v.scalar_tensor_tensor(out=tv2[:, :], in0=tv1[:, :], scalar=-1e30, in1=lgall[:, j, :], op0=ALU.mult, op1=ALU.add))
            v_op(lambda v: v.reduce_max(m2c[:, :], tv2[:, :], axis=AX.X))
            v_op(lambda v, j=j: v.tensor_scalar(out=tv1[:, :], in0=lgall[:, j, :], scalar1=m2c[:, 0:1], scalar2=None, op0=ALU.is_ge))
            waits("scalar", vm=P.val("vm"))
            sc_op(lambda s, j=j: s.activation(out=tv2[:, :], in_=lgall[:, j, :], func=AF.Exp, bias=nm1c[:, 0:1], scale=1.0))
            waits("vector", sm=P.val("sm"))
            v_op(lambda v: v.reduce_sum(rcpc[:, :], tv2[:, :], axis=AX.X))
            v_op(lambda v: v.reciprocal(out=rcpc[:, :], in_=rcpc[:, :]))
            if j >= 2:
                wait("vector", "sow%d" % buf, 16 * sow_cnt[buf])
            v_op(lambda v, buf=buf: v.scalar_tensor_tensor(
                out=wtile[:, buf, :], in0=tv2[:, :], scalar=rcpc[:, 0:1], in1=tv1[:, :],
                op0=ALU.mult, op1=ALU.mult))
            vW = P.val("vm")
            wait("sync", "vm", vW)
            s_dma(w_o[j * 128:(j + 1) * 128, :], wtile[:, buf, :], sem="sow%d" % buf)
            sow_cnt[buf] += 1
            if j >= 2:
                wait("vector", "sox%d" % buf, 16 * sox_cnt[buf])
            waits("vector", **{("xq%d" % buf): ldp, ("wq%d" % buf): ldg})
            v_op(lambda v, buf=buf: v.scalar_tensor_tensor(
                out=xpt[:, buf, :], in0=gbscr[:, buf, :], scalar=wtile[:, buf, 7:8],
                in1=xt[:, buf, :], op0=ALU.mult, op1=ALU.add))
            vXf = P.val("vm")
            wait("sync", "vm", vXf)
            s_dma(xpp_o[j * 128:(j + 1) * 128, :], xpt[:, buf, :], sem="sox%d" % buf)
            sox_cnt[buf] += 1
            v_op(lambda v, buf=buf: v.tensor_scalar(out=masksb[:, :], in0=wtile[:, buf, 0:7], scalar1=0.0, scalar2=None, op0=ALU.is_gt))
            waits("tensor", vm=P.val("vm"))
            t_op(lambda t: t.matmul(psRT[:, 256:263], ctriu[:, :], masksb[:, :], start=True, stop=True, skip_group_check=True))
            waits("vector", tm=P.val("tm"))
            v_op(lambda v: v.tensor_copy(out=Psb[:, :], in_=psRT[:, 256:263]))
            if j == 0:
                v_op(lambda v: v.memset(offs_sb[:, :], 0.0))
            else:
                v_op(lambda v: v.tensor_copy(out=offs_sb[:, :], in_=psOf[:, :]))
            waits("tensor", vm=P.val("vm"))
            t_op(lambda t, j=j: t.matmul(psOf[:, :], masksb[:, :], conesc[:, :],
                                         start=(j == 0), stop=(j == NT - 1), skip_group_check=True))
            v_op(lambda v, j=j: v.tensor_scalar_add(out=gidj[:, :], in0=gid0[:, :], scalar1=float(j * 128)))
            wait("vector", "sc", P.val("sc"))
            v_op(lambda v: v.tensor_add(out=idxf[:, :], in0=offs_sb[:, :], in1=ce4352[:, :]))
            v_op(lambda v: v.tensor_copy(out=idxu[:, :], in_=idxf[:, :]))
            for e in range(7):
                ob = e % 2
                wait("vector", "tm", ohticks[ob])
                v_op(lambda v, e=e, ob=ob: v.tensor_scalar(
                    out=ohbuf[:, ob, :], in0=ciota[:, :], scalar1=Psb[:, e:e + 1],
                    scalar2=masksb[:, e:e + 1], op0=ALU.is_equal, op1=ALU.mult))
                waits("tensor", vm=P.val("vm"))
                ohticks[ob] = t_op(lambda t, e=e, ob=ob: t.matmul(
                    psRT[:, 288 + e:289 + e], ohbuf[:, ob, :], gidj[:, :],
                    start=True, stop=True, skip_group_check=True))
            waits("vector", tm=P.val("tm"))
            v_op(lambda v: v.tensor_copy(out=cmpst[:, :], in_=psRT[:, 288:295]))
            waits("tensor", vm=P.val("vm"))
            t_op(lambda t: t.transpose(psRT[0:7, 320:448], cmpst[:, :], idf[:, :]))
            waits("vector", tm=P.val("tm"))
            if j >= 2:
                wait("vector", "sc", scatticks[j - 2])
            cbuf = j % 2
            v_op(lambda v, cbuf=cbuf: v.tensor_copy(out=cmpT[0:7, cbuf, :], in_=psRT[0:7, 320:448]))
            vT2 = P.val("vm")
            wait("gpsimd", "vm", vT2)
            wait("gpsimd", "sc", P.val("sc"))
            def scat(g, cbuf=cbuf):
                g.indirect_dma_start(
                    out=lists_o[:, :], out_offset=IndirectOffsetOnAxis(idxu[:, :], 0),
                    in_=cmpT[0:7, cbuf, :], in_offset=None).then_inc(SC, 16)
            P.on("gpsimd", scat)
            scatticks.append(P.inc("sc", 16))
            p7T.append(P.val("vm"))
            if j + 2 < NT:
                wait("sync", "vm", p7T[j])
                p7_load(j + 2)

        waits("vector", tm=P.val("tm"))
        vCnt = v_op(lambda v: v.tensor_copy(out=offs_sb[:, :], in_=psOf[:, :]))
        wait("sync", "vm", vCnt)
        s_dma(counts_o[:, :], offs_sb[:, :])

        wait("sync", "sq", P.val("sq"))
        for snm in ("soh0", "soh1", "sow0", "sow1", "sox0", "sox1", "xq0", "xq1", "wq0", "wq1"):
            wait("sync", snm, P.val(snm))
        wait("gpsimd", "sc", P.val("sc"))
        wait("gpsimd", "gq", P.val("gq"))
        loop.close()

        @block.sync
        def _(s):
            P.run("sync", s)

        @block.tensor
        def _(t):
            P.run("tensor", t)

        @block.vector
        def _(v):
            P.run("vector", v)

        @block.scalar
        def _(s):
            P.run("scalar", s)

        @block.gpsimd
        def _(g):
            P.run("gpsimd", g)

    return nc

def host_consts_l2(core_id):
    import ml_dtypes
    c = {}
    c["ident_bf16"] = np.eye(128, dtype=np.float32).astype(ml_dtypes.bfloat16)
    c["ident_f32b"] = np.eye(128, dtype=np.float32)
    c["trashrow"] = np.full((1, 8), GPAD, np.uint32)
    c["ident8"] = np.eye(8, dtype=np.float32)
    c["triu8"] = np.triu(np.ones((8, 8), np.float32), k=1)
    return c


def build_l2():
    nc = bass.Bass("TRN2", target_bir_lowering=False, debug=False)
    h2fa = nc.dram_tensor("h2fa", [NTOKG, 512], F32, kind="ExternalInput").ap()
    h2fb = nc.dram_tensor("h2fb", [NTOKG, 512], F32, kind="ExternalInput").ap()
    mylists = nc.dram_tensor("mylists", [8, LSTRIDE], F32, kind="ExternalInput").ap()
    counts_e = nc.dram_tensor("counts_e", [8, 1], F32, kind="ExternalInput").ap()
    wi = nc.dram_tensor("wi", [D, I], F32, kind="ExternalInput").ap()
    wo = nc.dram_tensor("wo", [I, D], F32, kind="ExternalInput").ap()
    ident_bf16 = nc.dram_tensor("ident_bf16", [128, 128], BF16, kind="ExternalInput").ap()
    ident_f32b = nc.dram_tensor("ident_f32b", [128, 128], F32, kind="ExternalInput").ap()
    trashrow = nc.dram_tensor("trashrow", [1, 8], U32, kind="ExternalInput").ap()
    ident8 = nc.dram_tensor("ident8", [8, 8], F32, kind="ExternalInput").ap()
    triu8 = nc.dram_tensor("triu8", [8, 8], F32, kind="ExternalInput").ap()
    eout = nc.dram_tensor("eout", [CAP, D], F32, kind="ExternalOutput").ap()
    glist_t = nc.dram_tensor("glist", [GPAD + 4096, 1], F32, kind="ExternalOutput")
    glist = glist_t.ap()
    glist2 = glist_t.ap()

    P = Plan()
    stack = ExitStack()
    with stack:
        idb = stack.enter_context(nc.sbuf_tensor("idb", [128, 128], BF16))
        idf2 = stack.enter_context(nc.sbuf_tensor("idf2", [128, 128], F32))
        idx2 = stack.enter_context(nc.sbuf_tensor("idx2", [2, 8], U32))
        seg2 = stack.enter_context(nc.sbuf_tensor("seg2", [2, LSTRIDE], F32))
        id8 = stack.enter_context(nc.sbuf_tensor("id8", [8, 8], F32))
        tr8 = stack.enter_context(nc.sbuf_tensor("tr8", [8, 8], F32))
        cnts = stack.enter_context(nc.sbuf_tensor("cnts", [8, 1], F32))
        cumo = stack.enter_context(nc.sbuf_tensor("cumo", [8, 1], F32))
        cumr = stack.enter_context(nc.sbuf_tensor("cumr", [1, 8], F32))
        cumru = stack.enter_context(nc.sbuf_tensor("cumru", [1, 8], U32))
        idlist = stack.enter_context(nc.sbuf_tensor("idlist", [128, 32], F32))
        idu = stack.enter_context(nc.sbuf_tensor("idu", [128, 32], U32))
        wi_sb = stack.enter_context(nc.sbuf_tensor("wi_sb", [128, 8, I], BF16))
        wo_sb = stack.enter_context(nc.sbuf_tensor("wo_sb", [128, 32, D], BF16))
        gbuf = stack.enter_context(nc.sbuf_tensor("gbuf", [128, 2, D], F32))
        h2fm = stack.enter_context(nc.sbuf_tensor("h2fm", [128, 8, 512], BF16))
        hbf = stack.enter_context(nc.sbuf_tensor("hbf", [128, 32, 512], BF16))
        outsb = stack.enter_context(nc.sbuf_tensor("outsb", [128, 2, D], F32))
        ps1a = stack.enter_context(nc.psum_tensor("ps1a", [128, 512], F32))
        ps1b = stack.enter_context(nc.psum_tensor("ps1b", [128, 512], F32))
        ps2a = stack.enter_context(nc.psum_tensor("ps2a", [128, 512], F32))
        ps2b = stack.enter_context(nc.psum_tensor("ps2b", [128, 512], F32))
        psta = stack.enter_context(nc.psum_tensor("psta", [128, 128], F32))
        pstb = stack.enter_context(nc.psum_tensor("pstb", [128, 128], F32))
        psm = stack.enter_context(nc.psum_tensor("psm", [8, 8], F32))
        SQ = stack.enter_context(nc.semaphore("sq"))
        GL = stack.enter_context(nc.semaphore("gl"))
        GA0 = stack.enter_context(nc.semaphore("ga0"))
        GA1 = stack.enter_context(nc.semaphore("ga1"))
        WIQ = stack.enter_context(nc.semaphore("wiq"))
        WOQ = stack.enter_context(nc.semaphore("woq"))
        EO0 = stack.enter_context(nc.semaphore("eo0"))
        EO1 = stack.enter_context(nc.semaphore("eo1"))
        TM = stack.enter_context(nc.semaphore("tm"))
        VM = stack.enter_context(nc.semaphore("vm"))
        SM = stack.enter_context(nc.semaphore("sm"))
        block = stack.enter_context(nc.Block())

        SEMS = {"sq": SQ, "gl": GL, "ga0": GA0, "ga1": GA1, "wiq": WIQ, "woq": WOQ,
                "eo0": EO0, "eo1": EO1, "tm": TM, "vm": VM, "sm": SM}

        def s_dma(out, in_, sem="sq"):
            S = SEMS[sem]
            def f(s, out=out, in_=in_, S=S):
                s.dma_start(out=out, in_=in_).then_inc(S, 16)
            P.on("sync", f)
            return P.inc(sem, 16)

        def g_dma(out, in_, sem):
            S = SEMS[sem]
            def f(g, out=out, in_=in_, S=S):
                g.dma_start(out=out, in_=in_).then_inc(S, 16)
            P.on("gpsimd", f)
            return P.inc(sem, 16)

        def _eng_op(eng, sem_name, fn, tick):
            prev = P.val(sem_name)
            S = SEMS[sem_name]
            if prev > 0:
                def w(e, S=S, prev=prev):
                    e.wait_ge(S, prev)
                P.on(eng, w)
            if tick:
                n = P.inc(sem_name, 1)
                def f(e, fn=fn, S=S):
                    fn(e).then_inc(S, 1)
            else:
                n = prev
                def f(e, fn=fn):
                    fn(e)
            P.on(eng, f)
            return n

        def t_op(fn, tick=True):
            return _eng_op("tensor", "tm", fn, tick)

        def v_op(fn, tick=True):
            return _eng_op("vector", "vm", fn, tick)

        def wait(eng_name, sem_name, val):
            if val <= 0:
                return
            sem = SEMS[sem_name]
            def f(e, sem=sem, val=val):
                e.wait_ge(sem, val)
            P.on(eng_name, f)

        def waits(eng_name, **kw):
            for k, v in kw.items():
                wait(eng_name, k, v)

        s_dma(idb[:, :], ident_bf16[:, :])
        s_dma(idf2[:, :], ident_f32b[:, :])
        s_dma(id8[:, :], ident8[:, :])
        s_dma(tr8[:, :], triu8[:, :])
        s_dma(cnts[:, :], counts_e[:, :])
        s_dma(idx2[1:2, :], trashrow[:, :])
        c0 = P.val("sq")
        wait("sync", "sq", c0)
        for pc in range(8):
            g_dma(wi_sb[:, pc, :], wi[pc * 128:(pc + 1) * 128, :], "wiq")
        wiT = P.val("wiq")
        for pc in range(32):
            g_dma(wo_sb[:, pc, :], wo[pc * 128:(pc + 1) * 128, :], "woq")
        woT = P.val("woq")

        waits("tensor", sq=c0)
        t_op(lambda t: t.matmul(psm[:, 0:1], tr8[:, :], cnts[:, :], start=True, stop=True))
        waits("vector", tm=P.val("tm"))
        v_op(lambda v: v.tensor_copy(out=cumo[:, :], in_=psm[:, 0:1]))
        v_op(lambda v: v.tensor_scalar_min(out=cumo[:, :], in0=cumo[:, :], scalar1=4224.0))
        waits("tensor", vm=P.val("vm"))
        t_op(lambda t: t.transpose(psm[0:1, 0:8], cumo[:, :], id8[:, :]))
        waits("vector", tm=P.val("tm"))
        v_op(lambda v: v.tensor_copy(out=cumr[:, :], in_=psm[0:1, 0:8]))
        v_op(lambda v: v.tensor_copy(out=cumru[:, :], in_=cumr[:, :]))
        vM0 = v_op(lambda v: v.memset(seg2[:, :], 0.0), tick=True)
        vCum = P.val("vm")
        wait("gpsimd", "vm", vCum)
        def _mvidx(g):
            g.dma_start(out=idx2[0:1, :], in_=cumru[0:1, :]).then_inc(GL, 16)
        P.on("gpsimd", _mvidx)
        P.inc("gl", 16)
        wait("gpsimd", "gl", P.val("gl"))
        wait("gpsimd", "vm", P.val("vm"))
        for b in range(8):
            nrow = 2 if b < 7 else 1
            def mv2(g, b=b, nrow=nrow):
                g.dma_start(out=seg2[0:nrow, :], in_=mylists[b:b + nrow, :]).then_inc(GL, 16)
            P.on("gpsimd", mv2)
            P.inc("gl", 16)
            wait("gpsimd", "gl", P.val("gl"))
            def seg(g, b=b):
                g.indirect_dma_start(
                    out=glist2[:, :], out_offset=IndirectOffsetOnAxis(idx2[:, b:b + 1], 0),
                    in_=seg2[0:2, 0:4096], in_offset=None).then_inc(GL, 16)
            P.on("gpsimd", seg)
            P.inc("gl", 16)
            wait("gpsimd", "gl", P.val("gl"))
        def ldid(s):
            src = glist[0:4096, 0:1].rearrange("(c p) a -> (a p) c", p=128)
            s.dma_start(out=idlist[:, :], in_=src, allow_slow_non_contiguous=True).then_inc(SQ, 16)
        wait("sync", "gl", P.val("gl"))
        P.on("sync", ldid)
        ilT = P.inc("sq", 16)
        wait("sync", "sq", ilT)
        waits("vector", sq=ilT)
        v_op(lambda v: v.tensor_scalar(out=idlist[:, :], in0=idlist[:, :], scalar1=0.0, scalar2=32767.0, op0=ALU.max, op1=ALU.min))
        v_op(lambda v: v.tensor_copy(out=idu[:, :], in_=idlist[:, :]))
        vIdu = P.val("vm")

        wait("gpsimd", "vm", vIdu)
        wait("tensor", "wiq", wiT)
        relu_ticks = []
        trT = []
        gtrT = {}
        for cc in range(NCC):
            for g4 in range(4):
                gslot = cc * 4 + g4
                gb = gslot % 2
                if gslot >= 2:
                    wait("gpsimd", "tm", gtrT[gslot - 2])
                gsem = SEMS["ga%d" % gb]
                def gat(g, cc=cc, g4=g4, gb=gb, gsem=gsem):
                    g.indirect_dma_start(
                        out=gbuf[:, gb, 0:512], out_offset=None,
                        in_=h2fa[:, :],
                        in_offset=IndirectOffsetOnAxis(idu[:, cc * 4 + g4:cc * 4 + g4 + 1], 0),
                    ).then_inc(gsem, 16)
                    g.indirect_dma_start(
                        out=gbuf[:, gb, 512:1024], out_offset=None,
                        in_=h2fb[:, :],
                        in_offset=IndirectOffsetOnAxis(idu[:, cc * 4 + g4:cc * 4 + g4 + 1], 0),
                    ).then_inc(gsem, 16)
                P.on("gpsimd", gat)
                P.inc("ga%d" % gb, 16)
                gaT = P.inc("ga%d" % gb, 16)
                wait("tensor", "ga%d" % gb, gaT)
                for dc in range(8):
                    it = g4 * 8 + dc
                    pst = psta if it % 2 == 0 else pstb
                    if True:
                        wait("tensor", "vm", max(P.val("vm") - 1, 0))
                    t_op(lambda t, gb=gb, dc=dc, pst=pst: t.transpose(
                        pst[:, :], gbuf[:, gb, dc * 128:(dc + 1) * 128], idf2[:, :]))
                    waits("vector", tm=P.val("tm"))
                    v_op(lambda v, g4=g4, dc=dc, pst=pst: v.tensor_copy(
                        out=h2fm[:, dc, g4 * 128:(g4 + 1) * 128], in_=pst[:, :]))
                gtrT[gslot] = P.val("tm")
            trT.append(P.val("tm"))
            vFM = P.val("vm")
            waits("tensor", vm=vFM)
            for ich in range(32):
                psu = ps1a if ich % 2 == 0 else ps1b
                if ich >= 2:
                    wait("tensor", "vm", relu_ticks[-2])
                for dc in range(8):
                    t_op(lambda t, ich=ich, dc=dc, psu=psu: t.matmul(
                        psu[:, :], wi_sb[:, dc, ich * 128:(ich + 1) * 128],
                        h2fm[:, dc, :], start=(dc == 0), stop=(dc == 7)),
                        tick=(dc == 7))
                waits("vector", tm=P.val("tm"))
                relu_ticks.append(v_op(lambda v, cc=cc, ich=ich, psu=psu: v.tensor_scalar_max(
                    out=hbf[:, ich, :], in0=psu[:, :], scalar1=0.0)))
            vRelu = P.val("vm")
            if cc == 0:
                wait("tensor", "woq", woT)
            waits("tensor", vm=vRelu)
            for csl in range(4):
                ob = csl % 2
                for hch in range(2):
                    psu = ps2a if hch == 0 else ps2b
                    for ich in range(32):
                        t_op(lambda t, csl=csl, hch=hch, ich=ich, psu=psu: t.matmul(
                            psu[:, :], hbf[:, ich, csl * 128:(csl + 1) * 128],
                            wo_sb[:, ich, hch * 512:(hch + 1) * 512],
                            start=(ich == 0), stop=(ich == 31)),
                            tick=(ich == 31))
                tmm2 = P.val("tm")
                eos = "eo%d" % ob
                wait("vector", eos, P.val(eos))
                waits("vector", tm=tmm2)
                v_op(lambda v, ob=ob: v.tensor_copy(out=outsb[:, ob, 0:512], in_=ps2a[:, :]))
                v_op(lambda v, ob=ob: v.tensor_copy(out=outsb[:, ob, 512:1024], in_=ps2b[:, :]))
                vEv = P.val("vm")
                wait("sync", "vm", vEv)
                row0 = cc * 512 + csl * 128
                s_dma(eout[row0:row0 + 128, :], outsb[:, ob, :], sem=eos)
                waits("tensor", vm=vEv)

        wait("sync", "sq", P.val("sq"))
        for snm in ("eo0", "eo1"):
            wait("sync", snm, P.val(snm))
        wait("gpsimd", "ga0", P.val("ga0"))
        wait("gpsimd", "ga1", P.val("ga1"))
        wait("gpsimd", "gl", P.val("gl"))
        wait("gpsimd", "wiq", P.val("wiq"))
        wait("gpsimd", "woq", P.val("woq"))

        @block.sync
        def _(s):
            P.run("sync", s)

        @block.tensor
        def _(t):
            P.run("tensor", t)

        @block.vector
        def _(v):
            P.run("vector", v)

        @block.scalar
        def _(s):
            P.run("scalar", s)

        @block.gpsimd
        def _(g):
            P.run("gpsimd", g)

    return nc

def host_consts_l3(core_id):
    c = {}
    c["triu_s"] = np.triu(np.ones((128, 128), np.float32), k=1)
    c["triu8"] = np.triu(np.ones((8, 8), np.float32), k=1)
    c["ones128"] = np.ones((128, 128), np.float32)
    c["onesrow"] = np.ones((1, 128), np.float32)
    c["selb"] = np.eye(8, dtype=np.float32)[:, core_id:core_id + 1].copy()
    c["e4096p1"] = np.tile((np.arange(7, dtype=np.float32) * CAP + 1.0)[None, :], (128, 1))
    return c



def build_l3():
    nc = bass.Bass("TRN2", target_bir_lowering=False, debug=False)
    xpp = nc.dram_tensor("xpp", [NTOK, D], F32, kind="ExternalInput").ap()
    wb = nc.dram_tensor("wb", [NTOK, 8], F32, kind="ExternalInput").ap()
    eoutf = nc.dram_tensor("eoutf", [7 * CAP, D], F32, kind="ExternalInput").ap()
    countsf = nc.dram_tensor("countsf", [8, 7], F32, kind="ExternalInput").ap()
    gbrow = nc.dram_tensor("gbrow", [1, D], F32, kind="ExternalInput").ap()
    triu_s = nc.dram_tensor("triu_s", [128, 128], F32, kind="ExternalInput").ap()
    triu8 = nc.dram_tensor("triu8", [8, 8], F32, kind="ExternalInput").ap()
    ones128 = nc.dram_tensor("ones128", [128, 128], F32, kind="ExternalInput").ap()
    onesrow = nc.dram_tensor("onesrow", [1, 128], F32, kind="ExternalInput").ap()
    selb = nc.dram_tensor("selb", [8, 1], F32, kind="ExternalInput").ap()
    e4096p1 = nc.dram_tensor("e4096p1", [128, 7], F32, kind="ExternalInput").ap()
    out_o = nc.dram_tensor("out", [NTOK, D], F32, kind="ExternalOutput").ap()

    def d3(ap, dims):
        return dataclasses.replace(ap, ap=type(ap.ap)([list(ap.ap[0])] + [list(d) for d in dims]))

    P = Plan()
    stack = ExitStack()
    with stack:
        ctriu = stack.enter_context(nc.sbuf_tensor("ctriu", [128, 128], F32))
        ctr8 = stack.enter_context(nc.sbuf_tensor("ctr8", [8, 8], F32))
        cones = stack.enter_context(nc.sbuf_tensor("cones", [128, 128], F32))
        conesr = stack.enter_context(nc.sbuf_tensor("conesr", [1, 128], F32))
        cselb = stack.enter_context(nc.sbuf_tensor("cselb", [8, 1], F32))
        ce4096 = stack.enter_context(nc.sbuf_tensor("ce4096", [128, 7], F32))
        cntf = stack.enter_context(nc.sbuf_tensor("cntf", [8, 7], F32))
        cumall = stack.enter_context(nc.sbuf_tensor("cumall", [8, 7], F32))
        baser = stack.enter_context(nc.sbuf_tensor("baser", [1, 7], F32))
        gbr = stack.enter_context(nc.sbuf_tensor("gbr", [1, D], F32))
        gbt = stack.enter_context(nc.sbuf_tensor("gbt", [128, D], F32))
        wful = stack.enter_context(nc.sbuf_tensor("wful", [128, NT, 8], F32))
        emf = stack.enter_context(nc.sbuf_tensor("emf", [128, NT * 7], F32))
        wkb = stack.enter_context(nc.sbuf_tensor("wkb", [128, NT * 7], F32))
        slotf = stack.enter_context(nc.sbuf_tensor("slotf", [128, NT * 7], F32))
        kept = stack.enter_context(nc.sbuf_tensor("kept", [128, NT * 7], F32))
        Am = stack.enter_context(nc.sbuf_tensor("Am", [128, NT * 7], F32))
        sel = stack.enter_context(nc.sbuf_tensor("sel", [128, NT * 7], F32))
        tmpb = stack.enter_context(nc.sbuf_tensor("tmpb", [128, NT * 7], F32))
        mm = stack.enter_context(nc.sbuf_tensor("mm", [128, NT], F32))
        rtmp = stack.enter_context(nc.sbuf_tensor("rtmp", [128, NT], F32))
        w1all = stack.enter_context(nc.sbuf_tensor("w1all", [128, NT], F32))
        w2all = stack.enter_context(nc.sbuf_tensor("w2all", [128, NT], F32))
        r1u32 = stack.enter_context(nc.sbuf_tensor("r1u32", [128, NT], U32))
        r2u32 = stack.enter_context(nc.sbuf_tensor("r2u32", [128, NT], U32))
        crow = stack.enter_context(nc.sbuf_tensor("crow", [1, NT * 7], F32))
        runsb = stack.enter_context(nc.sbuf_tensor("runsb", [1, NT * 7], F32))
        bcd = stack.enter_context(nc.sbuf_tensor("bcd", [1, 256], BF16))
        xt = stack.enter_context(nc.sbuf_tensor("xt", [128, 4, D], F32))
        g1 = stack.enter_context(nc.sbuf_tensor("g1", [128, 4, D], F32))
        g2 = stack.enter_context(nc.sbuf_tensor("g2", [128, 4, D], F32))
        outt = stack.enter_context(nc.sbuf_tensor("outt", [128, 4, D], F32))
        psA = stack.enter_context(nc.psum_tensor("psA", [128, 512], F32))
        psC = stack.enter_context(nc.psum_tensor("psC", [128, 224], F32))
        SQ = stack.enter_context(nc.semaphore("sq"))
        WQ0 = stack.enter_context(nc.semaphore("wq0"))
        XQ0 = stack.enter_context(nc.semaphore("xq0"))
        XQ1 = stack.enter_context(nc.semaphore("xq1"))
        XQ2 = stack.enter_context(nc.semaphore("xq2"))
        XQ3 = stack.enter_context(nc.semaphore("xq3"))
        GA0 = stack.enter_context(nc.semaphore("ga0"))
        GA1 = stack.enter_context(nc.semaphore("ga1"))
        GA2 = stack.enter_context(nc.semaphore("ga2"))
        GA3 = stack.enter_context(nc.semaphore("ga3"))
        OQ0 = stack.enter_context(nc.semaphore("oq0"))
        OQ1 = stack.enter_context(nc.semaphore("oq1"))
        OQ2 = stack.enter_context(nc.semaphore("oq2"))
        OQ3 = stack.enter_context(nc.semaphore("oq3"))
        TM = stack.enter_context(nc.semaphore("tm"))
        VM = stack.enter_context(nc.semaphore("vm"))
        SM = stack.enter_context(nc.semaphore("sm"))
        block = stack.enter_context(nc.Block())

        SEMS = {"sq": SQ, "wq0": WQ0, "xq0": XQ0, "xq1": XQ1, "xq2": XQ2,
                "xq3": XQ3, "ga0": GA0, "ga1": GA1, "ga2": GA2, "ga3": GA3,
                "oq0": OQ0, "oq1": OQ1, "oq2": OQ2, "oq3": OQ3,
                "tm": TM, "vm": VM, "sm": SM}

        def s_dma(out, in_, sem="sq"):
            S = SEMS[sem]
            def f(s, out=out, in_=in_, S=S):
                s.dma_start(out=out, in_=in_).then_inc(S, 16)
            P.on("sync", f)
            return P.inc(sem, 16)

        def _eng_op(eng, sem_name, fn, tick):
            prev = P.val(sem_name)
            S = SEMS[sem_name]
            if prev > 0:
                def w(e, S=S, prev=prev):
                    e.wait_ge(S, prev)
                P.on(eng, w)
            if tick:
                n = P.inc(sem_name, 1)
                def f(e, fn=fn, S=S):
                    fn(e).then_inc(S, 1)
            else:
                n = prev
                def f(e, fn=fn):
                    fn(e)
            P.on(eng, f)
            return n

        def t_op(fn, tick=True):
            return _eng_op("tensor", "tm", fn, tick)

        def v_op(fn, tick=True):
            return _eng_op("vector", "vm", fn, tick)

        def sc_op(fn, tick=True):
            return _eng_op("scalar", "sm", fn, tick)

        def wait(eng_name, sem_name, val):
            if val <= 0:
                return
            sem = SEMS[sem_name]
            def f(e, sem=sem, val=val):
                e.wait_ge(sem, val)
            P.on(eng_name, f)

        def waits(eng_name, **kw):
            for k, v in kw.items():
                wait(eng_name, k, v)

        def sv(buf, e):
            return d3(buf[:, e:e + 1], [[7, NT]])

        emf3 = d3(emf[:, :], [[7, NT], [1, 7]])
        wkb3 = d3(wkb[:, :], [[7, NT], [1, 7]])
        slotf3 = d3(slotf[:, :], [[7, NT], [1, 7]])
        Am3 = d3(Am[:, :], [[7, NT], [1, 7]])
        sel3 = d3(sel[:, :], [[7, NT], [1, 7]])
        ce3 = d3(ce4096[:, :], [[0, NT], [1, 7]])
        mmb3 = d3(mm[:, :], [[1, NT], [0, 7]])

        for dst, src_ in [(ctriu, triu_s), (ctr8, triu8), (cones, ones128),
                          (conesr, onesrow), (cselb, selb), (ce4096, e4096p1),
                          (cntf, countsf), (gbr, gbrow)]:
            sl = tuple(slice(None) for _ in range(len(dst.shape)))
            s_dma(dst[sl], src_[sl])
        c0 = P.val("sq")
        for j in range(NT):
            ldwf = s_dma(wful[:, j, :], wb[j * 128:(j + 1) * 128, :], sem="wq0")
        wait("sync", "sq", c0)
        waits("vector", sq=c0)
        v_op(lambda v: v.tensor_copy(out=bcd[0:1, :], in_=conesr[0:1, :].bitcast(BF16)))
        waits("tensor", sq=c0)
        t_op(lambda t: t.matmul(psA[0:8, 0:7], ctr8[:, :], cntf[:, :], start=True, stop=True))
        waits("vector", tm=P.val("tm"))
        v_op(lambda v: v.tensor_copy(out=cumall[:, :], in_=psA[0:8, 0:7]))
        waits("tensor", vm=P.val("vm"))
        t_op(lambda t: t.matmul(psA[0:1, 0:7], cselb[:, :], cumall[:, :], start=True, stop=True))
        waits("vector", tm=P.val("tm"))
        v_op(lambda v: v.tensor_copy(out=baser[:, :], in_=psA[0:1, 0:7]))
        for half in range(2):
            waits("tensor", vm=P.val("vm"))
            t_op(lambda t, half=half: t.matmul(psA[:, 0:512], conesr[:, :],
                                               gbr[0:1, half * 512:half * 512 + 512],
                                               start=True, stop=True))
            waits("vector", tm=P.val("tm"))
            v_op(lambda v, half=half: v.tensor_copy(out=gbt[:, half * 512:half * 512 + 512], in_=psA[:, 0:512]))

        waits("vector", wq0=ldwf)
        v_op(lambda v: v.tensor_scalar(out=emf3, in0=wful[:, :, 0:7], scalar1=0.0, scalar2=None, op0=ALU.is_gt))
        v_op(lambda v: v.tensor_copy(out=wkb3, in_=wful[:, :, 0:7]))
        vMask = P.val("vm")
        waits("tensor", vm=vMask)
        for j in range(NT):
            tCnt = t_op(lambda t, j=j: t.matmul(psC[0:1, j * 7:(j + 1) * 7], cones[:, 0:1],
                                                emf[:, j * 7:(j + 1) * 7], start=True, stop=True,
                                                skip_group_check=True))
        waits("vector", tm=tCnt)
        v_op(lambda v: v.tensor_copy(out=crow[0:1, :], in_=psC[0:1, :]))
        v_op(lambda v: v.tensor_copy(out=runsb[0:1, 0:7], in_=baser[0:1, :]))
        v_op(lambda v: v.tensor_copy(out=runsb[0:1, 7:NT * 7], in_=crow[0:1, 0:(NT - 1) * 7]))
        for e in range(7):
            v_op(lambda v, e=e: v.tensor_tensor_scan(
                d3(runsb[0:1, e:e + 1], [[7, NT]]), bcast_free(conesr[0:1, 0:1], NT),
                d3(runsb[0:1, e:e + 1], [[7, NT]]), 0.0, ALU.mult, ALU.add))
        vRun = P.val("vm")

        waits("tensor", vm=vRun)
        evacS = {}
        for j in range(NT):
            off = (j % 4) * 8
            if j >= 4:
                wait("tensor", "sm", evacS[j - 4])
            t_op(lambda t, j=j, off=off: t.matmul(psA[:, off:off + 7], ctriu[:, :],
                                                  emf[:, j * 7:(j + 1) * 7], start=True, stop=False,
                                                  skip_group_check=True))
            tR = t_op(lambda t, j=j, off=off: t.matmul(psA[:, off:off + 7], conesr[0:1, :],
                                                       runsb[0:1, j * 7:(j + 1) * 7], start=False, stop=True,
                                                       skip_group_check=True))
            wait("scalar", "tm", tR)
            evacS[j] = sc_op(lambda s, j=j, off=off: s.activation(
                out=slotf[:, j * 7:(j + 1) * 7], in_=psA[:, off:off + 7], func=AF.Copy))

        waits("vector", sm=evacS[NT - 1])
        v_op(lambda v: v.tensor_scalar(out=kept[:, :], in0=slotf[:, :], scalar1=float(CAP), scalar2=None, op0=ALU.is_lt))
        v_op(lambda v: v.tensor_mul(out=kept[:, :], in0=kept[:, :], in1=emf[:, :]))
        v_op(lambda v: v.tensor_tensor(out=Am3, in0=slotf3, in1=ce3, op=ALU.add))
        v_op(lambda v: v.tensor_mul(out=Am[:, :], in0=Am[:, :], in1=kept[:, :]))
        v_op(lambda v: v.tensor_copy(out=mm[:, :], in_=sv(Am, 0)))
        for e in range(1, 7):
            v_op(lambda v, e=e: v.tensor_tensor(out=mm[:, :], in0=mm[:, :], in1=sv(Am, e), op=ALU.max))
        v_op(lambda v: v.tensor_tensor(out=sel3, in0=Am3, in1=mmb3, op=ALU.is_equal))
        v_op(lambda v: v.tensor_mul(out=sel[:, :], in0=sel[:, :], in1=kept[:, :]))
        v_op(lambda v: v.tensor_mul(out=tmpb[:, :], in0=sel[:, :], in1=wkb[:, :]))
        v_op(lambda v: v.tensor_copy(out=w1all[:, :], in_=sv(tmpb, 0)))
        for e in range(1, 7):
            v_op(lambda v, e=e: v.tensor_tensor(out=w1all[:, :], in0=w1all[:, :], in1=sv(tmpb, e), op=ALU.add))
        v_op(lambda v: v.tensor_scalar(out=rtmp[:, :], in0=mm[:, :], scalar1=-1.0, scalar2=0.0, op0=ALU.add, op1=ALU.max))
        v_op(lambda v: v.tensor_copy(out=r1u32[:, :], in_=rtmp[:, :]))
        v_op(lambda v: v.tensor_scalar(out=sel[:, :], in0=sel[:, :], scalar1=-1.0, scalar2=1.0, op0=ALU.mult, op1=ALU.add))
        v_op(lambda v: v.tensor_mul(out=kept[:, :], in0=kept[:, :], in1=sel[:, :]))
        v_op(lambda v: v.tensor_mul(out=Am[:, :], in0=Am[:, :], in1=sel[:, :]))
        v_op(lambda v: v.tensor_copy(out=mm[:, :], in_=sv(Am, 0)))
        for e in range(1, 7):
            v_op(lambda v, e=e: v.tensor_tensor(out=mm[:, :], in0=mm[:, :], in1=sv(Am, e), op=ALU.max))
        v_op(lambda v: v.tensor_tensor(out=sel3, in0=Am3, in1=mmb3, op=ALU.is_equal))
        v_op(lambda v: v.tensor_mul(out=sel[:, :], in0=sel[:, :], in1=kept[:, :]))
        v_op(lambda v: v.tensor_mul(out=tmpb[:, :], in0=sel[:, :], in1=wkb[:, :]))
        v_op(lambda v: v.tensor_copy(out=w2all[:, :], in_=sv(tmpb, 0)))
        for e in range(1, 7):
            v_op(lambda v, e=e: v.tensor_tensor(out=w2all[:, :], in0=w2all[:, :], in1=sv(tmpb, e), op=ALU.add))
        v_op(lambda v: v.tensor_scalar(out=rtmp[:, :], in0=mm[:, :], scalar1=-1.0, scalar2=0.0, op0=ALU.add, op1=ALU.max))
        v_op(lambda v: v.tensor_copy(out=r2u32[:, :], in_=rtmp[:, :]))
        vMeta = P.val("vm")

        cmbT = {}
        stT = {}
        gac = [0, 0, 0, 0]
        xqc = [0, 0, 0, 0]
        oqc = [0, 0, 0, 0]
        for j in range(min(4, NT)):
            s_dma(xt[:, j, :], xpp[j * 128:(j + 1) * 128, :], sem="xq%d" % j)
            xqc[j] += 1
        for j in range(NT):
            buf = j % 4
            ges = "ga%d" % buf
            if j == 0:
                wait("gpsimd", "vm", vMeta)
            if j >= 4:
                wait("gpsimd", "vm", cmbT[j - 4])
            def gat1(g, buf=buf, j=j):
                g.indirect_dma_start(out=g1[:, buf, :], out_offset=None, in_=eoutf[:, :],
                                     in_offset=IndirectOffsetOnAxis(r1u32[:, j:j + 1], 0)).then_inc(SEMS["ga%d" % buf], 16)
            P.on("gpsimd", gat1)
            P.inc(ges, 16)
            def gat2(g, buf=buf, j=j):
                g.indirect_dma_start(out=g2[:, buf, :], out_offset=None, in_=eoutf[:, :],
                                     in_offset=IndirectOffsetOnAxis(r2u32[:, j:j + 1], 0)).then_inc(SEMS["ga%d" % buf], 16)
            P.on("gpsimd", gat2)
            P.inc(ges, 16)
            gac[buf] += 1
            waits("vector", **{ges: 32 * gac[buf], ("xq%d" % buf): 16 * xqc[buf]})
            if j >= 4:
                wait("vector", "oq%d" % buf, stT[j - 4])
            v_op(lambda v, buf=buf, j=j: v.tensor_scalar_mul(out=outt[:, buf, :], in0=g1[:, buf, :], scalar1=w1all[:, j:j + 1]))
            v_op(lambda v, buf=buf, j=j: v.scalar_tensor_tensor(out=outt[:, buf, :], in0=g2[:, buf, :], scalar=w2all[:, j:j + 1], in1=outt[:, buf, :], op0=ALU.mult, op1=ALU.add))
            v_op(lambda v, buf=buf: v.tensor_mul(out=outt[:, buf, :], in0=outt[:, buf, :], in1=gbt[:, :]))
            v_op(lambda v, buf=buf: v.tensor_add(out=outt[:, buf, :], in0=outt[:, buf, :], in1=xt[:, buf, :]))
            cmbT[j] = P.val("vm")
            wait("sync", "vm", cmbT[j])
            s_dma(out_o[j * 128:(j + 1) * 128, :], outt[:, buf, :], sem="oq%d" % buf)
            oqc[buf] += 1
            stT[j] = 16 * oqc[buf]
            if j + 4 < NT:
                s_dma(xt[:, buf, :], xpp[(j + 4) * 128:(j + 5) * 128, :], sem="xq%d" % buf)
                xqc[buf] += 1

        for s_ in ("oq0", "oq1", "oq2", "oq3", "sq", "wq0", "xq0", "xq1", "xq2", "xq3"):
            wait("sync", s_, P.val(s_))
        for s_ in ("ga0", "ga1", "ga2", "ga3"):
            wait("gpsimd", s_, P.val(s_))

        @block.sync
        def _(s):
            P.run("sync", s)

        @block.tensor
        def _(t):
            P.run("tensor", t)

        @block.vector
        def _(v):
            P.run("vector", v)

        @block.scalar
        def _(s):
            P.run("scalar", s)

        @block.gpsimd
        def _(g):
            P.run("gpsimd", g)

    return nc


_CACHE = {}


def _get_programs():
    if "nc1" not in _CACHE:
        _CACHE["nc1"] = build_l1()
        _CACHE["nc2"] = build_l2()
        _CACHE["nc3"] = build_l3()
        _CACHE["c1"] = [host_consts_l1(b) for b in range(8)]
        _CACHE["c2"] = [host_consts_l2(e) for e in range(8)]
        _CACHE["c3"] = [host_consts_l3(b) for b in range(8)]
    return _CACHE


def kernel(L0_attn_Z, L0_attn_decay, L0_attn_W, L0_attn_b, L0_attn_gamma,
           L0_attn_beta, L0_ffn_router_w, L0_ffn_router_b, L0_ffn_experts_inter,
           L0_ffn_experts_out, L0_c, L0_wmod):
    C = _get_programs()
    x = np.ascontiguousarray(np.asarray(L0_attn_Z, dtype=np.float32))
    W = np.ascontiguousarray(np.asarray(L0_attn_W, dtype=np.float32))
    battn = np.ascontiguousarray(np.asarray(L0_attn_b, dtype=np.float32)).reshape(1, D)
    gamma = np.ascontiguousarray(np.asarray(L0_attn_gamma, dtype=np.float32)).reshape(1, D)
    beta = np.ascontiguousarray(np.asarray(L0_attn_beta, dtype=np.float32)).reshape(1, D)
    decay = np.ascontiguousarray(np.asarray(L0_attn_decay, dtype=np.float32)).reshape(1, 1)
    rw = np.ascontiguousarray(np.asarray(L0_ffn_router_w, dtype=np.float32))
    rb = np.ascontiguousarray(np.asarray(L0_ffn_router_b, dtype=np.float32)).reshape(1, 8)
    wi_all = np.asarray(L0_ffn_experts_inter, dtype=np.float32)
    wo_all = np.asarray(L0_ffn_experts_out, dtype=np.float32)
    cfull = np.ascontiguousarray(np.asarray(L0_c, dtype=np.float32))
    wmod = np.ascontiguousarray(np.asarray(L0_wmod, dtype=np.float32))

    in1 = []
    for b in range(8):
        m = dict(C["c1"][b])
        m["x_in"] = np.ascontiguousarray(x[b])
        m["W_in"] = W
        m["battn"] = battn
        m["gamma"] = gamma
        m["beta"] = beta
        m["decay"] = decay
        m["rw"] = rw
        m["rb"] = rb
        m["cfull"] = cfull
        m["wmod"] = wmod
        in1.append(m)
    r1 = run_bass_kernel_spmd(C["nc1"], in1, core_ids=list(range(8))).results

    h2_full = np.concatenate([np.asarray(r1[b]["h2b"]) for b in range(8)], axis=0)
    h2a = np.ascontiguousarray(h2_full[:, 0:512])
    h2b_ = np.ascontiguousarray(h2_full[:, 512:1024])
    lists_all = np.stack([np.asarray(r1[b]["lists"]).reshape(7, LSTRIDE) for b in range(8)])
    counts_all = np.stack([np.asarray(r1[b]["counts"]).reshape(7) for b in range(8)])

    in2 = []
    for k in range(8):
        e = k if k < 7 else 0
        m = dict(C["c2"][k])
        m["h2fa"] = h2a
        m["h2fb"] = h2b_
        m["mylists"] = np.ascontiguousarray(lists_all[:, e, :])
        m["counts_e"] = np.ascontiguousarray(counts_all[:, e:e + 1])
        m["wi"] = np.ascontiguousarray(wi_all[e])
        m["wo"] = np.ascontiguousarray(wo_all[e])
        in2.append(m)
    r2 = run_bass_kernel_spmd(C["nc2"], in2, core_ids=list(range(8))).results
    eoutf = np.concatenate([np.asarray(r2[k]["eout"]) for k in range(7)], axis=0)

    in3 = []
    for b in range(8):
        m = dict(C["c3"][b])
        m["xpp"] = np.asarray(r1[b]["xpp"])
        m["wb"] = np.asarray(r1[b]["wout"])
        m["eoutf"] = eoutf
        m["countsf"] = np.ascontiguousarray(counts_all)
        m["gbrow"] = np.asarray(r1[b]["gbrow"])
        in3.append(m)
    r3 = run_bass_kernel_spmd(C["nc3"], in3, core_ids=list(range(8))).results
    out = np.stack([np.asarray(r3[b]["out"]) for b in range(8)], axis=0)
    return out.astype(np.float32, copy=False)


# version 11
# speedup vs baseline: 1.5803x; 1.0453x over previous
import dataclasses
import numpy as np
import ml_dtypes
import concourse.bass as bass
import concourse.mybir as mybir
from concourse.bass_core_rust import IndirectOffsetOnAxis
from concourse.bass_utils import run_bass_kernel_spmd
from contextlib import ExitStack

F32 = mybir.dt.float32
BF16 = mybir.dt.bfloat16
U32 = mybir.dt.uint32
ALU = mybir.AluOpType
AF = mybir.ActivationFunctionType
AX = mybir.AxisListType

NTOK = 4096
D = 1024
NT = NTOK // 128
NDC = D // 128
LSTRIDE = 4352
EPS = 1e-5
NTOKG = 32768
I = 4096
CAP = 4096
NCC = CAP // 512
GPAD = 8448


def bcast_free(ap, n):
    return dataclasses.replace(ap, ap=type(ap.ap)([list(ap.ap[0]), [0, n]]))


def d3(ap, dims):
    return dataclasses.replace(ap, ap=type(ap.ap)([list(ap.ap[0])] + [list(d) for d in dims]))


class Plan:
    def __init__(self):
        self.items = []
        self.ctr = {}

    def on(self, eng, fn):
        self.items.append((eng, fn))

    def inc(self, name, k):
        self.ctr[name] = self.ctr.get(name, 0) + k
        return self.ctr[name]

    def val(self, name):
        return self.ctr.get(name, 0)

    def run(self, eng_name, eng):
        for e, fn in self.items:
            if e == eng_name:
                fn(eng)


def host_consts_l1(core_id):
    import ml_dtypes
    c = {}
    c["ident_f32"] = np.eye(128, dtype=np.float32)
    c["ident_bf16"] = np.eye(128, dtype=np.float32).astype(ml_dtypes.bfloat16)
    c["triu_s"] = np.triu(np.ones((128, 128), np.float32), k=1)
    c["iota_row"] = np.tile(np.arange(128, dtype=np.float32)[None, :], (128, 1))
    c["iotaP"] = np.arange(128, dtype=np.float32).reshape(128, 1)
    c["gidbase"] = np.full((128, 1), core_id * NTOK, np.float32)
    c["e4352"] = (np.arange(7, dtype=np.float32) * LSTRIDE).reshape(7, 1)
    c["onesrow"] = np.ones((1, 128), np.float32)
    c["onescol"] = np.ones((128, 1), np.float32)
    c["selb"] = np.eye(8, dtype=np.float32)[:, core_id:core_id + 1].copy()
    return c


def build_l1():
    nc = bass.Bass("TRN2", target_bir_lowering=False, debug=False)
    x_in = nc.dram_tensor("x_in", [NTOK, D], F32, kind="ExternalInput").ap()
    W_in = nc.dram_tensor("W_in", [D, D], F32, kind="ExternalInput").ap()
    battn = nc.dram_tensor("battn", [1, D], F32, kind="ExternalInput").ap()
    gamma = nc.dram_tensor("gamma", [1, D], F32, kind="ExternalInput").ap()
    beta = nc.dram_tensor("beta", [1, D], F32, kind="ExternalInput").ap()
    decay = nc.dram_tensor("decay", [1, 1], F32, kind="ExternalInput").ap()
    rw = nc.dram_tensor("rw", [8, D], F32, kind="ExternalInput").ap()
    rb = nc.dram_tensor("rb", [1, 8], F32, kind="ExternalInput").ap()
    cfull = nc.dram_tensor("cfull", [8, D], F32, kind="ExternalInput").ap()
    wmod = nc.dram_tensor("wmod", [D, 6 * D], F32, kind="ExternalInput").ap()
    ident_f32 = nc.dram_tensor("ident_f32", [128, 128], F32, kind="ExternalInput").ap()
    ident_bf16 = nc.dram_tensor("ident_bf16", [128, 128], BF16, kind="ExternalInput").ap()
    triu_s = nc.dram_tensor("triu_s", [128, 128], F32, kind="ExternalInput").ap()
    iota_row = nc.dram_tensor("iota_row", [128, 128], F32, kind="ExternalInput").ap()
    iotaP = nc.dram_tensor("iotaP", [128, 1], F32, kind="ExternalInput").ap()
    gidbase = nc.dram_tensor("gidbase", [128, 1], F32, kind="ExternalInput").ap()
    e4352 = nc.dram_tensor("e4352", [7, 1], F32, kind="ExternalInput").ap()
    onesrow = nc.dram_tensor("onesrow", [1, 128], F32, kind="ExternalInput").ap()
    onescol = nc.dram_tensor("onescol", [128, 1], F32, kind="ExternalInput").ap()
    selb = nc.dram_tensor("selb", [8, 1], F32, kind="ExternalInput").ap()
    xpp_o = nc.dram_tensor("xpp", [NTOK, D], F32, kind="ExternalOutput").ap()
    h2b_o = nc.dram_tensor("h2b", [NTOK, D], F32, kind="ExternalOutput").ap()
    w_o = nc.dram_tensor("wout", [NTOK, 8], F32, kind="ExternalOutput").ap()
    lists_o = nc.dram_tensor("lists", [7 * LSTRIDE, 1], F32, kind="ExternalOutput").ap()
    counts_o = nc.dram_tensor("counts", [7, 1], F32, kind="ExternalOutput").ap()
    gbrow_o = nc.dram_tensor("gbrow", [1, D], F32, kind="ExternalOutput").ap()
    ycs_o = nc.dram_tensor("ycs", [NTOK, D], BF16, kind="ExternalOutput").ap()
    gbh2_o = nc.dram_tensor("gbh2", [NTOK, D], F32, kind="ExternalOutput").ap()

    P = Plan()
    stack = ExitStack()
    with stack:
        idf = stack.enter_context(nc.sbuf_tensor("cid_f", [128, 128], F32))
        idb = stack.enter_context(nc.sbuf_tensor("cid_b", [128, 128], BF16))
        ctriu = stack.enter_context(nc.sbuf_tensor("ctriu", [128, 128], F32))
        ciota = stack.enter_context(nc.sbuf_tensor("ciota", [128, 128], F32))
        ciop = stack.enter_context(nc.sbuf_tensor("ciop", [128, 1], F32))
        cgidb = stack.enter_context(nc.sbuf_tensor("cgidb", [128, 1], F32))
        ce4352 = stack.enter_context(nc.sbuf_tensor("ce4352", [7, 1], F32))
        conesr = stack.enter_context(nc.sbuf_tensor("conesr", [1, 128], F32))
        conesc = stack.enter_context(nc.sbuf_tensor("conesc", [128, 1], F32))
        cselb = stack.enter_context(nc.sbuf_tensor("cselb", [8, 1], F32))
        xnT = stack.enter_context(nc.sbuf_tensor("xnT", [128, NDC, NTOK], BF16))
        WTb = stack.enter_context(nc.sbuf_tensor("WTb", [128, NDC, D], BF16))
        Gt = stack.enter_context(nc.sbuf_tensor("Gt", [128, D], F32))
        Bt = stack.enter_context(nc.sbuf_tensor("Bt", [128, D], F32))
        cb1t = stack.enter_context(nc.sbuf_tensor("cb1t", [128, D], F32))
        sbt = stack.enter_context(nc.sbuf_tensor("sbt", [128, D], F32))
        gbt = stack.enter_context(nc.sbuf_tensor("gbt", [128, D], F32))
        rbt = stack.enter_context(nc.sbuf_tensor("rbt", [128, 8], F32))
        rwTb = stack.enter_context(nc.sbuf_tensor("rwTb", [128, NDC, 8], BF16))
        g2 = stack.enter_context(nc.sbuf_tensor("g2", [2, NTOK], BF16))
        vb2 = stack.enter_context(nc.sbuf_tensor("vb2", [2, D], BF16))
        stat_mu = stack.enter_context(nc.sbuf_tensor("stat_mu", [128, NT], F32))
        stat_rs = stack.enter_context(nc.sbuf_tensor("stat_rs", [128, NT], F32))
        stat_s1 = stack.enter_context(nc.sbuf_tensor("stat_s1", [128, NT], F32))
        stat_sq = stack.enter_context(nc.sbuf_tensor("stat_sq", [128, NT], F32))
        gid0 = stack.enter_context(nc.sbuf_tensor("gid0", [128, 1], F32))
        acol = stack.enter_context(nc.sbuf_tensor("acol", [128, 1], F32))
        asc = stack.enter_context(nc.sbuf_tensor("asc", [1, 4], F32))
        psY0 = stack.enter_context(nc.psum_tensor("psY0", [128, 512], F32))
        psY1 = stack.enter_context(nc.psum_tensor("psY1", [128, 512], F32))
        psY2 = stack.enter_context(nc.psum_tensor("psY2", [128, 512], F32))
        psY3 = stack.enter_context(nc.psum_tensor("psY3", [128, 512], F32))
        psTB0 = stack.enter_context(nc.psum_tensor("psTB0", [128, 128], BF16))
        psTB1 = stack.enter_context(nc.psum_tensor("psTB1", [128, 128], BF16))
        psOf = stack.enter_context(nc.psum_tensor("psOf", [7, 1], F32))
        psRT = stack.enter_context(nc.psum_tensor("psRT", [128, 512], F32))
        SQ = stack.enter_context(nc.semaphore("sq"))
        GQ = stack.enter_context(nc.semaphore("gq"))
        SC = stack.enter_context(nc.semaphore("sc"))
        TM = stack.enter_context(nc.semaphore("tm"))
        VM = stack.enter_context(nc.semaphore("vm"))
        SM = stack.enter_context(nc.semaphore("sm"))
        XQ0 = stack.enter_context(nc.semaphore("xq0"))
        XQ1 = stack.enter_context(nc.semaphore("xq1"))
        WQ0 = stack.enter_context(nc.semaphore("wq0"))
        WQ1 = stack.enter_context(nc.semaphore("wq1"))
        SOX0 = stack.enter_context(nc.semaphore("sox0"))
        SOX1 = stack.enter_context(nc.semaphore("sox1"))
        SOW0 = stack.enter_context(nc.semaphore("sow0"))
        SOW1 = stack.enter_context(nc.semaphore("sow1"))
        SOH0 = stack.enter_context(nc.semaphore("soh0"))
        SOH1 = stack.enter_context(nc.semaphore("soh1"))
        LZ = stack.enter_context(nc.semaphore("lz"))
        WSB = stack.enter_context(nc.semaphore("wsb"))
        WM0 = stack.enter_context(nc.semaphore("wm0"))
        WM1 = stack.enter_context(nc.semaphore("wm1"))
        WM2 = stack.enter_context(nc.semaphore("wm2"))
        WM3 = stack.enter_context(nc.semaphore("wm3"))
        block = stack.enter_context(nc.Block())

        SEMS = {"sq": SQ, "gq": GQ, "sc": SC, "tm": TM, "vm": VM, "sm": SM,
                "xq0": XQ0, "xq1": XQ1, "wq0": WQ0, "wq1": WQ1,
                "sox0": SOX0, "sox1": SOX1, "sow0": SOW0, "sow1": SOW1,
                "soh0": SOH0, "soh1": SOH1, "lz": LZ, "wsb": WSB,
                "wm0": WM0, "wm1": WM1, "wm2": WM2, "wm3": WM3}

        def s_dma(out, in_, sem="sq"):
            S = SEMS[sem]
            def f(s, out=out, in_=in_, S=S):
                s.dma_start(out=out, in_=in_).then_inc(S, 16)
            P.on("sync", f)
            return P.inc(sem, 16)

        def s_barrier(sem="sq"):
            wait("sync", sem, P.val(sem))
            return P.val(sem)

        def g_dma(out, in_, sem=None):
            S = GQ if sem is None else SEMS[sem]
            nm = "gq" if sem is None else sem
            def f(g, out=out, in_=in_, S=S):
                g.dma_start(out=out, in_=in_).then_inc(S, 16)
            P.on("gpsimd", f)
            return P.inc(nm, 16)

        def _eng_op(eng, sem_name, fn, tick):
            prev = P.val(sem_name)
            S = SEMS[sem_name]
            if prev > 0:
                def w(e, S=S, prev=prev):
                    e.wait_ge(S, prev)
                P.on(eng, w)
            if tick:
                n = P.inc(sem_name, 1)
                def f(e, fn=fn, S=S):
                    fn(e).then_inc(S, 1)
            else:
                n = prev
                def f(e, fn=fn):
                    fn(e)
            P.on(eng, f)
            return n

        def t_op(fn, tick=True):
            return _eng_op("tensor", "tm", fn, tick)

        def v_op(fn, tick=True):
            return _eng_op("vector", "vm", fn, tick)

        def sc_op(fn, tick=True):
            return _eng_op("scalar", "sm", fn, tick)

        def wait(eng_name, sem_name, val):
            if val <= 0:
                return
            sem = SEMS[sem_name]
            def f(e, sem=sem, val=val):
                e.wait_ge(sem, val)
            P.on(eng_name, f)

        def waits(eng_name, **kw):
            for k, v in kw.items():
                wait(eng_name, k, v)

        prep = ExitStack()
        msa = prep.enter_context(nc.sbuf_tensor("msa", [1, D], F32))
        mca = prep.enter_context(nc.sbuf_tensor("mca", [1, D], F32))
        mga = prep.enter_context(nc.sbuf_tensor("mga", [1, D], F32))
        msb = prep.enter_context(nc.sbuf_tensor("msb", [1, D], F32))
        mcb = prep.enter_context(nc.sbuf_tensor("mcb", [1, D], F32))
        mgb = prep.enter_context(nc.sbuf_tensor("mgb", [1, D], F32))
        r0 = prep.enter_context(nc.sbuf_tensor("r0", [1, D], F32))
        r1 = prep.enter_context(nc.sbuf_tensor("r1", [1, D], F32))
        modst = prep.enter_context(nc.sbuf_tensor("modst", [8, 512], F32))
        cstage = prep.enter_context(nc.sbuf_tensor("cstage", [8, D], F32))
        scT = prep.enter_context(nc.sbuf_tensor("scT", [128, 8, 8], BF16))
        cstb = prep.enter_context(nc.sbuf_tensor("cstb", [8, D], BF16))
        wmt = prep.enter_context(nc.sbuf_tensor("wmt", [128, 4, 512], BF16))
        Wsb = prep.enter_context(nc.sbuf_tensor("Wsb", [128, NDC, D], F32))
        wtf2 = prep.enter_context(nc.sbuf_tensor("wtf2", [128, 2, D], F32))
        safm = prep.enter_context(nc.sbuf_tensor("safm", [128, 8], F32))
        ca1fm = prep.enter_context(nc.sbuf_tensor("ca1fm", [128, 8], F32))
        wbar = prep.enter_context(nc.sbuf_tensor("wbar", [128, 8], F32))
        vrow = prep.enter_context(nc.sbuf_tensor("vrow", [1, D], F32))
        bcrow = prep.enter_context(nc.sbuf_tensor("bcrow", [1, D], F32))
        smallsc = prep.enter_context(nc.sbuf_tensor("smallsc", [8, 8], F32))

        for dst, src in [(idf, ident_f32), (idb, ident_bf16), (ctriu, triu_s),
                         (ciota, iota_row), (ciop, iotaP), (cgidb, gidbase),
                         (ce4352, e4352), (conesr, onesrow), (conesc, onescol),
                         (cselb, selb)]:
            sl = tuple(slice(None) for _ in range(len(dst.shape)))
            s_dma(dst[sl], src[sl])
        s_dma(cstage[0:8, :], cfull[:, :])
        s_dma(r0[:, :], gamma[:, :])
        s_dma(r1[:, :], beta[:, :])
        s_dma(asc[0:1, 0:1], decay[:, :])
        s_dma(smallsc[0:1, 0:8], rb[:, :])
        c0 = s_barrier()
        for pc in range(NDC):
            ldW = s_dma(Wsb[:, pc, :], W_in[pc * 128:(pc + 1) * 128, :], sem="wsb")

        waits("scalar", sq=c0)
        sc_op(lambda s: s.activation(out=cstage[0:8, :], in_=cstage[0:8, :], func=AF.Silu))
        tA = P.val("sm")
        waits("vector", sm=tA)
        v_op(lambda v: v.tensor_copy(out=cstb[0:8, :], in_=cstage[0:8, :]))
        waits("tensor", vm=P.val("vm"))
        scTev = {}
        for dc in range(NDC):
            pst = psTB0 if dc % 2 == 0 else psTB1
            if dc >= 2:
                wait("tensor", "vm", scTev[dc - 2])
            t_op(lambda t, dc=dc, pst=pst: t.transpose(pst[0:128, 0:8], cstb[0:8, dc * 128:(dc + 1) * 128], idb[0:8, 0:8]))
            waits("vector", tm=P.val("tm"))
            scTev[dc] = v_op(lambda v, dc=dc, pst=pst: v.tensor_copy(out=scT[:, dc, :], in_=pst[0:128, 0:8]))
        vSct = P.val("vm")
        waits("tensor", vm=vSct)
        rows6 = [msa, mca, mga, msb, mcb, mgb]
        mmticks = []
        wmsems = ["wm0", "wm1", "wm2", "wm3"]
        for nch in range(12):
            waits("tensor", vm=P.val("vm"))
            for dc in range(NDC):
                it = nch * NDC + dc
                buf = it % 4
                if it >= 4:
                    wait("gpsimd", "tm", mmticks[it - 4])
                ldw = g_dma(wmt[:, buf, :], wmod[dc * 128:(dc + 1) * 128, nch * 512:(nch + 1) * 512], sem=wmsems[buf])
                waits("tensor", **{wmsems[buf]: ldw})
                mmticks.append(t_op(lambda t, dc=dc, buf=buf: t.matmul(
                    psY0[0:8, :], scT[:, dc, :], wmt[:, buf, :],
                    start=(dc == 0), stop=(dc == NDC - 1))))
            waits("vector", tm=P.val("tm"))
            v_op(lambda v: v.tensor_copy(out=modst[0:8, :], in_=psY0[0:8, :]))
            waits("tensor", vm=P.val("vm"))
            t_op(lambda t: t.matmul(psRT[0:1, 0:512], cselb[:, :], modst[0:8, :], start=True, stop=True))
            waits("vector", tm=P.val("tm"))
            i, half = nch // 2, nch % 2
            v_op(lambda v, i=i, half=half: v.tensor_copy(
                out=rows6[i][0:1, half * 512:half * 512 + 512], in_=psRT[0:1, 0:512]))
        for gv in (mga, mgb):
            v_op(lambda v, gv=gv: v.tensor_scalar(out=bcrow[:, :], in0=gv[0:1, :], scalar1=0.0, scalar2=None, op0=ALU.is_ge))
            v_op(lambda v: v.tensor_scalar(out=bcrow[:, :], in0=bcrow[:, :], scalar1=2.0, scalar2=-1.0, op0=ALU.mult, op1=ALU.add))
            waits("scalar", vm=P.val("vm"))
            sc_op(lambda s, gv=gv: s.activation(out=vrow[:, :], in_=gv[0:1, :], func=AF.Abs))
            waits("vector", sm=P.val("sm"))
            v_op(lambda v: v.tensor_scalar_max(out=vrow[:, :], in0=vrow[:, :], scalar1=0.25))
            v_op(lambda v, gv=gv: v.tensor_mul(out=gv[0:1, :], in0=bcrow[:, :], in1=vrow[:, :]))
        v_op(lambda v: v.tensor_mul(out=r0[:, :], in0=r0[:, :], in1=mga[0:1, :]))
        v_op(lambda v: v.tensor_mul(out=r1[:, :], in0=r1[:, :], in1=mga[0:1, :]))
        v_op(lambda v: v.tensor_scalar_add(out=mcb[0:1, :], in0=mcb[0:1, :], scalar1=1.0))
        vClamp = P.val("vm")
        wait("sync", "vm", vClamp)
        s_dma(gbrow_o[:, :], mgb[0:1, :])
        waits("tensor", vm=vClamp)
        for row, tile in [(r0, Gt), (r1, Bt), (mcb, cb1t), (msb, sbt), (mgb, gbt)]:
            for half in range(2):
                waits("tensor", vm=P.val("vm"))
                t_op(lambda t, row=row, half=half: t.matmul(
                    psY1[:, :], conesr[:, :], row[0:1, half * 512:half * 512 + 512],
                    start=True, stop=True))
                waits("vector", tm=P.val("tm"))
                v_op(lambda v, tile=tile, half=half: v.tensor_copy(
                    out=tile[:, half * 512:half * 512 + 512], in_=psY1[:, :]))
        waits("tensor", vm=P.val("vm"))
        t_op(lambda t: t.matmul(psRT[:, 0:8], conesr[:, :], smallsc[0:1, 0:8], start=True, stop=True))
        waits("vector", tm=P.val("tm"))
        v_op(lambda v: v.tensor_copy(out=rbt[:, :], in_=psRT[:, 0:8]))

        waits("scalar", sq=c0)
        sc_op(lambda s: s.activation(out=asc[0:1, 1:2], in_=asc[0:1, 0:1], func=AF.Sigmoid))
        sA = P.val("sm")
        waits("vector", sm=sA)
        v_op(lambda v: v.memset(asc[0:1, 2:3], 1.0))
        v_op(lambda v: v.tensor_tensor_scan(
            g2[0:1, :], bcast_free(asc[0:1, 1:2], NTOK), bcast_free(asc[0:1, 2:3], NTOK),
            0.0, ALU.mult, ALU.add))
        vG = P.val("vm")
        wait("gpsimd", "vm", vG)
        for seg in range(NTOK // 128):
            g_dma(g2[1:2, seg * 128:(seg + 1) * 128], conesr[0:1, :])
        waits("tensor", vm=vG)
        waits("tensor", vm=P.val("vm"))
        t_op(lambda t: t.matmul(psRT[:, 8:9], conesr[:, :], asc[0:1, 1:2], start=True, stop=True))
        waits("vector", tm=P.val("tm"))
        v_op(lambda v: v.tensor_copy(out=acol[:, :], in_=psRT[:, 8:9]))
        v_op(lambda v: v.tensor_add(out=gid0[:, :], in0=ciop[:, :], in1=cgidb[:, :]))

        wait("sync", "vm", vSct)
        ldrw = s_dma(cstage[0:8, :], rw[:, :])
        s_barrier()
        waits("tensor", sq=ldrw, vm=vClamp)
        for dc in range(NDC):
            waits("tensor", vm=P.val("vm"))
            t_op(lambda t, dc=dc: t.transpose(psRT[0:128, 24:32], cstage[0:8, dc * 128:(dc + 1) * 128], idf[0:8, 0:8]))
            waits("vector", tm=P.val("tm"))
            v_op(lambda v, dc=dc: v.tensor_copy(out=rwTb[:, dc, :], in_=psRT[0:128, 24:32]))
            waits("tensor", vm=P.val("vm"))
            t_op(lambda t, dc=dc: t.transpose(psRT[0:128, 16:17], msa[0:1, dc * 128:(dc + 1) * 128], idf[0:1, 0:1]))
            waits("vector", tm=P.val("tm"))
            v_op(lambda v, dc=dc: v.tensor_copy(out=safm[:, dc:dc + 1], in_=psRT[0:128, 16:17]))
            waits("tensor", vm=P.val("vm"))
            t_op(lambda t, dc=dc: t.transpose(psRT[0:128, 17:18], mca[0:1, dc * 128:(dc + 1) * 128], idf[0:1, 0:1]))
            waits("vector", tm=P.val("tm"))
            v_op(lambda v, dc=dc: v.tensor_copy(out=ca1fm[:, dc:dc + 1], in_=psRT[0:128, 17:18]))
        v_op(lambda v: v.tensor_scalar_add(out=ca1fm[:, :], in0=ca1fm[:, :], scalar1=1.0))
        vFM = P.val("vm")
        waits("tensor", vm=vFM)
        evacS = {}
        vchain = {}
        for dcx in range(NDC + 1):
            if dcx < NDC:
                dc = dcx
                for pc in range(NDC):
                    ti = dc * 8 + pc
                    bank = psY2 if ti % 2 == 0 else psY3
                    if ti == 0:
                        waits("tensor", wsb=ldW)
                    if ti >= 2:
                        wait("tensor", "sm", evacS[ti - 2])
                    tt = t_op(lambda t, dc=dc, pc=pc, bank=bank: t.transpose(
                        bank[:, 0:128], Wsb[:, pc, dc * 128:(dc + 1) * 128], idf[:, :]))
                    if pc == 0 and dc >= 2:
                        wait("scalar", "vm", vchain[dc - 2])
                    wait("scalar", "tm", tt)
                    evacS[ti] = sc_op(lambda s, dc=dc, pc=pc, bank=bank: s.activation(
                        out=wtf2[:, dc % 2, pc * 128:(pc + 1) * 128], in_=bank[:, 0:128],
                        func=AF.Copy))
            if dcx >= 1:
                dcp = dcx - 1
                wtfv = wtf2[:, dcp % 2, :]
                wait("tensor", "sm", evacS[dcp * 8 + 7])
                t_op(lambda t, dcp=dcp, wtfv=wtfv: t.matmul(
                    psRT[0:1, 0:512], safm[:, dcp:dcp + 1], wtfv[:, 0:512],
                    start=(dcp == 0), stop=(dcp == NDC - 1), skip_group_check=True))
                t_op(lambda t, dcp=dcp, wtfv=wtfv: t.matmul(
                    psY1[0:1, 0:512], safm[:, dcp:dcp + 1], wtfv[:, 512:1024],
                    start=(dcp == 0), stop=(dcp == NDC - 1), skip_group_check=True))
                waits("vector", tm=P.val("tm"))
                v_op(lambda v, dcp=dcp, wtfv=wtfv: v.tensor_scalar_mul(
                    out=wtfv[:, :], in0=wtfv[:, :], scalar1=ca1fm[:, dcp:dcp + 1]))
                v_op(lambda v, dcp=dcp, wtfv=wtfv: v.reduce_sum(wbar[:, dcp:dcp + 1], wtfv[:, :], axis=AX.X))
                v_op(lambda v, dcp=dcp: v.tensor_scalar_mul(
                    out=wbar[:, dcp:dcp + 1], in0=wbar[:, dcp:dcp + 1], scalar1=1.0 / D))
                vchain[dcp] = v_op(lambda v, dcp=dcp, wtfv=wtfv: v.tensor_scalar(
                    out=WTb[:, dcp, :], in0=wtfv[:, :], scalar1=wbar[:, dcp:dcp + 1],
                    scalar2=None, op0=ALU.subtract))
        waits("vector", tm=P.val("tm"))
        v_op(lambda v: v.tensor_copy(out=vrow[0:1, 0:512], in_=psRT[0:1, 0:512]))
        v_op(lambda v: v.tensor_copy(out=vrow[0:1, 512:1024], in_=psY1[0:1, 0:512]))
        ldb2 = s_dma(bcrow[:, :], battn[:, :])
        s_barrier()
        v_op(lambda v: v.reduce_sum(smallsc[0:1, 1:2], vrow[:, :], axis=AX.X))
        v_op(lambda v: v.tensor_scalar_mul(out=smallsc[0:1, 1:2], in0=smallsc[0:1, 1:2], scalar1=1.0 / D))
        v_op(lambda v: v.tensor_scalar(out=vrow[:, :], in0=vrow[:, :], scalar1=smallsc[0:1, 1:2], scalar2=None, op0=ALU.subtract))
        waits("vector", sq=ldb2)
        v_op(lambda v: v.reduce_sum(smallsc[0:1, 2:3], bcrow[:, :], axis=AX.X))
        v_op(lambda v: v.tensor_scalar_mul(out=smallsc[0:1, 2:3], in0=smallsc[0:1, 2:3], scalar1=1.0 / D))
        v_op(lambda v: v.tensor_scalar(out=bcrow[:, :], in0=bcrow[:, :], scalar1=smallsc[0:1, 2:3], scalar2=None, op0=ALU.subtract))
        vVB = P.val("vm")
        wait("gpsimd", "vm", vVB)
        g_dma(vb2[0:1, :], vrow[0:1, :])
        g_dma(vb2[1:2, :], bcrow[0:1, :])

        bar = dict(sq=P.val("sq"), gq=P.val("gq"), tm=P.val("tm"),
                   vm=P.val("vm"), sm=P.val("sm"))
        for eng in ("sync", "gpsimd", "tensor", "vector", "scalar"):
            waits(eng, **bar)
        prep.close()

        loop = ExitStack()
        xt = loop.enter_context(nc.sbuf_tensor("xt", [128, 2, D], F32))
        yct = loop.enter_context(nc.sbuf_tensor("yct", [128, 2, D], BF16))
        xnbf = loop.enter_context(nc.sbuf_tensor("xnbf", [128, D], BF16))
        sqscr = loop.enter_context(nc.sbuf_tensor("sqscr", [128, D], F32))
        acscr = loop.enter_context(nc.sbuf_tensor("acscr", [128, D], F32))
        usb = loop.enter_context(nc.sbuf_tensor("usb", [128, D], F32))
        xpt = loop.enter_context(nc.sbuf_tensor("xpt", [128, 2, D], F32))
        h2t = loop.enter_context(nc.sbuf_tensor("h2t", [128, 2, D], F32))
        gbscr = loop.enter_context(nc.sbuf_tensor("gbscr", [128, 2, D], F32))
        h2bf = loop.enter_context(nc.sbuf_tensor("h2bf", [128, D], BF16))
        h2fm = loop.enter_context(nc.sbuf_tensor("h2fm", [128, 2, 128], BF16))
        lgfm = loop.enter_context(nc.sbuf_tensor("lgfm", [8, 128], F32))
        lgall = loop.enter_context(nc.sbuf_tensor("lgall", [128, NT, 8], F32))
        statYa = loop.enter_context(nc.sbuf_tensor("statYa", [128, NT], F32))
        statYb = loop.enter_context(nc.sbuf_tensor("statYb", [128, NT], F32))
        strsy = loop.enter_context(nc.sbuf_tensor("strsy", [128, NT], F32))
        st2s = loop.enter_context(nc.sbuf_tensor("st2s", [128, NT], F32))
        st2q = loop.enter_context(nc.sbuf_tensor("st2q", [128, NT], F32))
        stmu2 = loop.enter_context(nc.sbuf_tensor("stmu2", [128, NT], F32))
        strs2 = loop.enter_context(nc.sbuf_tensor("strs2", [128, NT], F32))
        tv1 = loop.enter_context(nc.sbuf_tensor("tv1", [128, 8], F32))
        tv2 = loop.enter_context(nc.sbuf_tensor("tv2", [128, 8], F32))
        m1c = loop.enter_context(nc.sbuf_tensor("m1c", [128, 1], F32))
        nm1c = loop.enter_context(nc.sbuf_tensor("nm1c", [128, 1], F32))
        m2c = loop.enter_context(nc.sbuf_tensor("m2c", [128, 1], F32))
        rcpc = loop.enter_context(nc.sbuf_tensor("rcpc", [128, 1], F32))
        wtile = loop.enter_context(nc.sbuf_tensor("wtile", [128, 2, 8], F32))
        masksb = loop.enter_context(nc.sbuf_tensor("masksb", [128, 7], F32))
        Psb = loop.enter_context(nc.sbuf_tensor("Psb", [128, 7], F32))
        cmpst = loop.enter_context(nc.sbuf_tensor("cmpst", [128, 7], F32))
        offs_sb = loop.enter_context(nc.sbuf_tensor("offs_sb", [7, 1], F32))
        idxf = loop.enter_context(nc.sbuf_tensor("idxf", [7, 1], F32))
        idxu = loop.enter_context(nc.sbuf_tensor("idxu", [7, 1], U32))
        gidj = loop.enter_context(nc.sbuf_tensor("gidj", [128, 1], F32))
        ohbuf = loop.enter_context(nc.sbuf_tensor("ohbuf", [128, 2, 128], F32))
        cmpT = loop.enter_context(nc.sbuf_tensor("cmpT", [7, 2, 128], F32))
        wfull = loop.enter_context(nc.sbuf_tensor("wfull", [128, NT * 8], F32))
        scrE = loop.enter_context(nc.sbuf_tensor("scrE", [128, NT * 8], F32))
        m1a = loop.enter_context(nc.sbuf_tensor("m1a", [128, NT], F32))
        m2a = loop.enter_context(nc.sbuf_tensor("m2a", [128, NT], F32))
        za = loop.enter_context(nc.sbuf_tensor("za", [128, NT], F32))
        rza = loop.enter_context(nc.sbuf_tensor("rza", [128, NT], F32))
        gidall = loop.enter_context(nc.sbuf_tensor("gidall", [128, NT], F32))
        emb = loop.enter_context(nc.sbuf_tensor("emb", [128, NT * 7], F32))
        rkb = loop.enter_context(nc.sbuf_tensor("rkb", [128, NT * 7], F32))
        cnt7 = loop.enter_context(nc.sbuf_tensor("cnt7", [7, NT], F32))
        run7 = loop.enter_context(nc.sbuf_tensor("run7", [7, NT], F32))
        idxf7 = loop.enter_context(nc.sbuf_tensor("idxf7", [7, NT], F32))
        idxu7 = loop.enter_context(nc.sbuf_tensor("idxu7", [7, NT], U32))
        countf = loop.enter_context(nc.sbuf_tensor("countf", [7, 1], F32))

        vZ = v_op(lambda v: v.memset(acscr[:, :], 0.0))
        wait("sync", "vm", vZ)
        def zl(s):
            s.dma_start(out=lists_o[:, :], in_=acscr[:, 0:238]).then_inc(LZ, 16)
        P.on("sync", zl)
        P.inc("lz", 16)
        wait("vector", "lz", 16)
        wait("gpsimd", "lz", 16)

        sticks, vticks = [], []
        for j in range(NT):
            buf = j % 2
            if j >= 2:
                wait("sync", "sm", sticks[j - 2])
                wait("sync", "vm", vticks[j - 2])
            ld = s_dma(xt[:, buf, :], x_in[j * 128:(j + 1) * 128, :], sem="xq%d" % buf)
            waits("scalar", **{("xq%d" % buf): ld})
            sticks.append(sc_op(lambda s, j=j, buf=buf: s.activation(
                out=sqscr[:, :], in_=xt[:, buf, :], func=AF.Square,
                accum_out=stat_sq[:, j:j + 1])))
            waits("vector", **{("xq%d" % buf): ld})
            vticks.append(v_op(lambda v, j=j, buf=buf: v.tensor_scalar(
                out=acscr[:, :], in0=xt[:, buf, :], scalar1=1.0, scalar2=None,
                op0=ALU.mult, op1=ALU.add, accum_out=stat_s1[:, j:j + 1])))
        waits("vector", sm=P.val("sm"))
        v_op(lambda v: v.tensor_scalar_mul(out=stat_mu[:, :], in0=stat_s1[:, :], scalar1=1.0 / D))
        v_op(lambda v: v.tensor_scalar_mul(out=stat_sq[:, :], in0=stat_sq[:, :], scalar1=1.0 / D))
        v_op(lambda v: v.tensor_mul(out=stat_s1[:, :], in0=stat_mu[:, :], in1=stat_mu[:, :]))
        v_op(lambda v: v.tensor_sub(out=stat_sq[:, :], in0=stat_sq[:, :], in1=stat_s1[:, :]))
        v_op(lambda v: v.tensor_scalar_add(out=stat_sq[:, :], in0=stat_sq[:, :], scalar1=EPS))
        v_op(lambda v: v.reciprocal(out=stat_rs[:, :], in_=stat_sq[:, :]))
        waits("scalar", vm=P.val("vm"))
        sc_op(lambda s: s.activation(out=stat_rs[:, :], in_=stat_rs[:, :], func=AF.Sqrt))
        sStats = P.val("sm")

        waits("vector", sm=sStats)
        wait("sync", "vm", P.val("vm"))
        wait("sync", "sm", P.val("sm"))
        xnticks = []
        trC = [0]
        pstTick = {0: 0, 1: 0}
        for j in range(NT):
            buf = j % 2
            if j >= 2:
                wait("sync", "vm", xnticks[j - 2][0])
                wait("sync", "sm", xnticks[j - 2][1])
            ld = s_dma(xt[:, buf, :], x_in[j * 128:(j + 1) * 128, :], sem="xq%d" % buf)
            waits("vector", **{("xq%d" % buf): ld})
            wait("vector", "tm", trC[0])
            v_op(lambda v, j=j, buf=buf: v.tensor_scalar(
                out=xnbf[:, :], in0=xt[:, buf, :], scalar1=stat_mu[:, j:j + 1],
                scalar2=stat_rs[:, j:j + 1], op0=ALU.subtract, op1=ALU.mult))
            waits("tensor", vm=P.val("vm"))
            for dc in range(NDC):
                pp = dc % 2
                pst = psTB0 if pp == 0 else psTB1
                waits("tensor", sm=pstTick[pp])
                t_op(lambda t, dc=dc, pst=pst: t.transpose(
                    pst[:, :], xnbf[:, dc * 128:(dc + 1) * 128], idb[:, :]))
                waits("scalar", tm=P.val("tm"))
                pstTick[pp] = sc_op(lambda s, j=j, dc=dc, pst=pst: s.copy(
                    out=xnT[:, dc, j * 128:(j + 1) * 128], in_=pst[:, :]))
            trC[0] = P.val("tm")
            xnticks.append((P.val("vm"), P.val("sm")))

        waits("vector", sm=P.val("sm"))
        for dc in range(NDC):
            v_op(lambda v, dc=dc: v.tensor_tensor_scan(
                xnT[:, dc, :], bcast_free(acol[:, 0:1], NTOK),
                xnT[:, dc, :], 0.0, ALU.mult, ALU.add))
        vScan = P.val("vm")

        wait("tensor", "vm", vScan)
        evT, scT4 = [], []
        soh_cnt = [0, 0]
        sow_cnt = [0, 0]
        for j in range(NT):
            pb = j % 2
            psA_ = psY0 if pb == 0 else psY2
            psB_ = psY1 if pb == 0 else psY3
            if j >= 2:
                wait("tensor", "vm", evT[j - 2])
                wait("tensor", "sm", scT4[j - 2])
            for ec in range(2):
                psY = psA_ if ec == 0 else psB_
                for dc in range(NDC):
                    t_op(lambda t, j=j, dc=dc, ec=ec, psY=psY: t.matmul(
                        psY[:, :], xnT[:, dc, j * 128:(j + 1) * 128],
                        WTb[:, dc, ec * 512:(ec + 1) * 512],
                        start=(dc == 0), stop=False), tick=False)
                t_op(lambda t, j=j, ec=ec, psY=psY: t.matmul(
                    psY[:, :], g2[0:2, j * 128:(j + 1) * 128],
                    vb2[0:2, ec * 512:(ec + 1) * 512], start=False, stop=True))
            tY = P.val("tm")
            waits("scalar", tm=tY)
            sc_op(lambda s, j=j, psA_=psA_: s.activation(out=sqscr[:, 0:512], in_=psA_[:, :], func=AF.Square, accum_out=statYa[:, j:j + 1]))
            sc_op(lambda s, j=j, psB_=psB_: s.activation(out=sqscr[:, 512:1024], in_=psB_[:, :], func=AF.Square, accum_out=statYb[:, j:j + 1]))
            scT4.append(P.val("sm"))
            waits("vector", tm=tY, sm=P.val("sm"))
            if j >= 2:
                wait("vector", "soh%d" % pb, 16 * soh_cnt[pb])
            v_op(lambda v, pb=pb, psA_=psA_: v.tensor_copy(out=yct[:, pb, 0:512], in_=psA_[:, :]))
            v_op(lambda v, pb=pb, psB_=psB_: v.tensor_copy(out=yct[:, pb, 512:1024], in_=psB_[:, :]))
            evT.append(P.val("vm"))
            wait("sync", "vm", evT[j])
            s_dma(ycs_o[j * 128:(j + 1) * 128, :], yct[:, pb, :], sem="soh%d" % pb)
            soh_cnt[pb] += 1
        waits("vector", sm=P.val("sm"))
        v_op(lambda v: v.tensor_add(out=statYa[:, :], in0=statYa[:, :], in1=statYb[:, :]))
        v_op(lambda v: v.tensor_scalar(out=statYa[:, :], in0=statYa[:, :], scalar1=1.0 / D, scalar2=EPS, op0=ALU.mult, op1=ALU.add))
        v_op(lambda v: v.reciprocal(out=strsy[:, :], in_=statYa[:, :]))
        waits("scalar", vm=P.val("vm"))
        sc_op(lambda s: s.activation(out=strsy[:, :], in_=strsy[:, :], func=AF.Sqrt))
        sRsy = P.val("sm")

        wait("sync", "vm", P.val("vm"))
        wait("sync", "sm", P.val("sm"))
        wait("sync", "soh0", P.val("soh0"))
        wait("sync", "soh1", P.val("soh1"))
        waits("vector", sm=sRsy)
        p5vT = []
        p5sT = []
        sox_cnt = [0, 0]
        ldx_t = {}
        ldy_t = {}
        def p5_load(j):
            buf = j % 2
            ldx_t[j] = s_dma(xt[:, buf, :], x_in[j * 128:(j + 1) * 128, :], sem="xq%d" % buf)
            ldy_t[j] = s_dma(yct[:, buf, :], ycs_o[j * 128:(j + 1) * 128, :], sem="wq%d" % buf)
        p5_load(0)
        p5_load(1)
        for j in range(NT):
            buf = j % 2
            waits("vector", **{("xq%d" % buf): ldx_t[j], ("wq%d" % buf): ldy_t[j]})
            v_op(lambda v, j=j, buf=buf: v.scalar_tensor_tensor(
                out=usb[:, :], in0=yct[:, buf, :], scalar=strsy[:, j:j + 1],
                in1=Gt[:, :], op0=ALU.mult, op1=ALU.mult))
            if j >= 2:
                wait("vector", "sox%d" % buf, 16 * sox_cnt[buf])
            v_op(lambda v, buf=buf: v.tensor_add(out=xpt[:, buf, :], in0=usb[:, :], in1=xt[:, buf, :]))
            v_op(lambda v, j=j, buf=buf: v.scalar_tensor_tensor(
                out=xpt[:, buf, :], in0=xpt[:, buf, :], scalar=1.0, in1=Bt[:, :],
                op0=ALU.mult, op1=ALU.add, accum_out=st2s[:, j:j + 1]))
            p5vT.append(P.val("vm"))
            waits("scalar", vm=P.val("vm"))
            sc_op(lambda s, j=j, buf=buf: s.activation(
                out=sqscr[:, :], in_=xpt[:, buf, :], func=AF.Square,
                accum_out=st2q[:, j:j + 1]))
            p5sT.append(P.val("sm"))
            wait("sync", "vm", p5vT[j])
            s_dma(xpp_o[j * 128:(j + 1) * 128, :], xpt[:, buf, :], sem="sox%d" % buf)
            sox_cnt[buf] += 1
            if j + 2 < NT:
                wait("sync", "sm", p5sT[j])
                p5_load(j + 2)
        waits("vector", sm=P.val("sm"))
        v_op(lambda v: v.tensor_scalar_mul(out=stmu2[:, :], in0=st2s[:, :], scalar1=1.0 / D))
        v_op(lambda v: v.tensor_scalar_mul(out=st2q[:, :], in0=st2q[:, :], scalar1=1.0 / D))
        v_op(lambda v: v.tensor_mul(out=st2s[:, :], in0=stmu2[:, :], in1=stmu2[:, :]))
        v_op(lambda v: v.tensor_sub(out=st2q[:, :], in0=st2q[:, :], in1=st2s[:, :]))
        v_op(lambda v: v.tensor_scalar_add(out=st2q[:, :], in0=st2q[:, :], scalar1=EPS))
        v_op(lambda v: v.reciprocal(out=strs2[:, :], in_=st2q[:, :]))
        waits("scalar", vm=P.val("vm"))
        sc_op(lambda s: s.activation(out=strs2[:, :], in_=strs2[:, :], func=AF.Sqrt))
        sRs2 = P.val("sm")

        wait("sync", "vm", P.val("vm"))
        wait("sync", "sm", P.val("sm"))
        wait("sync", "sox0", P.val("sox0"))
        wait("sync", "sox1", P.val("sox1"))
        waits("vector", sm=sRs2)
        p6T = []
        ldp_t = {}
        def p6_load(j):
            buf = j % 2
            ldp_t[j] = s_dma(xt[:, buf, :], xpp_o[j * 128:(j + 1) * 128, :], sem="xq%d" % buf)
        p6_load(0)
        p6_load(1)
        def h2slice(j):
            return xnT[:, j // 4, (j % 4) * 1024:(j % 4 + 1) * 1024]
        for j in range(NT):
            buf = j % 2
            waits("vector", **{("xq%d" % buf): ldp_t[j]})
            v_op(lambda v, j=j, buf=buf: v.tensor_scalar(
                out=usb[:, :], in0=xt[:, buf, :], scalar1=stmu2[:, j:j + 1],
                scalar2=strs2[:, j:j + 1], op0=ALU.subtract, op1=ALU.mult))
            if j >= 2:
                wait("vector", "soh%d" % buf, 16 * soh_cnt[buf])
                wait("vector", "sow%d" % buf, 16 * sow_cnt[buf])
            v_op(lambda v, buf=buf: v.scalar_tensor_tensor(
                out=h2t[:, buf, :], in0=usb[:, :], scalar=1.0, in1=cb1t[:, :],
                op0=ALU.mult, op1=ALU.mult))
            v_op(lambda v, buf=buf: v.tensor_add(out=h2t[:, buf, :], in0=h2t[:, buf, :], in1=sbt[:, :]))
            vH2 = P.val("vm")
            wait("sync", "vm", vH2)
            s_dma(h2b_o[j * 128:(j + 1) * 128, :], h2t[:, buf, :], sem="soh%d" % buf)
            soh_cnt[buf] += 1
            v_op(lambda v, buf=buf: v.tensor_mul(out=gbscr[:, buf, :], in0=h2t[:, buf, :], in1=gbt[:, :]))
            vGB = P.val("vm")
            wait("sync", "vm", vGB)
            s_dma(gbh2_o[j * 128:(j + 1) * 128, :], gbscr[:, buf, :], sem="sow%d" % buf)
            sow_cnt[buf] += 1
            v_op(lambda v, j=j, buf=buf: v.tensor_copy(out=h2slice(j), in_=h2t[:, buf, :]))
            p6T.append(P.val("vm"))
            if j + 2 < NT:
                wait("sync", "vm", p6T[j])
                p6_load(j + 2)

        waits("tensor", vm=P.val("vm"))
        for j in range(NT):
            cpT = {}
            for dc in range(NDC + 2):
                if dc < NDC:
                    pp = dc % 2
                    pst = psTB0 if pp == 0 else psTB1
                    waits("tensor", sm=pstTick[pp])
                    t_op(lambda t, j=j, dc=dc, pst=pst: t.transpose(
                        pst[:, :], h2slice(j)[:, dc * 128:(dc + 1) * 128], idb[:, :]))
                if dc >= 2:
                    e = dc - 2
                    waits("tensor", sm=cpT[e])
                    t_op(lambda t, e=e: t.matmul(
                        psRT[0:8, 0:128], rwTb[:, e, :], h2fm[:, e % 2, :],
                        start=(e == 0), stop=(e == NDC - 1), skip_group_check=True))
                if dc < NDC:
                    pp = dc % 2
                    pst = psTB0 if pp == 0 else psTB1
                    waits("scalar", tm=P.val("tm"))
                    pstTick[pp] = sc_op(lambda s, dc=dc, pst=pst: s.copy(
                        out=h2fm[:, dc % 2, :], in_=pst[:, :]))
                    cpT[dc] = pstTick[pp]
            waits("scalar", tm=P.val("tm"), vm=P.val("vm"))
            sc_op(lambda s: s.copy(out=lgfm[:, :], in_=psRT[0:8, 0:128]))
            waits("tensor", sm=P.val("sm"))
            t_op(lambda t: t.transpose(psRT[0:128, 128:136], lgfm[:, :], idf[0:8, 0:8]))
            waits("vector", tm=P.val("tm"))
            v_op(lambda v, j=j: v.tensor_add(out=lgall[:, j, :], in0=psRT[0:128, 128:136], in1=rbt[:, :]))
        wait("sync", "vm", P.val("vm"))
        wait("sync", "sm", P.val("sm"))
        wait("sync", "sow0", P.val("sow0"))
        wait("sync", "sow1", P.val("sow1"))
        lg3 = lgall[:, :, :]
        lgF = d3(lgall[:, 0, :], [[1, NT * 8]])
        wf3 = d3(wfull[:, :], [[8, NT], [1, 8]])
        scr3 = d3(scrE[:, :], [[8, NT], [1, 8]])
        m1b3 = d3(m1a[:, :], [[1, NT], [0, 8]])
        m2b3 = d3(m2a[:, :], [[1, NT], [0, 8]])
        rzb3 = d3(rza[:, :], [[1, NT], [0, 8]])
        emb3v = d3(emb[:, :], [[7, NT], [1, 7]])
        wcols7 = d3(wfull[:, 0:1], [[8, NT], [1, 7]])

        def sv8(buf, e):
            return d3(buf[:, e:e + 1], [[8, NT]])

        def sv8l(e):
            return d3(lgall[:, 0, e:e + 1], [[8, NT]])

        v_op(lambda v: v.tensor_copy(out=m1a[:, :], in_=sv8l(0)))
        for e in range(1, 8):
            v_op(lambda v, e=e: v.tensor_tensor(out=m1a[:, :], in0=m1a[:, :], in1=sv8l(e), op=ALU.max))
        v_op(lambda v: v.tensor_tensor(out=scr3, in0=lg3, in1=m1b3, op=ALU.subtract))
        v_op(lambda v: v.tensor_tensor(out=wf3, in0=lg3, in1=m1b3, op=ALU.is_equal))
        v_op(lambda v: v.scalar_tensor_tensor(out=wfull[:, :], in0=wfull[:, :], scalar=-1e30, in1=lgF, op0=ALU.mult, op1=ALU.add))
        v_op(lambda v: v.tensor_copy(out=m2a[:, :], in_=sv8(wfull, 0)))
        for e in range(1, 8):
            v_op(lambda v, e=e: v.tensor_tensor(out=m2a[:, :], in0=m2a[:, :], in1=sv8(wfull, e), op=ALU.max))
        v_op(lambda v: v.tensor_tensor(out=wf3, in0=lg3, in1=m2b3, op=ALU.is_ge))
        waits("scalar", vm=P.val("vm"))
        sc_op(lambda s: s.activation(out=scrE[:, :], in_=scrE[:, :], func=AF.Exp))
        waits("vector", sm=P.val("sm"))
        v_op(lambda v: v.tensor_copy(out=za[:, :], in_=sv8(scrE, 0)))
        for e in range(1, 8):
            v_op(lambda v, e=e: v.tensor_tensor(out=za[:, :], in0=za[:, :], in1=sv8(scrE, e), op=ALU.add))
        v_op(lambda v: v.reciprocal(out=rza[:, :], in_=za[:, :]))
        v_op(lambda v: v.tensor_mul(out=wfull[:, :], in0=wfull[:, :], in1=scrE[:, :]))
        v_op(lambda v: v.tensor_tensor(out=wf3, in0=wf3, in1=rzb3, op=ALU.mult))
        vWall = P.val("vm")
        wait("sync", "vm", vWall)
        for j in range(NT):
            s_dma(w_o[j * 128:(j + 1) * 128, :], wfull[:, j * 8:j * 8 + 8], sem="sow0")
            sow_cnt[0] += 1
        v_op(lambda v: v.tensor_scalar(out=emb3v, in0=wcols7, scalar1=0.0, scalar2=None, op0=ALU.is_gt))
        v_op(lambda v: v.tensor_scalar_mul(out=gidall[:, :], in0=ciota[:, 0:NT], scalar1=128.0))
        v_op(lambda v: v.tensor_scalar(out=gidall[:, :], in0=gidall[:, :], scalar1=gid0[:, 0:1], scalar2=None, op0=ALU.add))
        vEmb = P.val("vm")
        waits("tensor", vm=vEmb)
        for j in range(NT):
            tC = t_op(lambda t, j=j: t.matmul(psY1[0:7, j:j + 1], emb[:, j * 7:(j + 1) * 7],
                                              conesc[:, :], start=True, stop=True, skip_group_check=True))
        waits("vector", tm=tC)
        v_op(lambda v: v.tensor_copy(out=cnt7[0:7, :], in_=psY1[0:7, 0:NT]))
        v_op(lambda v: v.memset(run7[0:7, 0:1], 0.0))
        v_op(lambda v: v.tensor_copy(out=run7[0:7, 1:NT], in_=cnt7[0:7, 0:NT - 1]))
        v_op(lambda v: v.tensor_tensor_scan(run7[0:7, :], bcast_free(conesc[0:7, 0:1], NT),
                                            run7[0:7, :], 0.0, ALU.mult, ALU.add))
        v_op(lambda v: v.tensor_tensor(out=idxf7[0:7, :], in0=run7[0:7, :],
                                       in1=bcast_free(ce4352[:, 0:1], NT), op=ALU.add))
        v_op(lambda v: v.tensor_copy(out=idxu7[0:7, :], in_=idxf7[0:7, :]))
        v_op(lambda v: v.tensor_add(out=countf[0:7, :], in0=cnt7[0:7, NT - 1:NT], in1=run7[0:7, NT - 1:NT]))
        vIdx = P.val("vm")
        wait("sync", "vm", vIdx)
        s_dma(counts_o[:, :], countf[0:7, :])
        evacR = {}
        for j in range(NT):
            off = 64 + (j % 4) * 8
            if j >= 4:
                wait("tensor", "sm", evacR[j - 4])
            tR = t_op(lambda t, j=j, off=off: t.matmul(psY0[:, off:off + 7], ctriu[:, :],
                                                       emb[:, j * 7:(j + 1) * 7], start=True, stop=True,
                                                       skip_group_check=True))
            wait("scalar", "tm", tR)
            evacR[j] = sc_op(lambda s, j=j, off=off: s.copy(out=rkb[:, j * 7:(j + 1) * 7], in_=psY0[:, off:off + 7]))

        scatticks = []
        ohticks = {0: 0, 1: 0}
        p7T = []
        l7x = {}
        l7g = {}
        def p7_load(j):
            buf = j % 2
            l7x[j] = s_dma(xt[:, buf, :], xpp_o[j * 128:(j + 1) * 128, :], sem="xq%d" % buf)
            l7g[j] = s_dma(gbscr[:, buf, :], gbh2_o[j * 128:(j + 1) * 128, :], sem="wq%d" % buf)
        p7_load(0)
        p7_load(1)
        waits("vector", sm=evacR[NT - 1])
        for j in range(NT):
            buf = j % 2
            if j >= 2:
                wait("vector", "sox%d" % buf, 16 * sox_cnt[buf])
            waits("vector", **{("xq%d" % buf): l7x[j], ("wq%d" % buf): l7g[j]})
            v_op(lambda v, j=j, buf=buf: v.scalar_tensor_tensor(
                out=xpt[:, buf, :], in0=gbscr[:, buf, :], scalar=wfull[:, j * 8 + 7:j * 8 + 8],
                in1=xt[:, buf, :], op0=ALU.mult, op1=ALU.add))
            vXf = P.val("vm")
            wait("sync", "vm", vXf)
            s_dma(xpp_o[j * 128:(j + 1) * 128, :], xpt[:, buf, :], sem="sox%d" % buf)
            sox_cnt[buf] += 1
            for e in range(7):
                ob = e % 2
                wait("vector", "tm", ohticks[ob])
                v_op(lambda v, j=j, e=e, ob=ob: v.tensor_scalar(
                    out=ohbuf[:, ob, :], in0=ciota[:, :], scalar1=rkb[:, j * 7 + e:j * 7 + e + 1],
                    scalar2=emb[:, j * 7 + e:j * 7 + e + 1], op0=ALU.is_equal, op1=ALU.mult))
                waits("tensor", vm=P.val("vm"))
                ohticks[ob] = t_op(lambda t, j=j, e=e, ob=ob: t.matmul(
                    psRT[:, 288 + e:289 + e], ohbuf[:, ob, :], gidall[:, j:j + 1],
                    start=True, stop=True, skip_group_check=True))
            waits("vector", tm=P.val("tm"))
            v_op(lambda v: v.tensor_copy(out=cmpst[:, :], in_=psRT[:, 288:295]))
            waits("tensor", vm=P.val("vm"))
            t_op(lambda t: t.transpose(psRT[0:7, 320:448], cmpst[:, :], idf[:, :]))
            waits("vector", tm=P.val("tm"))
            if j >= 2:
                wait("vector", "sc", scatticks[j - 2])
            cbuf = j % 2
            v_op(lambda v, cbuf=cbuf: v.tensor_copy(out=cmpT[0:7, cbuf, :], in_=psRT[0:7, 320:448]))
            vT2 = P.val("vm")
            wait("gpsimd", "vm", vT2)
            wait("gpsimd", "sc", P.val("sc"))
            def scat(g, cbuf=cbuf, j=j):
                g.indirect_dma_start(
                    out=lists_o[:, :], out_offset=IndirectOffsetOnAxis(idxu7[:, j:j + 1], 0),
                    in_=cmpT[0:7, cbuf, :], in_offset=None).then_inc(SC, 16)
            P.on("gpsimd", scat)
            scatticks.append(P.inc("sc", 16))
            p7T.append(P.val("vm"))
            if j + 2 < NT:
                wait("sync", "vm", p7T[j])
                p7_load(j + 2)

        wait("sync", "sq", P.val("sq"))
        for snm in ("soh0", "soh1", "sow0", "sow1", "sox0", "sox1", "xq0", "xq1", "wq0", "wq1"):
            wait("sync", snm, P.val(snm))
        wait("gpsimd", "sc", P.val("sc"))
        wait("gpsimd", "gq", P.val("gq"))
        loop.close()

        @block.sync
        def _(s):
            P.run("sync", s)

        @block.tensor
        def _(t):
            P.run("tensor", t)

        @block.vector
        def _(v):
            P.run("vector", v)

        @block.scalar
        def _(s):
            P.run("scalar", s)

        @block.gpsimd
        def _(g):
            P.run("gpsimd", g)

    return nc

def host_consts_l2(core_id):
    import ml_dtypes
    c = {}
    c["ident_bf16"] = np.eye(128, dtype=np.float32).astype(ml_dtypes.bfloat16)
    c["ident_f32b"] = np.eye(128, dtype=np.float32)
    c["trashrow"] = np.full((1, 8), GPAD, np.uint32)
    c["ident8"] = np.eye(8, dtype=np.float32)
    c["triu8"] = np.triu(np.ones((8, 8), np.float32), k=1)
    return c


def build_l2():
    nc = bass.Bass("TRN2", target_bir_lowering=False, debug=False)
    h2fa = nc.dram_tensor("h2fa", [NTOKG, 512], F32, kind="ExternalInput").ap()
    h2fb = nc.dram_tensor("h2fb", [NTOKG, 512], F32, kind="ExternalInput").ap()
    mylists = nc.dram_tensor("mylists", [8, LSTRIDE], F32, kind="ExternalInput").ap()
    counts_e = nc.dram_tensor("counts_e", [8, 1], F32, kind="ExternalInput").ap()
    wi = nc.dram_tensor("wi", [D, I], F32, kind="ExternalInput").ap()
    wo = nc.dram_tensor("wo", [I, D], F32, kind="ExternalInput").ap()
    ident_bf16 = nc.dram_tensor("ident_bf16", [128, 128], BF16, kind="ExternalInput").ap()
    ident_f32b = nc.dram_tensor("ident_f32b", [128, 128], F32, kind="ExternalInput").ap()
    trashrow = nc.dram_tensor("trashrow", [1, 8], U32, kind="ExternalInput").ap()
    ident8 = nc.dram_tensor("ident8", [8, 8], F32, kind="ExternalInput").ap()
    triu8 = nc.dram_tensor("triu8", [8, 8], F32, kind="ExternalInput").ap()
    eout = nc.dram_tensor("eout", [CAP, D], F32, kind="ExternalOutput").ap()
    glist_t = nc.dram_tensor("glist", [GPAD + 4096, 1], F32, kind="ExternalOutput")
    glist = glist_t.ap()
    glist2 = glist_t.ap()

    P = Plan()
    stack = ExitStack()
    with stack:
        idb = stack.enter_context(nc.sbuf_tensor("idb", [128, 128], BF16))
        idf2 = stack.enter_context(nc.sbuf_tensor("idf2", [128, 128], F32))
        idx2 = stack.enter_context(nc.sbuf_tensor("idx2", [2, 8], U32))
        seg2 = stack.enter_context(nc.sbuf_tensor("seg2", [2, LSTRIDE], F32))
        id8 = stack.enter_context(nc.sbuf_tensor("id8", [8, 8], F32))
        tr8 = stack.enter_context(nc.sbuf_tensor("tr8", [8, 8], F32))
        cnts = stack.enter_context(nc.sbuf_tensor("cnts", [8, 1], F32))
        cumo = stack.enter_context(nc.sbuf_tensor("cumo", [8, 1], F32))
        cumr = stack.enter_context(nc.sbuf_tensor("cumr", [1, 8], F32))
        cumru = stack.enter_context(nc.sbuf_tensor("cumru", [1, 8], U32))
        idlist = stack.enter_context(nc.sbuf_tensor("idlist", [128, 32], F32))
        idu = stack.enter_context(nc.sbuf_tensor("idu", [128, 32], U32))
        wi_sb = stack.enter_context(nc.sbuf_tensor("wi_sb", [128, 8, I], BF16))
        wo_sb = stack.enter_context(nc.sbuf_tensor("wo_sb", [128, 32, D], BF16))
        gbuf = stack.enter_context(nc.sbuf_tensor("gbuf", [128, 2, D], F32))
        h2fm = stack.enter_context(nc.sbuf_tensor("h2fm", [128, 8, 512], BF16))
        hbf = stack.enter_context(nc.sbuf_tensor("hbf", [128, 32, 512], BF16))
        outsb = stack.enter_context(nc.sbuf_tensor("outsb", [128, 2, D], F32))
        ps1a = stack.enter_context(nc.psum_tensor("ps1a", [128, 512], F32))
        ps1b = stack.enter_context(nc.psum_tensor("ps1b", [128, 512], F32))
        ps2a = stack.enter_context(nc.psum_tensor("ps2a", [128, 512], F32))
        ps2b = stack.enter_context(nc.psum_tensor("ps2b", [128, 512], F32))
        psta = stack.enter_context(nc.psum_tensor("psta", [128, 128], F32))
        pstb = stack.enter_context(nc.psum_tensor("pstb", [128, 128], F32))
        psm = stack.enter_context(nc.psum_tensor("psm", [8, 8], F32))
        SQ = stack.enter_context(nc.semaphore("sq"))
        GL = stack.enter_context(nc.semaphore("gl"))
        GA0 = stack.enter_context(nc.semaphore("ga0"))
        GA1 = stack.enter_context(nc.semaphore("ga1"))
        WIQ = stack.enter_context(nc.semaphore("wiq"))
        WOQ = stack.enter_context(nc.semaphore("woq"))
        EO0 = stack.enter_context(nc.semaphore("eo0"))
        EO1 = stack.enter_context(nc.semaphore("eo1"))
        TM = stack.enter_context(nc.semaphore("tm"))
        VM = stack.enter_context(nc.semaphore("vm"))
        SM = stack.enter_context(nc.semaphore("sm"))
        block = stack.enter_context(nc.Block())

        SEMS = {"sq": SQ, "gl": GL, "ga0": GA0, "ga1": GA1, "wiq": WIQ, "woq": WOQ,
                "eo0": EO0, "eo1": EO1, "tm": TM, "vm": VM, "sm": SM}

        def s_dma(out, in_, sem="sq"):
            S = SEMS[sem]
            def f(s, out=out, in_=in_, S=S):
                s.dma_start(out=out, in_=in_).then_inc(S, 16)
            P.on("sync", f)
            return P.inc(sem, 16)

        def g_dma(out, in_, sem):
            S = SEMS[sem]
            def f(g, out=out, in_=in_, S=S):
                g.dma_start(out=out, in_=in_).then_inc(S, 16)
            P.on("gpsimd", f)
            return P.inc(sem, 16)

        def _eng_op(eng, sem_name, fn, tick):
            prev = P.val(sem_name)
            S = SEMS[sem_name]
            if prev > 0:
                def w(e, S=S, prev=prev):
                    e.wait_ge(S, prev)
                P.on(eng, w)
            if tick:
                n = P.inc(sem_name, 1)
                def f(e, fn=fn, S=S):
                    fn(e).then_inc(S, 1)
            else:
                n = prev
                def f(e, fn=fn):
                    fn(e)
            P.on(eng, f)
            return n

        def t_op(fn, tick=True):
            return _eng_op("tensor", "tm", fn, tick)

        def v_op(fn, tick=True):
            return _eng_op("vector", "vm", fn, tick)

        def wait(eng_name, sem_name, val):
            if val <= 0:
                return
            sem = SEMS[sem_name]
            def f(e, sem=sem, val=val):
                e.wait_ge(sem, val)
            P.on(eng_name, f)

        def waits(eng_name, **kw):
            for k, v in kw.items():
                wait(eng_name, k, v)

        s_dma(idb[:, :], ident_bf16[:, :])
        s_dma(idf2[:, :], ident_f32b[:, :])
        s_dma(id8[:, :], ident8[:, :])
        s_dma(tr8[:, :], triu8[:, :])
        s_dma(cnts[:, :], counts_e[:, :])
        s_dma(idx2[1:2, :], trashrow[:, :])
        c0 = P.val("sq")
        wait("sync", "sq", c0)
        for pc in range(8):
            g_dma(wi_sb[:, pc, :], wi[pc * 128:(pc + 1) * 128, :], "wiq")
        wiT = P.val("wiq")
        for pc in range(32):
            g_dma(wo_sb[:, pc, :], wo[pc * 128:(pc + 1) * 128, :], "woq")
        woT = P.val("woq")

        waits("tensor", sq=c0)
        t_op(lambda t: t.matmul(psm[:, 0:1], tr8[:, :], cnts[:, :], start=True, stop=True))
        waits("vector", tm=P.val("tm"))
        v_op(lambda v: v.tensor_copy(out=cumo[:, :], in_=psm[:, 0:1]))
        v_op(lambda v: v.tensor_scalar_min(out=cumo[:, :], in0=cumo[:, :], scalar1=4224.0))
        waits("tensor", vm=P.val("vm"))
        t_op(lambda t: t.transpose(psm[0:1, 0:8], cumo[:, :], id8[:, :]))
        waits("vector", tm=P.val("tm"))
        v_op(lambda v: v.tensor_copy(out=cumr[:, :], in_=psm[0:1, 0:8]))
        v_op(lambda v: v.tensor_copy(out=cumru[:, :], in_=cumr[:, :]))
        vM0 = v_op(lambda v: v.memset(seg2[:, :], 0.0), tick=True)
        vCum = P.val("vm")
        wait("gpsimd", "vm", vCum)
        def _mvidx(g):
            g.dma_start(out=idx2[0:1, :], in_=cumru[0:1, :]).then_inc(GL, 16)
        P.on("gpsimd", _mvidx)
        P.inc("gl", 16)
        wait("gpsimd", "gl", P.val("gl"))
        wait("gpsimd", "vm", P.val("vm"))
        for b in range(8):
            nrow = 2 if b < 7 else 1
            def mv2(g, b=b, nrow=nrow):
                g.dma_start(out=seg2[0:nrow, :], in_=mylists[b:b + nrow, :]).then_inc(GL, 16)
            P.on("gpsimd", mv2)
            P.inc("gl", 16)
            wait("gpsimd", "gl", P.val("gl"))
            def seg(g, b=b):
                g.indirect_dma_start(
                    out=glist2[:, :], out_offset=IndirectOffsetOnAxis(idx2[:, b:b + 1], 0),
                    in_=seg2[0:2, 0:4096], in_offset=None).then_inc(GL, 16)
            P.on("gpsimd", seg)
            P.inc("gl", 16)
            wait("gpsimd", "gl", P.val("gl"))
        def ldid(s):
            src = glist[0:4096, 0:1].rearrange("(c p) a -> (a p) c", p=128)
            s.dma_start(out=idlist[:, :], in_=src, allow_slow_non_contiguous=True).then_inc(SQ, 16)
        wait("sync", "gl", P.val("gl"))
        P.on("sync", ldid)
        ilT = P.inc("sq", 16)
        wait("sync", "sq", ilT)
        waits("vector", sq=ilT)
        v_op(lambda v: v.tensor_scalar(out=idlist[:, :], in0=idlist[:, :], scalar1=0.0, scalar2=32767.0, op0=ALU.max, op1=ALU.min))
        v_op(lambda v: v.tensor_copy(out=idu[:, :], in_=idlist[:, :]))
        vIdu = P.val("vm")

        wait("gpsimd", "vm", vIdu)
        wait("tensor", "wiq", wiT)
        relu_ticks = []
        trT = []
        gtrT = {}
        for cc in range(NCC):
            for g4 in range(4):
                gslot = cc * 4 + g4
                gb = gslot % 2
                if gslot >= 2:
                    wait("gpsimd", "tm", gtrT[gslot - 2])
                gsem = SEMS["ga%d" % gb]
                def gat(g, cc=cc, g4=g4, gb=gb, gsem=gsem):
                    g.indirect_dma_start(
                        out=gbuf[:, gb, 0:512], out_offset=None,
                        in_=h2fa[:, :],
                        in_offset=IndirectOffsetOnAxis(idu[:, cc * 4 + g4:cc * 4 + g4 + 1], 0),
                    ).then_inc(gsem, 16)
                    g.indirect_dma_start(
                        out=gbuf[:, gb, 512:1024], out_offset=None,
                        in_=h2fb[:, :],
                        in_offset=IndirectOffsetOnAxis(idu[:, cc * 4 + g4:cc * 4 + g4 + 1], 0),
                    ).then_inc(gsem, 16)
                P.on("gpsimd", gat)
                P.inc("ga%d" % gb, 16)
                gaT = P.inc("ga%d" % gb, 16)
                wait("tensor", "ga%d" % gb, gaT)
                for dc in range(8):
                    it = g4 * 8 + dc
                    pst = psta if it % 2 == 0 else pstb
                    if True:
                        wait("tensor", "vm", max(P.val("vm") - 1, 0))
                    t_op(lambda t, gb=gb, dc=dc, pst=pst: t.transpose(
                        pst[:, :], gbuf[:, gb, dc * 128:(dc + 1) * 128], idf2[:, :]))
                    waits("vector", tm=P.val("tm"))
                    v_op(lambda v, g4=g4, dc=dc, pst=pst: v.tensor_copy(
                        out=h2fm[:, dc, g4 * 128:(g4 + 1) * 128], in_=pst[:, :]))
                gtrT[gslot] = P.val("tm")
            trT.append(P.val("tm"))
            vFM = P.val("vm")
            waits("tensor", vm=vFM)
            for ich in range(32):
                psu = ps1a if ich % 2 == 0 else ps1b
                if ich >= 2:
                    wait("tensor", "vm", relu_ticks[-2])
                for dc in range(8):
                    t_op(lambda t, ich=ich, dc=dc, psu=psu: t.matmul(
                        psu[:, :], wi_sb[:, dc, ich * 128:(ich + 1) * 128],
                        h2fm[:, dc, :], start=(dc == 0), stop=(dc == 7)),
                        tick=(dc == 7))
                waits("vector", tm=P.val("tm"))
                relu_ticks.append(v_op(lambda v, cc=cc, ich=ich, psu=psu: v.tensor_scalar_max(
                    out=hbf[:, ich, :], in0=psu[:, :], scalar1=0.0)))
            vRelu = P.val("vm")
            if cc == 0:
                wait("tensor", "woq", woT)
            waits("tensor", vm=vRelu)
            for csl in range(4):
                ob = csl % 2
                for hch in range(2):
                    psu = ps2a if hch == 0 else ps2b
                    for ich in range(32):
                        t_op(lambda t, csl=csl, hch=hch, ich=ich, psu=psu: t.matmul(
                            psu[:, :], hbf[:, ich, csl * 128:(csl + 1) * 128],
                            wo_sb[:, ich, hch * 512:(hch + 1) * 512],
                            start=(ich == 0), stop=(ich == 31)),
                            tick=(ich == 31))
                tmm2 = P.val("tm")
                eos = "eo%d" % ob
                wait("vector", eos, P.val(eos))
                waits("vector", tm=tmm2)
                v_op(lambda v, ob=ob: v.tensor_copy(out=outsb[:, ob, 0:512], in_=ps2a[:, :]))
                v_op(lambda v, ob=ob: v.tensor_copy(out=outsb[:, ob, 512:1024], in_=ps2b[:, :]))
                vEv = P.val("vm")
                wait("sync", "vm", vEv)
                row0 = cc * 512 + csl * 128
                s_dma(eout[row0:row0 + 128, :], outsb[:, ob, :], sem=eos)
                waits("tensor", vm=vEv)

        wait("sync", "sq", P.val("sq"))
        for snm in ("eo0", "eo1"):
            wait("sync", snm, P.val(snm))
        wait("gpsimd", "ga0", P.val("ga0"))
        wait("gpsimd", "ga1", P.val("ga1"))
        wait("gpsimd", "gl", P.val("gl"))
        wait("gpsimd", "wiq", P.val("wiq"))
        wait("gpsimd", "woq", P.val("woq"))

        @block.sync
        def _(s):
            P.run("sync", s)

        @block.tensor
        def _(t):
            P.run("tensor", t)

        @block.vector
        def _(v):
            P.run("vector", v)

        @block.scalar
        def _(s):
            P.run("scalar", s)

        @block.gpsimd
        def _(g):
            P.run("gpsimd", g)

    return nc

def host_consts_l3(core_id):
    c = {}
    c["triu_s"] = np.triu(np.ones((128, 128), np.float32), k=1)
    c["triu8"] = np.triu(np.ones((8, 8), np.float32), k=1)
    c["ones128"] = np.ones((128, 128), np.float32)
    c["onesrow"] = np.ones((1, 128), np.float32)
    c["selb"] = np.eye(8, dtype=np.float32)[:, core_id:core_id + 1].copy()
    c["e4096p1"] = np.tile((np.arange(7, dtype=np.float32) * CAP + 1.0)[None, :], (128, 1))
    return c



def build_l3():
    nc = bass.Bass("TRN2", target_bir_lowering=False, debug=False)
    xpp = nc.dram_tensor("xpp", [NTOK, D], F32, kind="ExternalInput").ap()
    wb = nc.dram_tensor("wb", [NTOK, 8], F32, kind="ExternalInput").ap()
    eoutf = nc.dram_tensor("eoutf", [7 * CAP, D], F32, kind="ExternalInput").ap()
    countsf = nc.dram_tensor("countsf", [8, 7], F32, kind="ExternalInput").ap()
    gbrow = nc.dram_tensor("gbrow", [1, D], F32, kind="ExternalInput").ap()
    triu_s = nc.dram_tensor("triu_s", [128, 128], F32, kind="ExternalInput").ap()
    triu8 = nc.dram_tensor("triu8", [8, 8], F32, kind="ExternalInput").ap()
    ones128 = nc.dram_tensor("ones128", [128, 128], F32, kind="ExternalInput").ap()
    onesrow = nc.dram_tensor("onesrow", [1, 128], F32, kind="ExternalInput").ap()
    selb = nc.dram_tensor("selb", [8, 1], F32, kind="ExternalInput").ap()
    e4096p1 = nc.dram_tensor("e4096p1", [128, 7], F32, kind="ExternalInput").ap()
    out_o = nc.dram_tensor("out", [NTOK, D], F32, kind="ExternalOutput").ap()

    def d3(ap, dims):
        return dataclasses.replace(ap, ap=type(ap.ap)([list(ap.ap[0])] + [list(d) for d in dims]))

    P = Plan()
    stack = ExitStack()
    with stack:
        ctriu = stack.enter_context(nc.sbuf_tensor("ctriu", [128, 128], F32))
        ctr8 = stack.enter_context(nc.sbuf_tensor("ctr8", [8, 8], F32))
        cones = stack.enter_context(nc.sbuf_tensor("cones", [128, 128], F32))
        conesr = stack.enter_context(nc.sbuf_tensor("conesr", [1, 128], F32))
        cselb = stack.enter_context(nc.sbuf_tensor("cselb", [8, 1], F32))
        ce4096 = stack.enter_context(nc.sbuf_tensor("ce4096", [128, 7], F32))
        cntf = stack.enter_context(nc.sbuf_tensor("cntf", [8, 7], F32))
        cumall = stack.enter_context(nc.sbuf_tensor("cumall", [8, 7], F32))
        baser = stack.enter_context(nc.sbuf_tensor("baser", [1, 7], F32))
        gbr = stack.enter_context(nc.sbuf_tensor("gbr", [1, D], F32))
        gbt = stack.enter_context(nc.sbuf_tensor("gbt", [128, D], F32))
        wful = stack.enter_context(nc.sbuf_tensor("wful", [128, NT, 8], F32))
        emf = stack.enter_context(nc.sbuf_tensor("emf", [128, NT * 7], F32))
        wkb = stack.enter_context(nc.sbuf_tensor("wkb", [128, NT * 7], F32))
        slotf = stack.enter_context(nc.sbuf_tensor("slotf", [128, NT * 7], F32))
        kept = stack.enter_context(nc.sbuf_tensor("kept", [128, NT * 7], F32))
        Am = stack.enter_context(nc.sbuf_tensor("Am", [128, NT * 7], F32))
        sel = stack.enter_context(nc.sbuf_tensor("sel", [128, NT * 7], F32))
        tmpb = stack.enter_context(nc.sbuf_tensor("tmpb", [128, NT * 7], F32))
        mm = stack.enter_context(nc.sbuf_tensor("mm", [128, NT], F32))
        rtmp = stack.enter_context(nc.sbuf_tensor("rtmp", [128, NT], F32))
        w1all = stack.enter_context(nc.sbuf_tensor("w1all", [128, NT], F32))
        w2all = stack.enter_context(nc.sbuf_tensor("w2all", [128, NT], F32))
        r1u32 = stack.enter_context(nc.sbuf_tensor("r1u32", [128, NT], U32))
        r2u32 = stack.enter_context(nc.sbuf_tensor("r2u32", [128, NT], U32))
        crow = stack.enter_context(nc.sbuf_tensor("crow", [1, NT * 7], F32))
        runsb = stack.enter_context(nc.sbuf_tensor("runsb", [1, NT * 7], F32))
        bcd = stack.enter_context(nc.sbuf_tensor("bcd", [1, 256], BF16))
        xt = stack.enter_context(nc.sbuf_tensor("xt", [128, 4, D], F32))
        g1 = stack.enter_context(nc.sbuf_tensor("g1", [128, 4, D], F32))
        g2 = stack.enter_context(nc.sbuf_tensor("g2", [128, 4, D], F32))
        outt = stack.enter_context(nc.sbuf_tensor("outt", [128, 4, D], F32))
        psA = stack.enter_context(nc.psum_tensor("psA", [128, 512], F32))
        psC = stack.enter_context(nc.psum_tensor("psC", [128, 224], F32))
        SQ = stack.enter_context(nc.semaphore("sq"))
        WQ0 = stack.enter_context(nc.semaphore("wq0"))
        XQ0 = stack.enter_context(nc.semaphore("xq0"))
        XQ1 = stack.enter_context(nc.semaphore("xq1"))
        XQ2 = stack.enter_context(nc.semaphore("xq2"))
        XQ3 = stack.enter_context(nc.semaphore("xq3"))
        GA0 = stack.enter_context(nc.semaphore("ga0"))
        GA1 = stack.enter_context(nc.semaphore("ga1"))
        GA2 = stack.enter_context(nc.semaphore("ga2"))
        GA3 = stack.enter_context(nc.semaphore("ga3"))
        OQ0 = stack.enter_context(nc.semaphore("oq0"))
        OQ1 = stack.enter_context(nc.semaphore("oq1"))
        OQ2 = stack.enter_context(nc.semaphore("oq2"))
        OQ3 = stack.enter_context(nc.semaphore("oq3"))
        TM = stack.enter_context(nc.semaphore("tm"))
        VM = stack.enter_context(nc.semaphore("vm"))
        SM = stack.enter_context(nc.semaphore("sm"))
        block = stack.enter_context(nc.Block())

        SEMS = {"sq": SQ, "wq0": WQ0, "xq0": XQ0, "xq1": XQ1, "xq2": XQ2,
                "xq3": XQ3, "ga0": GA0, "ga1": GA1, "ga2": GA2, "ga3": GA3,
                "oq0": OQ0, "oq1": OQ1, "oq2": OQ2, "oq3": OQ3,
                "tm": TM, "vm": VM, "sm": SM}

        def s_dma(out, in_, sem="sq"):
            S = SEMS[sem]
            def f(s, out=out, in_=in_, S=S):
                s.dma_start(out=out, in_=in_).then_inc(S, 16)
            P.on("sync", f)
            return P.inc(sem, 16)

        def _eng_op(eng, sem_name, fn, tick):
            prev = P.val(sem_name)
            S = SEMS[sem_name]
            if prev > 0:
                def w(e, S=S, prev=prev):
                    e.wait_ge(S, prev)
                P.on(eng, w)
            if tick:
                n = P.inc(sem_name, 1)
                def f(e, fn=fn, S=S):
                    fn(e).then_inc(S, 1)
            else:
                n = prev
                def f(e, fn=fn):
                    fn(e)
            P.on(eng, f)
            return n

        def t_op(fn, tick=True):
            return _eng_op("tensor", "tm", fn, tick)

        def v_op(fn, tick=True):
            return _eng_op("vector", "vm", fn, tick)

        def sc_op(fn, tick=True):
            return _eng_op("scalar", "sm", fn, tick)

        def wait(eng_name, sem_name, val):
            if val <= 0:
                return
            sem = SEMS[sem_name]
            def f(e, sem=sem, val=val):
                e.wait_ge(sem, val)
            P.on(eng_name, f)

        def waits(eng_name, **kw):
            for k, v in kw.items():
                wait(eng_name, k, v)

        def sv(buf, e):
            return d3(buf[:, e:e + 1], [[7, NT]])

        emf3 = d3(emf[:, :], [[7, NT], [1, 7]])
        wkb3 = d3(wkb[:, :], [[7, NT], [1, 7]])
        slotf3 = d3(slotf[:, :], [[7, NT], [1, 7]])
        Am3 = d3(Am[:, :], [[7, NT], [1, 7]])
        sel3 = d3(sel[:, :], [[7, NT], [1, 7]])
        ce3 = d3(ce4096[:, :], [[0, NT], [1, 7]])
        mmb3 = d3(mm[:, :], [[1, NT], [0, 7]])

        for dst, src_ in [(ctriu, triu_s), (ctr8, triu8), (cones, ones128),
                          (conesr, onesrow), (cselb, selb), (ce4096, e4096p1),
                          (cntf, countsf), (gbr, gbrow)]:
            sl = tuple(slice(None) for _ in range(len(dst.shape)))
            s_dma(dst[sl], src_[sl])
        c0 = P.val("sq")
        for j in range(NT):
            ldwf = s_dma(wful[:, j, :], wb[j * 128:(j + 1) * 128, :], sem="wq0")
        wait("sync", "sq", c0)
        waits("vector", sq=c0)
        v_op(lambda v: v.tensor_copy(out=bcd[0:1, :], in_=conesr[0:1, :].bitcast(BF16)))
        waits("tensor", sq=c0)
        t_op(lambda t: t.matmul(psA[0:8, 0:7], ctr8[:, :], cntf[:, :], start=True, stop=True))
        waits("vector", tm=P.val("tm"))
        v_op(lambda v: v.tensor_copy(out=cumall[:, :], in_=psA[0:8, 0:7]))
        waits("tensor", vm=P.val("vm"))
        t_op(lambda t: t.matmul(psA[0:1, 0:7], cselb[:, :], cumall[:, :], start=True, stop=True))
        waits("vector", tm=P.val("tm"))
        v_op(lambda v: v.tensor_copy(out=baser[:, :], in_=psA[0:1, 0:7]))
        for half in range(2):
            waits("tensor", vm=P.val("vm"))
            t_op(lambda t, half=half: t.matmul(psA[:, 0:512], conesr[:, :],
                                               gbr[0:1, half * 512:half * 512 + 512],
                                               start=True, stop=True))
            waits("vector", tm=P.val("tm"))
            v_op(lambda v, half=half: v.tensor_copy(out=gbt[:, half * 512:half * 512 + 512], in_=psA[:, 0:512]))

        waits("vector", wq0=ldwf)
        v_op(lambda v: v.tensor_scalar(out=emf3, in0=wful[:, :, 0:7], scalar1=0.0, scalar2=None, op0=ALU.is_gt))
        v_op(lambda v: v.tensor_copy(out=wkb3, in_=wful[:, :, 0:7]))
        vMask = P.val("vm")
        waits("tensor", vm=vMask)
        for j in range(NT):
            tCnt = t_op(lambda t, j=j: t.matmul(psC[0:1, j * 7:(j + 1) * 7], cones[:, 0:1],
                                                emf[:, j * 7:(j + 1) * 7], start=True, stop=True,
                                                skip_group_check=True))
        waits("vector", tm=tCnt)
        v_op(lambda v: v.tensor_copy(out=crow[0:1, :], in_=psC[0:1, :]))
        v_op(lambda v: v.tensor_copy(out=runsb[0:1, 0:7], in_=baser[0:1, :]))
        v_op(lambda v: v.tensor_copy(out=runsb[0:1, 7:NT * 7], in_=crow[0:1, 0:(NT - 1) * 7]))
        for e in range(7):
            v_op(lambda v, e=e: v.tensor_tensor_scan(
                d3(runsb[0:1, e:e + 1], [[7, NT]]), bcast_free(conesr[0:1, 0:1], NT),
                d3(runsb[0:1, e:e + 1], [[7, NT]]), 0.0, ALU.mult, ALU.add))
        vRun = P.val("vm")

        waits("tensor", vm=vRun)
        evacS = {}
        for j in range(NT):
            off = (j % 4) * 8
            if j >= 4:
                wait("tensor", "sm", evacS[j - 4])
            t_op(lambda t, j=j, off=off: t.matmul(psA[:, off:off + 7], ctriu[:, :],
                                                  emf[:, j * 7:(j + 1) * 7], start=True, stop=False,
                                                  skip_group_check=True))
            tR = t_op(lambda t, j=j, off=off: t.matmul(psA[:, off:off + 7], conesr[0:1, :],
                                                       runsb[0:1, j * 7:(j + 1) * 7], start=False, stop=True,
                                                       skip_group_check=True))
            wait("scalar", "tm", tR)
            evacS[j] = sc_op(lambda s, j=j, off=off: s.activation(
                out=slotf[:, j * 7:(j + 1) * 7], in_=psA[:, off:off + 7], func=AF.Copy))

        waits("vector", sm=evacS[NT - 1])
        v_op(lambda v: v.tensor_scalar(out=kept[:, :], in0=slotf[:, :], scalar1=float(CAP), scalar2=None, op0=ALU.is_lt))
        v_op(lambda v: v.tensor_mul(out=kept[:, :], in0=kept[:, :], in1=emf[:, :]))
        v_op(lambda v: v.tensor_tensor(out=Am3, in0=slotf3, in1=ce3, op=ALU.add))
        v_op(lambda v: v.tensor_mul(out=Am[:, :], in0=Am[:, :], in1=kept[:, :]))
        v_op(lambda v: v.tensor_copy(out=mm[:, :], in_=sv(Am, 0)))
        for e in range(1, 7):
            v_op(lambda v, e=e: v.tensor_tensor(out=mm[:, :], in0=mm[:, :], in1=sv(Am, e), op=ALU.max))
        v_op(lambda v: v.tensor_tensor(out=sel3, in0=Am3, in1=mmb3, op=ALU.is_equal))
        v_op(lambda v: v.tensor_mul(out=sel[:, :], in0=sel[:, :], in1=kept[:, :]))
        v_op(lambda v: v.tensor_mul(out=tmpb[:, :], in0=sel[:, :], in1=wkb[:, :]))
        v_op(lambda v: v.tensor_copy(out=w1all[:, :], in_=sv(tmpb, 0)))
        for e in range(1, 7):
            v_op(lambda v, e=e: v.tensor_tensor(out=w1all[:, :], in0=w1all[:, :], in1=sv(tmpb, e), op=ALU.add))
        v_op(lambda v: v.tensor_scalar(out=rtmp[:, :], in0=mm[:, :], scalar1=-1.0, scalar2=0.0, op0=ALU.add, op1=ALU.max))
        v_op(lambda v: v.tensor_copy(out=r1u32[:, :], in_=rtmp[:, :]))
        v_op(lambda v: v.tensor_scalar(out=sel[:, :], in0=sel[:, :], scalar1=-1.0, scalar2=1.0, op0=ALU.mult, op1=ALU.add))
        v_op(lambda v: v.tensor_mul(out=kept[:, :], in0=kept[:, :], in1=sel[:, :]))
        v_op(lambda v: v.tensor_mul(out=Am[:, :], in0=Am[:, :], in1=sel[:, :]))
        v_op(lambda v: v.tensor_copy(out=mm[:, :], in_=sv(Am, 0)))
        for e in range(1, 7):
            v_op(lambda v, e=e: v.tensor_tensor(out=mm[:, :], in0=mm[:, :], in1=sv(Am, e), op=ALU.max))
        v_op(lambda v: v.tensor_tensor(out=sel3, in0=Am3, in1=mmb3, op=ALU.is_equal))
        v_op(lambda v: v.tensor_mul(out=sel[:, :], in0=sel[:, :], in1=kept[:, :]))
        v_op(lambda v: v.tensor_mul(out=tmpb[:, :], in0=sel[:, :], in1=wkb[:, :]))
        v_op(lambda v: v.tensor_copy(out=w2all[:, :], in_=sv(tmpb, 0)))
        for e in range(1, 7):
            v_op(lambda v, e=e: v.tensor_tensor(out=w2all[:, :], in0=w2all[:, :], in1=sv(tmpb, e), op=ALU.add))
        v_op(lambda v: v.tensor_scalar(out=rtmp[:, :], in0=mm[:, :], scalar1=-1.0, scalar2=0.0, op0=ALU.add, op1=ALU.max))
        v_op(lambda v: v.tensor_copy(out=r2u32[:, :], in_=rtmp[:, :]))
        vMeta = P.val("vm")

        cmbT = {}
        stT = {}
        gac = [0, 0, 0, 0]
        xqc = [0, 0, 0, 0]
        oqc = [0, 0, 0, 0]
        for j in range(min(4, NT)):
            s_dma(xt[:, j, :], xpp[j * 128:(j + 1) * 128, :], sem="xq%d" % j)
            xqc[j] += 1
        for j in range(NT):
            buf = j % 4
            ges = "ga%d" % buf
            if j == 0:
                wait("gpsimd", "vm", vMeta)
            if j >= 4:
                wait("gpsimd", "vm", cmbT[j - 4])
            def gat1(g, buf=buf, j=j):
                g.indirect_dma_start(out=g1[:, buf, :], out_offset=None, in_=eoutf[:, :],
                                     in_offset=IndirectOffsetOnAxis(r1u32[:, j:j + 1], 0)).then_inc(SEMS["ga%d" % buf], 16)
            P.on("gpsimd", gat1)
            P.inc(ges, 16)
            def gat2(g, buf=buf, j=j):
                g.indirect_dma_start(out=g2[:, buf, :], out_offset=None, in_=eoutf[:, :],
                                     in_offset=IndirectOffsetOnAxis(r2u32[:, j:j + 1], 0)).then_inc(SEMS["ga%d" % buf], 16)
            P.on("gpsimd", gat2)
            P.inc(ges, 16)
            gac[buf] += 1
            waits("vector", **{ges: 32 * gac[buf], ("xq%d" % buf): 16 * xqc[buf]})
            if j >= 4:
                wait("vector", "oq%d" % buf, stT[j - 4])
            v_op(lambda v, buf=buf, j=j: v.tensor_scalar_mul(out=outt[:, buf, :], in0=g1[:, buf, :], scalar1=w1all[:, j:j + 1]))
            v_op(lambda v, buf=buf, j=j: v.scalar_tensor_tensor(out=outt[:, buf, :], in0=g2[:, buf, :], scalar=w2all[:, j:j + 1], in1=outt[:, buf, :], op0=ALU.mult, op1=ALU.add))
            v_op(lambda v, buf=buf: v.tensor_mul(out=outt[:, buf, :], in0=outt[:, buf, :], in1=gbt[:, :]))
            v_op(lambda v, buf=buf: v.tensor_add(out=outt[:, buf, :], in0=outt[:, buf, :], in1=xt[:, buf, :]))
            cmbT[j] = P.val("vm")
            wait("sync", "vm", cmbT[j])
            s_dma(out_o[j * 128:(j + 1) * 128, :], outt[:, buf, :], sem="oq%d" % buf)
            oqc[buf] += 1
            stT[j] = 16 * oqc[buf]
            if j + 4 < NT:
                s_dma(xt[:, buf, :], xpp[(j + 4) * 128:(j + 5) * 128, :], sem="xq%d" % buf)
                xqc[buf] += 1

        for s_ in ("oq0", "oq1", "oq2", "oq3", "sq", "wq0", "xq0", "xq1", "xq2", "xq3"):
            wait("sync", s_, P.val(s_))
        for s_ in ("ga0", "ga1", "ga2", "ga3"):
            wait("gpsimd", s_, P.val(s_))

        @block.sync
        def _(s):
            P.run("sync", s)

        @block.tensor
        def _(t):
            P.run("tensor", t)

        @block.vector
        def _(v):
            P.run("vector", v)

        @block.scalar
        def _(s):
            P.run("scalar", s)

        @block.gpsimd
        def _(g):
            P.run("gpsimd", g)

    return nc


_CACHE = {}


def _get_programs():
    if "nc1" not in _CACHE:
        _CACHE["nc1"] = build_l1()
        _CACHE["nc2"] = build_l2()
        _CACHE["nc3"] = build_l3()
        _CACHE["c1"] = [host_consts_l1(b) for b in range(8)]
        _CACHE["c2"] = [host_consts_l2(e) for e in range(8)]
        _CACHE["c3"] = [host_consts_l3(b) for b in range(8)]
    return _CACHE


def kernel(L0_attn_Z, L0_attn_decay, L0_attn_W, L0_attn_b, L0_attn_gamma,
           L0_attn_beta, L0_ffn_router_w, L0_ffn_router_b, L0_ffn_experts_inter,
           L0_ffn_experts_out, L0_c, L0_wmod):
    C = _get_programs()
    x = np.ascontiguousarray(np.asarray(L0_attn_Z, dtype=np.float32))
    W = np.ascontiguousarray(np.asarray(L0_attn_W, dtype=np.float32))
    battn = np.ascontiguousarray(np.asarray(L0_attn_b, dtype=np.float32)).reshape(1, D)
    gamma = np.ascontiguousarray(np.asarray(L0_attn_gamma, dtype=np.float32)).reshape(1, D)
    beta = np.ascontiguousarray(np.asarray(L0_attn_beta, dtype=np.float32)).reshape(1, D)
    decay = np.ascontiguousarray(np.asarray(L0_attn_decay, dtype=np.float32)).reshape(1, 1)
    rw = np.ascontiguousarray(np.asarray(L0_ffn_router_w, dtype=np.float32))
    rb = np.ascontiguousarray(np.asarray(L0_ffn_router_b, dtype=np.float32)).reshape(1, 8)
    wi_all = np.asarray(L0_ffn_experts_inter, dtype=np.float32)
    wo_all = np.asarray(L0_ffn_experts_out, dtype=np.float32)
    cfull = np.ascontiguousarray(np.asarray(L0_c, dtype=np.float32))
    wmod = np.ascontiguousarray(np.asarray(L0_wmod, dtype=np.float32))

    in1 = []
    for b in range(8):
        m = dict(C["c1"][b])
        m["x_in"] = np.ascontiguousarray(x[b])
        m["W_in"] = W
        m["battn"] = battn
        m["gamma"] = gamma
        m["beta"] = beta
        m["decay"] = decay
        m["rw"] = rw
        m["rb"] = rb
        m["cfull"] = cfull
        m["wmod"] = wmod
        in1.append(m)
    r1 = run_bass_kernel_spmd(C["nc1"], in1, core_ids=list(range(8))).results

    h2_full = np.concatenate([np.asarray(r1[b]["h2b"]) for b in range(8)], axis=0)
    h2a = np.ascontiguousarray(h2_full[:, 0:512])
    h2b_ = np.ascontiguousarray(h2_full[:, 512:1024])
    lists_all = np.stack([np.asarray(r1[b]["lists"]).reshape(7, LSTRIDE) for b in range(8)])
    counts_all = np.stack([np.asarray(r1[b]["counts"]).reshape(7) for b in range(8)])

    in2 = []
    for k in range(8):
        e = k if k < 7 else 0
        m = dict(C["c2"][k])
        m["h2fa"] = h2a
        m["h2fb"] = h2b_
        m["mylists"] = np.ascontiguousarray(lists_all[:, e, :])
        m["counts_e"] = np.ascontiguousarray(counts_all[:, e:e + 1])
        m["wi"] = np.ascontiguousarray(wi_all[e])
        m["wo"] = np.ascontiguousarray(wo_all[e])
        in2.append(m)
    r2 = run_bass_kernel_spmd(C["nc2"], in2, core_ids=list(range(8))).results
    eoutf = np.concatenate([np.asarray(r2[k]["eout"]) for k in range(7)], axis=0)

    in3 = []
    for b in range(8):
        m = dict(C["c3"][b])
        m["xpp"] = np.asarray(r1[b]["xpp"])
        m["wb"] = np.asarray(r1[b]["wout"])
        m["eoutf"] = eoutf
        m["countsf"] = np.ascontiguousarray(counts_all)
        m["gbrow"] = np.asarray(r1[b]["gbrow"])
        in3.append(m)
    r3 = run_bass_kernel_spmd(C["nc3"], in3, core_ids=list(range(8))).results
    out = np.stack([np.asarray(r3[b]["out"]) for b in range(8)], axis=0)
    return out.astype(np.float32, copy=False)
